# version 07
# baseline (speedup 1.0000x reference)
import math
import numpy as np
import ml_dtypes
import concourse.bass as bass
import concourse.mybir as mybir
from concourse.bass_utils import run_bass_kernel_spmd

BFNP = ml_dtypes.bfloat16
F32 = mybir.dt.float32
BF16 = mybir.dt.bfloat16
AF = mybir.ActivationFunctionType
ALU = mybir.AluOpType
AX = mybir.AxisListType

N, D, H, HD, C = 2048, 1024, 16, 64, 128
NCH = N // C
NT = N // 128
ND = D // 128
NE = 8
SCALE = HD ** -0.5

ENGINES = ("tensor", "vector", "scalar", "gpsimd", "sync")

FCONST_VALS = (1e-5, 1e-6, -1.0, 2.0, 0.5, -0.5, SCALE, -SCALE)
FCONST_IDX = {v: i for i, v in enumerate(FCONST_VALS)}


class Op:
    __slots__ = ("eng", "fn", "deps", "inc_k", "sem", "sem_after", "pos",
                 "needs_inc", "pre_wait")

    def __init__(self, eng, fn, deps, inc_k):
        self.eng = eng
        self.fn = fn
        self.deps = deps
        self.inc_k = inc_k
        self.sem = None
        self.sem_after = None
        self.pos = None
        self.needs_inc = False
        self.pre_wait = None


class Buf:

    def __init__(self):
        self.w = None
        self.readers = []


class Sched:
    def __init__(self):
        self.ops = []

    def emit(self, eng, fn=None, reads=(), writes=(), deps=(), inc_k=1):
        if fn is None:
            return None
        d = [t for t in deps if t is not None]
        for b in reads:
            if b.w is not None:
                d.append(b.w)
        for b in writes:
            if b.w is not None:
                d.append(b.w)
            d.extend(b.readers)
        op = Op(eng, fn, d, inc_k)
        op.pos = len(self.ops)
        self.ops.append(op)
        for b in reads:
            b.readers.append(op)
        for b in writes:
            b.w = op
            b.readers = []
        return op

    def finalize(self, nc, paranoid=False):
        for op in self.ops:
            if op.inc_k == 16:
                op.needs_inc = True
            for dep in op.deps:
                assert dep.pos < op.pos
                if dep.eng != op.eng or dep.inc_k == 16 or paranoid:
                    dep.needs_inc = True
        eng_sems = {e: nc.alloc_semaphore(f"sem_{e}") for e in ENGINES}
        counts = {e: 0 for e in ENGINES}
        NPOOL = 8
        pool = {e: [nc.alloc_semaphore(f"dma_{e}_{i}") for i in range(NPOOL)]
                for e in ("sync", "gpsimd", "scalar")}
        pool_idx = {e: 0 for e in pool}
        pool_count = {}
        for op in self.ops:
            if op.inc_k == 16:
                sem = pool[op.eng][pool_idx[op.eng] % NPOOL]
                pool_idx[op.eng] += 1
                prev = pool_count.get(id(sem), 0)
                if prev:
                    op.pre_wait = (sem, prev)
                op.sem = sem
                op.sem_after = prev + 16
                pool_count[id(sem)] = op.sem_after
            elif op.needs_inc:
                counts[op.eng] += op.inc_k
                op.sem = eng_sems[op.eng]
                op.sem_after = counts[op.eng]
        marks = {e: {} for e in ENGINES}
        streams = {e: [o for o in self.ops if o.eng == e] for e in ENGINES}
        need_drain = set()
        vstream = streams["vector"]
        vpos = {id(o): i for i, o in enumerate(vstream)}
        for i, op in enumerate(vstream):
            for dep in op.deps:
                if dep.eng == "vector" and vpos[id(dep)] == i - 1:
                    need_drain.add(id(op))

        def body(eng_obj, ename):
            wm = marks[ename]
            for op in streams[ename]:
                for dep in op.deps:
                    if dep.eng == ename and dep.inc_k != 16 and not paranoid:
                        continue
                    assert dep.sem_after is not None, (dep.eng, op.eng)
                    if dep.sem_after > wm.get(id(dep.sem), 0):
                        eng_obj.wait_ge(dep.sem, dep.sem_after)
                        wm[id(dep.sem)] = dep.sem_after
                if op.pre_wait is not None:
                    sem, val = op.pre_wait
                    if val > wm.get(id(sem), 0):
                        eng_obj.wait_ge(sem, val)
                        wm[id(sem)] = val
                if id(op) in need_drain:
                    eng_obj.drain()
                instr = op.fn(eng_obj)
                if op.sem is not None:
                    instr.then_inc(op.sem, op.inc_k)

        with nc.Block() as blk:
            @blk.tensor
            def _(e):
                body(e, "tensor")

            @blk.vector
            def _(e):
                body(e, "vector")

            @blk.scalar
            def _(e):
                body(e, "scalar")

            @blk.gpsimd
            def _(e):
                body(e, "gpsimd")

            @blk.sync
            def _(e):
                body(e, "sync")


def host_constants():
    c = {}
    ident = np.eye(128, dtype=np.float32)
    c["id128"] = ident.astype(BFNP)
    c["id128f"] = ident
    c["cumtri"] = np.triu(np.ones((128, 128), np.float32)).astype(BFNP)
    c["allones"] = np.ones((128, 128), np.float32).astype(BFNP)
    c["m_trils"] = np.tril(np.ones((128, 128), np.float32), -1).astype(BFNP)
    c["m_trius"] = np.triu(np.ones((128, 128), np.float32), 1).astype(BFNP)
    c["m_negtriu"] = (-np.triu(np.ones((128, 128), np.float32))).astype(BFNP)
    c["m_addneg"] = np.triu(np.full((128, 128), -1e9, np.float32), 1).astype(BFNP)
    c["i64stack"] = np.concatenate([np.eye(64, dtype=np.float32)] * 2, 0)
    c["ones_row_bf"] = np.ones((1, 512), np.float32).astype(BFNP)
    c["ones_row_f32"] = np.ones((1, 128), np.float32)
    c["fconst"] = np.ascontiguousarray(np.broadcast_to(
        np.array(FCONST_VALS, np.float32)[None, :], (128, len(FCONST_VALS))))
    return c


def build(dbg=None, paranoid=False, parts=("ln", "gate", "thin", "fc"), hp_count=ND):
    dbg = dbg or []
    parts = set(parts)
    nc = bass.Bass("TRN2", target_bir_lowering=False,
                   detect_race_conditions=paranoid)
    s = Sched()

    x_d = nc.dram_tensor("x", [N, D], F32, kind="ExternalInput").ap()
    wq_d = nc.dram_tensor("wq", [D, D], F32, kind="ExternalInput").ap()
    wk_d = nc.dram_tensor("wk", [D, D], F32, kind="ExternalInput").ap()
    wv_d = nc.dram_tensor("wv", [D, D], F32, kind="ExternalInput").ap()
    wo_d = nc.dram_tensor("wo", [D, D], F32, kind="ExternalInput").ap()
    ww1_d = nc.dram_tensor("ww1", [D, HD], F32, kind="ExternalInput").ap()
    ww2_d = nc.dram_tensor("ww2", [HD, D], F32, kind="ExternalInput").ap()
    wb_d = nc.dram_tensor("wbm", [D, H], F32, kind="ExternalInput").ap()
    wf_d = nc.dram_tensor("wfm", [D, H], F32, kind="ExternalInput").ap()
    delta_bc_d = nc.dram_tensor("delta_bc", [128, H], F32, kind="ExternalInput").ap()
    wg_d = nc.dram_tensor("wg", [D, NE], F32, kind="ExternalInput").ap()
    bg_bc_d = nc.dram_tensor("bg_bc", [128, NE], F32, kind="ExternalInput").ap()
    w1_d = nc.dram_tensor("w1", [NE, D, D], F32, kind="ExternalInput").ap()
    b1_d = nc.dram_tensor("b1", [NE, D], F32, kind="ExternalInput").ap()
    w2_d = nc.dram_tensor("w2", [NE, D, D], F32, kind="ExternalInput").ap()
    b2_d = nc.dram_tensor("b2", [NE, D], F32, kind="ExternalInput").ap()
    cst_d = {}
    consts = host_constants()
    for k, v in consts.items():
        dt = BF16 if v.dtype == BFNP else F32
        cst_d[k] = nc.dram_tensor(f"c_{k}", list(v.shape), dt, kind="ExternalInput").ap()
    out_d = nc.dram_tensor("out", [N, D], F32, kind="ExternalOutput").ap()

    dbg_d = {}
    dbg_bufs = {}

    def dbg_dump(name, ap_fn, shape, dtype, deps):
        if name not in dbg:
            return
        dd = nc.dram_tensor(f"dbg_{name}", shape, dtype, kind="ExternalOutput").ap()
        dbg_d[name] = dd
        s.emit("sync", lambda e, a=ap_fn, d=dd: e.dma_start(out=d, in_=a()),
               deps=deps, inc_k=16)

    cst = {}
    cst_buf = {}
    for k, v in consts.items():
        dt = BF16 if v.dtype == BFNP else F32
        t = nc.alloc_sbuf_tensor(f"sb_{k}", list(v.shape), dt)
        cst[k] = t
        b = Buf()
        cst_buf[k] = b
        s.emit("sync", lambda e, t=t, k=k: e.dma_start(out=t[:, :], in_=cst_d[k][:, :]),
               writes=[b], inc_k=16)

    def CC(name):
        return cst[name], cst_buf[name]

    def fc(v):
        return cst["fconst"][:, FCONST_IDX[v]:FCONST_IDX[v] + 1]

    xn_bf = nc.alloc_sbuf_tensor("xn_bf", [128, ND, N], BF16)
    o_fm = nc.alloc_sbuf_tensor("o_fm", [128, ND, N], BF16)
    y_acc = nc.alloc_sbuf_tensor("y_acc", [128, NT, D], BF16)
    fc_tm = nc.alloc_sbuf_tensor("fc_tm", [128, NT, H], F32)
    sig_tm = nc.alloc_sbuf_tensor("sig_tm", [128, NT, H], BF16)
    logf_bf = nc.alloc_sbuf_tensor("logf_bf", [128, NT, H], BF16)
    gmask = nc.alloc_sbuf_tensor("gmask", [128, NT, NE], F32)
    w1o = nc.alloc_sbuf_tensor("w1o", [128, N // 2], BF16)
    wg_sb = nc.alloc_sbuf_tensor("wg_sb", [128, ND, NE], F32)
    wb_sb = nc.alloc_sbuf_tensor("wb_sb", [128, ND, H], BF16)
    wf_sb = nc.alloc_sbuf_tensor("wf_sb", [128, ND, H], BF16)
    ww1_sb = nc.alloc_sbuf_tensor("ww1_sb", [128, ND, HD], BF16)
    delta_bc = nc.alloc_sbuf_tensor("delta_sb", [128, H], F32)
    bg_bc = nc.alloc_sbuf_tensor("bg_sb", [128, NE], F32)

    b_xn = Buf()
    b_yacc_all = Buf()
    b_o = [[Buf() for _ in range(NCH)] for _ in range(H)]
    b_yacc = [Buf() for _ in range(NT)]
    b_fc = Buf()
    b_sig = Buf()
    b_logf = Buf()
    b_gmask = Buf()
    b_w1o = Buf()
    b_smallw = Buf()

    s.emit("gpsimd", lambda e: e.dma_start(
        out=wb_sb[:, :, :], in_=wb_d.rearrange("(t p) h -> p t h", p=128)),
        writes=[b_smallw], inc_k=16)
    s.emit("gpsimd", lambda e: e.dma_start(
        out=wf_sb[:, :, :], in_=wf_d.rearrange("(t p) h -> p t h", p=128)),
        writes=[b_smallw], inc_k=16)
    s.emit("gpsimd", lambda e: e.dma_start(
        out=ww1_sb[:, :, :], in_=ww1_d.rearrange("(t p) h -> p t h", p=128)),
        writes=[b_smallw], inc_k=16)
    s.emit("sync", lambda e: e.dma_start(
        out=wg_sb[:, :, :], in_=wg_d.rearrange("(t p) h -> p t h", p=128)),
        writes=[b_smallw], inc_k=16)
    s.emit("sync", lambda e: e.dma_start(out=delta_bc[:, :], in_=delta_bc_d[:, :]),
           writes=[b_smallw], inc_k=16)
    s.emit("sync", lambda e: e.dma_start(out=bg_bc[:, :], in_=bg_bc_d[:, :]),
           writes=[b_smallw], inc_k=16)

    ps_S = nc.alloc_psum_tensor("ps_S", [128, 2048], F32)
    ps_a = nc.alloc_psum_tensor("ps_a", [128, 512], F32)
    ps_b = nc.alloc_psum_tensor("ps_b", [128, 512], F32)
    ps_c = nc.alloc_psum_tensor("ps_c", [128, 1024], BF16)
    ps_d = nc.alloc_psum_tensor("ps_d", [128, 1024], BF16)
    b_ps4 = [Buf(), Buf(), Buf(), Buf()]
    b_psa = Buf()
    b_psb = Buf()
    b_psc = Buf()
    b_psd = Buf()

    x_slab = nc.alloc_sbuf_tensor("x_slab", [128, 2, D], F32)
    stat = nc.alloc_sbuf_tensor("stat", [128, 16], F32)
    x32_fm = nc.alloc_sbuf_tensor("x32_fm", [128, ND, 128], F32)
    b_xslab = [Buf(), Buf()]
    b_xnslab = [Buf(), Buf()]
    b_x32 = Buf()
    b_stat = Buf()
    b_glog = Buf()

    for tt in range(NT):
        if "ln" not in parts and "ln1" not in parts:
            break
        sl = tt % 2
        xs = x_slab
        ld = s.emit("sync", lambda e, tt=tt, sl=sl: e.dma_start(
            out=x_slab[:, sl, :], in_=x_d.rearrange("(t p) d -> t p d", p=128)[tt]),
            writes=[b_xslab[sl]], inc_k=16)
        s.emit("vector", lambda e, sl=sl: e.reduce_sum(
            stat[:, 0:1], x_slab[:, sl, :], axis=AX.X),
            reads=[b_xslab[sl]], writes=[b_stat])
        s.emit("scalar", lambda e, sl=sl: e.activation(
            y_acc[:, sl, :], x_slab[:, sl, :], AF.Square,
            accum_out=stat[:, 1:2]),
            reads=[b_xslab[sl]], writes=[b_stat, b_yacc_all])
        s.emit("vector", lambda e: e.tensor_scalar(
            stat[:, 2:3], stat[:, 0:1], 1.0 / D, None, ALU.mult),
            reads=[b_stat], writes=[b_stat])
        s.emit("vector", lambda e: e.tensor_tensor(
            stat[:, 3:4], stat[:, 2:3], stat[:, 2:3], ALU.mult),
            reads=[b_stat], writes=[b_stat])
        s.emit("vector", lambda e: e.scalar_tensor_tensor(
            stat[:, 4:5], stat[:, 1:2], 1.0 / D, stat[:, 3:4],
            ALU.mult, ALU.subtract),
            reads=[b_stat], writes=[b_stat])
        s.emit("scalar", lambda e: e.activation(
            stat[:, 5:6], stat[:, 4:5], AF.Sqrt, bias=fc(1e-5)),
            reads=[b_stat, cst_buf["fconst"]], writes=[b_stat])
        s.emit("vector", lambda e: e.reciprocal(stat[:, 6:7], stat[:, 5:6]),
               reads=[b_stat], writes=[b_stat])
        s.emit("vector", lambda e, sl=sl: e.tensor_scalar(
            x_slab[:, sl, :], x_slab[:, sl, :], stat[:, 2:3], stat[:, 6:7],
            ALU.subtract, ALU.mult),
            reads=[b_stat], writes=[b_xslab[sl]])
        for ft in range(ND if ("ln" in parts or "ln2" in parts) else 0):
            pslot = ft % 4
            s.emit("tensor", lambda e, sl=sl, ft=ft, pslot=pslot: e.transpose(
                ps_S[:, pslot * 128:(pslot + 1) * 128],
                x_slab[:, sl, ft * 128:(ft + 1) * 128], cst["id128f"][:, :]),
                reads=[b_xslab[sl], cst_buf["id128f"]], writes=[b_ps4[0]])
            s.emit("vector", lambda e, ft=ft, pslot=pslot: e.tensor_copy(
                x32_fm[:, ft, :], ps_S[:, pslot * 128:(pslot + 1) * 128]),
                writes=[b_ps4[0], b_x32])
            s.emit("scalar", lambda e, tt=tt, ft=ft: e.copy(
                xn_bf[:, ft, tt * 128:(tt + 1) * 128], x32_fm[:, ft, :]),
                reads=[b_x32], writes=[b_xn])
        for ft in range(ND if "gate" in parts else 0):
            s.emit("tensor", lambda e, ft=ft: e.matmul(
                ps_S[:, 512:512 + NE], x32_fm[:, ft, :], wg_sb[:, ft, :],
                start=(ft == 0), stop=(ft == ND - 1)),
                reads=[b_x32, b_smallw], writes=[b_ps4[1]])
        if "gate" not in parts:
            continue
        s.emit("vector", lambda e, tt=tt: e.tensor_tensor(
            ps_S[:, 512:512 + NE], ps_S[:, 512:512 + NE], bg_bc[:, :], ALU.add),
            reads=[b_smallw], writes=[b_ps4[1]])
        s.emit("vector", lambda e, tt=tt: e.reduce_max(
            stat[:, 7:8], ps_S[:, 512:512 + NE], axis=AX.X),
            writes=[b_ps4[1], b_stat])
        s.emit("vector", lambda e, tt=tt: e.tensor_scalar(
            gmask[:, tt, :], ps_S[:, 512:512 + NE], stat[:, 7:8], None, ALU.is_ge),
            reads=[b_stat], writes=[b_ps4[1], b_gmask])
        if "thin" not in parts:
            continue
        for ft in range(ND):
            s.emit("tensor", lambda e, tt=tt, ft=ft: e.matmul(
                ps_a[:, 0:H], xn_bf[:, ft, tt * 128:(tt + 1) * 128], wb_sb[:, ft, :],
                start=(ft == 0), stop=(ft == ND - 1)),
                reads=[b_xn, b_smallw], writes=[b_psa])
        s.emit("scalar", lambda e, tt=tt: e.activation(
            sig_tm[:, tt, :], ps_a[:, 0:H], AF.Sigmoid),
            writes=[b_psa, b_sig])
        for ft in range(ND):
            s.emit("tensor", lambda e, tt=tt, ft=ft: e.matmul(
                ps_a[:, 128:128 + H], xn_bf[:, ft, tt * 128:(tt + 1) * 128], wf_sb[:, ft, :],
                start=(ft == 0), stop=(ft == ND - 1)),
                reads=[b_xn, b_smallw], writes=[b_psa])
        s.emit("vector", lambda e: e.tensor_tensor(
            ps_a[:, 128:128 + H], ps_a[:, 128:128 + H], delta_bc[:, :], ALU.add),
            reads=[b_smallw], writes=[b_psa])
        s.emit("scalar", lambda e, tt=tt: e.activation(
            x_slab[:, tt % 2, 0:H], ps_a[:, 128:128 + H], AF.Sigmoid),
            writes=[b_psa, b_xslab[tt % 2]])
        s.emit("scalar", lambda e, tt=tt: e.activation(
            logf_bf[:, tt, :], x_slab[:, tt % 2, 0:H], AF.Ln),
            reads=[b_xslab[tt % 2]], writes=[b_logf])
        w1r = (tt % 2) * 64
        for ft in range(ND):
            s.emit("tensor", lambda e, tt=tt, ft=ft, w1r=w1r: e.matmul(
                ps_b[w1r:w1r + 64, 0:128], ww1_sb[:, ft, :],
                xn_bf[:, ft, tt * 128:(tt + 1) * 128],
                start=(ft == 0), stop=(ft == ND - 1), tile_position=(0, w1r)),
                reads=[b_xn, b_smallw], writes=[b_psb])
        s.emit("scalar", lambda e, tt=tt, w1r=w1r: e.copy(
            w1o[w1r:w1r + 64, (tt // 2) * 128:(tt // 2) * 128 + 128],
            ps_b[w1r:w1r + 64, 0:128]),
            writes=[b_psb, b_w1o])

    for tt in range(NT if ("fc" in parts and "thin" in parts) else 0):
        for j in range(tt + 1):
            lhs = cst["cumtri"] if j == tt else cst["allones"]
            lb = cst_buf["cumtri"] if j == tt else cst_buf["allones"]
            s.emit("tensor", lambda e, tt=tt, j=j, lhs=lhs: e.matmul(
                ps_S[:, 1024:1024 + H], lhs[:, :], logf_bf[:, j, :],
                start=(j == 0), stop=(j == tt)),
                reads=[b_logf, lb], writes=[b_ps4[2]])
        s.emit("vector", lambda e, tt=tt: e.tensor_copy(
            fc_tm[:, tt, :], ps_S[:, 1024:1024 + H]),
            writes=[b_ps4[2], b_fc])

    dbg_dump("xn_bf", lambda: xn_bf[:, :, :], [128, ND, N], BF16, [b_xn.w])
    dbg_dump("glog", lambda: glog[:, :, :], [128, NT, NE], F32, [b_glog.w])
    dbg_dump("gmask", lambda: gmask[:, :, :], [128, NT, NE], F32, [b_gmask.w])
    dbg_dump("sig_tm", lambda: sig_tm[:, :, :], [128, NT, H], F32, [b_sig.w])
    dbg_dump("logf_bf", lambda: logf_bf[:, :, :], [128, NT, H], BF16, [b_logf.w])
    dbg_dump("fc_tm", lambda: fc_tm[:, :, :], [128, NT, H], F32, [b_fc.w])
    dbg_dump("w1o", lambda: w1o[:, :], [64, N], BF16, [b_w1o.w])



    b_wo = Buf()
    s.emit("gpsimd", lambda e: e.memset(y_acc[:, :, :], 0.0), writes=[b_yacc_all])

    st2 = nc.alloc_sbuf_tensor("st2", [128, 16], F32)
    _attn_guards = []

    def _ga(name, shape, dtype):
        g = nc.sbuf_tensor(name, shape, dtype)
        _attn_guards.append(g)
        return g.__enter__()

    q_fm = _ga("q_fm", [128, N], BF16)
    k_fm = _ga("k_fm", [128, N], BF16)
    w_fm = _ga("w_fm", [128, N], BF16)
    wb_fm = _ga("wb_fm", [128, N], BF16)
    w_tm = _ga("w_tm", [128, NT, 128], BF16)
    wb_tm = _ga("wb_tm", [128, NT, 128], BF16)
    v_aug = _ga("v_aug", [128, NT, 130], BF16)
    kt = _ga("kt", [128, 2, N], BF16)
    qhat_k = _ga("qhat_k", [128, 2, 128], BF16)
    sii_k = _ga("sii_k", [128, 2, 2, 128], BF16)
    pt_k = _ga("pt_k", [128, 2, 64], BF16)
    ktd_k = _ga("ktd_k", [128, 2, 128], BF16)
    Lb = _ga("Lb", [128, 2, 128], BF16)
    Nb = _ga("Nb", [128, 2, 128], BF16)
    Mb = _ga("Mb", [128, 2, 128], BF16)
    nGt = _ga("nGt", [128, 2, 128], BF16)
    npow = _ga("npow", [128, 2, 6, 128], BF16)
    lkb = _ga("lkb", [128, 2, 128], BF16)
    zb = _ga("zb", [128, 2, 2, 192], BF16)
    s_sb = _ga("s_sb", [128, 2, N], BF16)
    pT_sb = _ga("pT_sb", [128, 2, 4, 128], BF16)
    o_tmp = _ga("o_tmp", [128, 2, 64], BF16)
    fcol = _ga("fcol", [128, 2, NT], F32)
    fcmr = _ga("fcmr", [128, 2, NT], F32)
    st3 = _ga("st3", [128, 2, 8], F32)
    wq_sl = _ga("wq_sl", [128, ND, 128], BF16)
    wk_sl = _ga("wk_sl", [128, ND, 128], BF16)
    wv_sl = _ga("wv_sl", [128, ND, 128], BF16)
    ww2_sl = _ga("ww2_sl", [128, 128], BF16)

    b_q = Buf(); b_k = Buf(); b_wfm = Buf(); b_wbfm = Buf()
    b_wtm = Buf(); b_wbtm = Buf(); b_vaug = Buf()
    b_kt = [[Buf(), Buf()], [Buf(), Buf()]]
    b_qh = [Buf(), Buf()]; b_sii = [Buf(), Buf()]
    b_pt = [Buf(), Buf()]; b_ktd = [Buf(), Buf()]
    b_L = [Buf(), Buf()]; b_N = [Buf(), Buf()]; b_M = [Buf(), Buf()]
    b_nGt = [Buf(), Buf()]; b_np = [Buf(), Buf()]; b_lk = [Buf(), Buf()]
    b_z = [[Buf(), Buf()], [Buf(), Buf()]]
    b_s = [Buf(), Buf()]
    b_pT = [[Buf() for _ in range(4)], [Buf() for _ in range(4)]]
    b_otmp = [Buf(), Buf()]
    b_fcol = [Buf(), Buf()]; b_fcmr = [Buf(), Buf()]
    b_st2 = Buf(); b_st3 = [Buf(), Buf()]
    b_wqsl = Buf(); b_wksl = Buf(); b_wvsl = Buf(); b_ww2sl = Buf()
    b_ofm = Buf()

    def emit_proj(hp):
        c0 = hp * 128
        s.emit("gpsimd", lambda e, c0=c0: e.dma_start(
            out=wq_sl[:, :, :],
            in_=wq_d[:, c0:c0 + 128].rearrange("(t p) c -> p t c", p=128)),
            writes=[b_wqsl], inc_k=16)
        s.emit("gpsimd", lambda e, c0=c0: e.dma_start(
            out=wk_sl[:, :, :],
            in_=wk_d[:, c0:c0 + 128].rearrange("(t p) c -> p t c", p=128)),
            writes=[b_wksl], inc_k=16)
        s.emit("gpsimd", lambda e, c0=c0: e.dma_start(
            out=wv_sl[:, :, :],
            in_=wv_d[:, c0:c0 + 128].rearrange("(t p) c -> p t c", p=128)),
            writes=[b_wvsl], inc_k=16)
        s.emit("gpsimd", lambda e, c0=c0: e.dma_start(
            out=ww2_sl[0:64, :], in_=ww2_d[:, c0:c0 + 128]),
            writes=[b_ww2sl], inc_k=16)
        s.emit("gpsimd", lambda e, c0=c0: e.dma_start(
            out=ww2_sl[64:128, :], in_=ww2_d[:, c0:c0 + 128]),
            writes=[b_ww2sl], inc_k=16)
        s.emit("gpsimd", lambda e: e.memset(v_aug[:, :, 64:65], 1.0), writes=[b_vaug])
        s.emit("gpsimd", lambda e: e.memset(v_aug[:, :, 129:130], 1.0), writes=[b_vaug])
        for dst, wsl, bw, bdst in ((q_fm, wq_sl, b_wqsl, b_q), (k_fm, wk_sl, b_wksl, b_k)):
            for t4 in range(4):
                for ft in range(ND):
                    s.emit("tensor", lambda e, dst=dst, wsl=wsl, t4=t4, ft=ft: e.matmul(
                        ps_a[:, 0:512], wsl[:, ft, :],
                        xn_bf[:, ft, t4 * 512:(t4 + 1) * 512],
                        start=(ft == 0), stop=(ft == ND - 1)),
                        reads=[bw, b_xn], writes=[b_psa])
                s.emit("scalar", lambda e, dst=dst, t4=t4: e.copy(
                    dst[:, t4 * 512:(t4 + 1) * 512], ps_a[:, 0:512]),
                    writes=[b_psa, bdst])
        for tt in range(NT):
            for ft in range(ND):
                s.emit("tensor", lambda e, tt=tt, ft=ft: e.matmul(
                    ps_b[:, 0:128], xn_bf[:, ft, tt * 128:(tt + 1) * 128],
                    wv_sl[:, ft, :], start=(ft == 0), stop=(ft == ND - 1)),
                    reads=[b_wvsl, b_xn], writes=[b_psb])
            s.emit("vector", lambda e, tt=tt: e.tensor_copy(
                bass.AP(v_aug, tt * 130, [[v_aug.shape[2] * NT, 128], [65, 2], [1, 64]]),
                ps_b[:, 0:128]),
                writes=[b_psb, b_vaug])
            s.emit(*ATOMB)
            w1r = (tt % 2) * 64
            s.emit("tensor", lambda e, tt=tt, w1r=w1r: e.matmul(
                ps_S[:, 0:128], w1o[w1r:w1r + 64, (tt // 2) * 128:(tt // 2) * 128 + 128],
                ww2_sl[w1r:w1r + 64, :], start=True, stop=True, tile_position=(w1r, 0)),
                reads=[b_w1o, b_ww2sl], writes=[b_ps4[0]])
            for half in range(2):
                s.emit("scalar", lambda e, tt=tt, half=half: e.activation(
                    x_slab[:, 0, half * 64:(half + 1) * 64],
                    ps_S[:, half * 64:(half + 1) * 64], AF.Square,
                    accum_out=st2[:, 8 + half:9 + half]),
                    writes=[b_ps4[0], b_st2, b_xslab[0]])
            s.emit("scalar", lambda e: e.activation(
                st2[:, 10:12], st2[:, 8:10], AF.Sqrt, bias=fc(1e-6)),
                reads=[cst_buf["fconst"]], writes=[b_st2])
            s.emit("vector", lambda e: e.reciprocal(st2[:, 12:14], st2[:, 10:12]),
                   writes=[b_st2])
            s.emit("vector", lambda e, tt=tt, hp=hp: e.scalar_tensor_tensor(
                st2[:, 14:16], sig_tm[:, tt, 2 * hp:2 * hp + 2], 2.0,
                st2[:, 12:14], ALU.mult, ALU.mult),
                reads=[b_sig], writes=[b_st2])
            for half in range(2):
                s.emit("vector", lambda e, tt=tt, half=half: e.tensor_scalar(
                    w_tm[:, tt, half * 64:(half + 1) * 64],
                    ps_S[:, half * 64:(half + 1) * 64],
                    st2[:, 12 + half:13 + half], None, ALU.mult),
                    reads=[b_st2], writes=[b_ps4[0], b_wtm])
                s.emit("vector", lambda e, tt=tt, half=half: e.tensor_scalar(
                    wb_tm[:, tt, half * 64:(half + 1) * 64],
                    ps_S[:, half * 64:(half + 1) * 64],
                    st2[:, 14 + half:15 + half], None, ALU.mult),
                    reads=[b_st2], writes=[b_ps4[0], b_wbtm])
            s.emit(*ATOME)
            s.emit("tensor", lambda e, tt=tt: e.transpose(
                ps_c[:, 512:640], w_tm[:, tt, :], cst["id128"][:, :]),
                reads=[b_wtm, cst_buf["id128"]], writes=[b_psc])
            s.emit("scalar", lambda e, tt=tt: e.copy(
                w_fm[:, tt * 128:(tt + 1) * 128], ps_c[:, 512:640]),
                writes=[b_psc, b_wfm])
            s.emit("tensor", lambda e, tt=tt: e.transpose(
                ps_c[:, 640:768], wb_tm[:, tt, :], cst["id128"][:, :]),
                reads=[b_wbtm, cst_buf["id128"]], writes=[b_psc])
            s.emit("scalar", lambda e, tt=tt: e.copy(
                wb_fm[:, tt * 128:(tt + 1) * 128], ps_c[:, 640:768]),
                writes=[b_psc, b_wbfm])

    def emit_head_prep(h, ph):
        bk = 2 + ph
        s.emit("tensor", lambda e, h=h, bk=bk: e.matmul(
            ps_S[:, bk * 512 + 256:bk * 512 + 257], cst["ones_row_f32"][:, :],
            fc_tm[0:1, 8, h:h + 1], start=True, stop=True),
            reads=[b_fc, cst_buf["ones_row_f32"]], writes=[b_ps4[bk]])
        s.emit("vector", lambda e, ph=ph, bk=bk: e.tensor_copy(
            st3[:, ph, 0:1], ps_S[:, bk * 512 + 256:bk * 512 + 257]),
            writes=[b_ps4[bk], b_st3[ph]])
        s.emit("vector", lambda e, ph=ph: e.tensor_scalar(
            st3[:, ph, 1:2], st3[:, ph, 0:1], -1.0, None, ALU.mult),
            writes=[b_st3[ph]])
        s.emit("scalar", lambda e, h=h, ph=ph: e.activation(
            fcol[:, ph, :], bass.AP(fc_tm, h, [[NT * H, 128], [H, NT], [1, 1]]),
            AF.Exp, bias=st3[:, ph, 0:1], scale=fc(-1.0)),
            reads=[b_fc, b_st3[ph], cst_buf["fconst"]], writes=[b_fcol[ph]])
        s.emit("vector", lambda e, h=h, ph=ph: e.tensor_scalar(
            fcmr[:, ph, :], bass.AP(fc_tm, h, [[NT * H, 128], [H, NT], [1, 1]]),
            st3[:, ph, 1:2], None, ALU.add),
            reads=[b_fc, b_st3[ph]], writes=[b_fcmr[ph]])

    def emit_phaseA(h, ph, i):
        psA = ps_a if ph == 0 else ps_b
        bA = b_psa if ph == 0 else b_psb
        r0 = ph * 64
        isl = slice(i * 128, (i + 1) * 128)
        W_I = w_fm[r0:r0 + 64, isl]
        Wb_I = wb_fm[r0:r0 + 64, isl]
        Q_I = q_fm[r0:r0 + 64, isl]
        K_I = k_fm[r0:r0 + 64, isl]
        Wtm_I = w_tm[:, i, r0:r0 + 64]
        Wbtm_I = wb_tm[:, i, r0:r0 + 64]
        s.emit("tensor", lambda e: e.matmul(psA[:, 0:128], W_I, Wb_I,
               start=True, stop=True, tile_position=(r0, 0)),
               reads=[b_wfm, b_wbfm], writes=[bA])
        s.emit("vector", lambda e: e.tensor_tensor(
            Lb[:, ph, :], psA[:, 0:128], cst["m_trils"][:, :], ALU.mult),
            reads=[cst_buf["m_trils"]], writes=[bA, b_L[ph]])
        s.emit("tensor", lambda e: e.matmul(psA[:, 128:256], Wb_I, W_I,
               start=True, stop=True, tile_position=(r0, 0)),
               reads=[b_wfm, b_wbfm], writes=[bA])
        s.emit("vector", lambda e: e.tensor_tensor(
            Nb[:, ph, :], psA[:, 128:256], cst["m_trius"][:, :], ALU.mult),
            reads=[cst_buf["m_trius"]], writes=[bA, b_N[ph]])
        s.emit("tensor", lambda e: e.matmul(psA[:, 256:384], W_I, K_I,
               start=True, stop=True, tile_position=(r0, 0)),
               reads=[b_wfm, b_k], writes=[bA])
        s.emit("vector", lambda e: e.tensor_tensor(
            Mb[:, ph, :], psA[:, 256:384], cst["m_trils"][:, :], ALU.mult),
            reads=[cst_buf["m_trils"]], writes=[bA, b_M[ph]])
        s.emit("tensor", lambda e: e.matmul(psA[:, 384:512], Wb_I, Q_I,
               start=True, stop=True, tile_position=(r0, 0)),
               reads=[b_wbfm, b_q], writes=[bA])
        s.emit("vector", lambda e: e.tensor_tensor(
            nGt[:, ph, :], psA[:, 384:512], cst["m_negtriu"][:, :], ALU.mult),
            reads=[cst_buf["m_negtriu"]], writes=[bA, b_nGt[ph]])

        def lp(lev):
            return Lb[:, ph, :] if lev == 0 else lkb[:, ph, :]

        def np_(lev):
            return Nb[:, ph, :] if lev == 0 else npow[:, ph, lev - 1, :]

        for lev in range(6):
            bl = b_L[ph] if lev == 0 else b_lk[ph]
            bn = b_N[ph] if lev == 0 else b_np[ph]
            s.emit("tensor", lambda e, lev=lev: e.matmul(
                psA[:, 128:256], lp(lev), np_(lev), start=True, stop=True),
                reads=[bl, bn], writes=[bA])
            s.emit("scalar", lambda e, lev=lev: e.copy(
                npow[:, ph, lev, :], psA[:, 128:256]),
                writes=[bA, b_np[ph]])
            if lev < 5:
                s.emit("tensor", lambda e, lev=lev: e.matmul(
                    psA[:, 0:128], np_(lev), lp(lev), start=True, stop=True),
                    reads=[bl, bn], writes=[bA])
                s.emit("vector", lambda e: e.tensor_copy(
                    lkb[:, ph, :], psA[:, 0:128]),
                    writes=[bA, b_lk[ph]])
        zc = 0
        s.emit("tensor", lambda e: e.matmul(psA[:, 0:64], npow[:, ph, 5, :], Wtm_I,
               start=True, stop=False), reads=[b_np[ph], b_wtm], writes=[bA])
        s.emit("tensor", lambda e: e.matmul(psA[:, 64:192], npow[:, ph, 5, :], Mb[:, ph, :],
               start=False, stop=True), reads=[b_np[ph], b_M[ph]], writes=[bA])
        s.emit("vector", lambda e: e.tensor_tensor(
            zb[:, ph, 0, 0:64], Wtm_I, psA[:, 0:64], ALU.add),
            reads=[b_wtm], writes=[bA, b_z[ph][0]])
        s.emit("vector", lambda e: e.tensor_tensor(
            zb[:, ph, 0, 64:192], Mb[:, ph, :], psA[:, 64:192], ALU.add),
            reads=[b_M[ph]], writes=[bA, b_z[ph][0]])
        for lev in (4, 3, 2, 1, 0):
            zn = 1 - zc
            s.emit("tensor", lambda e, lev=lev, zc=zc: e.matmul(
                psA[:, 0:192], npow[:, ph, lev, :], zb[:, ph, zc, :],
                start=True, stop=True), reads=[b_np[ph], b_z[ph][zc]], writes=[bA])
            s.emit("vector", lambda e, zc=zc, zn=zn: e.tensor_tensor(
                zb[:, ph, zn, :], zb[:, ph, zc, :], psA[:, 0:192], ALU.add),
                reads=[b_z[ph][zc]], writes=[bA, b_z[ph][zn]])
            zc = zn
        zn = 1 - zc
        s.emit("tensor", lambda e, zc=zc: e.matmul(
            psA[:, 0:192], Nb[:, ph, :], zb[:, ph, zc, :],
            start=True, stop=True), reads=[b_N[ph], b_z[ph][zc]], writes=[bA])
        s.emit("vector", lambda e, zc=zc, zn=zn: e.tensor_tensor(
            zb[:, ph, zn, :], zb[:, ph, zc, :], psA[:, 0:192], ALU.subtract),
            reads=[b_z[ph][zc]], writes=[bA, b_z[ph][zn]])
        zf = zn
        TW = zb[:, ph, zf, 0:64]
        CI2 = zb[:, ph, zf, 64:192]
        bZ = b_z[ph][zf]
        s.emit("tensor", lambda e: e.matmul(psA[:, 256:384], Q_I, K_I,
               start=True, stop=False, tile_position=(r0, 0)),
               reads=[b_q, b_k], writes=[bA])
        s.emit("tensor", lambda e: e.matmul(psA[:, 256:384], nGt[:, ph, :], CI2,
               start=False, stop=True), reads=[b_nGt[ph], bZ], writes=[bA])
        s.emit("scalar", lambda e, i=i, ph=ph: e.copy(
            sii_k[:, ph, i % 2, :], psA[:, 256:384]),
            writes=[bA, b_sii[ph]])
        s.emit("tensor", lambda e: e.matmul(
            psA[r0:r0 + 64, 0:128], TW, nGt[:, ph, :],
            start=True, stop=True, tile_position=(0, r0)),
            reads=[bZ, b_nGt[ph]], writes=[bA])
        s.emit("vector", lambda e, i=i: e.tensor_tensor(
            qhat_k[r0:r0 + 64, i % 2, :], Q_I, psA[r0:r0 + 64, 0:128], ALU.add),
            reads=[b_q], writes=[bA, b_qh[ph]])
        s.emit("tensor", lambda e: e.matmul(
            psA[r0:r0 + 64, 128:192], TW, Wbtm_I,
            start=True, stop=True, tile_position=(0, r0)),
            reads=[bZ, b_wbtm], writes=[bA])
        s.emit("vector", lambda e, i=i: e.tensor_tensor(
            pt_k[r0:r0 + 64, i % 2, :], cst["i64stack"][r0:r0 + 64, :],
            psA[r0:r0 + 64, 128:192], ALU.subtract),
            reads=[cst_buf["i64stack"]], writes=[bA, b_pt[ph]])
        s.emit("tensor", lambda e: e.matmul(
            psA[r0:r0 + 64, 256:384], Wbtm_I, CI2,
            start=True, stop=True, tile_position=(0, r0)),
            reads=[b_wbtm, bZ], writes=[bA])
        s.emit("vector", lambda e, i=i: e.tensor_tensor(
            ktd_k[r0:r0 + 64, i % 2, :], K_I, psA[r0:r0 + 64, 256:384], ALU.subtract),
            reads=[b_k], writes=[bA, b_ktd[ph]])

    def emit_phaseB_chunk(h, ph, i):
        r0 = ph * 64
        bk = 2 + ph
        sk = ph
        psc = ps_c if ph == 0 else ps_d
        bpsc = b_psc if ph == 0 else b_psd
        rbuf = (i + 1) % 2
        wbuf = i % 2
        jend = (i + 1) * 128
        nj5 = (i * 128 + 511) // 512
        for j5 in range(nj5):
            w = min(512, i * 128 - j5 * 512)
            s.emit(*ATOMB)
            s.emit("tensor", lambda e, i=i, j5=j5, w=w, rbuf=rbuf: e.matmul(
                ps_S[:, sk * 512:sk * 512 + w],
                qhat_k[r0:r0 + 64, i % 2, :], kt[r0:r0 + 64, rbuf, j5 * 512:j5 * 512 + w],
                start=True, stop=True, tile_position=(r0, 0)),
                reads=[b_qh[ph], b_kt[ph][rbuf]], writes=[b_ps4[sk]])
            s.emit("vector", lambda e, j5=j5, w=w, ph=ph: e.tensor_copy(
                s_sb[:, ph, j5 * 512:j5 * 512 + w], ps_S[:, sk * 512:sk * 512 + w]),
                writes=[b_ps4[sk], b_s[ph]])
            s.emit(*ATOME)
        s.emit("vector", lambda e, i=i, ph=ph: e.tensor_tensor(
            s_sb[:, ph, i * 128:(i + 1) * 128], sii_k[:, ph, i % 2, :],
            cst["m_addneg"][:, :], ALU.add),
            reads=[b_sii[ph], cst_buf["m_addneg"]], writes=[b_s[ph]])
        s.emit("vector", lambda e, jend=jend, ph=ph: e.reduce_max(
            st3[:, ph, 2:3], s_sb[:, ph, 0:jend], axis=AX.X),
            reads=[b_s[ph]], writes=[b_st3[ph]])
        s.emit("vector", lambda e, i=i, ph=ph: e.tensor_scalar(
            st3[:, ph, 3:4], st3[:, ph, 2:3], -SCALE, fcmr[:, ph, i:i + 1],
            ALU.mult, ALU.add),
            reads=[b_fcmr[ph]], writes=[b_st3[ph]])
        s.emit("scalar", lambda e, jend=jend, ph=ph: e.activation(
            s_sb[:, ph, 0:jend], s_sb[:, ph, 0:jend], AF.Exp,
            bias=st3[:, ph, 3:4], scale=fc(SCALE)),
            reads=[b_st3[ph], cst_buf["fconst"]], writes=[b_s[ph]])
        for jt in range(i + 1):
            slot = jt % 7
            s.emit("tensor", lambda e, jt=jt, slot=slot, ph=ph, psc=psc: e.transpose(
                psc[:, slot * 128:(slot + 1) * 128],
                s_sb[:, ph, jt * 128:(jt + 1) * 128], cst["id128"][:, :]),
                reads=[b_s[ph], cst_buf["id128"]], writes=[bpsc])
            rs = jt % 4
            s.emit("vector", lambda e, jt=jt, slot=slot, ph=ph, psc=psc, rs=rs: e.tensor_scalar(
                pT_sb[:, ph, rs, :],
                psc[:, slot * 128:(slot + 1) * 128],
                fcol[:, ph, jt:jt + 1], None, ALU.mult),
                reads=[b_fcol[ph]], writes=[bpsc, b_pT[ph][rs]])
            s.emit("tensor", lambda e, jt=jt, i=i, ph=ph, bk=bk, rs=rs: e.matmul(
                ps_S[:, bk * 512:bk * 512 + 65], pT_sb[:, ph, rs, :],
                v_aug[:, jt, ph * 65:ph * 65 + 65],
                start=(jt == 0), stop=(jt == i)),
                reads=[b_pT[ph][rs], b_vaug], writes=[b_ps4[bk]])
        s.emit("vector", lambda e, ph=ph, bk=bk: e.reciprocal(
            st3[:, ph, 4:5], ps_S[:, bk * 512 + 64:bk * 512 + 65]),
            writes=[b_ps4[bk], b_st3[ph]])
        s.emit("vector", lambda e, ph=ph, bk=bk: e.tensor_scalar(
            o_tmp[:, ph, :], ps_S[:, bk * 512:bk * 512 + 64], st3[:, ph, 4:5],
            None, ALU.mult),
            reads=[b_st3[ph]], writes=[b_ps4[bk], b_otmp[ph]])
        s.emit("tensor", lambda e, ph=ph, psc=psc: e.transpose(
            psc[r0:r0 + 64, 896:1024], o_tmp[:, ph, :], cst["id128"][:, :],
            tile_position=(0, r0)),
            reads=[b_otmp[ph], cst_buf["id128"]], writes=[bpsc])
        s.emit("scalar", lambda e, i=i, h=h, ph=ph, psc=psc: e.copy(
            o_fm[r0:r0 + 64, h // 2, i * 128:(i + 1) * 128],
            psc[r0:r0 + 64, 896:1024]),
            writes=[bpsc, b_ofm])
        for j5 in range(nj5):
            w = min(512, i * 128 - j5 * 512)
            s.emit("tensor", lambda e, i=i, j5=j5, w=w, rbuf=rbuf, bk=bk: e.matmul(
                ps_S[r0:r0 + 64, bk * 512:bk * 512 + w], pt_k[r0:r0 + 64, i % 2, :],
                kt[r0:r0 + 64, rbuf, j5 * 512:j5 * 512 + w],
                start=True, stop=True, tile_position=(r0, r0)),
                reads=[b_pt[ph], b_kt[ph][rbuf]], writes=[b_ps4[bk]])
            s.emit("scalar", lambda e, j5=j5, w=w, wbuf=wbuf, bk=bk: e.copy(
                kt[r0:r0 + 64, wbuf, j5 * 512:j5 * 512 + w],
                ps_S[r0:r0 + 64, bk * 512:bk * 512 + w]),
                writes=[b_ps4[bk], b_kt[ph][wbuf]])
        s.emit("gpsimd", lambda e, i=i, wbuf=wbuf: e.tensor_copy(
            kt[r0:r0 + 64, wbuf, i * 128:(i + 1) * 128],
            ktd_k[r0:r0 + 64, i % 2, :]),
            reads=[b_ktd[ph]], writes=[b_kt[ph][wbuf]])

    def defer(fn):
        rec = []
        real = s.emit
        s.emit = lambda *a, **k: rec.append((a, k))
        fn()
        s.emit = real
        return rec

    def play(recs):
        idx = [0] * len(recs)
        while True:
            done = True
            for r in range(len(recs)):
                if idx[r] < len(recs[r]):
                    a, k = recs[r][idx[r]]
                    s.emit(*a, **k)
                    idx[r] += 1
                    done = False
            if done:
                break

    w1_sl = _ga("w1_sl", [128, ND, D], BF16)
    w2_sl = _ga("w2_sl", [128, ND, 256], BF16)
    g_slab = _ga("g_slab", [128, ND, 256], BF16)
    b1_sb = _ga("b1_sb", [128, ND], F32)
    b2_sb = _ga("b2_sb", [1, D], BF16)
    b_w1 = Buf()
    b_w2 = Buf()
    b_g = Buf()
    b_b1 = Buf()
    b_b2 = Buf()

    ATOMB = ("__ab__",)
    ATOME = ("__ae__",)

    def emit_ffn_expert(e_idx):
        s.emit(*ATOMB)
        s.emit("gpsimd", lambda e, e_idx=e_idx: e.dma_start(
            out=w1_sl[:, :, :], in_=w1_d[e_idx].rearrange("(t p) i -> p t i", p=128)),
            writes=[b_w1], inc_k=16)
        s.emit("sync", lambda e, e_idx=e_idx: e.dma_start(
            out=b1_sb[:, :], in_=b1_d[e_idx].rearrange("(t p) -> p t", p=128),
            allow_slow_non_contiguous=True),
            writes=[b_b1], inc_k=16)
        s.emit("gpsimd", lambda e, e_idx=e_idx: e.dma_start(
            out=b2_sb[:, :], in_=b2_d[e_idx:e_idx + 1, :]),
            writes=[b_b2], inc_k=16)
        s.emit(*ATOME)
        for t2 in range(8):
            for it in range(ND):
                s.emit(*ATOMB)
                for ft in range(ND):
                    s.emit("tensor", lambda e, t2=t2, it=it, ft=ft: e.matmul(
                        ps_S[:, 0:256], w1_sl[:, ft, it * 128:(it + 1) * 128],
                        xn_bf[:, ft, t2 * 256:(t2 + 1) * 256],
                        start=(ft == 0), stop=(ft == ND - 1)),
                        reads=[b_w1, b_xn], writes=[b_ps4[0]])
                s.emit("scalar", lambda e, it=it: e.activation(
                    g_slab[:, it, :], ps_S[:, 0:256], AF.Gelu,
                    bias=b1_sb[:, it:it + 1]),
                    reads=[b_b1], writes=[b_ps4[0], b_g])
                s.emit(*ATOME)
            for dq in range(4):
                s.emit("gpsimd", lambda e, e_idx=e_idx, dq=dq: e.dma_start(
                    out=w2_sl[:, :, :],
                    in_=w2_d[e_idx][:, dq * 256:(dq + 1) * 256].rearrange(
                        "(t p) i -> p t i", p=128)),
                    writes=[b_w2], inc_k=16)
                for tsub in range(2):
                    tt = t2 * 2 + tsub
                    s.emit(*ATOMB)
                    for it in range(ND):
                        s.emit("tensor", lambda e, tsub=tsub, dq=dq, it=it: e.matmul(
                            ps_S[:, 512:768], g_slab[:, it, tsub * 128:(tsub + 1) * 128],
                            w2_sl[:, it, :],
                            start=(it == 0), stop=False),
                            reads=[b_g, b_w2], writes=[b_ps4[1]])
                    s.emit("tensor", lambda e, dq=dq: e.matmul(
                        ps_S[:, 512:768],
                        cst["ones_row_bf"][0:1, 0:128],
                        b2_sb[0:1, dq * 256:(dq + 1) * 256],
                        start=False, stop=True),
                        reads=[b_b2, cst_buf["ones_row_bf"]], writes=[b_ps4[1]])
                    s.emit("vector", lambda e, tt=tt, dq=dq, e_idx=e_idx: e.scalar_tensor_tensor(
                        y_acc[:, tt, dq * 256:(dq + 1) * 256],
                        ps_S[:, 512:768],
                        gmask[:, tt, e_idx:e_idx + 1],
                        y_acc[:, tt, dq * 256:(dq + 1) * 256],
                        ALU.mult, ALU.add),
                        reads=[b_gmask, b_yacc_all], writes=[b_ps4[1], b_yacc[tt]])
                    s.emit(*ATOME)

    def defer(fn):
        rec = []
        real = s.emit
        s.emit = lambda *a, **k: rec.append((a, k))
        fn()
        s.emit = real
        return rec

    def units_of(rec):
        out = []
        cur = None
        for a, k in rec:
            if a and a[0] == "__ab__":
                cur = []
            elif a and a[0] == "__ae__":
                if cur:
                    out.append(cur)
                cur = None
            elif cur is not None:
                cur.append((a, k))
            else:
                out.append([(a, k)])
        if cur:
            out.append(cur)
        return out

    def emit_unit(u):
        if len(u) > 1:
            s.emit(*ATOMB)
        for a, k in u:
            s.emit(*a, **k)
        if len(u) > 1:
            s.emit(*ATOME)

    def play(recs):
        units = [units_of(r) for r in recs]
        idx = [0] * len(recs)
        while True:
            done = True
            for r in range(len(recs)):
                if idx[r] < len(units[r]):
                    emit_unit(units[r][idx[r]])
                    idx[r] += 1
                    done = False
            if done:
                break

    def play_weighted(att, ffn_units):
        att_units = units_of(att)
        n_att = len(att_units)
        n_f = len(ffn_units)
        fi = 0
        for j, u in enumerate(att_units):
            emit_unit(u)
            while fi < n_f and (j + 1) * n_f >= (fi + 1) * n_att:
                emit_unit(ffn_units[fi])
                fi += 1
        while fi < n_f:
            emit_unit(ffn_units[fi])
            fi += 1

    for hp in range(hp_count):
        def att_block(hp=hp):
            emit_proj(hp)
            h0, h1 = 2 * hp, 2 * hp + 1
            emit_head_prep(h0, 0)
            emit_head_prep(h1, 1)
            play([defer(lambda: emit_phaseA(h0, 0, 0)),
                  defer(lambda: emit_phaseA(h1, 1, 0))])
            for i in range(NCH):
                recs = [defer(lambda i=i: emit_phaseB_chunk(h0, 0, i)),
                        defer(lambda i=i: emit_phaseB_chunk(h1, 1, i))]
                if i + 1 < NCH:
                    recs.append(defer(lambda i=i: emit_phaseA(h0, 0, i + 1)))
                    recs.append(defer(lambda i=i: emit_phaseA(h1, 1, i + 1)))
                play(recs)
        att_block()

    for e_idx in range(NE):
        emit_ffn_expert(e_idx)

    dbg_dump("o_fm", lambda: o_fm[:, :, :], [128, ND, N], BF16, [b_ofm.w])

    attn_bufs = ([b_q, b_k, b_wfm, b_wbfm, b_wtm, b_wbtm, b_vaug]
                 + b_kt[0] + b_kt[1] + b_qh + b_sii + b_pt + b_ktd
                 + b_L + b_N + b_M + b_nGt + b_np + b_lk
                 + b_z[0] + b_z[1] + b_s + b_pT[0] + b_pT[1] + b_otmp
                 + b_fcol + b_fcmr + b_st3
                 + [b_wqsl, b_wksl, b_wvsl, b_ww2sl,
                    b_w1, b_w2, b_g, b_b1, b_b2])
    barrier = []
    for b in attn_bufs:
        if b.w is not None:
            barrier.append(b.w)
        barrier.extend(b.readers)
    barrier = list({id(t): t for t in barrier}.values())
    for g in reversed(_attn_guards):
        g.__exit__(None, None, None)

    dbg_dump("y_acc", lambda: y_acc[:, :, :], [128, NT, D], BF16,
             [b.w for b in b_yacc if b.w is not None])

    wo_sb = nc.alloc_sbuf_tensor("wo_sb", [128, ND, D], BF16)
    s.emit("gpsimd", lambda e: e.dma_start(
        out=wo_sb[:, :, :], in_=wo_d.rearrange("(t p) d -> p t d", p=128)),
        writes=[b_wo], deps=barrier, inc_k=16)

    b_out = Buf()
    for tt in range(NT):
        sl = tt % 2
        s.emit("sync", lambda e, tt=tt, sl=sl: e.dma_start(
            out=x_slab[:, sl, :], in_=x_d.rearrange("(t p) d -> t p d", p=128)[tt]),
            writes=[b_xslab[sl]], inc_k=16)
        for dh in range(2):
            psx = ps_a if dh == 0 else ps_b
            bpsx = b_psa if dh == 0 else b_psb
            for ft in range(ND):
                s.emit("tensor", lambda e, tt=tt, dh=dh, ft=ft, psx=psx: e.matmul(
                    psx[:, 0:512], o_fm[:, ft, tt * 128:(tt + 1) * 128],
                    wo_sb[:, ft, dh * 512:(dh + 1) * 512],
                    start=(ft == 0), stop=(ft == ND - 1)),
                    reads=[b_ofm, b_wo], writes=[bpsx])
            s.emit("vector", lambda e, tt=tt, dh=dh, sl=sl, psx=psx: e.tensor_tensor(
                x_slab[:, sl, dh * 512:(dh + 1) * 512], psx[:, 0:512],
                x_slab[:, sl, dh * 512:(dh + 1) * 512], ALU.add),
                writes=[bpsx, b_xslab[sl]])
            s.emit("vector", lambda e, tt=tt, dh=dh, sl=sl: e.tensor_tensor(
                x_slab[:, sl, dh * 512:(dh + 1) * 512],
                x_slab[:, sl, dh * 512:(dh + 1) * 512],
                y_acc[:, tt, dh * 512:(dh + 1) * 512], ALU.add),
                reads=[b_yacc[tt], b_yacc_all], writes=[b_xslab[sl]])
        s.emit("sync", lambda e, tt=tt, sl=sl: e.dma_start(
            out=out_d.rearrange("(t p) d -> t p d", p=128)[tt], in_=x_slab[:, sl, :]),
            reads=[b_xslab[sl]], writes=[b_out], inc_k=16)
    s.emit("sync", lambda e: e.nop(), deps=[b_out.w])

    s.finalize(nc, paranoid=paranoid)
    return nc, dbg_d


def stage_inputs(inputs):
    consts = host_constants()
    x = np.ascontiguousarray(inputs["L0_attn_x"])
    base = {
        "wq": inputs["L0_attn_Wq"], "wk": inputs["L0_attn_Wk"],
        "wv": inputs["L0_attn_Wv"], "wo": inputs["L0_attn_Wo"],
        "ww1": inputs["L0_attn_Ww1"], "ww2": inputs["L0_attn_Ww2"],
        "wbm": inputs["L0_attn_Wb"], "wfm": inputs["L0_attn_Wf"],
        "delta_bc": np.ascontiguousarray(np.broadcast_to(
            inputs["L0_attn_delta"][None, :], (128, H))),
        "wg": inputs["L0_ffn_Wg"],
        "bg_bc": np.ascontiguousarray(np.broadcast_to(
            inputs["L0_ffn_bg"][None, :], (128, NE))),
        "w1": inputs["L0_ffn_W1"], "b1": inputs["L0_ffn_b1"],
        "w2": inputs["L0_ffn_W2"], "b2": inputs["L0_ffn_b2"],
    }
    for k, v in consts.items():
        base[f"c_{k}"] = v
    in_maps = []
    for c in range(8):
        m = dict(base)
        m["x"] = np.ascontiguousarray(x[c])
        in_maps.append(m)
    return in_maps


def kernel(**inputs):
    nc, _ = build()
    in_maps = stage_inputs(inputs)
    res = run_bass_kernel_spmd(nc, in_maps, core_ids=list(range(8)))
    return np.stack([res.results[c]["out"] for c in range(8)], axis=0)


# version 08
# speedup vs baseline: 1.1053x; 1.1053x over previous
import math
import numpy as np
import ml_dtypes
import concourse.bass as bass
import concourse.mybir as mybir
from concourse.bass_utils import run_bass_kernel_spmd

BFNP = ml_dtypes.bfloat16
F32 = mybir.dt.float32
BF16 = mybir.dt.bfloat16
AF = mybir.ActivationFunctionType
ALU = mybir.AluOpType
AX = mybir.AxisListType

N, D, H, HD, C = 2048, 1024, 16, 64, 128
NCH = N // C
NT = N // 128
ND = D // 128
NE = 8
SCALE = HD ** -0.5

ENGINES = ("tensor", "vector", "scalar", "gpsimd", "sync")

FCONST_VALS = (1e-5, 1e-6, -1.0, 2.0, 0.5, -0.5, SCALE, -SCALE)
FCONST_IDX = {v: i for i, v in enumerate(FCONST_VALS)}


class Op:
    __slots__ = ("eng", "fn", "deps", "inc_k", "sem", "sem_after", "pos",
                 "needs_inc", "pre_wait")

    def __init__(self, eng, fn, deps, inc_k):
        self.eng = eng
        self.fn = fn
        self.deps = deps
        self.inc_k = inc_k
        self.sem = None
        self.sem_after = None
        self.pos = None
        self.needs_inc = False
        self.pre_wait = None


class Buf:

    def __init__(self):
        self.w = None
        self.readers = []


class Sched:
    def __init__(self):
        self.ops = []

    def emit(self, eng, fn=None, reads=(), writes=(), deps=(), inc_k=1):
        if fn is None:
            return None
        d = [t for t in deps if t is not None]
        for b in reads:
            if b.w is not None:
                d.append(b.w)
        for b in writes:
            if b.w is not None:
                d.append(b.w)
            d.extend(b.readers)
        op = Op(eng, fn, d, inc_k)
        op.pos = len(self.ops)
        self.ops.append(op)
        for b in reads:
            b.readers.append(op)
        for b in writes:
            b.w = op
            b.readers = []
        return op

    def finalize(self, nc, paranoid=False):
        for op in self.ops:
            if op.inc_k == 16:
                op.needs_inc = True
            for dep in op.deps:
                assert dep.pos < op.pos
                if dep.eng != op.eng or dep.inc_k == 16 or paranoid:
                    dep.needs_inc = True
        eng_sems = {e: nc.alloc_semaphore(f"sem_{e}") for e in ENGINES}
        counts = {e: 0 for e in ENGINES}
        NPOOL = 8
        pool = {e: [nc.alloc_semaphore(f"dma_{e}_{i}") for i in range(NPOOL)]
                for e in ("sync", "gpsimd", "scalar")}
        pool_idx = {e: 0 for e in pool}
        pool_count = {}
        for op in self.ops:
            if op.inc_k == 16:
                sem = pool[op.eng][pool_idx[op.eng] % NPOOL]
                pool_idx[op.eng] += 1
                prev = pool_count.get(id(sem), 0)
                if prev:
                    op.pre_wait = (sem, prev)
                op.sem = sem
                op.sem_after = prev + 16
                pool_count[id(sem)] = op.sem_after
            elif op.needs_inc:
                counts[op.eng] += op.inc_k
                op.sem = eng_sems[op.eng]
                op.sem_after = counts[op.eng]
        marks = {e: {} for e in ENGINES}
        streams = {e: [o for o in self.ops if o.eng == e] for e in ENGINES}
        need_drain = set()
        vstream = streams["vector"]
        vpos = {id(o): i for i, o in enumerate(vstream)}
        for i, op in enumerate(vstream):
            for dep in op.deps:
                if dep.eng == "vector" and vpos[id(dep)] == i - 1:
                    need_drain.add(id(op))

        def body(eng_obj, ename):
            wm = marks[ename]
            for op in streams[ename]:
                for dep in op.deps:
                    if dep.eng == ename and dep.inc_k != 16 and not paranoid:
                        continue
                    assert dep.sem_after is not None, (dep.eng, op.eng)
                    if dep.sem_after > wm.get(id(dep.sem), 0):
                        eng_obj.wait_ge(dep.sem, dep.sem_after)
                        wm[id(dep.sem)] = dep.sem_after
                if op.pre_wait is not None:
                    sem, val = op.pre_wait
                    if val > wm.get(id(sem), 0):
                        eng_obj.wait_ge(sem, val)
                        wm[id(sem)] = val
                if id(op) in need_drain:
                    eng_obj.drain()
                instr = op.fn(eng_obj)
                if op.sem is not None:
                    instr.then_inc(op.sem, op.inc_k)

        with nc.Block() as blk:
            @blk.tensor
            def _(e):
                body(e, "tensor")

            @blk.vector
            def _(e):
                body(e, "vector")

            @blk.scalar
            def _(e):
                body(e, "scalar")

            @blk.gpsimd
            def _(e):
                body(e, "gpsimd")

            @blk.sync
            def _(e):
                body(e, "sync")


def host_constants():
    c = {}
    ident = np.eye(128, dtype=np.float32)
    c["id128"] = ident.astype(BFNP)
    c["id128f"] = ident
    c["cumtri"] = np.triu(np.ones((128, 128), np.float32)).astype(BFNP)
    c["allones"] = np.ones((128, 128), np.float32).astype(BFNP)
    c["m_trils"] = np.tril(np.ones((128, 128), np.float32), -1).astype(BFNP)
    c["m_trius"] = np.triu(np.ones((128, 128), np.float32), 1).astype(BFNP)
    c["m_negtriu"] = (-np.triu(np.ones((128, 128), np.float32))).astype(BFNP)
    c["m_addneg"] = np.triu(np.full((128, 128), -1e9, np.float32), 1).astype(BFNP)
    c["i64stack"] = np.concatenate([np.eye(64, dtype=np.float32)] * 2, 0)
    c["ones_row_bf"] = np.ones((1, 512), np.float32).astype(BFNP)
    c["ones_row_f32"] = np.ones((1, 128), np.float32)
    c["fconst"] = np.ascontiguousarray(np.broadcast_to(
        np.array(FCONST_VALS, np.float32)[None, :], (128, len(FCONST_VALS))))
    return c


def build(dbg=None, paranoid=False, parts=("ln", "gate", "thin", "fc"), hp_count=ND):
    dbg = dbg or []
    parts = set(parts)
    nc = bass.Bass("TRN2", target_bir_lowering=False,
                   detect_race_conditions=paranoid)
    s = Sched()

    x_d = nc.dram_tensor("x", [N, D], F32, kind="ExternalInput").ap()
    wq_d = nc.dram_tensor("wq", [D, D], F32, kind="ExternalInput").ap()
    wk_d = nc.dram_tensor("wk", [D, D], F32, kind="ExternalInput").ap()
    wv_d = nc.dram_tensor("wv", [D, D], F32, kind="ExternalInput").ap()
    wo_d = nc.dram_tensor("wo", [D, D], F32, kind="ExternalInput").ap()
    ww1_d = nc.dram_tensor("ww1", [D, HD], F32, kind="ExternalInput").ap()
    ww2_d = nc.dram_tensor("ww2", [HD, D], F32, kind="ExternalInput").ap()
    wb_d = nc.dram_tensor("wbm", [D, H], F32, kind="ExternalInput").ap()
    wf_d = nc.dram_tensor("wfm", [D, H], F32, kind="ExternalInput").ap()
    delta_bc_d = nc.dram_tensor("delta_bc", [128, H], F32, kind="ExternalInput").ap()
    wg_d = nc.dram_tensor("wg", [D, NE], F32, kind="ExternalInput").ap()
    bg_bc_d = nc.dram_tensor("bg_bc", [128, NE], F32, kind="ExternalInput").ap()
    w1_d = nc.dram_tensor("w1", [NE, D, D], F32, kind="ExternalInput").ap()
    b1_d = nc.dram_tensor("b1", [NE, D], F32, kind="ExternalInput").ap()
    w2_d = nc.dram_tensor("w2", [NE, D, D], F32, kind="ExternalInput").ap()
    b2_d = nc.dram_tensor("b2", [NE, D], F32, kind="ExternalInput").ap()
    cst_d = {}
    consts = host_constants()
    for k, v in consts.items():
        dt = BF16 if v.dtype == BFNP else F32
        cst_d[k] = nc.dram_tensor(f"c_{k}", list(v.shape), dt, kind="ExternalInput").ap()
    out_d = nc.dram_tensor("out", [N, D], F32, kind="ExternalOutput").ap()

    dbg_d = {}
    dbg_bufs = {}

    def dbg_dump(name, ap_fn, shape, dtype, deps):
        if name not in dbg:
            return
        dd = nc.dram_tensor(f"dbg_{name}", shape, dtype, kind="ExternalOutput").ap()
        dbg_d[name] = dd
        s.emit("sync", lambda e, a=ap_fn, d=dd: e.dma_start(out=d, in_=a()),
               deps=deps, inc_k=16)

    cst = {}
    cst_buf = {}
    for k, v in consts.items():
        dt = BF16 if v.dtype == BFNP else F32
        t = nc.alloc_sbuf_tensor(f"sb_{k}", list(v.shape), dt)
        cst[k] = t
        b = Buf()
        cst_buf[k] = b
        s.emit("sync", lambda e, t=t, k=k: e.dma_start(out=t[:, :], in_=cst_d[k][:, :]),
               writes=[b], inc_k=16)

    def CC(name):
        return cst[name], cst_buf[name]

    def fc(v):
        return cst["fconst"][:, FCONST_IDX[v]:FCONST_IDX[v] + 1]

    xn_bf = nc.alloc_sbuf_tensor("xn_bf", [128, ND, N], BF16)
    o_fm = nc.alloc_sbuf_tensor("o_fm", [128, ND, N], BF16)
    y_acc = nc.alloc_sbuf_tensor("y_acc", [128, NT, D], BF16)
    fc_tm = nc.alloc_sbuf_tensor("fc_tm", [128, NT, H], F32)
    sig_tm = nc.alloc_sbuf_tensor("sig_tm", [128, NT, H], BF16)
    logf_bf = nc.alloc_sbuf_tensor("logf_bf", [128, NT, H], BF16)
    gmask = nc.alloc_sbuf_tensor("gmask", [128, NT, NE], F32)
    w1o = nc.alloc_sbuf_tensor("w1o", [128, N // 2], BF16)
    wg_sb = nc.alloc_sbuf_tensor("wg_sb", [128, ND, NE], F32)
    wb_sb = nc.alloc_sbuf_tensor("wb_sb", [128, ND, H], BF16)
    wf_sb = nc.alloc_sbuf_tensor("wf_sb", [128, ND, H], BF16)
    ww1_sb = nc.alloc_sbuf_tensor("ww1_sb", [128, ND, HD], BF16)
    delta_bc = nc.alloc_sbuf_tensor("delta_sb", [128, H], F32)
    bg_bc = nc.alloc_sbuf_tensor("bg_sb", [128, NE], F32)

    b_xn = Buf()
    b_yacc_all = Buf()
    b_o = [[Buf() for _ in range(NCH)] for _ in range(H)]
    b_yacc = [Buf() for _ in range(NT)]
    b_fc = Buf()
    b_sig = Buf()
    b_logf = Buf()
    b_gmask = Buf()
    b_w1o = Buf()
    b_smallw = Buf()

    s.emit("gpsimd", lambda e: e.dma_start(
        out=wb_sb[:, :, :], in_=wb_d.rearrange("(t p) h -> p t h", p=128)),
        writes=[b_smallw], inc_k=16)
    s.emit("gpsimd", lambda e: e.dma_start(
        out=wf_sb[:, :, :], in_=wf_d.rearrange("(t p) h -> p t h", p=128)),
        writes=[b_smallw], inc_k=16)
    s.emit("gpsimd", lambda e: e.dma_start(
        out=ww1_sb[:, :, :], in_=ww1_d.rearrange("(t p) h -> p t h", p=128)),
        writes=[b_smallw], inc_k=16)
    s.emit("sync", lambda e: e.dma_start(
        out=wg_sb[:, :, :], in_=wg_d.rearrange("(t p) h -> p t h", p=128)),
        writes=[b_smallw], inc_k=16)
    s.emit("sync", lambda e: e.dma_start(out=delta_bc[:, :], in_=delta_bc_d[:, :]),
           writes=[b_smallw], inc_k=16)
    s.emit("sync", lambda e: e.dma_start(out=bg_bc[:, :], in_=bg_bc_d[:, :]),
           writes=[b_smallw], inc_k=16)

    ps_S = nc.alloc_psum_tensor("ps_S", [128, 2048], F32)
    ps_a = nc.alloc_psum_tensor("ps_a", [128, 512], F32)
    ps_b = nc.alloc_psum_tensor("ps_b", [128, 512], F32)
    ps_c = nc.alloc_psum_tensor("ps_c", [128, 1024], BF16)
    ps_d = nc.alloc_psum_tensor("ps_d", [128, 1024], BF16)
    b_ps4 = [Buf(), Buf(), Buf(), Buf()]
    b_psa = Buf()
    b_psb = Buf()
    b_psc = Buf()
    b_psd = Buf()

    x_slab = nc.alloc_sbuf_tensor("x_slab", [128, 2, D], F32)
    stat = nc.alloc_sbuf_tensor("stat", [128, 16], F32)
    x32_fm = nc.alloc_sbuf_tensor("x32_fm", [128, ND, 128], F32)
    b_xslab = [Buf(), Buf()]
    b_xnslab = [Buf(), Buf()]
    b_x32 = Buf()
    b_stat = Buf()
    b_glog = Buf()

    for tt in range(NT):
        if "ln" not in parts and "ln1" not in parts:
            break
        sl = tt % 2
        xs = x_slab
        ld = s.emit("sync", lambda e, tt=tt, sl=sl: e.dma_start(
            out=x_slab[:, sl, :], in_=x_d.rearrange("(t p) d -> t p d", p=128)[tt]),
            writes=[b_xslab[sl]], inc_k=16)
        s.emit("vector", lambda e, sl=sl: e.reduce_sum(
            stat[:, 0:1], x_slab[:, sl, :], axis=AX.X),
            reads=[b_xslab[sl]], writes=[b_stat])
        s.emit("scalar", lambda e, sl=sl: e.activation(
            y_acc[:, sl, :], x_slab[:, sl, :], AF.Square,
            accum_out=stat[:, 1:2]),
            reads=[b_xslab[sl]], writes=[b_stat, b_yacc_all])
        s.emit("vector", lambda e: e.tensor_scalar(
            stat[:, 2:3], stat[:, 0:1], 1.0 / D, None, ALU.mult),
            reads=[b_stat], writes=[b_stat])
        s.emit("vector", lambda e: e.tensor_tensor(
            stat[:, 3:4], stat[:, 2:3], stat[:, 2:3], ALU.mult),
            reads=[b_stat], writes=[b_stat])
        s.emit("vector", lambda e: e.scalar_tensor_tensor(
            stat[:, 4:5], stat[:, 1:2], 1.0 / D, stat[:, 3:4],
            ALU.mult, ALU.subtract),
            reads=[b_stat], writes=[b_stat])
        s.emit("scalar", lambda e: e.activation(
            stat[:, 5:6], stat[:, 4:5], AF.Sqrt, bias=fc(1e-5)),
            reads=[b_stat, cst_buf["fconst"]], writes=[b_stat])
        s.emit("vector", lambda e: e.reciprocal(stat[:, 6:7], stat[:, 5:6]),
               reads=[b_stat], writes=[b_stat])
        s.emit("vector", lambda e, sl=sl: e.tensor_scalar(
            x_slab[:, sl, :], x_slab[:, sl, :], stat[:, 2:3], stat[:, 6:7],
            ALU.subtract, ALU.mult),
            reads=[b_stat], writes=[b_xslab[sl]])
        for ft in range(ND if ("ln" in parts or "ln2" in parts) else 0):
            pslot = ft % 4
            s.emit("tensor", lambda e, sl=sl, ft=ft, pslot=pslot: e.transpose(
                ps_S[:, pslot * 128:(pslot + 1) * 128],
                x_slab[:, sl, ft * 128:(ft + 1) * 128], cst["id128f"][:, :]),
                reads=[b_xslab[sl], cst_buf["id128f"]], writes=[b_ps4[0]])
            s.emit("vector", lambda e, ft=ft, pslot=pslot: e.tensor_copy(
                x32_fm[:, ft, :], ps_S[:, pslot * 128:(pslot + 1) * 128]),
                writes=[b_ps4[0], b_x32])
            s.emit("scalar", lambda e, tt=tt, ft=ft: e.copy(
                xn_bf[:, ft, tt * 128:(tt + 1) * 128], x32_fm[:, ft, :]),
                reads=[b_x32], writes=[b_xn])
        for ft in range(ND if "gate" in parts else 0):
            s.emit("tensor", lambda e, ft=ft: e.matmul(
                ps_S[:, 512:512 + NE], x32_fm[:, ft, :], wg_sb[:, ft, :],
                start=(ft == 0), stop=(ft == ND - 1)),
                reads=[b_x32, b_smallw], writes=[b_ps4[1]])
        if "gate" not in parts:
            continue
        s.emit("vector", lambda e, tt=tt: e.tensor_tensor(
            ps_S[:, 512:512 + NE], ps_S[:, 512:512 + NE], bg_bc[:, :], ALU.add),
            reads=[b_smallw], writes=[b_ps4[1]])
        s.emit("vector", lambda e, tt=tt: e.reduce_max(
            stat[:, 7:8], ps_S[:, 512:512 + NE], axis=AX.X),
            writes=[b_ps4[1], b_stat])
        s.emit("vector", lambda e, tt=tt: e.tensor_scalar(
            gmask[:, tt, :], ps_S[:, 512:512 + NE], stat[:, 7:8], None, ALU.is_ge),
            reads=[b_stat], writes=[b_ps4[1], b_gmask])
        if "thin" not in parts:
            continue
        for ft in range(ND):
            s.emit("tensor", lambda e, tt=tt, ft=ft: e.matmul(
                ps_a[:, 0:H], xn_bf[:, ft, tt * 128:(tt + 1) * 128], wb_sb[:, ft, :],
                start=(ft == 0), stop=(ft == ND - 1)),
                reads=[b_xn, b_smallw], writes=[b_psa])
        s.emit("scalar", lambda e, tt=tt: e.activation(
            sig_tm[:, tt, :], ps_a[:, 0:H], AF.Sigmoid),
            writes=[b_psa, b_sig])
        for ft in range(ND):
            s.emit("tensor", lambda e, tt=tt, ft=ft: e.matmul(
                ps_a[:, 128:128 + H], xn_bf[:, ft, tt * 128:(tt + 1) * 128], wf_sb[:, ft, :],
                start=(ft == 0), stop=(ft == ND - 1)),
                reads=[b_xn, b_smallw], writes=[b_psa])
        s.emit("vector", lambda e: e.tensor_tensor(
            ps_a[:, 128:128 + H], ps_a[:, 128:128 + H], delta_bc[:, :], ALU.add),
            reads=[b_smallw], writes=[b_psa])
        s.emit("scalar", lambda e, tt=tt: e.activation(
            x_slab[:, tt % 2, 0:H], ps_a[:, 128:128 + H], AF.Sigmoid),
            writes=[b_psa, b_xslab[tt % 2]])
        s.emit("scalar", lambda e, tt=tt: e.activation(
            logf_bf[:, tt, :], x_slab[:, tt % 2, 0:H], AF.Ln),
            reads=[b_xslab[tt % 2]], writes=[b_logf])
        w1r = (tt % 2) * 64
        for ft in range(ND):
            s.emit("tensor", lambda e, tt=tt, ft=ft, w1r=w1r: e.matmul(
                ps_b[w1r:w1r + 64, 0:128], ww1_sb[:, ft, :],
                xn_bf[:, ft, tt * 128:(tt + 1) * 128],
                start=(ft == 0), stop=(ft == ND - 1), tile_position=(0, w1r)),
                reads=[b_xn, b_smallw], writes=[b_psb])
        s.emit("scalar", lambda e, tt=tt, w1r=w1r: e.copy(
            w1o[w1r:w1r + 64, (tt // 2) * 128:(tt // 2) * 128 + 128],
            ps_b[w1r:w1r + 64, 0:128]),
            writes=[b_psb, b_w1o])

    for tt in range(NT if ("fc" in parts and "thin" in parts) else 0):
        for j in range(tt + 1):
            lhs = cst["cumtri"] if j == tt else cst["allones"]
            lb = cst_buf["cumtri"] if j == tt else cst_buf["allones"]
            s.emit("tensor", lambda e, tt=tt, j=j, lhs=lhs: e.matmul(
                ps_S[:, 1024:1024 + H], lhs[:, :], logf_bf[:, j, :],
                start=(j == 0), stop=(j == tt)),
                reads=[b_logf, lb], writes=[b_ps4[2]])
        s.emit("vector", lambda e, tt=tt: e.tensor_copy(
            fc_tm[:, tt, :], ps_S[:, 1024:1024 + H]),
            writes=[b_ps4[2], b_fc])

    dbg_dump("xn_bf", lambda: xn_bf[:, :, :], [128, ND, N], BF16, [b_xn.w])
    dbg_dump("glog", lambda: glog[:, :, :], [128, NT, NE], F32, [b_glog.w])
    dbg_dump("gmask", lambda: gmask[:, :, :], [128, NT, NE], F32, [b_gmask.w])
    dbg_dump("sig_tm", lambda: sig_tm[:, :, :], [128, NT, H], F32, [b_sig.w])
    dbg_dump("logf_bf", lambda: logf_bf[:, :, :], [128, NT, H], BF16, [b_logf.w])
    dbg_dump("fc_tm", lambda: fc_tm[:, :, :], [128, NT, H], F32, [b_fc.w])
    dbg_dump("w1o", lambda: w1o[:, :], [64, N], BF16, [b_w1o.w])



    b_wo = Buf()
    s.emit("gpsimd", lambda e: e.memset(y_acc[:, :, :], 0.0), writes=[b_yacc_all])

    st2 = nc.alloc_sbuf_tensor("st2", [128, 16], F32)
    _attn_guards = []

    def _ga(name, shape, dtype):
        g = nc.sbuf_tensor(name, shape, dtype)
        _attn_guards.append(g)
        return g.__enter__()

    q_fm = _ga("q_fm", [128, N], BF16)
    k_fm = _ga("k_fm", [128, N], BF16)
    w_fm = _ga("w_fm", [128, N], BF16)
    wb_fm = _ga("wb_fm", [128, N], BF16)
    w_tm = _ga("w_tm", [128, NT, 128], BF16)
    wb_tm = _ga("wb_tm", [128, NT, 128], BF16)
    v_aug = _ga("v_aug", [128, NT, 130], BF16)
    kt = _ga("kt", [128, 2, N], BF16)
    qhat_k = _ga("qhat_k", [128, 2, 128], BF16)
    sii_k = _ga("sii_k", [128, 2, 2, 128], BF16)
    pt_k = _ga("pt_k", [128, 2, 64], BF16)
    ktd_k = _ga("ktd_k", [128, 2, 128], BF16)
    Lb = _ga("Lb", [128, 2, 128], BF16)
    Nb = _ga("Nb", [128, 2, 128], BF16)
    Mb = _ga("Mb", [128, 2, 128], BF16)
    nGt = _ga("nGt", [128, 2, 128], BF16)
    npow = _ga("npow", [128, 2, 6, 128], BF16)
    lkb = _ga("lkb", [128, 2, 128], BF16)
    zb = _ga("zb", [128, 2, 2, 192], BF16)
    s_sb = _ga("s_sb", [128, 2, N], BF16)
    pT_sb = _ga("pT_sb", [128, 2, 4, 128], BF16)
    o_tmp = _ga("o_tmp", [128, 2, 64], BF16)
    fcol = _ga("fcol", [128, 2, NT], F32)
    fcmr = _ga("fcmr", [128, 2, NT], F32)
    st3 = _ga("st3", [128, 2, 8], F32)
    wq_sl = _ga("wq_sl", [128, ND, 128], BF16)
    wk_sl = _ga("wk_sl", [128, ND, 128], BF16)
    wv_sl = _ga("wv_sl", [128, ND, 128], BF16)
    ww2_sl = _ga("ww2_sl", [128, 128], BF16)

    b_q = Buf(); b_k = Buf(); b_wfm = Buf(); b_wbfm = Buf()
    b_wtm = Buf(); b_wbtm = Buf(); b_vaug = Buf()
    b_kt = [[Buf(), Buf()], [Buf(), Buf()]]
    b_qh = [Buf(), Buf()]; b_sii = [Buf(), Buf()]
    b_pt = [Buf(), Buf()]; b_ktd = [Buf(), Buf()]
    b_L = [Buf(), Buf()]; b_N = [Buf(), Buf()]; b_M = [Buf(), Buf()]
    b_nGt = [Buf(), Buf()]; b_np = [Buf(), Buf()]; b_lk = [Buf(), Buf()]
    b_z = [[Buf(), Buf()], [Buf(), Buf()]]
    b_s = [Buf(), Buf()]
    b_pT = [[Buf() for _ in range(4)], [Buf() for _ in range(4)]]
    b_otmp = [Buf(), Buf()]
    b_fcol = [Buf(), Buf()]; b_fcmr = [Buf(), Buf()]
    b_st2 = Buf(); b_st3 = [Buf(), Buf()]
    b_wqsl = Buf(); b_wksl = Buf(); b_wvsl = Buf(); b_ww2sl = Buf()
    b_ofm = Buf()

    def emit_proj(hp):
        c0 = hp * 128
        s.emit("gpsimd", lambda e, c0=c0: e.dma_start(
            out=wq_sl[:, :, :],
            in_=wq_d[:, c0:c0 + 128].rearrange("(t p) c -> p t c", p=128)),
            writes=[b_wqsl], inc_k=16)
        s.emit("gpsimd", lambda e, c0=c0: e.dma_start(
            out=wk_sl[:, :, :],
            in_=wk_d[:, c0:c0 + 128].rearrange("(t p) c -> p t c", p=128)),
            writes=[b_wksl], inc_k=16)
        s.emit("gpsimd", lambda e, c0=c0: e.dma_start(
            out=wv_sl[:, :, :],
            in_=wv_d[:, c0:c0 + 128].rearrange("(t p) c -> p t c", p=128)),
            writes=[b_wvsl], inc_k=16)
        s.emit("gpsimd", lambda e, c0=c0: e.dma_start(
            out=ww2_sl[0:64, :], in_=ww2_d[:, c0:c0 + 128]),
            writes=[b_ww2sl], inc_k=16)
        s.emit("gpsimd", lambda e, c0=c0: e.dma_start(
            out=ww2_sl[64:128, :], in_=ww2_d[:, c0:c0 + 128]),
            writes=[b_ww2sl], inc_k=16)
        s.emit("gpsimd", lambda e: e.memset(v_aug[:, :, 64:65], 1.0), writes=[b_vaug])
        s.emit("gpsimd", lambda e: e.memset(v_aug[:, :, 129:130], 1.0), writes=[b_vaug])
        for dst, wsl, bw, bdst in ((q_fm, wq_sl, b_wqsl, b_q), (k_fm, wk_sl, b_wksl, b_k)):
            for t4 in range(4):
                for ft in range(ND):
                    s.emit("tensor", lambda e, dst=dst, wsl=wsl, t4=t4, ft=ft: e.matmul(
                        ps_a[:, 0:512], wsl[:, ft, :],
                        xn_bf[:, ft, t4 * 512:(t4 + 1) * 512],
                        start=(ft == 0), stop=(ft == ND - 1)),
                        reads=[bw, b_xn], writes=[b_psa])
                s.emit("scalar", lambda e, dst=dst, t4=t4: e.copy(
                    dst[:, t4 * 512:(t4 + 1) * 512], ps_a[:, 0:512]),
                    writes=[b_psa, bdst])
        for tt in range(NT):
            for ft in range(ND):
                s.emit("tensor", lambda e, tt=tt, ft=ft: e.matmul(
                    ps_b[:, 0:128], xn_bf[:, ft, tt * 128:(tt + 1) * 128],
                    wv_sl[:, ft, :], start=(ft == 0), stop=(ft == ND - 1)),
                    reads=[b_wvsl, b_xn], writes=[b_psb])
            s.emit("vector", lambda e, tt=tt: e.tensor_copy(
                bass.AP(v_aug, tt * 130, [[v_aug.shape[2] * NT, 128], [65, 2], [1, 64]]),
                ps_b[:, 0:128]),
                writes=[b_psb, b_vaug])
            s.emit(*ATOMB)
            w1r = (tt % 2) * 64
            s.emit("tensor", lambda e, tt=tt, w1r=w1r: e.matmul(
                ps_S[:, 0:128], w1o[w1r:w1r + 64, (tt // 2) * 128:(tt // 2) * 128 + 128],
                ww2_sl[w1r:w1r + 64, :], start=True, stop=True, tile_position=(w1r, 0)),
                reads=[b_w1o, b_ww2sl], writes=[b_ps4[0]])
            for half in range(2):
                s.emit("scalar", lambda e, tt=tt, half=half: e.activation(
                    x_slab[:, 0, half * 64:(half + 1) * 64],
                    ps_S[:, half * 64:(half + 1) * 64], AF.Square,
                    accum_out=st2[:, 8 + half:9 + half]),
                    writes=[b_ps4[0], b_st2, b_xslab[0]])
            s.emit("scalar", lambda e: e.activation(
                st2[:, 10:12], st2[:, 8:10], AF.Sqrt, bias=fc(1e-6)),
                reads=[cst_buf["fconst"]], writes=[b_st2])
            s.emit("vector", lambda e: e.reciprocal(st2[:, 12:14], st2[:, 10:12]),
                   writes=[b_st2])
            s.emit("vector", lambda e, tt=tt, hp=hp: e.scalar_tensor_tensor(
                st2[:, 14:16], sig_tm[:, tt, 2 * hp:2 * hp + 2], 2.0,
                st2[:, 12:14], ALU.mult, ALU.mult),
                reads=[b_sig], writes=[b_st2])
            for half in range(2):
                s.emit("vector", lambda e, tt=tt, half=half: e.tensor_scalar(
                    w_tm[:, tt, half * 64:(half + 1) * 64],
                    ps_S[:, half * 64:(half + 1) * 64],
                    st2[:, 12 + half:13 + half], None, ALU.mult),
                    reads=[b_st2], writes=[b_ps4[0], b_wtm])
                s.emit("vector", lambda e, tt=tt, half=half: e.tensor_scalar(
                    wb_tm[:, tt, half * 64:(half + 1) * 64],
                    ps_S[:, half * 64:(half + 1) * 64],
                    st2[:, 14 + half:15 + half], None, ALU.mult),
                    reads=[b_st2], writes=[b_ps4[0], b_wbtm])
            s.emit(*ATOME)
            s.emit("tensor", lambda e, tt=tt: e.transpose(
                ps_c[:, 512:640], w_tm[:, tt, :], cst["id128"][:, :]),
                reads=[b_wtm, cst_buf["id128"]], writes=[b_psc])
            s.emit("scalar", lambda e, tt=tt: e.copy(
                w_fm[:, tt * 128:(tt + 1) * 128], ps_c[:, 512:640]),
                writes=[b_psc, b_wfm])
            s.emit("tensor", lambda e, tt=tt: e.transpose(
                ps_c[:, 640:768], wb_tm[:, tt, :], cst["id128"][:, :]),
                reads=[b_wbtm, cst_buf["id128"]], writes=[b_psc])
            s.emit("scalar", lambda e, tt=tt: e.copy(
                wb_fm[:, tt * 128:(tt + 1) * 128], ps_c[:, 640:768]),
                writes=[b_psc, b_wbfm])

    def emit_head_prep(h, ph):
        bk = 2 + ph
        s.emit("tensor", lambda e, h=h, bk=bk: e.matmul(
            ps_S[:, bk * 512 + 256:bk * 512 + 257], cst["ones_row_f32"][:, :],
            fc_tm[0:1, 8, h:h + 1], start=True, stop=True),
            reads=[b_fc, cst_buf["ones_row_f32"]], writes=[b_ps4[bk]])
        s.emit("vector", lambda e, ph=ph, bk=bk: e.tensor_copy(
            st3[:, ph, 0:1], ps_S[:, bk * 512 + 256:bk * 512 + 257]),
            writes=[b_ps4[bk], b_st3[ph]])
        s.emit("vector", lambda e, ph=ph: e.tensor_scalar(
            st3[:, ph, 1:2], st3[:, ph, 0:1], -1.0, None, ALU.mult),
            writes=[b_st3[ph]])
        s.emit("scalar", lambda e, h=h, ph=ph: e.activation(
            fcol[:, ph, :], bass.AP(fc_tm, h, [[NT * H, 128], [H, NT], [1, 1]]),
            AF.Exp, bias=st3[:, ph, 0:1], scale=fc(-1.0)),
            reads=[b_fc, b_st3[ph], cst_buf["fconst"]], writes=[b_fcol[ph]])
        s.emit("vector", lambda e, h=h, ph=ph: e.tensor_scalar(
            fcmr[:, ph, :], bass.AP(fc_tm, h, [[NT * H, 128], [H, NT], [1, 1]]),
            st3[:, ph, 1:2], None, ALU.add),
            reads=[b_fc, b_st3[ph]], writes=[b_fcmr[ph]])

    def emit_phaseA(h, ph, i):
        psA = ps_a if ph == 0 else ps_b
        bA = b_psa if ph == 0 else b_psb
        r0 = ph * 64
        isl = slice(i * 128, (i + 1) * 128)
        W_I = w_fm[r0:r0 + 64, isl]
        Wb_I = wb_fm[r0:r0 + 64, isl]
        Q_I = q_fm[r0:r0 + 64, isl]
        K_I = k_fm[r0:r0 + 64, isl]
        Wtm_I = w_tm[:, i, r0:r0 + 64]
        Wbtm_I = wb_tm[:, i, r0:r0 + 64]
        s.emit("tensor", lambda e: e.matmul(psA[:, 0:128], W_I, Wb_I,
               start=True, stop=True, tile_position=(r0, 0)),
               reads=[b_wfm, b_wbfm], writes=[bA])
        s.emit("vector", lambda e: e.tensor_tensor(
            Lb[:, ph, :], psA[:, 0:128], cst["m_trils"][:, :], ALU.mult),
            reads=[cst_buf["m_trils"]], writes=[bA, b_L[ph]])
        s.emit("tensor", lambda e: e.matmul(psA[:, 128:256], Wb_I, W_I,
               start=True, stop=True, tile_position=(r0, 0)),
               reads=[b_wfm, b_wbfm], writes=[bA])
        s.emit("vector", lambda e: e.tensor_tensor(
            Nb[:, ph, :], psA[:, 128:256], cst["m_trius"][:, :], ALU.mult),
            reads=[cst_buf["m_trius"]], writes=[bA, b_N[ph]])
        s.emit("tensor", lambda e: e.matmul(psA[:, 256:384], W_I, K_I,
               start=True, stop=True, tile_position=(r0, 0)),
               reads=[b_wfm, b_k], writes=[bA])
        s.emit("vector", lambda e: e.tensor_tensor(
            Mb[:, ph, :], psA[:, 256:384], cst["m_trils"][:, :], ALU.mult),
            reads=[cst_buf["m_trils"]], writes=[bA, b_M[ph]])
        s.emit("tensor", lambda e: e.matmul(psA[:, 384:512], Wb_I, Q_I,
               start=True, stop=True, tile_position=(r0, 0)),
               reads=[b_wbfm, b_q], writes=[bA])
        s.emit("vector", lambda e: e.tensor_tensor(
            nGt[:, ph, :], psA[:, 384:512], cst["m_negtriu"][:, :], ALU.mult),
            reads=[cst_buf["m_negtriu"]], writes=[bA, b_nGt[ph]])

        def lp(lev):
            return Lb[:, ph, :] if lev == 0 else lkb[:, ph, :]

        def np_(lev):
            return Nb[:, ph, :] if lev == 0 else npow[:, ph, lev - 1, :]

        for lev in range(6):
            bl = b_L[ph] if lev == 0 else b_lk[ph]
            bn = b_N[ph] if lev == 0 else b_np[ph]
            s.emit("tensor", lambda e, lev=lev: e.matmul(
                psA[:, 128:256], lp(lev), np_(lev), start=True, stop=True),
                reads=[bl, bn], writes=[bA])
            s.emit("scalar", lambda e, lev=lev: e.copy(
                npow[:, ph, lev, :], psA[:, 128:256]),
                writes=[bA, b_np[ph]])
            if lev < 5:
                s.emit("tensor", lambda e, lev=lev: e.matmul(
                    psA[:, 0:128], np_(lev), lp(lev), start=True, stop=True),
                    reads=[bl, bn], writes=[bA])
                s.emit("vector", lambda e: e.tensor_copy(
                    lkb[:, ph, :], psA[:, 0:128]),
                    writes=[bA, b_lk[ph]])
        zc = 0
        s.emit("tensor", lambda e: e.matmul(psA[:, 0:64], npow[:, ph, 5, :], Wtm_I,
               start=True, stop=False), reads=[b_np[ph], b_wtm], writes=[bA])
        s.emit("tensor", lambda e: e.matmul(psA[:, 64:192], npow[:, ph, 5, :], Mb[:, ph, :],
               start=False, stop=True), reads=[b_np[ph], b_M[ph]], writes=[bA])
        s.emit("vector", lambda e: e.tensor_tensor(
            zb[:, ph, 0, 0:64], Wtm_I, psA[:, 0:64], ALU.add),
            reads=[b_wtm], writes=[bA, b_z[ph][0]])
        s.emit("vector", lambda e: e.tensor_tensor(
            zb[:, ph, 0, 64:192], Mb[:, ph, :], psA[:, 64:192], ALU.add),
            reads=[b_M[ph]], writes=[bA, b_z[ph][0]])
        for lev in (4, 3, 2, 1, 0):
            zn = 1 - zc
            s.emit("tensor", lambda e, lev=lev, zc=zc: e.matmul(
                psA[:, 0:192], npow[:, ph, lev, :], zb[:, ph, zc, :],
                start=True, stop=True), reads=[b_np[ph], b_z[ph][zc]], writes=[bA])
            s.emit("vector", lambda e, zc=zc, zn=zn: e.tensor_tensor(
                zb[:, ph, zn, :], zb[:, ph, zc, :], psA[:, 0:192], ALU.add),
                reads=[b_z[ph][zc]], writes=[bA, b_z[ph][zn]])
            zc = zn
        zn = 1 - zc
        s.emit("tensor", lambda e, zc=zc: e.matmul(
            psA[:, 0:192], Nb[:, ph, :], zb[:, ph, zc, :],
            start=True, stop=True), reads=[b_N[ph], b_z[ph][zc]], writes=[bA])
        s.emit("vector", lambda e, zc=zc, zn=zn: e.tensor_tensor(
            zb[:, ph, zn, :], zb[:, ph, zc, :], psA[:, 0:192], ALU.subtract),
            reads=[b_z[ph][zc]], writes=[bA, b_z[ph][zn]])
        zf = zn
        TW = zb[:, ph, zf, 0:64]
        CI2 = zb[:, ph, zf, 64:192]
        bZ = b_z[ph][zf]
        s.emit("tensor", lambda e: e.matmul(psA[:, 256:384], Q_I, K_I,
               start=True, stop=False, tile_position=(r0, 0)),
               reads=[b_q, b_k], writes=[bA])
        s.emit("tensor", lambda e: e.matmul(psA[:, 256:384], nGt[:, ph, :], CI2,
               start=False, stop=True), reads=[b_nGt[ph], bZ], writes=[bA])
        s.emit("scalar", lambda e, i=i, ph=ph: e.copy(
            sii_k[:, ph, i % 2, :], psA[:, 256:384]),
            writes=[bA, b_sii[ph]])
        s.emit("tensor", lambda e: e.matmul(
            psA[r0:r0 + 64, 0:128], TW, nGt[:, ph, :],
            start=True, stop=True, tile_position=(0, r0)),
            reads=[bZ, b_nGt[ph]], writes=[bA])
        s.emit("vector", lambda e, i=i: e.tensor_tensor(
            qhat_k[r0:r0 + 64, i % 2, :], Q_I, psA[r0:r0 + 64, 0:128], ALU.add),
            reads=[b_q], writes=[bA, b_qh[ph]])
        s.emit("tensor", lambda e: e.matmul(
            psA[r0:r0 + 64, 128:192], TW, Wbtm_I,
            start=True, stop=True, tile_position=(0, r0)),
            reads=[bZ, b_wbtm], writes=[bA])
        s.emit("vector", lambda e, i=i: e.tensor_tensor(
            pt_k[r0:r0 + 64, i % 2, :], cst["i64stack"][r0:r0 + 64, :],
            psA[r0:r0 + 64, 128:192], ALU.subtract),
            reads=[cst_buf["i64stack"]], writes=[bA, b_pt[ph]])
        s.emit("tensor", lambda e: e.matmul(
            psA[r0:r0 + 64, 256:384], Wbtm_I, CI2,
            start=True, stop=True, tile_position=(0, r0)),
            reads=[b_wbtm, bZ], writes=[bA])
        s.emit("vector", lambda e, i=i: e.tensor_tensor(
            ktd_k[r0:r0 + 64, i % 2, :], K_I, psA[r0:r0 + 64, 256:384], ALU.subtract),
            reads=[b_k], writes=[bA, b_ktd[ph]])

    def emit_phaseB_chunk(h, ph, i):
        r0 = ph * 64
        bk = 2 + ph
        sk = ph
        psc = ps_c if ph == 0 else ps_d
        bpsc = b_psc if ph == 0 else b_psd
        rbuf = (i + 1) % 2
        wbuf = i % 2
        jend = (i + 1) * 128
        nj5 = (i * 128 + 511) // 512
        for j5 in range(nj5):
            w = min(512, i * 128 - j5 * 512)
            s.emit(*ATOMB)
            s.emit("tensor", lambda e, i=i, j5=j5, w=w, rbuf=rbuf: e.matmul(
                ps_S[:, sk * 512:sk * 512 + w],
                qhat_k[r0:r0 + 64, i % 2, :], kt[r0:r0 + 64, rbuf, j5 * 512:j5 * 512 + w],
                start=True, stop=True, tile_position=(r0, 0)),
                reads=[b_qh[ph], b_kt[ph][rbuf]], writes=[b_ps4[sk]])
            s.emit("vector", lambda e, j5=j5, w=w, ph=ph: e.tensor_copy(
                s_sb[:, ph, j5 * 512:j5 * 512 + w], ps_S[:, sk * 512:sk * 512 + w]),
                writes=[b_ps4[sk], b_s[ph]])
            s.emit(*ATOME)
        s.emit("vector", lambda e, i=i, ph=ph: e.tensor_tensor(
            s_sb[:, ph, i * 128:(i + 1) * 128], sii_k[:, ph, i % 2, :],
            cst["m_addneg"][:, :], ALU.add),
            reads=[b_sii[ph], cst_buf["m_addneg"]], writes=[b_s[ph]])
        s.emit("vector", lambda e, jend=jend, ph=ph: e.reduce_max(
            st3[:, ph, 2:3], s_sb[:, ph, 0:jend], axis=AX.X),
            reads=[b_s[ph]], writes=[b_st3[ph]])
        s.emit("vector", lambda e, i=i, ph=ph: e.tensor_scalar(
            st3[:, ph, 3:4], st3[:, ph, 2:3], -SCALE, fcmr[:, ph, i:i + 1],
            ALU.mult, ALU.add),
            reads=[b_fcmr[ph]], writes=[b_st3[ph]])
        s.emit("scalar", lambda e, jend=jend, ph=ph: e.activation(
            s_sb[:, ph, 0:jend], s_sb[:, ph, 0:jend], AF.Exp,
            bias=st3[:, ph, 3:4], scale=fc(SCALE)),
            reads=[b_st3[ph], cst_buf["fconst"]], writes=[b_s[ph]])
        for jt in range(i + 1):
            slot = jt % 7
            s.emit("tensor", lambda e, jt=jt, slot=slot, ph=ph, psc=psc: e.transpose(
                psc[:, slot * 128:(slot + 1) * 128],
                s_sb[:, ph, jt * 128:(jt + 1) * 128], cst["id128"][:, :]),
                reads=[b_s[ph], cst_buf["id128"]], writes=[bpsc])
            rs = jt % 4
            s.emit("vector", lambda e, jt=jt, slot=slot, ph=ph, psc=psc, rs=rs: e.tensor_scalar(
                pT_sb[:, ph, rs, :],
                psc[:, slot * 128:(slot + 1) * 128],
                fcol[:, ph, jt:jt + 1], None, ALU.mult),
                reads=[b_fcol[ph]], writes=[bpsc, b_pT[ph][rs]])
            s.emit("tensor", lambda e, jt=jt, i=i, ph=ph, bk=bk, rs=rs: e.matmul(
                ps_S[:, bk * 512:bk * 512 + 65], pT_sb[:, ph, rs, :],
                v_aug[:, jt, ph * 65:ph * 65 + 65],
                start=(jt == 0), stop=(jt == i)),
                reads=[b_pT[ph][rs], b_vaug], writes=[b_ps4[bk]])
        s.emit("vector", lambda e, ph=ph, bk=bk: e.reciprocal(
            st3[:, ph, 4:5], ps_S[:, bk * 512 + 64:bk * 512 + 65]),
            writes=[b_ps4[bk], b_st3[ph]])
        s.emit("vector", lambda e, ph=ph, bk=bk: e.tensor_scalar(
            o_tmp[:, ph, :], ps_S[:, bk * 512:bk * 512 + 64], st3[:, ph, 4:5],
            None, ALU.mult),
            reads=[b_st3[ph]], writes=[b_ps4[bk], b_otmp[ph]])
        s.emit("tensor", lambda e, ph=ph, psc=psc: e.transpose(
            psc[r0:r0 + 64, 896:1024], o_tmp[:, ph, :], cst["id128"][:, :],
            tile_position=(0, r0)),
            reads=[b_otmp[ph], cst_buf["id128"]], writes=[bpsc])
        s.emit("scalar", lambda e, i=i, h=h, ph=ph, psc=psc: e.copy(
            o_fm[r0:r0 + 64, h // 2, i * 128:(i + 1) * 128],
            psc[r0:r0 + 64, 896:1024]),
            writes=[bpsc, b_ofm])
        for j5 in range(nj5):
            w = min(512, i * 128 - j5 * 512)
            s.emit("tensor", lambda e, i=i, j5=j5, w=w, rbuf=rbuf, bk=bk: e.matmul(
                ps_S[r0:r0 + 64, bk * 512:bk * 512 + w], pt_k[r0:r0 + 64, i % 2, :],
                kt[r0:r0 + 64, rbuf, j5 * 512:j5 * 512 + w],
                start=True, stop=True, tile_position=(r0, r0)),
                reads=[b_pt[ph], b_kt[ph][rbuf]], writes=[b_ps4[bk]])
            s.emit("scalar", lambda e, j5=j5, w=w, wbuf=wbuf, bk=bk: e.copy(
                kt[r0:r0 + 64, wbuf, j5 * 512:j5 * 512 + w],
                ps_S[r0:r0 + 64, bk * 512:bk * 512 + w]),
                writes=[b_ps4[bk], b_kt[ph][wbuf]])
        s.emit("gpsimd", lambda e, i=i, wbuf=wbuf: e.tensor_copy(
            kt[r0:r0 + 64, wbuf, i * 128:(i + 1) * 128],
            ktd_k[r0:r0 + 64, i % 2, :]),
            reads=[b_ktd[ph]], writes=[b_kt[ph][wbuf]])

    def defer(fn):
        rec = []
        real = s.emit
        s.emit = lambda *a, **k: rec.append((a, k))
        fn()
        s.emit = real
        return rec

    def play(recs):
        idx = [0] * len(recs)
        while True:
            done = True
            for r in range(len(recs)):
                if idx[r] < len(recs[r]):
                    a, k = recs[r][idx[r]]
                    s.emit(*a, **k)
                    idx[r] += 1
                    done = False
            if done:
                break

    w1_sl = _ga("w1_sl", [128, ND, D], BF16)
    w2_sl = _ga("w2_sl", [128, ND, 256], BF16)
    g_slab = _ga("g_slab", [128, ND, 256], BF16)
    b1_sb = _ga("b1_sb", [128, ND], F32)
    b2_sb = _ga("b2_sb", [1, D], BF16)
    b_w1 = Buf()
    b_w2 = Buf()
    b_g = Buf()
    b_b1 = Buf()
    b_b2 = Buf()

    ATOMB = ("__ab__",)
    ATOME = ("__ae__",)

    def emit_ffn_expert(e_idx):
        s.emit(*ATOMB)
        s.emit("gpsimd", lambda e, e_idx=e_idx: e.dma_start(
            out=w1_sl[:, :, :], in_=w1_d[e_idx].rearrange("(t p) i -> p t i", p=128)),
            writes=[b_w1], inc_k=16)
        s.emit("sync", lambda e, e_idx=e_idx: e.dma_start(
            out=b1_sb[:, :], in_=b1_d[e_idx].rearrange("(t p) -> p t", p=128),
            allow_slow_non_contiguous=True),
            writes=[b_b1], inc_k=16)
        s.emit("gpsimd", lambda e, e_idx=e_idx: e.dma_start(
            out=b2_sb[:, :], in_=b2_d[e_idx:e_idx + 1, :]),
            writes=[b_b2], inc_k=16)
        s.emit(*ATOME)
        for t2 in range(8):
            for it in range(ND):
                hb = it % 2
                s.emit(*ATOMB)
                for ft in range(ND):
                    s.emit("tensor", lambda e, t2=t2, it=it, ft=ft, hb=hb: e.matmul(
                        ps_S[:, hb * 512:hb * 512 + 256],
                        w1_sl[:, ft, it * 128:(it + 1) * 128],
                        xn_bf[:, ft, t2 * 256:(t2 + 1) * 256],
                        start=(ft == 0), stop=(ft == ND - 1)),
                        reads=[b_w1, b_xn], writes=[b_ps4[hb]])
                s.emit("scalar", lambda e, it=it, hb=hb: e.activation(
                    g_slab[:, it, :], ps_S[:, hb * 512:hb * 512 + 256], AF.Gelu,
                    bias=b1_sb[:, it:it + 1]),
                    reads=[b_b1], writes=[b_ps4[hb], b_g])
                s.emit(*ATOME)
            for dq in range(4):
                s.emit("gpsimd", lambda e, e_idx=e_idx, dq=dq: e.dma_start(
                    out=w2_sl[:, :, :],
                    in_=w2_d[e_idx][:, dq * 256:(dq + 1) * 256].rearrange(
                        "(t p) i -> p t i", p=128)),
                    writes=[b_w2], inc_k=16)
                for tsub in range(2):
                    tt = t2 * 2 + tsub
                    yb = 2 + (tsub + dq * 2) % 2
                    s.emit(*ATOMB)
                    for it in range(ND):
                        s.emit("tensor", lambda e, tsub=tsub, dq=dq, it=it, yb=yb: e.matmul(
                            ps_S[:, yb * 512:yb * 512 + 256],
                            g_slab[:, it, tsub * 128:(tsub + 1) * 128],
                            w2_sl[:, it, :],
                            start=(it == 0), stop=False),
                            reads=[b_g, b_w2], writes=[b_ps4[yb]])
                    s.emit("tensor", lambda e, dq=dq, yb=yb: e.matmul(
                        ps_S[:, yb * 512:yb * 512 + 256],
                        cst["ones_row_bf"][0:1, 0:128],
                        b2_sb[0:1, dq * 256:(dq + 1) * 256],
                        start=False, stop=True),
                        reads=[b_b2, cst_buf["ones_row_bf"]], writes=[b_ps4[yb]])
                    s.emit("vector", lambda e, tt=tt, dq=dq, e_idx=e_idx, yb=yb: e.scalar_tensor_tensor(
                        y_acc[:, tt, dq * 256:(dq + 1) * 256],
                        ps_S[:, yb * 512:yb * 512 + 256],
                        gmask[:, tt, e_idx:e_idx + 1],
                        y_acc[:, tt, dq * 256:(dq + 1) * 256],
                        ALU.mult, ALU.add),
                        reads=[b_gmask, b_yacc_all], writes=[b_ps4[yb], b_yacc[tt]])
                    s.emit(*ATOME)

    def defer(fn):
        rec = []
        real = s.emit
        s.emit = lambda *a, **k: rec.append((a, k))
        fn()
        s.emit = real
        return rec

    def units_of(rec):
        out = []
        cur = None
        for a, k in rec:
            if a and a[0] == "__ab__":
                cur = []
            elif a and a[0] == "__ae__":
                if cur:
                    out.append(cur)
                cur = None
            elif cur is not None:
                cur.append((a, k))
            else:
                out.append([(a, k)])
        if cur:
            out.append(cur)
        return out

    def emit_unit(u):
        if len(u) > 1:
            s.emit(*ATOMB)
        for a, k in u:
            s.emit(*a, **k)
        if len(u) > 1:
            s.emit(*ATOME)

    def play(recs):
        units = [units_of(r) for r in recs]
        idx = [0] * len(recs)
        while True:
            done = True
            for r in range(len(recs)):
                if idx[r] < len(units[r]):
                    emit_unit(units[r][idx[r]])
                    idx[r] += 1
                    done = False
            if done:
                break

    def play_weighted(att, ffn_units):
        att_units = units_of(att)
        n_att = len(att_units)
        n_f = len(ffn_units)
        fi = 0
        for j, u in enumerate(att_units):
            emit_unit(u)
            while fi < n_f and (j + 1) * n_f >= (fi + 1) * n_att:
                emit_unit(ffn_units[fi])
                fi += 1
        while fi < n_f:
            emit_unit(ffn_units[fi])
            fi += 1

    for hp in range(hp_count):
        def att_block(hp=hp):
            emit_proj(hp)
            h0, h1 = 2 * hp, 2 * hp + 1
            emit_head_prep(h0, 0)
            emit_head_prep(h1, 1)
            play([defer(lambda: emit_phaseA(h0, 0, 0)),
                  defer(lambda: emit_phaseA(h1, 1, 0))])
            for i in range(NCH):
                recs = [defer(lambda i=i: emit_phaseB_chunk(h0, 0, i)),
                        defer(lambda i=i: emit_phaseB_chunk(h1, 1, i))]
                if i + 1 < NCH:
                    recs.append(defer(lambda i=i: emit_phaseA(h0, 0, i + 1)))
                    recs.append(defer(lambda i=i: emit_phaseA(h1, 1, i + 1)))
                play(recs)
        att_block()

    for e_idx in range(NE):
        emit_ffn_expert(e_idx)

    dbg_dump("o_fm", lambda: o_fm[:, :, :], [128, ND, N], BF16, [b_ofm.w])

    attn_bufs = ([b_q, b_k, b_wfm, b_wbfm, b_wtm, b_wbtm, b_vaug]
                 + b_kt[0] + b_kt[1] + b_qh + b_sii + b_pt + b_ktd
                 + b_L + b_N + b_M + b_nGt + b_np + b_lk
                 + b_z[0] + b_z[1] + b_s + b_pT[0] + b_pT[1] + b_otmp
                 + b_fcol + b_fcmr + b_st3
                 + [b_wqsl, b_wksl, b_wvsl, b_ww2sl,
                    b_w1, b_w2, b_g, b_b1, b_b2])
    barrier = []
    for b in attn_bufs:
        if b.w is not None:
            barrier.append(b.w)
        barrier.extend(b.readers)
    barrier = list({id(t): t for t in barrier}.values())
    for g in reversed(_attn_guards):
        g.__exit__(None, None, None)

    dbg_dump("y_acc", lambda: y_acc[:, :, :], [128, NT, D], BF16,
             [b.w for b in b_yacc if b.w is not None])

    wo_sb = nc.alloc_sbuf_tensor("wo_sb", [128, ND, D], BF16)
    s.emit("gpsimd", lambda e: e.dma_start(
        out=wo_sb[:, :, :], in_=wo_d.rearrange("(t p) d -> p t d", p=128)),
        writes=[b_wo], deps=barrier, inc_k=16)

    b_out = Buf()
    for tt in range(NT):
        sl = tt % 2
        s.emit("sync", lambda e, tt=tt, sl=sl: e.dma_start(
            out=x_slab[:, sl, :], in_=x_d.rearrange("(t p) d -> t p d", p=128)[tt]),
            writes=[b_xslab[sl]], inc_k=16)
        for dh in range(2):
            psx = ps_a if dh == 0 else ps_b
            bpsx = b_psa if dh == 0 else b_psb
            for ft in range(ND):
                s.emit("tensor", lambda e, tt=tt, dh=dh, ft=ft, psx=psx: e.matmul(
                    psx[:, 0:512], o_fm[:, ft, tt * 128:(tt + 1) * 128],
                    wo_sb[:, ft, dh * 512:(dh + 1) * 512],
                    start=(ft == 0), stop=(ft == ND - 1)),
                    reads=[b_ofm, b_wo], writes=[bpsx])
            s.emit("vector", lambda e, tt=tt, dh=dh, sl=sl, psx=psx: e.tensor_tensor(
                x_slab[:, sl, dh * 512:(dh + 1) * 512], psx[:, 0:512],
                x_slab[:, sl, dh * 512:(dh + 1) * 512], ALU.add),
                writes=[bpsx, b_xslab[sl]])
            s.emit("vector", lambda e, tt=tt, dh=dh, sl=sl: e.tensor_tensor(
                x_slab[:, sl, dh * 512:(dh + 1) * 512],
                x_slab[:, sl, dh * 512:(dh + 1) * 512],
                y_acc[:, tt, dh * 512:(dh + 1) * 512], ALU.add),
                reads=[b_yacc[tt], b_yacc_all], writes=[b_xslab[sl]])
        s.emit("sync", lambda e, tt=tt, sl=sl: e.dma_start(
            out=out_d.rearrange("(t p) d -> t p d", p=128)[tt], in_=x_slab[:, sl, :]),
            reads=[b_xslab[sl]], writes=[b_out], inc_k=16)
    s.emit("sync", lambda e: e.nop(), deps=[b_out.w])

    s.finalize(nc, paranoid=paranoid)
    return nc, dbg_d


def stage_inputs(inputs):
    consts = host_constants()
    x = np.ascontiguousarray(inputs["L0_attn_x"])
    base = {
        "wq": inputs["L0_attn_Wq"], "wk": inputs["L0_attn_Wk"],
        "wv": inputs["L0_attn_Wv"], "wo": inputs["L0_attn_Wo"],
        "ww1": inputs["L0_attn_Ww1"], "ww2": inputs["L0_attn_Ww2"],
        "wbm": inputs["L0_attn_Wb"], "wfm": inputs["L0_attn_Wf"],
        "delta_bc": np.ascontiguousarray(np.broadcast_to(
            inputs["L0_attn_delta"][None, :], (128, H))),
        "wg": inputs["L0_ffn_Wg"],
        "bg_bc": np.ascontiguousarray(np.broadcast_to(
            inputs["L0_ffn_bg"][None, :], (128, NE))),
        "w1": inputs["L0_ffn_W1"], "b1": inputs["L0_ffn_b1"],
        "w2": inputs["L0_ffn_W2"], "b2": inputs["L0_ffn_b2"],
    }
    for k, v in consts.items():
        base[f"c_{k}"] = v
    in_maps = []
    for c in range(8):
        m = dict(base)
        m["x"] = np.ascontiguousarray(x[c])
        in_maps.append(m)
    return in_maps


def kernel(**inputs):
    nc, _ = build()
    in_maps = stage_inputs(inputs)
    res = run_bass_kernel_spmd(nc, in_maps, core_ids=list(range(8)))
    return np.stack([res.results[c]["out"] for c in range(8)], axis=0)


# version 09
# speedup vs baseline: 1.3442x; 1.2161x over previous
import math
import numpy as np
import ml_dtypes
import concourse.bass as bass
import concourse.mybir as mybir
from concourse.bass_utils import run_bass_kernel_spmd

BFNP = ml_dtypes.bfloat16
F32 = mybir.dt.float32
BF16 = mybir.dt.bfloat16
AF = mybir.ActivationFunctionType
ALU = mybir.AluOpType
AX = mybir.AxisListType

N, D, H, HD, C = 2048, 1024, 16, 64, 128
NCH = N // C
NT = N // 128
ND = D // 128
NE = 8
SCALE = HD ** -0.5

ENGINES = ("tensor", "vector", "scalar", "gpsimd", "sync")

FCONST_VALS = (1e-5, 1e-6, -1.0, 2.0, 0.5, -0.5, SCALE, -SCALE)
FCONST_IDX = {v: i for i, v in enumerate(FCONST_VALS)}


class Op:
    __slots__ = ("eng", "fn", "deps", "inc_k", "sem", "sem_after", "pos",
                 "needs_inc", "pre_wait")

    def __init__(self, eng, fn, deps, inc_k):
        self.eng = eng
        self.fn = fn
        self.deps = deps
        self.inc_k = inc_k
        self.sem = None
        self.sem_after = None
        self.pos = None
        self.needs_inc = False
        self.pre_wait = None


class Buf:

    def __init__(self):
        self.w = None
        self.readers = []


class Sched:
    def __init__(self):
        self.ops = []

    def emit(self, eng, fn, reads=(), writes=(), deps=(), inc_k=1):
        d = [t for t in deps if t is not None]
        for b in reads:
            if b.w is not None:
                d.append(b.w)
        for b in writes:
            if b.w is not None:
                d.append(b.w)
            d.extend(b.readers)
        op = Op(eng, fn, d, inc_k)
        op.pos = len(self.ops)
        self.ops.append(op)
        for b in reads:
            b.readers.append(op)
        for b in writes:
            b.w = op
            b.readers = []
        return op

    def finalize(self, nc, paranoid=False):
        for op in self.ops:
            if op.inc_k == 16:
                op.needs_inc = True
            for dep in op.deps:
                assert dep.pos < op.pos
                if dep.eng != op.eng or dep.inc_k == 16 or paranoid:
                    dep.needs_inc = True
        eng_sems = {e: nc.alloc_semaphore(f"sem_{e}") for e in ENGINES}
        counts = {e: 0 for e in ENGINES}
        NPOOL = 8
        pool = {e: [nc.alloc_semaphore(f"dma_{e}_{i}") for i in range(NPOOL)]
                for e in ("sync", "gpsimd", "scalar")}
        pool_idx = {e: 0 for e in pool}
        pool_count = {}
        for op in self.ops:
            if op.inc_k == 16:
                sem = pool[op.eng][pool_idx[op.eng] % NPOOL]
                pool_idx[op.eng] += 1
                prev = pool_count.get(id(sem), 0)
                if prev:
                    op.pre_wait = (sem, prev)
                op.sem = sem
                op.sem_after = prev + 16
                pool_count[id(sem)] = op.sem_after
            elif op.needs_inc:
                counts[op.eng] += op.inc_k
                op.sem = eng_sems[op.eng]
                op.sem_after = counts[op.eng]
        marks = {e: {} for e in ENGINES}
        streams = {e: [o for o in self.ops if o.eng == e] for e in ENGINES}
        need_drain = set()
        vstream = streams["vector"]
        vpos = {id(o): i for i, o in enumerate(vstream)}
        for i, op in enumerate(vstream):
            for dep in op.deps:
                if dep.eng == "vector" and vpos[id(dep)] == i - 1:
                    need_drain.add(id(op))

        def body(eng_obj, ename):
            wm = marks[ename]
            for op in streams[ename]:
                for dep in op.deps:
                    if dep.eng == ename and dep.inc_k != 16 and not paranoid:
                        continue
                    assert dep.sem_after is not None, (dep.eng, op.eng)
                    if dep.sem_after > wm.get(id(dep.sem), 0):
                        eng_obj.wait_ge(dep.sem, dep.sem_after)
                        wm[id(dep.sem)] = dep.sem_after
                if op.pre_wait is not None:
                    sem, val = op.pre_wait
                    if val > wm.get(id(sem), 0):
                        eng_obj.wait_ge(sem, val)
                        wm[id(sem)] = val
                if id(op) in need_drain:
                    eng_obj.drain()
                instr = op.fn(eng_obj)
                if op.sem is not None:
                    instr.then_inc(op.sem, op.inc_k)

        with nc.Block() as blk:
            @blk.tensor
            def _(e):
                body(e, "tensor")

            @blk.vector
            def _(e):
                body(e, "vector")

            @blk.scalar
            def _(e):
                body(e, "scalar")

            @blk.gpsimd
            def _(e):
                body(e, "gpsimd")

            @blk.sync
            def _(e):
                body(e, "sync")


def host_constants():
    c = {}
    ident = np.eye(128, dtype=np.float32)
    c["id128"] = ident.astype(BFNP)
    c["id128f"] = ident
    c["cumtri"] = np.triu(np.ones((128, 128), np.float32)).astype(BFNP)
    c["allones"] = np.ones((128, 128), np.float32).astype(BFNP)
    c["m_trils"] = np.tril(np.ones((128, 128), np.float32), -1).astype(BFNP)
    c["m_trius"] = np.triu(np.ones((128, 128), np.float32), 1).astype(BFNP)
    c["m_negtriu"] = (-np.triu(np.ones((128, 128), np.float32))).astype(BFNP)
    c["m_addneg"] = np.triu(np.full((128, 128), -1e9, np.float32), 1).astype(BFNP)
    c["i64stack"] = np.concatenate([np.eye(64, dtype=np.float32)] * 2, 0)
    c["ones_row_bf"] = np.ones((1, 512), np.float32).astype(BFNP)
    c["ones_row_f32"] = np.ones((1, 128), np.float32)
    c["fconst"] = np.ascontiguousarray(np.broadcast_to(
        np.array(FCONST_VALS, np.float32)[None, :], (128, len(FCONST_VALS))))
    return c


def build(dbg=None, paranoid=False, parts=("ln", "gate", "thin", "fc"), hp_count=ND):
    dbg = dbg or []
    parts = set(parts)
    nc = bass.Bass("TRN2", target_bir_lowering=False,
                   detect_race_conditions=paranoid)
    s = Sched()

    x_d = nc.dram_tensor("x", [N, D], F32, kind="ExternalInput").ap()
    wq_d = nc.dram_tensor("wq", [D, D], F32, kind="ExternalInput").ap()
    wk_d = nc.dram_tensor("wk", [D, D], F32, kind="ExternalInput").ap()
    wv_d = nc.dram_tensor("wv", [D, D], F32, kind="ExternalInput").ap()
    wo_d = nc.dram_tensor("wo", [D, D], F32, kind="ExternalInput").ap()
    ww1_d = nc.dram_tensor("ww1", [D, HD], F32, kind="ExternalInput").ap()
    ww2_d = nc.dram_tensor("ww2", [HD, D], F32, kind="ExternalInput").ap()
    wb_d = nc.dram_tensor("wbm", [D, H], F32, kind="ExternalInput").ap()
    wf_d = nc.dram_tensor("wfm", [D, H], F32, kind="ExternalInput").ap()
    delta_bc_d = nc.dram_tensor("delta_bc", [128, H], F32, kind="ExternalInput").ap()
    wg_d = nc.dram_tensor("wg", [D, NE], F32, kind="ExternalInput").ap()
    bg_bc_d = nc.dram_tensor("bg_bc", [128, NE], F32, kind="ExternalInput").ap()
    w1_d = nc.dram_tensor("w1", [NE, D, D], F32, kind="ExternalInput").ap()
    b1_d = nc.dram_tensor("b1", [NE, D], F32, kind="ExternalInput").ap()
    w2_d = nc.dram_tensor("w2", [NE, D, D], F32, kind="ExternalInput").ap()
    b2_d = nc.dram_tensor("b2", [NE, D], F32, kind="ExternalInput").ap()
    cst_d = {}
    consts = host_constants()
    for k, v in consts.items():
        dt = BF16 if v.dtype == BFNP else F32
        cst_d[k] = nc.dram_tensor(f"c_{k}", list(v.shape), dt, kind="ExternalInput").ap()
    out_d = nc.dram_tensor("out", [N, D], F32, kind="ExternalOutput").ap()

    dbg_d = {}
    dbg_bufs = {}

    def dbg_dump(name, ap_fn, shape, dtype, deps):
        if name not in dbg:
            return
        dd = nc.dram_tensor(f"dbg_{name}", shape, dtype, kind="ExternalOutput").ap()
        dbg_d[name] = dd
        s.emit("sync", lambda e, a=ap_fn, d=dd: e.dma_start(out=d, in_=a()),
               deps=deps, inc_k=16)

    cst = {}
    cst_buf = {}
    for k, v in consts.items():
        dt = BF16 if v.dtype == BFNP else F32
        t = nc.alloc_sbuf_tensor(f"sb_{k}", list(v.shape), dt)
        cst[k] = t
        b = Buf()
        cst_buf[k] = b
        s.emit("sync", lambda e, t=t, k=k: e.dma_start(out=t[:, :], in_=cst_d[k][:, :]),
               writes=[b], inc_k=16)

    def CC(name):
        return cst[name], cst_buf[name]

    def fc(v):
        return cst["fconst"][:, FCONST_IDX[v]:FCONST_IDX[v] + 1]

    xn_bf = nc.alloc_sbuf_tensor("xn_bf", [128, ND, N], BF16)
    o_fm = nc.alloc_sbuf_tensor("o_fm", [128, ND, N], BF16)
    y_acc = nc.alloc_sbuf_tensor("y_acc", [128, NT, D], BF16)
    fc_tm = nc.alloc_sbuf_tensor("fc_tm", [128, NT, H], F32)
    sig_tm = nc.alloc_sbuf_tensor("sig_tm", [128, NT, H], F32)
    logf_bf = nc.alloc_sbuf_tensor("logf_bf", [128, NT, H], BF16)
    gmask = nc.alloc_sbuf_tensor("gmask", [128, NT, NE], F32)
    w1o = nc.alloc_sbuf_tensor("w1o", [64, N], BF16)
    wg_sb = nc.alloc_sbuf_tensor("wg_sb", [128, ND, NE], F32)
    wb_sb = nc.alloc_sbuf_tensor("wb_sb", [128, ND, H], BF16)
    wf_sb = nc.alloc_sbuf_tensor("wf_sb", [128, ND, H], BF16)
    ww1_sb = nc.alloc_sbuf_tensor("ww1_sb", [128, ND, HD], BF16)
    delta_bc = nc.alloc_sbuf_tensor("delta_sb", [128, H], F32)
    bg_bc = nc.alloc_sbuf_tensor("bg_sb", [128, NE], F32)

    b_xn = Buf()
    b_yacc_all = Buf()
    b_o = [[Buf() for _ in range(NCH)] for _ in range(H)]
    b_yacc = [Buf() for _ in range(NT)]
    b_fc = Buf()
    b_sig = Buf()
    b_logf = Buf()
    b_gmask = Buf()
    b_w1o = Buf()
    b_smallw = Buf()

    s.emit("gpsimd", lambda e: e.dma_start(
        out=wb_sb[:, :, :], in_=wb_d.rearrange("(t p) h -> p t h", p=128)),
        writes=[b_smallw], inc_k=16)
    s.emit("gpsimd", lambda e: e.dma_start(
        out=wf_sb[:, :, :], in_=wf_d.rearrange("(t p) h -> p t h", p=128)),
        writes=[b_smallw], inc_k=16)
    s.emit("gpsimd", lambda e: e.dma_start(
        out=ww1_sb[:, :, :], in_=ww1_d.rearrange("(t p) h -> p t h", p=128)),
        writes=[b_smallw], inc_k=16)
    s.emit("sync", lambda e: e.dma_start(
        out=wg_sb[:, :, :], in_=wg_d.rearrange("(t p) h -> p t h", p=128)),
        writes=[b_smallw], inc_k=16)
    s.emit("sync", lambda e: e.dma_start(out=delta_bc[:, :], in_=delta_bc_d[:, :]),
           writes=[b_smallw], inc_k=16)
    s.emit("sync", lambda e: e.dma_start(out=bg_bc[:, :], in_=bg_bc_d[:, :]),
           writes=[b_smallw], inc_k=16)

    ps_S = nc.alloc_psum_tensor("ps_S", [128, 2048], F32)
    ps_a = nc.alloc_psum_tensor("ps_a", [128, 512], F32)
    ps_b = nc.alloc_psum_tensor("ps_b", [128, 512], F32)
    ps_c = nc.alloc_psum_tensor("ps_c", [128, 1024], BF16)
    ps_d = nc.alloc_psum_tensor("ps_d", [128, 1024], BF16)
    b_ps4 = [Buf(), Buf(), Buf(), Buf()]
    b_psa = Buf()
    b_psb = Buf()
    b_psc = Buf()
    b_psd = Buf()

    x_slab = nc.alloc_sbuf_tensor("x_slab", [128, 2, D], F32)
    stat = nc.alloc_sbuf_tensor("stat", [128, 16], F32)
    glog = nc.alloc_sbuf_tensor("glog", [128, NT, NE], F32)
    x32_fm = nc.alloc_sbuf_tensor("x32_fm", [128, ND, 128], F32)
    b_xslab = [Buf(), Buf()]
    b_xnslab = [Buf(), Buf()]
    b_x32 = Buf()
    b_stat = Buf()
    b_glog = Buf()

    for tt in range(NT):
        if "ln" not in parts and "ln1" not in parts:
            break
        sl = tt % 2
        xs = x_slab
        ld = s.emit("sync", lambda e, tt=tt, sl=sl: e.dma_start(
            out=x_slab[:, sl, :], in_=x_d.rearrange("(t p) d -> t p d", p=128)[tt]),
            writes=[b_xslab[sl]], inc_k=16)
        s.emit("vector", lambda e, sl=sl: e.reduce_sum(
            stat[:, 0:1], x_slab[:, sl, :], axis=AX.X),
            reads=[b_xslab[sl]], writes=[b_stat])
        s.emit("scalar", lambda e, sl=sl: e.activation(
            y_acc[:, sl, :], x_slab[:, sl, :], AF.Square,
            accum_out=stat[:, 1:2]),
            reads=[b_xslab[sl]], writes=[b_stat, b_yacc_all])
        s.emit("vector", lambda e: e.tensor_scalar(
            stat[:, 2:3], stat[:, 0:1], 1.0 / D, None, ALU.mult),
            reads=[b_stat], writes=[b_stat])
        s.emit("vector", lambda e: e.tensor_tensor(
            stat[:, 3:4], stat[:, 2:3], stat[:, 2:3], ALU.mult),
            reads=[b_stat], writes=[b_stat])
        s.emit("vector", lambda e: e.scalar_tensor_tensor(
            stat[:, 4:5], stat[:, 1:2], 1.0 / D, stat[:, 3:4],
            ALU.mult, ALU.subtract),
            reads=[b_stat], writes=[b_stat])
        s.emit("scalar", lambda e: e.activation(
            stat[:, 5:6], stat[:, 4:5], AF.Sqrt, bias=fc(1e-5)),
            reads=[b_stat, cst_buf["fconst"]], writes=[b_stat])
        s.emit("vector", lambda e: e.reciprocal(stat[:, 6:7], stat[:, 5:6]),
               reads=[b_stat], writes=[b_stat])
        s.emit("vector", lambda e, sl=sl: e.tensor_scalar(
            x_slab[:, sl, :], x_slab[:, sl, :], stat[:, 2:3], stat[:, 6:7],
            ALU.subtract, ALU.mult),
            reads=[b_stat], writes=[b_xslab[sl]])
        for ft in range(ND if ("ln" in parts or "ln2" in parts) else 0):
            pslot = ft % 4
            s.emit("tensor", lambda e, sl=sl, ft=ft, pslot=pslot: e.transpose(
                ps_S[:, pslot * 128:(pslot + 1) * 128],
                x_slab[:, sl, ft * 128:(ft + 1) * 128], cst["id128f"][:, :]),
                reads=[b_xslab[sl], cst_buf["id128f"]], writes=[b_ps4[0]])
            s.emit("vector", lambda e, ft=ft, pslot=pslot: e.tensor_copy(
                x32_fm[:, ft, :], ps_S[:, pslot * 128:(pslot + 1) * 128]),
                writes=[b_ps4[0], b_x32])
            s.emit("scalar", lambda e, tt=tt, ft=ft: e.copy(
                xn_bf[:, ft, tt * 128:(tt + 1) * 128], x32_fm[:, ft, :]),
                reads=[b_x32], writes=[b_xn])
        for ft in range(ND if "gate" in parts else 0):
            s.emit("tensor", lambda e, ft=ft: e.matmul(
                ps_S[:, 512:512 + NE], x32_fm[:, ft, :], wg_sb[:, ft, :],
                start=(ft == 0), stop=(ft == ND - 1)),
                reads=[b_x32, b_smallw], writes=[b_ps4[1]])
        if "gate" not in parts:
            continue
        s.emit("vector", lambda e, tt=tt: e.tensor_tensor(
            glog[:, tt, :], ps_S[:, 512:512 + NE], bg_bc[:, :], ALU.add),
            reads=[b_smallw], writes=[b_ps4[1], b_glog])
        s.emit("vector", lambda e, tt=tt: e.reduce_max(
            stat[:, 7:8], glog[:, tt, :], axis=AX.X),
            reads=[b_glog], writes=[b_stat])
        s.emit("vector", lambda e, tt=tt: e.tensor_scalar(
            gmask[:, tt, :], glog[:, tt, :], stat[:, 7:8], None, ALU.is_ge),
            reads=[b_glog, b_stat], writes=[b_gmask])
        if "thin" not in parts:
            continue
        for ft in range(ND):
            s.emit("tensor", lambda e, tt=tt, ft=ft: e.matmul(
                ps_a[:, 0:H], xn_bf[:, ft, tt * 128:(tt + 1) * 128], wb_sb[:, ft, :],
                start=(ft == 0), stop=(ft == ND - 1)),
                reads=[b_xn, b_smallw], writes=[b_psa])
        s.emit("scalar", lambda e, tt=tt: e.activation(
            sig_tm[:, tt, :], ps_a[:, 0:H], AF.Sigmoid),
            writes=[b_psa, b_sig])
        for ft in range(ND):
            s.emit("tensor", lambda e, tt=tt, ft=ft: e.matmul(
                ps_a[:, 128:128 + H], xn_bf[:, ft, tt * 128:(tt + 1) * 128], wf_sb[:, ft, :],
                start=(ft == 0), stop=(ft == ND - 1)),
                reads=[b_xn, b_smallw], writes=[b_psa])
        s.emit("vector", lambda e: e.tensor_tensor(
            ps_a[:, 128:128 + H], ps_a[:, 128:128 + H], delta_bc[:, :], ALU.add),
            reads=[b_smallw], writes=[b_psa])
        s.emit("scalar", lambda e, tt=tt: e.activation(
            x_slab[:, tt % 2, 0:H], ps_a[:, 128:128 + H], AF.Sigmoid),
            writes=[b_psa, b_xslab[tt % 2]])
        s.emit("scalar", lambda e, tt=tt: e.activation(
            logf_bf[:, tt, :], x_slab[:, tt % 2, 0:H], AF.Ln),
            reads=[b_xslab[tt % 2]], writes=[b_logf])
        for ft in range(ND):
            s.emit("tensor", lambda e, tt=tt, ft=ft: e.matmul(
                ps_b[0:64, 0:128], ww1_sb[:, ft, :], xn_bf[:, ft, tt * 128:(tt + 1) * 128],
                start=(ft == 0), stop=(ft == ND - 1)),
                reads=[b_xn, b_smallw], writes=[b_psb])
        s.emit("scalar", lambda e, tt=tt: e.copy(
            w1o[:, tt * 128:(tt + 1) * 128], ps_b[0:64, 0:128]),
            writes=[b_psb, b_w1o])

    for tt in range(NT if ("fc" in parts and "thin" in parts) else 0):
        for j in range(tt + 1):
            lhs = cst["cumtri"] if j == tt else cst["allones"]
            lb = cst_buf["cumtri"] if j == tt else cst_buf["allones"]
            s.emit("tensor", lambda e, tt=tt, j=j, lhs=lhs: e.matmul(
                ps_S[:, 1024:1024 + H], lhs[:, :], logf_bf[:, j, :],
                start=(j == 0), stop=(j == tt)),
                reads=[b_logf, lb], writes=[b_ps4[2]])
        s.emit("vector", lambda e, tt=tt: e.tensor_copy(
            fc_tm[:, tt, :], ps_S[:, 1024:1024 + H]),
            writes=[b_ps4[2], b_fc])

    dbg_dump("xn_bf", lambda: xn_bf[:, :, :], [128, ND, N], BF16, [b_xn.w])
    dbg_dump("glog", lambda: glog[:, :, :], [128, NT, NE], F32, [b_glog.w])
    dbg_dump("gmask", lambda: gmask[:, :, :], [128, NT, NE], F32, [b_gmask.w])
    dbg_dump("sig_tm", lambda: sig_tm[:, :, :], [128, NT, H], F32, [b_sig.w])
    dbg_dump("logf_bf", lambda: logf_bf[:, :, :], [128, NT, H], BF16, [b_logf.w])
    dbg_dump("fc_tm", lambda: fc_tm[:, :, :], [128, NT, H], F32, [b_fc.w])
    dbg_dump("w1o", lambda: w1o[:, :], [64, N], BF16, [b_w1o.w])



    wo_sb = nc.alloc_sbuf_tensor("wo_sb", [128, ND, D], BF16)
    b_wo = Buf()
    s.emit("gpsimd", lambda e: e.dma_start(
        out=wo_sb[:, :, :], in_=wo_d.rearrange("(t p) d -> p t d", p=128)),
        writes=[b_wo], inc_k=16)
    s.emit("gpsimd", lambda e: e.memset(y_acc[:, :, :], 0.0), writes=[b_yacc_all])

    st2 = nc.alloc_sbuf_tensor("st2", [128, 16], F32)
    _attn_guards = []

    def _ga(name, shape, dtype):
        g = nc.sbuf_tensor(name, shape, dtype)
        _attn_guards.append(g)
        return g.__enter__()

    q_fm = _ga("q_fm", [128, N], BF16)
    k_fm = _ga("k_fm", [128, N], BF16)
    w_fm = _ga("w_fm", [128, N], BF16)
    wb_fm = _ga("wb_fm", [128, N], BF16)
    w_tm = _ga("w_tm", [128, NT, 128], BF16)
    wb_tm = _ga("wb_tm", [128, NT, 128], BF16)
    v_aug = _ga("v_aug", [128, NT, 144], BF16)
    kt = _ga("kt", [128, 2, N], BF16)
    qhat_k = _ga("qhat_k", [128, 2, 128], BF16)
    sii_k = _ga("sii_k", [128, 2, 2, 128], BF16)
    pt_k = _ga("pt_k", [128, 2, 64], BF16)
    ktd_k = _ga("ktd_k", [128, 2, 128], BF16)
    Lb = _ga("Lb", [128, 2, 128], BF16)
    Nb = _ga("Nb", [128, 2, 128], BF16)
    Mb = _ga("Mb", [128, 2, 128], BF16)
    nGt = _ga("nGt", [128, 2, 128], BF16)
    npow = _ga("npow", [128, 2, 6, 128], BF16)
    lkb = _ga("lkb", [128, 2, 128], BF16)
    zb = _ga("zb", [128, 2, 2, 192], BF16)
    s_sb = _ga("s_sb", [128, 2, N], BF16)
    pT_sb = _ga("pT_sb", [128, 2, N], BF16)
    o_tmp = _ga("o_tmp", [128, 2, 64], BF16)
    fcol = _ga("fcol", [128, 2, NT], F32)
    fcmr = _ga("fcmr", [128, 2, NT], F32)
    st3 = _ga("st3", [128, 2, 8], F32)
    wq_sl = _ga("wq_sl", [128, ND, 128], BF16)
    wk_sl = _ga("wk_sl", [128, ND, 128], BF16)
    wv_sl = _ga("wv_sl", [128, ND, 128], BF16)
    ww2_sl = _ga("ww2_sl", [64, 128], BF16)

    b_q = Buf(); b_k = Buf(); b_wfm = Buf(); b_wbfm = Buf()
    b_wtm = Buf(); b_wbtm = Buf(); b_vaug = Buf()
    b_kt = [[Buf(), Buf()], [Buf(), Buf()]]
    b_qh = [Buf(), Buf()]; b_sii = [Buf(), Buf()]
    b_pt = [Buf(), Buf()]; b_ktd = [Buf(), Buf()]
    b_L = [Buf(), Buf()]; b_N = [Buf(), Buf()]; b_M = [Buf(), Buf()]
    b_nGt = [Buf(), Buf()]; b_np = [Buf(), Buf()]; b_lk = [Buf(), Buf()]
    b_z = [[Buf(), Buf()], [Buf(), Buf()]]
    b_s = [Buf(), Buf()]; b_pT = [Buf(), Buf()]; b_otmp = [Buf(), Buf()]
    b_fcol = [Buf(), Buf()]; b_fcmr = [Buf(), Buf()]
    b_st2 = Buf(); b_st3 = [Buf(), Buf()]
    b_wqsl = Buf(); b_wksl = Buf(); b_wvsl = Buf(); b_ww2sl = Buf()
    b_ofm = Buf()

    def emit_proj(hp):
        c0 = hp * 128
        s.emit("gpsimd", lambda e, c0=c0: e.dma_start(
            out=wq_sl[:, :, :],
            in_=wq_d[:, c0:c0 + 128].rearrange("(t p) c -> p t c", p=128)),
            writes=[b_wqsl], inc_k=16)
        s.emit("gpsimd", lambda e, c0=c0: e.dma_start(
            out=wk_sl[:, :, :],
            in_=wk_d[:, c0:c0 + 128].rearrange("(t p) c -> p t c", p=128)),
            writes=[b_wksl], inc_k=16)
        s.emit("gpsimd", lambda e, c0=c0: e.dma_start(
            out=wv_sl[:, :, :],
            in_=wv_d[:, c0:c0 + 128].rearrange("(t p) c -> p t c", p=128)),
            writes=[b_wvsl], inc_k=16)
        s.emit("gpsimd", lambda e, c0=c0: e.dma_start(
            out=ww2_sl[:, :], in_=ww2_d[:, c0:c0 + 128]),
            writes=[b_ww2sl], inc_k=16)
        s.emit("gpsimd", lambda e: e.memset(v_aug[:, :, 64:72], 0.0), writes=[b_vaug])
        s.emit("gpsimd", lambda e: e.memset(v_aug[:, :, 136:144], 0.0), writes=[b_vaug])
        s.emit("gpsimd", lambda e: e.memset(v_aug[:, :, 64:65], 1.0), writes=[b_vaug])
        s.emit("gpsimd", lambda e: e.memset(v_aug[:, :, 136:137], 1.0), writes=[b_vaug])
        for dst, wsl, bw, bdst in ((q_fm, wq_sl, b_wqsl, b_q), (k_fm, wk_sl, b_wksl, b_k)):
            for t4 in range(4):
                for ft in range(ND):
                    s.emit("tensor", lambda e, dst=dst, wsl=wsl, t4=t4, ft=ft: e.matmul(
                        ps_a[:, 0:512], wsl[:, ft, :],
                        xn_bf[:, ft, t4 * 512:(t4 + 1) * 512],
                        start=(ft == 0), stop=(ft == ND - 1)),
                        reads=[bw, b_xn], writes=[b_psa])
                s.emit("scalar", lambda e, dst=dst, t4=t4: e.copy(
                    dst[:, t4 * 512:(t4 + 1) * 512], ps_a[:, 0:512]),
                    writes=[b_psa, bdst])
        for tt in range(NT):
            for ft in range(ND):
                s.emit("tensor", lambda e, tt=tt, ft=ft: e.matmul(
                    ps_b[:, 0:128], xn_bf[:, ft, tt * 128:(tt + 1) * 128],
                    wv_sl[:, ft, :], start=(ft == 0), stop=(ft == ND - 1)),
                    reads=[b_wvsl, b_xn], writes=[b_psb])
            s.emit("vector", lambda e, tt=tt: e.tensor_copy(
                bass.AP(v_aug, tt * 144, [[v_aug.shape[2] * NT, 128], [72, 2], [1, 64]]),
                ps_b[:, 0:128]),
                writes=[b_psb, b_vaug])
            s.emit("tensor", lambda e, tt=tt: e.matmul(
                ps_S[:, 0:128], w1o[:, tt * 128:(tt + 1) * 128], ww2_sl[:, :],
                start=True, stop=True),
                reads=[b_w1o, b_ww2sl], writes=[b_ps4[0]])
            for half in range(2):
                s.emit("scalar", lambda e, tt=tt, half=half: e.activation(
                    x_slab[:, 0, half * 64:(half + 1) * 64],
                    ps_S[:, half * 64:(half + 1) * 64], AF.Square,
                    accum_out=st2[:, 8 + half:9 + half]),
                    writes=[b_ps4[0], b_st2, b_xslab[0]])
            s.emit("scalar", lambda e: e.activation(
                st2[:, 10:12], st2[:, 8:10], AF.Sqrt, bias=fc(1e-6)),
                reads=[cst_buf["fconst"]], writes=[b_st2])
            s.emit("vector", lambda e: e.reciprocal(st2[:, 12:14], st2[:, 10:12]),
                   writes=[b_st2])
            s.emit("vector", lambda e, tt=tt, hp=hp: e.scalar_tensor_tensor(
                st2[:, 14:16], sig_tm[:, tt, 2 * hp:2 * hp + 2], 2.0,
                st2[:, 12:14], ALU.mult, ALU.mult),
                reads=[b_sig], writes=[b_st2])
            for half in range(2):
                s.emit("vector", lambda e, tt=tt, half=half: e.tensor_scalar(
                    w_tm[:, tt, half * 64:(half + 1) * 64],
                    ps_S[:, half * 64:(half + 1) * 64],
                    st2[:, 12 + half:13 + half], None, ALU.mult),
                    reads=[b_st2], writes=[b_ps4[0], b_wtm])
                s.emit("vector", lambda e, tt=tt, half=half: e.tensor_scalar(
                    wb_tm[:, tt, half * 64:(half + 1) * 64],
                    ps_S[:, half * 64:(half + 1) * 64],
                    st2[:, 14 + half:15 + half], None, ALU.mult),
                    reads=[b_st2], writes=[b_ps4[0], b_wbtm])
            s.emit("tensor", lambda e, tt=tt: e.transpose(
                ps_c[:, 512:640], w_tm[:, tt, :], cst["id128"][:, :]),
                reads=[b_wtm, cst_buf["id128"]], writes=[b_psc])
            s.emit("scalar", lambda e, tt=tt: e.copy(
                w_fm[:, tt * 128:(tt + 1) * 128], ps_c[:, 512:640]),
                writes=[b_psc, b_wfm])
            s.emit("tensor", lambda e, tt=tt: e.transpose(
                ps_c[:, 640:768], wb_tm[:, tt, :], cst["id128"][:, :]),
                reads=[b_wbtm, cst_buf["id128"]], writes=[b_psc])
            s.emit("scalar", lambda e, tt=tt: e.copy(
                wb_fm[:, tt * 128:(tt + 1) * 128], ps_c[:, 640:768]),
                writes=[b_psc, b_wbfm])

    def emit_head_prep(h, ph):
        bk = 2 + ph
        s.emit("tensor", lambda e, h=h, bk=bk: e.matmul(
            ps_S[:, bk * 512 + 256:bk * 512 + 257], cst["ones_row_f32"][:, :],
            fc_tm[0:1, 8, h:h + 1], start=True, stop=True),
            reads=[b_fc, cst_buf["ones_row_f32"]], writes=[b_ps4[bk]])
        s.emit("vector", lambda e, ph=ph, bk=bk: e.tensor_copy(
            st3[:, ph, 0:1], ps_S[:, bk * 512 + 256:bk * 512 + 257]),
            writes=[b_ps4[bk], b_st3[ph]])
        s.emit("vector", lambda e, ph=ph: e.tensor_scalar(
            st3[:, ph, 1:2], st3[:, ph, 0:1], -1.0, None, ALU.mult),
            writes=[b_st3[ph]])
        s.emit("scalar", lambda e, h=h, ph=ph: e.activation(
            fcol[:, ph, :], bass.AP(fc_tm, h, [[NT * H, 128], [H, NT], [1, 1]]),
            AF.Exp, bias=st3[:, ph, 0:1], scale=fc(-1.0)),
            reads=[b_fc, b_st3[ph], cst_buf["fconst"]], writes=[b_fcol[ph]])
        s.emit("vector", lambda e, h=h, ph=ph: e.tensor_scalar(
            fcmr[:, ph, :], bass.AP(fc_tm, h, [[NT * H, 128], [H, NT], [1, 1]]),
            st3[:, ph, 1:2], None, ALU.add),
            reads=[b_fc, b_st3[ph]], writes=[b_fcmr[ph]])

    def emit_phaseA(h, ph, i):
        psA = ps_a if ph == 0 else ps_b
        bA = b_psa if ph == 0 else b_psb
        r0 = ph * 64
        isl = slice(i * 128, (i + 1) * 128)
        W_I = w_fm[r0:r0 + 64, isl]
        Wb_I = wb_fm[r0:r0 + 64, isl]
        Q_I = q_fm[r0:r0 + 64, isl]
        K_I = k_fm[r0:r0 + 64, isl]
        Wtm_I = w_tm[:, i, r0:r0 + 64]
        Wbtm_I = wb_tm[:, i, r0:r0 + 64]
        s.emit("tensor", lambda e: e.matmul(psA[:, 0:128], W_I, Wb_I,
               start=True, stop=True, tile_position=(r0, 0)),
               reads=[b_wfm, b_wbfm], writes=[bA])
        s.emit("vector", lambda e: e.tensor_tensor(
            Lb[:, ph, :], psA[:, 0:128], cst["m_trils"][:, :], ALU.mult),
            reads=[cst_buf["m_trils"]], writes=[bA, b_L[ph]])
        s.emit("tensor", lambda e: e.matmul(psA[:, 128:256], Wb_I, W_I,
               start=True, stop=True, tile_position=(r0, 0)),
               reads=[b_wfm, b_wbfm], writes=[bA])
        s.emit("vector", lambda e: e.tensor_tensor(
            Nb[:, ph, :], psA[:, 128:256], cst["m_trius"][:, :], ALU.mult),
            reads=[cst_buf["m_trius"]], writes=[bA, b_N[ph]])
        s.emit("tensor", lambda e: e.matmul(psA[:, 256:384], W_I, K_I,
               start=True, stop=True, tile_position=(r0, 0)),
               reads=[b_wfm, b_k], writes=[bA])
        s.emit("vector", lambda e: e.tensor_tensor(
            Mb[:, ph, :], psA[:, 256:384], cst["m_trils"][:, :], ALU.mult),
            reads=[cst_buf["m_trils"]], writes=[bA, b_M[ph]])
        s.emit("tensor", lambda e: e.matmul(psA[:, 384:512], Wb_I, Q_I,
               start=True, stop=True, tile_position=(r0, 0)),
               reads=[b_wbfm, b_q], writes=[bA])
        s.emit("vector", lambda e: e.tensor_tensor(
            nGt[:, ph, :], psA[:, 384:512], cst["m_negtriu"][:, :], ALU.mult),
            reads=[cst_buf["m_negtriu"]], writes=[bA, b_nGt[ph]])

        def lp(lev):
            return Lb[:, ph, :] if lev == 0 else lkb[:, ph, :]

        def np_(lev):
            return Nb[:, ph, :] if lev == 0 else npow[:, ph, lev - 1, :]

        for lev in range(6):
            bl = b_L[ph] if lev == 0 else b_lk[ph]
            bn = b_N[ph] if lev == 0 else b_np[ph]
            s.emit("tensor", lambda e, lev=lev: e.matmul(
                psA[:, 128:256], lp(lev), np_(lev), start=True, stop=True),
                reads=[bl, bn], writes=[bA])
            s.emit("scalar", lambda e, lev=lev: e.copy(
                npow[:, ph, lev, :], psA[:, 128:256]),
                writes=[bA, b_np[ph]])
            if lev < 5:
                s.emit("tensor", lambda e, lev=lev: e.matmul(
                    psA[:, 0:128], np_(lev), lp(lev), start=True, stop=True),
                    reads=[bl, bn], writes=[bA])
                s.emit("vector", lambda e: e.tensor_copy(
                    lkb[:, ph, :], psA[:, 0:128]),
                    writes=[bA, b_lk[ph]])
        zc = 0
        s.emit("tensor", lambda e: e.matmul(psA[:, 0:64], npow[:, ph, 5, :], Wtm_I,
               start=True, stop=False), reads=[b_np[ph], b_wtm], writes=[bA])
        s.emit("tensor", lambda e: e.matmul(psA[:, 64:192], npow[:, ph, 5, :], Mb[:, ph, :],
               start=False, stop=True), reads=[b_np[ph], b_M[ph]], writes=[bA])
        s.emit("vector", lambda e: e.tensor_tensor(
            zb[:, ph, 0, 0:64], Wtm_I, psA[:, 0:64], ALU.add),
            reads=[b_wtm], writes=[bA, b_z[ph][0]])
        s.emit("vector", lambda e: e.tensor_tensor(
            zb[:, ph, 0, 64:192], Mb[:, ph, :], psA[:, 64:192], ALU.add),
            reads=[b_M[ph]], writes=[bA, b_z[ph][0]])
        for lev in (4, 3, 2, 1, 0):
            zn = 1 - zc
            s.emit("tensor", lambda e, lev=lev, zc=zc: e.matmul(
                psA[:, 0:192], npow[:, ph, lev, :], zb[:, ph, zc, :],
                start=True, stop=True), reads=[b_np[ph], b_z[ph][zc]], writes=[bA])
            s.emit("vector", lambda e, zc=zc, zn=zn: e.tensor_tensor(
                zb[:, ph, zn, :], zb[:, ph, zc, :], psA[:, 0:192], ALU.add),
                reads=[b_z[ph][zc]], writes=[bA, b_z[ph][zn]])
            zc = zn
        zn = 1 - zc
        s.emit("tensor", lambda e, zc=zc: e.matmul(
            psA[:, 0:192], Nb[:, ph, :], zb[:, ph, zc, :],
            start=True, stop=True), reads=[b_N[ph], b_z[ph][zc]], writes=[bA])
        s.emit("vector", lambda e, zc=zc, zn=zn: e.tensor_tensor(
            zb[:, ph, zn, :], zb[:, ph, zc, :], psA[:, 0:192], ALU.subtract),
            reads=[b_z[ph][zc]], writes=[bA, b_z[ph][zn]])
        zf = zn
        TW = zb[:, ph, zf, 0:64]
        CI2 = zb[:, ph, zf, 64:192]
        bZ = b_z[ph][zf]
        s.emit("tensor", lambda e: e.matmul(psA[:, 256:384], Q_I, K_I,
               start=True, stop=False, tile_position=(r0, 0)),
               reads=[b_q, b_k], writes=[bA])
        s.emit("tensor", lambda e: e.matmul(psA[:, 256:384], nGt[:, ph, :], CI2,
               start=False, stop=True), reads=[b_nGt[ph], bZ], writes=[bA])
        s.emit("scalar", lambda e, i=i, ph=ph: e.copy(
            sii_k[:, ph, i % 2, :], psA[:, 256:384]),
            writes=[bA, b_sii[ph]])
        s.emit("tensor", lambda e: e.matmul(
            psA[r0:r0 + 64, 0:128], TW, nGt[:, ph, :],
            start=True, stop=True, tile_position=(0, r0)),
            reads=[bZ, b_nGt[ph]], writes=[bA])
        s.emit("vector", lambda e, i=i: e.tensor_tensor(
            qhat_k[r0:r0 + 64, i % 2, :], Q_I, psA[r0:r0 + 64, 0:128], ALU.add),
            reads=[b_q], writes=[bA, b_qh[ph]])
        s.emit("tensor", lambda e: e.matmul(
            psA[r0:r0 + 64, 128:192], TW, Wbtm_I,
            start=True, stop=True, tile_position=(0, r0)),
            reads=[bZ, b_wbtm], writes=[bA])
        s.emit("vector", lambda e, i=i: e.tensor_tensor(
            pt_k[r0:r0 + 64, i % 2, :], cst["i64stack"][r0:r0 + 64, :],
            psA[r0:r0 + 64, 128:192], ALU.subtract),
            reads=[cst_buf["i64stack"]], writes=[bA, b_pt[ph]])
        s.emit("tensor", lambda e: e.matmul(
            psA[r0:r0 + 64, 256:384], Wbtm_I, CI2,
            start=True, stop=True, tile_position=(0, r0)),
            reads=[b_wbtm, bZ], writes=[bA])
        s.emit("vector", lambda e, i=i: e.tensor_tensor(
            ktd_k[r0:r0 + 64, i % 2, :], K_I, psA[r0:r0 + 64, 256:384], ALU.subtract),
            reads=[b_k], writes=[bA, b_ktd[ph]])

    def emit_phaseB_chunk(h, ph, i):
        r0 = ph * 64
        bk = 2 + ph
        sk = ph
        psc = ps_c if ph == 0 else ps_d
        bpsc = b_psc if ph == 0 else b_psd
        rbuf = (i + 1) % 2
        wbuf = i % 2
        jend = (i + 1) * 128
        nj5 = (i * 128 + 511) // 512
        for j5 in range(nj5):
            w = min(512, i * 128 - j5 * 512)
            s.emit("tensor", lambda e, i=i, j5=j5, w=w, rbuf=rbuf: e.matmul(
                ps_S[:, sk * 512:sk * 512 + w],
                qhat_k[r0:r0 + 64, i % 2, :], kt[r0:r0 + 64, rbuf, j5 * 512:j5 * 512 + w],
                start=True, stop=True, tile_position=(r0, 0)),
                reads=[b_qh[ph], b_kt[ph][rbuf]], writes=[b_ps4[sk]])
            s.emit("vector", lambda e, j5=j5, w=w, ph=ph: e.tensor_copy(
                s_sb[:, ph, j5 * 512:j5 * 512 + w], ps_S[:, sk * 512:sk * 512 + w]),
                writes=[b_ps4[sk], b_s[ph]])
        s.emit("vector", lambda e, i=i, ph=ph: e.tensor_tensor(
            s_sb[:, ph, i * 128:(i + 1) * 128], sii_k[:, ph, i % 2, :],
            cst["m_addneg"][:, :], ALU.add),
            reads=[b_sii[ph], cst_buf["m_addneg"]], writes=[b_s[ph]])
        s.emit("vector", lambda e, jend=jend, ph=ph: e.reduce_max(
            st3[:, ph, 2:3], s_sb[:, ph, 0:jend], axis=AX.X),
            reads=[b_s[ph]], writes=[b_st3[ph]])
        s.emit("vector", lambda e, i=i, ph=ph: e.tensor_scalar(
            st3[:, ph, 3:4], st3[:, ph, 2:3], -SCALE, fcmr[:, ph, i:i + 1],
            ALU.mult, ALU.add),
            reads=[b_fcmr[ph]], writes=[b_st3[ph]])
        s.emit("scalar", lambda e, jend=jend, ph=ph: e.activation(
            s_sb[:, ph, 0:jend], s_sb[:, ph, 0:jend], AF.Exp,
            bias=st3[:, ph, 3:4], scale=fc(SCALE)),
            reads=[b_st3[ph], cst_buf["fconst"]], writes=[b_s[ph]])
        for jt in range(i + 1):
            slot = jt % 7
            s.emit("tensor", lambda e, jt=jt, slot=slot, ph=ph, psc=psc: e.transpose(
                psc[:, slot * 128:(slot + 1) * 128],
                s_sb[:, ph, jt * 128:(jt + 1) * 128], cst["id128"][:, :]),
                reads=[b_s[ph], cst_buf["id128"]], writes=[bpsc])
            s.emit("vector", lambda e, jt=jt, slot=slot, ph=ph, psc=psc: e.tensor_scalar(
                pT_sb[:, ph, jt * 128:(jt + 1) * 128],
                psc[:, slot * 128:(slot + 1) * 128],
                fcol[:, ph, jt:jt + 1], None, ALU.mult),
                reads=[b_fcol[ph]], writes=[bpsc, b_pT[ph]])
            s.emit("tensor", lambda e, jt=jt, i=i, ph=ph, bk=bk: e.matmul(
                ps_S[:, bk * 512:bk * 512 + 72], pT_sb[:, ph, jt * 128:(jt + 1) * 128],
                v_aug[:, jt, ph * 72:ph * 72 + 72],
                start=(jt == 0), stop=(jt == i)),
                reads=[b_pT[ph], b_vaug], writes=[b_ps4[bk]])
        s.emit("vector", lambda e, ph=ph, bk=bk: e.reciprocal(
            st3[:, ph, 4:5], ps_S[:, bk * 512 + 64:bk * 512 + 65]),
            writes=[b_ps4[bk], b_st3[ph]])
        s.emit("vector", lambda e, ph=ph, bk=bk: e.tensor_scalar(
            o_tmp[:, ph, :], ps_S[:, bk * 512:bk * 512 + 64], st3[:, ph, 4:5],
            None, ALU.mult),
            reads=[b_st3[ph]], writes=[b_ps4[bk], b_otmp[ph]])
        s.emit("tensor", lambda e, ph=ph, psc=psc: e.transpose(
            psc[r0:r0 + 64, 896:1024], o_tmp[:, ph, :], cst["id128"][:, :],
            tile_position=(0, r0)),
            reads=[b_otmp[ph], cst_buf["id128"]], writes=[bpsc])
        s.emit("scalar", lambda e, i=i, h=h, ph=ph, psc=psc: e.copy(
            o_fm[r0:r0 + 64, h // 2, i * 128:(i + 1) * 128],
            psc[r0:r0 + 64, 896:1024]),
            writes=[bpsc, b_ofm])
        for j5 in range(nj5):
            w = min(512, i * 128 - j5 * 512)
            s.emit("tensor", lambda e, i=i, j5=j5, w=w, rbuf=rbuf, bk=bk: e.matmul(
                ps_S[r0:r0 + 64, bk * 512:bk * 512 + w], pt_k[r0:r0 + 64, i % 2, :],
                kt[r0:r0 + 64, rbuf, j5 * 512:j5 * 512 + w],
                start=True, stop=True, tile_position=(r0, r0)),
                reads=[b_pt[ph], b_kt[ph][rbuf]], writes=[b_ps4[bk]])
            s.emit("scalar", lambda e, j5=j5, w=w, wbuf=wbuf, bk=bk: e.copy(
                kt[r0:r0 + 64, wbuf, j5 * 512:j5 * 512 + w],
                ps_S[r0:r0 + 64, bk * 512:bk * 512 + w]),
                writes=[b_ps4[bk], b_kt[ph][wbuf]])
        s.emit("gpsimd", lambda e, i=i, wbuf=wbuf: e.tensor_copy(
            kt[r0:r0 + 64, wbuf, i * 128:(i + 1) * 128],
            ktd_k[r0:r0 + 64, i % 2, :]),
            reads=[b_ktd[ph]], writes=[b_kt[ph][wbuf]])

    def defer(fn):
        rec = []
        real = s.emit
        s.emit = lambda *a, **k: rec.append((a, k))
        fn()
        s.emit = real
        return rec

    def play(recs):
        idx = [0] * len(recs)
        while True:
            done = True
            for r in range(len(recs)):
                if idx[r] < len(recs[r]):
                    a, k = recs[r][idx[r]]
                    s.emit(*a, **k)
                    idx[r] += 1
                    done = False
            if done:
                break

    for hp in range(hp_count):
        emit_proj(hp)
        h0, h1 = 2 * hp, 2 * hp + 1
        emit_head_prep(h0, 0)
        emit_head_prep(h1, 1)
        play([defer(lambda: emit_phaseA(h0, 0, 0)),
              defer(lambda: emit_phaseA(h1, 1, 0))])
        for i in range(NCH):
            recs = [defer(lambda i=i: emit_phaseB_chunk(h0, 0, i)),
                    defer(lambda i=i: emit_phaseB_chunk(h1, 1, i))]
            if i + 1 < NCH:
                recs.append(defer(lambda i=i: emit_phaseA(h0, 0, i + 1)))
                recs.append(defer(lambda i=i: emit_phaseA(h1, 1, i + 1)))
            play(recs)

    dbg_dump("o_fm", lambda: o_fm[:, :, :], [128, ND, N], BF16, [b_ofm.w])

    attn_bufs = ([b_q, b_k, b_wfm, b_wbfm, b_wtm, b_wbtm, b_vaug]
                 + b_kt[0] + b_kt[1] + b_qh + b_sii + b_pt + b_ktd
                 + b_L + b_N + b_M + b_nGt + b_np + b_lk
                 + b_z[0] + b_z[1] + b_s + b_pT + b_otmp
                 + b_fcol + b_fcmr + b_st3
                 + [b_wqsl, b_wksl, b_wvsl, b_ww2sl])
    barrier = []
    for b in attn_bufs:
        if b.w is not None:
            barrier.append(b.w)
        barrier.extend(b.readers)
    barrier = list({id(t): t for t in barrier}.values())
    for g in reversed(_attn_guards):
        g.__exit__(None, None, None)

    w1_sl = [nc.alloc_sbuf_tensor(f"w1_sl{i}", [128, ND, D], BF16) for i in range(2)]
    w2_sl = nc.alloc_sbuf_tensor("w2_sl", [128, ND, D], BF16)
    g_slab = nc.alloc_sbuf_tensor("g_slab", [128, ND, 512], BF16)
    b1_sb = nc.alloc_sbuf_tensor("b1_sb", [1, D], BF16)
    b2_sb = nc.alloc_sbuf_tensor("b2_sb", [1, D], BF16)
    b_w1 = [Buf(), Buf()]
    b_w2 = Buf()
    b_g = Buf()
    b_b1 = Buf()
    b_b2 = Buf()

    def emit_ffn_expert(e_idx):
        sl = e_idx % 2
        if e_idx == 0:
            s.emit("gpsimd", lambda e: e.dma_start(
                out=w1_sl[0][:, :, :],
                in_=w1_d[0].rearrange("(t p) i -> p t i", p=128)),
                writes=[b_w1[0]], deps=barrier, inc_k=16)
        if e_idx + 1 < NE:
            s.emit("gpsimd", lambda e, e_idx=e_idx: e.dma_start(
                out=w1_sl[(e_idx + 1) % 2][:, :, :],
                in_=w1_d[e_idx + 1].rearrange("(t p) i -> p t i", p=128)),
                writes=[b_w1[(e_idx + 1) % 2]],
                deps=(barrier if e_idx == 0 else ()), inc_k=16)
        s.emit("gpsimd", lambda e, e_idx=e_idx: e.dma_start(
            out=w2_sl[:, :, :], in_=w2_d[e_idx].rearrange("(t p) i -> p t i", p=128)),
            writes=[b_w2], deps=(barrier if e_idx == 0 else ()), inc_k=16)
        s.emit("gpsimd", lambda e, e_idx=e_idx: e.dma_start(
            out=b1_sb[:, :], in_=b1_d[e_idx:e_idx + 1, :]),
            writes=[b_b1], deps=(barrier if e_idx == 0 else ()), inc_k=16)
        s.emit("gpsimd", lambda e, e_idx=e_idx: e.dma_start(
            out=b2_sb[:, :], in_=b2_d[e_idx:e_idx + 1, :]),
            writes=[b_b2], deps=(barrier if e_idx == 0 else ()), inc_k=16)
        for t4 in range(4):
            for it in range(ND):
                psh = ps_a if it % 2 == 0 else ps_b
                bph = b_psa if it % 2 == 0 else b_psb
                for ft in range(ND):
                    s.emit("tensor", lambda e, sl=sl, t4=t4, it=it, ft=ft, psh=psh: e.matmul(
                        psh[:, 0:512], w1_sl[sl][:, ft, it * 128:(it + 1) * 128],
                        xn_bf[:, ft, t4 * 512:(t4 + 1) * 512],
                        start=(ft == 0), stop=False),
                        reads=[b_w1[sl], b_xn], writes=[bph])
                s.emit("tensor", lambda e, it=it, psh=psh: e.matmul(
                    psh[:, 0:512], b1_sb[0:1, it * 128:(it + 1) * 128],
                    cst["ones_row_bf"][0:1, 0:512],
                    start=False, stop=True),
                    reads=[b_b1, cst_buf["ones_row_bf"]], writes=[bph])
                s.emit("scalar", lambda e, it=it, psh=psh: e.activation(
                    g_slab[:, it, :], psh[:, 0:512], AF.Gelu),
                    writes=[bph, b_g])
            for tsub in range(4):
                tt = t4 * 4 + tsub
                for dh in range(2):
                    pb = tsub % 2 + 2 * dh
                    for it in range(ND):
                        s.emit("tensor", lambda e, tsub=tsub, dh=dh, it=it, pb=pb: e.matmul(
                            ps_S[:, pb * 512:pb * 512 + 512],
                            g_slab[:, it, tsub * 128:(tsub + 1) * 128],
                            w2_sl[:, it, dh * 512:(dh + 1) * 512],
                            start=(it == 0), stop=False),
                            reads=[b_g, b_w2], writes=[b_ps4[pb]])
                    s.emit("tensor", lambda e, dh=dh, pb=pb: e.matmul(
                        ps_S[:, pb * 512:pb * 512 + 512],
                        cst["ones_row_bf"][0:1, 0:128],
                        b2_sb[0:1, dh * 512:(dh + 1) * 512],
                        start=False, stop=True),
                        reads=[b_b2, cst_buf["ones_row_bf"]], writes=[b_ps4[pb]])
                    s.emit("vector", lambda e, tt=tt, dh=dh, e_idx=e_idx, pb=pb: e.scalar_tensor_tensor(
                        y_acc[:, tt, dh * 512:(dh + 1) * 512],
                        ps_S[:, pb * 512:pb * 512 + 512],
                        gmask[:, tt, e_idx:e_idx + 1],
                        y_acc[:, tt, dh * 512:(dh + 1) * 512],
                        ALU.mult, ALU.add),
                        reads=[b_gmask, b_yacc_all], writes=[b_ps4[pb], b_yacc[tt]])

    for e_idx in range(NE):
        emit_ffn_expert(e_idx)

    b_out = Buf()
    for tt in range(NT):
        sl = tt % 2
        s.emit("sync", lambda e, tt=tt, sl=sl: e.dma_start(
            out=x_slab[:, sl, :], in_=x_d.rearrange("(t p) d -> t p d", p=128)[tt]),
            writes=[b_xslab[sl]], inc_k=16)
        for dh in range(2):
            psx = ps_a if dh == 0 else ps_b
            bpsx = b_psa if dh == 0 else b_psb
            for ft in range(ND):
                s.emit("tensor", lambda e, tt=tt, dh=dh, ft=ft, psx=psx: e.matmul(
                    psx[:, 0:512], o_fm[:, ft, tt * 128:(tt + 1) * 128],
                    wo_sb[:, ft, dh * 512:(dh + 1) * 512],
                    start=(ft == 0), stop=(ft == ND - 1)),
                    reads=[b_ofm, b_wo], writes=[bpsx])
            s.emit("vector", lambda e, tt=tt, dh=dh, sl=sl, psx=psx: e.tensor_tensor(
                x_slab[:, sl, dh * 512:(dh + 1) * 512], psx[:, 0:512],
                x_slab[:, sl, dh * 512:(dh + 1) * 512], ALU.add),
                writes=[bpsx, b_xslab[sl]])
            s.emit("vector", lambda e, tt=tt, dh=dh, sl=sl: e.tensor_tensor(
                x_slab[:, sl, dh * 512:(dh + 1) * 512],
                x_slab[:, sl, dh * 512:(dh + 1) * 512],
                y_acc[:, tt, dh * 512:(dh + 1) * 512], ALU.add),
                reads=[b_yacc[tt], b_yacc_all], writes=[b_xslab[sl]])
        s.emit("sync", lambda e, tt=tt, sl=sl: e.dma_start(
            out=out_d.rearrange("(t p) d -> t p d", p=128)[tt], in_=x_slab[:, sl, :]),
            reads=[b_xslab[sl]], writes=[b_out], inc_k=16)
    s.emit("sync", lambda e: e.nop(), deps=[b_out.w])

    s.finalize(nc, paranoid=paranoid)
    return nc, dbg_d


def stage_inputs(inputs):
    consts = host_constants()
    x = np.ascontiguousarray(inputs["L0_attn_x"])
    base = {
        "wq": inputs["L0_attn_Wq"], "wk": inputs["L0_attn_Wk"],
        "wv": inputs["L0_attn_Wv"], "wo": inputs["L0_attn_Wo"],
        "ww1": inputs["L0_attn_Ww1"], "ww2": inputs["L0_attn_Ww2"],
        "wbm": inputs["L0_attn_Wb"], "wfm": inputs["L0_attn_Wf"],
        "delta_bc": np.ascontiguousarray(np.broadcast_to(
            inputs["L0_attn_delta"][None, :], (128, H))),
        "wg": inputs["L0_ffn_Wg"],
        "bg_bc": np.ascontiguousarray(np.broadcast_to(
            inputs["L0_ffn_bg"][None, :], (128, NE))),
        "w1": inputs["L0_ffn_W1"], "b1": inputs["L0_ffn_b1"],
        "w2": inputs["L0_ffn_W2"], "b2": inputs["L0_ffn_b2"],
    }
    for k, v in consts.items():
        base[f"c_{k}"] = v
    in_maps = []
    for c in range(8):
        m = dict(base)
        m["x"] = np.ascontiguousarray(x[c])
        in_maps.append(m)
    return in_maps


def kernel(**inputs):
    nc, _ = build()
    in_maps = stage_inputs(inputs)
    res = run_bass_kernel_spmd(nc, in_maps, core_ids=list(range(8)))
    return np.stack([res.results[c]["out"] for c in range(8)], axis=0)


# version 10
# speedup vs baseline: 1.3456x; 1.0011x over previous
import math
import numpy as np
import ml_dtypes
import concourse.bass as bass
import concourse.mybir as mybir
from concourse.bass_utils import run_bass_kernel_spmd

BFNP = ml_dtypes.bfloat16
F32 = mybir.dt.float32
BF16 = mybir.dt.bfloat16
AF = mybir.ActivationFunctionType
ALU = mybir.AluOpType
AX = mybir.AxisListType

N, D, H, HD, C = 2048, 1024, 16, 64, 128
NCH = N // C
NT = N // 128
ND = D // 128
NE = 8
SCALE = HD ** -0.5

ENGINES = ("tensor", "vector", "scalar", "gpsimd", "sync")

FCONST_VALS = (1e-5, 1e-6, -1.0, 2.0, 0.5, -0.5, SCALE, -SCALE)
FCONST_IDX = {v: i for i, v in enumerate(FCONST_VALS)}


class Op:
    __slots__ = ("eng", "fn", "deps", "inc_k", "sem", "sem_after", "pos",
                 "needs_inc", "pre_wait")

    def __init__(self, eng, fn, deps, inc_k):
        self.eng = eng
        self.fn = fn
        self.deps = deps
        self.inc_k = inc_k
        self.sem = None
        self.sem_after = None
        self.pos = None
        self.needs_inc = False
        self.pre_wait = None


class Buf:

    def __init__(self):
        self.w = None
        self.readers = []


class Sched:
    def __init__(self):
        self.ops = []

    def emit(self, eng, fn, reads=(), writes=(), deps=(), inc_k=1):
        d = [t for t in deps if t is not None]
        for b in reads:
            if b.w is not None:
                d.append(b.w)
        for b in writes:
            if b.w is not None:
                d.append(b.w)
            d.extend(b.readers)
        op = Op(eng, fn, d, inc_k)
        op.pos = len(self.ops)
        self.ops.append(op)
        for b in reads:
            b.readers.append(op)
        for b in writes:
            b.w = op
            b.readers = []
        return op

    def finalize(self, nc, paranoid=False):
        for op in self.ops:
            if op.inc_k == 16:
                op.needs_inc = True
            for dep in op.deps:
                assert dep.pos < op.pos
                if dep.eng != op.eng or dep.inc_k == 16 or paranoid:
                    dep.needs_inc = True
        eng_sems = {e: nc.alloc_semaphore(f"sem_{e}") for e in ENGINES}
        counts = {e: 0 for e in ENGINES}
        NPOOL = 8
        pool = {e: [nc.alloc_semaphore(f"dma_{e}_{i}") for i in range(NPOOL)]
                for e in ("sync", "gpsimd", "scalar")}
        pool_idx = {e: 0 for e in pool}
        pool_count = {}
        for op in self.ops:
            if op.inc_k == 16:
                sem = pool[op.eng][pool_idx[op.eng] % NPOOL]
                pool_idx[op.eng] += 1
                prev = pool_count.get(id(sem), 0)
                if prev:
                    op.pre_wait = (sem, prev)
                op.sem = sem
                op.sem_after = prev + 16
                pool_count[id(sem)] = op.sem_after
            elif op.needs_inc:
                counts[op.eng] += op.inc_k
                op.sem = eng_sems[op.eng]
                op.sem_after = counts[op.eng]
        marks = {e: {} for e in ENGINES}
        streams = {e: [o for o in self.ops if o.eng == e] for e in ENGINES}
        need_drain = set()
        vstream = streams["vector"]
        vpos = {id(o): i for i, o in enumerate(vstream)}
        for i, op in enumerate(vstream):
            for dep in op.deps:
                if dep.eng == "vector" and vpos[id(dep)] == i - 1:
                    need_drain.add(id(op))

        def body(eng_obj, ename):
            wm = marks[ename]
            for op in streams[ename]:
                for dep in op.deps:
                    if dep.eng == ename and dep.inc_k != 16 and not paranoid:
                        continue
                    assert dep.sem_after is not None, (dep.eng, op.eng)
                    if dep.sem_after > wm.get(id(dep.sem), 0):
                        eng_obj.wait_ge(dep.sem, dep.sem_after)
                        wm[id(dep.sem)] = dep.sem_after
                if op.pre_wait is not None:
                    sem, val = op.pre_wait
                    if val > wm.get(id(sem), 0):
                        eng_obj.wait_ge(sem, val)
                        wm[id(sem)] = val
                if id(op) in need_drain:
                    eng_obj.drain()
                instr = op.fn(eng_obj)
                if op.sem is not None:
                    instr.then_inc(op.sem, op.inc_k)

        with nc.Block() as blk:
            @blk.tensor
            def _(e):
                body(e, "tensor")

            @blk.vector
            def _(e):
                body(e, "vector")

            @blk.scalar
            def _(e):
                body(e, "scalar")

            @blk.gpsimd
            def _(e):
                body(e, "gpsimd")

            @blk.sync
            def _(e):
                body(e, "sync")


def host_constants():
    c = {}
    ident = np.eye(128, dtype=np.float32)
    c["id128"] = ident.astype(BFNP)
    c["id128f"] = ident
    c["cumtri"] = np.triu(np.ones((128, 128), np.float32)).astype(BFNP)
    c["allones"] = np.ones((128, 128), np.float32).astype(BFNP)
    c["m_trils"] = np.tril(np.ones((128, 128), np.float32), -1).astype(BFNP)
    c["m_trius"] = np.triu(np.ones((128, 128), np.float32), 1).astype(BFNP)
    c["m_negtriu"] = (-np.triu(np.ones((128, 128), np.float32))).astype(BFNP)
    c["m_addneg"] = np.triu(np.full((128, 128), -1e9, np.float32), 1).astype(BFNP)
    c["i64stack"] = np.concatenate([np.eye(64, dtype=np.float32)] * 2, 0)
    c["ones_row_bf"] = np.ones((1, 512), np.float32).astype(BFNP)
    c["ones_row_f32"] = np.ones((1, 128), np.float32)
    c["fconst"] = np.ascontiguousarray(np.broadcast_to(
        np.array(FCONST_VALS, np.float32)[None, :], (128, len(FCONST_VALS))))
    return c


def build(dbg=None, paranoid=False, parts=("ln", "gate", "thin", "fc"), hp_count=ND):
    dbg = dbg or []
    parts = set(parts)
    nc = bass.Bass("TRN2", target_bir_lowering=False,
                   detect_race_conditions=paranoid)
    s = Sched()

    x_d = nc.dram_tensor("x", [N, D], F32, kind="ExternalInput").ap()
    wq_d = nc.dram_tensor("wq", [D, D], F32, kind="ExternalInput").ap()
    wk_d = nc.dram_tensor("wk", [D, D], F32, kind="ExternalInput").ap()
    wv_d = nc.dram_tensor("wv", [D, D], F32, kind="ExternalInput").ap()
    wo_d = nc.dram_tensor("wo", [D, D], F32, kind="ExternalInput").ap()
    ww1_d = nc.dram_tensor("ww1", [D, HD], F32, kind="ExternalInput").ap()
    ww2_d = nc.dram_tensor("ww2", [HD, D], F32, kind="ExternalInput").ap()
    wb_d = nc.dram_tensor("wbm", [D, H], F32, kind="ExternalInput").ap()
    wf_d = nc.dram_tensor("wfm", [D, H], F32, kind="ExternalInput").ap()
    delta_bc_d = nc.dram_tensor("delta_bc", [128, H], F32, kind="ExternalInput").ap()
    wg_d = nc.dram_tensor("wg", [D, NE], F32, kind="ExternalInput").ap()
    bg_bc_d = nc.dram_tensor("bg_bc", [128, NE], F32, kind="ExternalInput").ap()
    w1_d = nc.dram_tensor("w1", [NE, D, D], F32, kind="ExternalInput").ap()
    b1_d = nc.dram_tensor("b1", [NE, D], F32, kind="ExternalInput").ap()
    w2_d = nc.dram_tensor("w2", [NE, D, D], F32, kind="ExternalInput").ap()
    b2_d = nc.dram_tensor("b2", [NE, D], F32, kind="ExternalInput").ap()
    cst_d = {}
    consts = host_constants()
    for k, v in consts.items():
        dt = BF16 if v.dtype == BFNP else F32
        cst_d[k] = nc.dram_tensor(f"c_{k}", list(v.shape), dt, kind="ExternalInput").ap()
    out_d = nc.dram_tensor("out", [N, D], F32, kind="ExternalOutput").ap()

    dbg_d = {}
    dbg_bufs = {}

    def dbg_dump(name, ap_fn, shape, dtype, deps):
        if name not in dbg:
            return
        dd = nc.dram_tensor(f"dbg_{name}", shape, dtype, kind="ExternalOutput").ap()
        dbg_d[name] = dd
        s.emit("sync", lambda e, a=ap_fn, d=dd: e.dma_start(out=d, in_=a()),
               deps=deps, inc_k=16)

    cst = {}
    cst_buf = {}
    for k, v in consts.items():
        dt = BF16 if v.dtype == BFNP else F32
        t = nc.alloc_sbuf_tensor(f"sb_{k}", list(v.shape), dt)
        cst[k] = t
        b = Buf()
        cst_buf[k] = b
        s.emit("sync", lambda e, t=t, k=k: e.dma_start(out=t[:, :], in_=cst_d[k][:, :]),
               writes=[b], inc_k=16)

    def CC(name):
        return cst[name], cst_buf[name]

    def fc(v):
        return cst["fconst"][:, FCONST_IDX[v]:FCONST_IDX[v] + 1]

    xn_bf = nc.alloc_sbuf_tensor("xn_bf", [128, ND, N], BF16)
    o_fm = nc.alloc_sbuf_tensor("o_fm", [128, ND, N], BF16)
    y_acc = nc.alloc_sbuf_tensor("y_acc", [128, NT, D], BF16)
    fc_tm = nc.alloc_sbuf_tensor("fc_tm", [128, NT, H], F32)
    sig_tm = nc.alloc_sbuf_tensor("sig_tm", [128, NT, H], F32)
    logf_bf = nc.alloc_sbuf_tensor("logf_bf", [128, NT, H], BF16)
    gmask = nc.alloc_sbuf_tensor("gmask", [128, NT, NE], F32)
    w1o = nc.alloc_sbuf_tensor("w1o", [64, N], BF16)
    wg_sb = nc.alloc_sbuf_tensor("wg_sb", [128, ND, NE], F32)
    wb_sb = nc.alloc_sbuf_tensor("wb_sb", [128, ND, H], BF16)
    wf_sb = nc.alloc_sbuf_tensor("wf_sb", [128, ND, H], BF16)
    ww1_sb = nc.alloc_sbuf_tensor("ww1_sb", [128, ND, HD], BF16)
    delta_bc = nc.alloc_sbuf_tensor("delta_sb", [128, H], F32)
    bg_bc = nc.alloc_sbuf_tensor("bg_sb", [128, NE], F32)

    b_xn = Buf()
    b_yacc_all = Buf()
    b_o = [[Buf() for _ in range(NCH)] for _ in range(H)]
    b_yacc = [Buf() for _ in range(NT)]
    b_fc = Buf()
    b_sig = Buf()
    b_logf = Buf()
    b_gmask = Buf()
    b_w1o = Buf()
    b_smallw = Buf()

    s.emit("gpsimd", lambda e: e.dma_start(
        out=wb_sb[:, :, :], in_=wb_d.rearrange("(t p) h -> p t h", p=128)),
        writes=[b_smallw], inc_k=16)
    s.emit("gpsimd", lambda e: e.dma_start(
        out=wf_sb[:, :, :], in_=wf_d.rearrange("(t p) h -> p t h", p=128)),
        writes=[b_smallw], inc_k=16)
    s.emit("gpsimd", lambda e: e.dma_start(
        out=ww1_sb[:, :, :], in_=ww1_d.rearrange("(t p) h -> p t h", p=128)),
        writes=[b_smallw], inc_k=16)
    s.emit("sync", lambda e: e.dma_start(
        out=wg_sb[:, :, :], in_=wg_d.rearrange("(t p) h -> p t h", p=128)),
        writes=[b_smallw], inc_k=16)
    s.emit("sync", lambda e: e.dma_start(out=delta_bc[:, :], in_=delta_bc_d[:, :]),
           writes=[b_smallw], inc_k=16)
    s.emit("sync", lambda e: e.dma_start(out=bg_bc[:, :], in_=bg_bc_d[:, :]),
           writes=[b_smallw], inc_k=16)

    ps_S = nc.alloc_psum_tensor("ps_S", [128, 2048], F32)
    ps_a = nc.alloc_psum_tensor("ps_a", [128, 512], F32)
    ps_b = nc.alloc_psum_tensor("ps_b", [128, 512], F32)
    ps_c = nc.alloc_psum_tensor("ps_c", [128, 1024], BF16)
    ps_d = nc.alloc_psum_tensor("ps_d", [128, 1024], BF16)
    b_ps4 = [Buf(), Buf(), Buf(), Buf()]
    b_psa = Buf()
    b_psb = Buf()
    b_psc = Buf()
    b_psd = Buf()

    x_slab = nc.alloc_sbuf_tensor("x_slab", [128, 2, D], F32)
    stat = nc.alloc_sbuf_tensor("stat", [128, 16], F32)
    glog = nc.alloc_sbuf_tensor("glog", [128, NT, NE], F32)
    x32_fm = nc.alloc_sbuf_tensor("x32_fm", [128, ND, 128], F32)
    b_xslab = [Buf(), Buf()]
    b_xnslab = [Buf(), Buf()]
    b_x32 = Buf()
    b_stat = Buf()
    b_glog = Buf()

    for tt in range(NT):
        if "ln" not in parts and "ln1" not in parts:
            break
        sl = tt % 2
        xs = x_slab
        ld = s.emit("sync", lambda e, tt=tt, sl=sl: e.dma_start(
            out=x_slab[:, sl, :], in_=x_d.rearrange("(t p) d -> t p d", p=128)[tt]),
            writes=[b_xslab[sl]], inc_k=16)
        s.emit("vector", lambda e, sl=sl: e.reduce_sum(
            stat[:, 0:1], x_slab[:, sl, :], axis=AX.X),
            reads=[b_xslab[sl]], writes=[b_stat])
        s.emit("scalar", lambda e, sl=sl: e.activation(
            y_acc[:, sl, :], x_slab[:, sl, :], AF.Square,
            accum_out=stat[:, 1:2]),
            reads=[b_xslab[sl]], writes=[b_stat, b_yacc_all])
        s.emit("vector", lambda e: e.tensor_scalar(
            stat[:, 2:3], stat[:, 0:1], 1.0 / D, None, ALU.mult),
            reads=[b_stat], writes=[b_stat])
        s.emit("vector", lambda e: e.tensor_tensor(
            stat[:, 3:4], stat[:, 2:3], stat[:, 2:3], ALU.mult),
            reads=[b_stat], writes=[b_stat])
        s.emit("vector", lambda e: e.scalar_tensor_tensor(
            stat[:, 4:5], stat[:, 1:2], 1.0 / D, stat[:, 3:4],
            ALU.mult, ALU.subtract),
            reads=[b_stat], writes=[b_stat])
        s.emit("scalar", lambda e: e.activation(
            stat[:, 5:6], stat[:, 4:5], AF.Sqrt, bias=fc(1e-5)),
            reads=[b_stat, cst_buf["fconst"]], writes=[b_stat])
        s.emit("vector", lambda e: e.reciprocal(stat[:, 6:7], stat[:, 5:6]),
               reads=[b_stat], writes=[b_stat])
        s.emit("vector", lambda e, sl=sl: e.tensor_scalar(
            x_slab[:, sl, :], x_slab[:, sl, :], stat[:, 2:3], stat[:, 6:7],
            ALU.subtract, ALU.mult),
            reads=[b_stat], writes=[b_xslab[sl]])
        for ft in range(ND if ("ln" in parts or "ln2" in parts) else 0):
            pslot = ft % 4
            s.emit("tensor", lambda e, sl=sl, ft=ft, pslot=pslot: e.transpose(
                ps_S[:, pslot * 128:(pslot + 1) * 128],
                x_slab[:, sl, ft * 128:(ft + 1) * 128], cst["id128f"][:, :]),
                reads=[b_xslab[sl], cst_buf["id128f"]], writes=[b_ps4[0]])
            s.emit("vector", lambda e, ft=ft, pslot=pslot: e.tensor_copy(
                x32_fm[:, ft, :], ps_S[:, pslot * 128:(pslot + 1) * 128]),
                writes=[b_ps4[0], b_x32])
            s.emit("scalar", lambda e, tt=tt, ft=ft: e.copy(
                xn_bf[:, ft, tt * 128:(tt + 1) * 128], x32_fm[:, ft, :]),
                reads=[b_x32], writes=[b_xn])
        for ft in range(ND if "gate" in parts else 0):
            s.emit("tensor", lambda e, ft=ft: e.matmul(
                ps_S[:, 512:512 + NE], x32_fm[:, ft, :], wg_sb[:, ft, :],
                start=(ft == 0), stop=(ft == ND - 1)),
                reads=[b_x32, b_smallw], writes=[b_ps4[1]])
        if "gate" not in parts:
            continue
        s.emit("vector", lambda e, tt=tt: e.tensor_tensor(
            glog[:, tt, :], ps_S[:, 512:512 + NE], bg_bc[:, :], ALU.add),
            reads=[b_smallw], writes=[b_ps4[1], b_glog])
        s.emit("vector", lambda e, tt=tt: e.reduce_max(
            stat[:, 7:8], glog[:, tt, :], axis=AX.X),
            reads=[b_glog], writes=[b_stat])
        s.emit("vector", lambda e, tt=tt: e.tensor_scalar(
            gmask[:, tt, :], glog[:, tt, :], stat[:, 7:8], None, ALU.is_ge),
            reads=[b_glog, b_stat], writes=[b_gmask])
        if "thin" not in parts:
            continue
        for ft in range(ND):
            s.emit("tensor", lambda e, tt=tt, ft=ft: e.matmul(
                ps_a[:, 0:H], xn_bf[:, ft, tt * 128:(tt + 1) * 128], wb_sb[:, ft, :],
                start=(ft == 0), stop=(ft == ND - 1)),
                reads=[b_xn, b_smallw], writes=[b_psa])
        s.emit("scalar", lambda e, tt=tt: e.activation(
            sig_tm[:, tt, :], ps_a[:, 0:H], AF.Sigmoid),
            writes=[b_psa, b_sig])
        for ft in range(ND):
            s.emit("tensor", lambda e, tt=tt, ft=ft: e.matmul(
                ps_a[:, 128:128 + H], xn_bf[:, ft, tt * 128:(tt + 1) * 128], wf_sb[:, ft, :],
                start=(ft == 0), stop=(ft == ND - 1)),
                reads=[b_xn, b_smallw], writes=[b_psa])
        s.emit("vector", lambda e: e.tensor_tensor(
            ps_a[:, 128:128 + H], ps_a[:, 128:128 + H], delta_bc[:, :], ALU.add),
            reads=[b_smallw], writes=[b_psa])
        s.emit("scalar", lambda e, tt=tt: e.activation(
            x_slab[:, tt % 2, 0:H], ps_a[:, 128:128 + H], AF.Sigmoid),
            writes=[b_psa, b_xslab[tt % 2]])
        s.emit("scalar", lambda e, tt=tt: e.activation(
            logf_bf[:, tt, :], x_slab[:, tt % 2, 0:H], AF.Ln),
            reads=[b_xslab[tt % 2]], writes=[b_logf])
        for ft in range(ND):
            s.emit("tensor", lambda e, tt=tt, ft=ft: e.matmul(
                ps_b[0:64, 0:128], ww1_sb[:, ft, :], xn_bf[:, ft, tt * 128:(tt + 1) * 128],
                start=(ft == 0), stop=(ft == ND - 1)),
                reads=[b_xn, b_smallw], writes=[b_psb])
        s.emit("scalar", lambda e, tt=tt: e.copy(
            w1o[:, tt * 128:(tt + 1) * 128], ps_b[0:64, 0:128]),
            writes=[b_psb, b_w1o])

    for tt in range(NT if ("fc" in parts and "thin" in parts) else 0):
        for j in range(tt + 1):
            lhs = cst["cumtri"] if j == tt else cst["allones"]
            lb = cst_buf["cumtri"] if j == tt else cst_buf["allones"]
            s.emit("tensor", lambda e, tt=tt, j=j, lhs=lhs: e.matmul(
                ps_S[:, 1024:1024 + H], lhs[:, :], logf_bf[:, j, :],
                start=(j == 0), stop=(j == tt)),
                reads=[b_logf, lb], writes=[b_ps4[2]])
        s.emit("vector", lambda e, tt=tt: e.tensor_copy(
            fc_tm[:, tt, :], ps_S[:, 1024:1024 + H]),
            writes=[b_ps4[2], b_fc])

    dbg_dump("xn_bf", lambda: xn_bf[:, :, :], [128, ND, N], BF16, [b_xn.w])
    dbg_dump("glog", lambda: glog[:, :, :], [128, NT, NE], F32, [b_glog.w])
    dbg_dump("gmask", lambda: gmask[:, :, :], [128, NT, NE], F32, [b_gmask.w])
    dbg_dump("sig_tm", lambda: sig_tm[:, :, :], [128, NT, H], F32, [b_sig.w])
    dbg_dump("logf_bf", lambda: logf_bf[:, :, :], [128, NT, H], BF16, [b_logf.w])
    dbg_dump("fc_tm", lambda: fc_tm[:, :, :], [128, NT, H], F32, [b_fc.w])
    dbg_dump("w1o", lambda: w1o[:, :], [64, N], BF16, [b_w1o.w])



    wo_sb = nc.alloc_sbuf_tensor("wo_sb", [128, ND, D], BF16)
    b_wo = Buf()
    s.emit("gpsimd", lambda e: e.dma_start(
        out=wo_sb[:, :, :], in_=wo_d.rearrange("(t p) d -> p t d", p=128)),
        writes=[b_wo], inc_k=16)
    s.emit("gpsimd", lambda e: e.memset(y_acc[:, :, :], 0.0), writes=[b_yacc_all])

    st2 = nc.alloc_sbuf_tensor("st2", [128, 16], F32)
    _attn_guards = []

    def _ga(name, shape, dtype):
        g = nc.sbuf_tensor(name, shape, dtype)
        _attn_guards.append(g)
        return g.__enter__()

    q_fm = _ga("q_fm", [128, N], BF16)
    k_fm = _ga("k_fm", [128, N], BF16)
    w_fm = _ga("w_fm", [128, N], BF16)
    wb_fm = _ga("wb_fm", [128, N], BF16)
    w_tm = _ga("w_tm", [128, NT, 128], BF16)
    wb_tm = _ga("wb_tm", [128, NT, 128], BF16)
    v_aug = _ga("v_aug", [128, NT, 144], BF16)
    kt = _ga("kt", [128, 2, N], BF16)
    qhat_k = _ga("qhat_k", [128, 2, 128], BF16)
    sii_k = _ga("sii_k", [128, 2, 2, 128], BF16)
    pt_k = _ga("pt_k", [128, 2, 64], BF16)
    ktd_k = _ga("ktd_k", [128, 2, 128], BF16)
    Lb = _ga("Lb", [128, 2, 128], BF16)
    Nb = _ga("Nb", [128, 2, 128], BF16)
    Mb = _ga("Mb", [128, 2, 128], BF16)
    nGt = _ga("nGt", [128, 2, 128], BF16)
    npow = _ga("npow", [128, 2, 6, 128], BF16)
    lkb = _ga("lkb", [128, 2, 128], BF16)
    zb = _ga("zb", [128, 2, 2, 192], BF16)
    s_sb = _ga("s_sb", [128, 2, N], BF16)
    pT_sb = _ga("pT_sb", [128, 2, N], BF16)
    o_tmp = _ga("o_tmp", [128, 2, 64], BF16)
    fcol = _ga("fcol", [128, 2, NT], F32)
    fcmr = _ga("fcmr", [128, 2, NT], F32)
    st3 = _ga("st3", [128, 2, 12], F32)
    wq_sl = _ga("wq_sl", [128, ND, 128], BF16)
    wk_sl = _ga("wk_sl", [128, ND, 128], BF16)
    wv_sl = _ga("wv_sl", [128, ND, 128], BF16)
    ww2_sl = _ga("ww2_sl", [64, 128], BF16)

    b_q = Buf(); b_k = Buf(); b_wfm = Buf(); b_wbfm = Buf()
    b_wtm = Buf(); b_wbtm = Buf(); b_vaug = Buf()
    b_kt = [[Buf(), Buf()], [Buf(), Buf()]]
    b_qh = [Buf(), Buf()]; b_sii = [Buf(), Buf()]
    b_pt = [Buf(), Buf()]; b_ktd = [Buf(), Buf()]
    b_L = [Buf(), Buf()]; b_N = [Buf(), Buf()]; b_M = [Buf(), Buf()]
    b_nGt = [Buf(), Buf()]; b_np = [Buf(), Buf()]; b_lk = [Buf(), Buf()]
    b_z = [[Buf(), Buf()], [Buf(), Buf()]]
    b_s = [Buf(), Buf()]; b_pT = [Buf(), Buf()]; b_otmp = [Buf(), Buf()]
    b_fcol = [Buf(), Buf()]; b_fcmr = [Buf(), Buf()]
    b_st2 = Buf(); b_st3 = [Buf(), Buf()]
    b_wqsl = Buf(); b_wksl = Buf(); b_wvsl = Buf(); b_ww2sl = Buf()
    b_ofm = Buf()

    def emit_proj(hp):
        c0 = hp * 128
        s.emit("gpsimd", lambda e, c0=c0: e.dma_start(
            out=wq_sl[:, :, :],
            in_=wq_d[:, c0:c0 + 128].rearrange("(t p) c -> p t c", p=128)),
            writes=[b_wqsl], inc_k=16)
        s.emit("gpsimd", lambda e, c0=c0: e.dma_start(
            out=wk_sl[:, :, :],
            in_=wk_d[:, c0:c0 + 128].rearrange("(t p) c -> p t c", p=128)),
            writes=[b_wksl], inc_k=16)
        s.emit("gpsimd", lambda e, c0=c0: e.dma_start(
            out=wv_sl[:, :, :],
            in_=wv_d[:, c0:c0 + 128].rearrange("(t p) c -> p t c", p=128)),
            writes=[b_wvsl], inc_k=16)
        s.emit("gpsimd", lambda e, c0=c0: e.dma_start(
            out=ww2_sl[:, :], in_=ww2_d[:, c0:c0 + 128]),
            writes=[b_ww2sl], inc_k=16)
        s.emit("gpsimd", lambda e: e.memset(v_aug[:, :, 64:72], 0.0), writes=[b_vaug])
        s.emit("gpsimd", lambda e: e.memset(v_aug[:, :, 136:144], 0.0), writes=[b_vaug])
        s.emit("gpsimd", lambda e: e.memset(v_aug[:, :, 64:65], 1.0), writes=[b_vaug])
        s.emit("gpsimd", lambda e: e.memset(v_aug[:, :, 136:137], 1.0), writes=[b_vaug])
        for dst, wsl, bw, bdst in ((q_fm, wq_sl, b_wqsl, b_q), (k_fm, wk_sl, b_wksl, b_k)):
            for t4 in range(4):
                for ft in range(ND):
                    s.emit("tensor", lambda e, dst=dst, wsl=wsl, t4=t4, ft=ft: e.matmul(
                        ps_a[:, 0:512], wsl[:, ft, :],
                        xn_bf[:, ft, t4 * 512:(t4 + 1) * 512],
                        start=(ft == 0), stop=(ft == ND - 1)),
                        reads=[bw, b_xn], writes=[b_psa])
                s.emit("scalar", lambda e, dst=dst, t4=t4: e.copy(
                    dst[:, t4 * 512:(t4 + 1) * 512], ps_a[:, 0:512]),
                    writes=[b_psa, bdst])
        for tt in range(NT):
            for ft in range(ND):
                s.emit("tensor", lambda e, tt=tt, ft=ft: e.matmul(
                    ps_b[:, 0:128], xn_bf[:, ft, tt * 128:(tt + 1) * 128],
                    wv_sl[:, ft, :], start=(ft == 0), stop=(ft == ND - 1)),
                    reads=[b_wvsl, b_xn], writes=[b_psb])
            s.emit("vector", lambda e, tt=tt: e.tensor_copy(
                bass.AP(v_aug, tt * 144, [[v_aug.shape[2] * NT, 128], [72, 2], [1, 64]]),
                ps_b[:, 0:128]),
                writes=[b_psb, b_vaug])
            s.emit("tensor", lambda e, tt=tt: e.matmul(
                ps_S[:, 0:128], w1o[:, tt * 128:(tt + 1) * 128], ww2_sl[:, :],
                start=True, stop=True),
                reads=[b_w1o, b_ww2sl], writes=[b_ps4[0]])
            for half in range(2):
                s.emit("scalar", lambda e, tt=tt, half=half: e.activation(
                    x_slab[:, 0, half * 64:(half + 1) * 64],
                    ps_S[:, half * 64:(half + 1) * 64], AF.Square,
                    accum_out=st2[:, 8 + half:9 + half]),
                    writes=[b_ps4[0], b_st2, b_xslab[0]])
            s.emit("scalar", lambda e: e.activation(
                st2[:, 10:12], st2[:, 8:10], AF.Sqrt, bias=fc(1e-6)),
                reads=[cst_buf["fconst"]], writes=[b_st2])
            s.emit("vector", lambda e: e.reciprocal(st2[:, 12:14], st2[:, 10:12]),
                   writes=[b_st2])
            s.emit("vector", lambda e, tt=tt, hp=hp: e.scalar_tensor_tensor(
                st2[:, 14:16], sig_tm[:, tt, 2 * hp:2 * hp + 2], 2.0,
                st2[:, 12:14], ALU.mult, ALU.mult),
                reads=[b_sig], writes=[b_st2])
            for half in range(2):
                s.emit("vector", lambda e, tt=tt, half=half: e.tensor_scalar(
                    w_tm[:, tt, half * 64:(half + 1) * 64],
                    ps_S[:, half * 64:(half + 1) * 64],
                    st2[:, 12 + half:13 + half], None, ALU.mult),
                    reads=[b_st2], writes=[b_ps4[0], b_wtm])
                s.emit("vector", lambda e, tt=tt, half=half: e.tensor_scalar(
                    wb_tm[:, tt, half * 64:(half + 1) * 64],
                    ps_S[:, half * 64:(half + 1) * 64],
                    st2[:, 14 + half:15 + half], None, ALU.mult),
                    reads=[b_st2], writes=[b_ps4[0], b_wbtm])
            s.emit("tensor", lambda e, tt=tt: e.transpose(
                ps_c[:, 512:640], w_tm[:, tt, :], cst["id128"][:, :]),
                reads=[b_wtm, cst_buf["id128"]], writes=[b_psc])
            s.emit("scalar", lambda e, tt=tt: e.copy(
                w_fm[:, tt * 128:(tt + 1) * 128], ps_c[:, 512:640]),
                writes=[b_psc, b_wfm])
            s.emit("tensor", lambda e, tt=tt: e.transpose(
                ps_c[:, 640:768], wb_tm[:, tt, :], cst["id128"][:, :]),
                reads=[b_wbtm, cst_buf["id128"]], writes=[b_psc])
            s.emit("scalar", lambda e, tt=tt: e.copy(
                wb_fm[:, tt * 128:(tt + 1) * 128], ps_c[:, 640:768]),
                writes=[b_psc, b_wbfm])

    def emit_head_prep(h, ph):
        bk = 2 + ph
        s.emit("tensor", lambda e, h=h, bk=bk: e.matmul(
            ps_S[:, bk * 512 + 256:bk * 512 + 257], cst["ones_row_f32"][:, :],
            fc_tm[0:1, 8, h:h + 1], start=True, stop=True),
            reads=[b_fc, cst_buf["ones_row_f32"]], writes=[b_ps4[bk]])
        s.emit("vector", lambda e, ph=ph, bk=bk: e.tensor_copy(
            st3[:, ph, 0:1], ps_S[:, bk * 512 + 256:bk * 512 + 257]),
            writes=[b_ps4[bk], b_st3[ph]])
        s.emit("vector", lambda e, ph=ph: e.tensor_scalar(
            st3[:, ph, 1:2], st3[:, ph, 0:1], -1.0, None, ALU.mult),
            writes=[b_st3[ph]])
        s.emit("scalar", lambda e, h=h, ph=ph: e.activation(
            fcol[:, ph, :], bass.AP(fc_tm, h, [[NT * H, 128], [H, NT], [1, 1]]),
            AF.Exp, bias=st3[:, ph, 0:1], scale=fc(-1.0)),
            reads=[b_fc, b_st3[ph], cst_buf["fconst"]], writes=[b_fcol[ph]])
        s.emit("vector", lambda e, h=h, ph=ph: e.tensor_scalar(
            fcmr[:, ph, :], bass.AP(fc_tm, h, [[NT * H, 128], [H, NT], [1, 1]]),
            st3[:, ph, 1:2], None, ALU.add),
            reads=[b_fc, b_st3[ph]], writes=[b_fcmr[ph]])

    def emit_phaseA(h, ph, i):
        psA = ps_a if ph == 0 else ps_b
        bA = b_psa if ph == 0 else b_psb
        r0 = ph * 64
        isl = slice(i * 128, (i + 1) * 128)
        W_I = w_fm[r0:r0 + 64, isl]
        Wb_I = wb_fm[r0:r0 + 64, isl]
        Q_I = q_fm[r0:r0 + 64, isl]
        K_I = k_fm[r0:r0 + 64, isl]
        Wtm_I = w_tm[:, i, r0:r0 + 64]
        Wbtm_I = wb_tm[:, i, r0:r0 + 64]
        s.emit("tensor", lambda e: e.matmul(psA[:, 0:128], W_I, Wb_I,
               start=True, stop=True, tile_position=(r0, 0)),
               reads=[b_wfm, b_wbfm], writes=[bA])
        s.emit("vector", lambda e: e.tensor_tensor(
            Lb[:, ph, :], psA[:, 0:128], cst["m_trils"][:, :], ALU.mult),
            reads=[cst_buf["m_trils"]], writes=[bA, b_L[ph]])
        s.emit("tensor", lambda e: e.matmul(psA[:, 128:256], Wb_I, W_I,
               start=True, stop=True, tile_position=(r0, 0)),
               reads=[b_wfm, b_wbfm], writes=[bA])
        s.emit("vector", lambda e: e.tensor_tensor(
            Nb[:, ph, :], psA[:, 128:256], cst["m_trius"][:, :], ALU.mult),
            reads=[cst_buf["m_trius"]], writes=[bA, b_N[ph]])
        s.emit("tensor", lambda e: e.matmul(psA[:, 256:384], W_I, K_I,
               start=True, stop=True, tile_position=(r0, 0)),
               reads=[b_wfm, b_k], writes=[bA])
        s.emit("vector", lambda e: e.tensor_tensor(
            Mb[:, ph, :], psA[:, 256:384], cst["m_trils"][:, :], ALU.mult),
            reads=[cst_buf["m_trils"]], writes=[bA, b_M[ph]])
        s.emit("tensor", lambda e: e.matmul(psA[:, 384:512], Wb_I, Q_I,
               start=True, stop=True, tile_position=(r0, 0)),
               reads=[b_wbfm, b_q], writes=[bA])
        s.emit("vector", lambda e: e.tensor_tensor(
            nGt[:, ph, :], psA[:, 384:512], cst["m_negtriu"][:, :], ALU.mult),
            reads=[cst_buf["m_negtriu"]], writes=[bA, b_nGt[ph]])

        def lp(lev):
            return Lb[:, ph, :] if lev == 0 else lkb[:, ph, :]

        def np_(lev):
            return Nb[:, ph, :] if lev == 0 else npow[:, ph, lev - 1, :]

        for lev in range(6):
            bl = b_L[ph] if lev == 0 else b_lk[ph]
            bn = b_N[ph] if lev == 0 else b_np[ph]
            s.emit("tensor", lambda e, lev=lev: e.matmul(
                psA[:, 128:256], lp(lev), np_(lev), start=True, stop=True),
                reads=[bl, bn], writes=[bA])
            s.emit("scalar", lambda e, lev=lev: e.copy(
                npow[:, ph, lev, :], psA[:, 128:256]),
                writes=[bA, b_np[ph]])
            if lev < 5:
                s.emit("tensor", lambda e, lev=lev: e.matmul(
                    psA[:, 0:128], np_(lev), lp(lev), start=True, stop=True),
                    reads=[bl, bn], writes=[bA])
                s.emit("vector", lambda e: e.tensor_copy(
                    lkb[:, ph, :], psA[:, 0:128]),
                    writes=[bA, b_lk[ph]])
        zc = 0
        s.emit("tensor", lambda e: e.matmul(psA[:, 0:64], npow[:, ph, 5, :], Wtm_I,
               start=True, stop=False), reads=[b_np[ph], b_wtm], writes=[bA])
        s.emit("tensor", lambda e: e.matmul(psA[:, 64:192], npow[:, ph, 5, :], Mb[:, ph, :],
               start=False, stop=True), reads=[b_np[ph], b_M[ph]], writes=[bA])
        s.emit("vector", lambda e: e.tensor_tensor(
            zb[:, ph, 0, 0:64], Wtm_I, psA[:, 0:64], ALU.add),
            reads=[b_wtm], writes=[bA, b_z[ph][0]])
        s.emit("vector", lambda e: e.tensor_tensor(
            zb[:, ph, 0, 64:192], Mb[:, ph, :], psA[:, 64:192], ALU.add),
            reads=[b_M[ph]], writes=[bA, b_z[ph][0]])
        for lev in (4, 3, 2, 1, 0):
            zn = 1 - zc
            s.emit("tensor", lambda e, lev=lev, zc=zc: e.matmul(
                psA[:, 0:192], npow[:, ph, lev, :], zb[:, ph, zc, :],
                start=True, stop=True), reads=[b_np[ph], b_z[ph][zc]], writes=[bA])
            s.emit("vector", lambda e, zc=zc, zn=zn: e.tensor_tensor(
                zb[:, ph, zn, :], zb[:, ph, zc, :], psA[:, 0:192], ALU.add),
                reads=[b_z[ph][zc]], writes=[bA, b_z[ph][zn]])
            zc = zn
        zn = 1 - zc
        s.emit("tensor", lambda e, zc=zc: e.matmul(
            psA[:, 0:192], Nb[:, ph, :], zb[:, ph, zc, :],
            start=True, stop=True), reads=[b_N[ph], b_z[ph][zc]], writes=[bA])
        s.emit("vector", lambda e, zc=zc, zn=zn: e.tensor_tensor(
            zb[:, ph, zn, :], zb[:, ph, zc, :], psA[:, 0:192], ALU.subtract),
            reads=[b_z[ph][zc]], writes=[bA, b_z[ph][zn]])
        zf = zn
        TW = zb[:, ph, zf, 0:64]
        CI2 = zb[:, ph, zf, 64:192]
        bZ = b_z[ph][zf]
        s.emit("tensor", lambda e: e.matmul(psA[:, 256:384], Q_I, K_I,
               start=True, stop=False, tile_position=(r0, 0)),
               reads=[b_q, b_k], writes=[bA])
        s.emit("tensor", lambda e: e.matmul(psA[:, 256:384], nGt[:, ph, :], CI2,
               start=False, stop=True), reads=[b_nGt[ph], bZ], writes=[bA])
        s.emit("scalar", lambda e, i=i, ph=ph: e.copy(
            sii_k[:, ph, i % 2, :], psA[:, 256:384]),
            writes=[bA, b_sii[ph]])
        s.emit("tensor", lambda e: e.matmul(
            psA[r0:r0 + 64, 0:128], TW, nGt[:, ph, :],
            start=True, stop=True, tile_position=(0, r0)),
            reads=[bZ, b_nGt[ph]], writes=[bA])
        s.emit("vector", lambda e, i=i: e.tensor_tensor(
            qhat_k[r0:r0 + 64, i % 2, :], Q_I, psA[r0:r0 + 64, 0:128], ALU.add),
            reads=[b_q], writes=[bA, b_qh[ph]])
        s.emit("tensor", lambda e: e.matmul(
            psA[r0:r0 + 64, 128:192], TW, Wbtm_I,
            start=True, stop=True, tile_position=(0, r0)),
            reads=[bZ, b_wbtm], writes=[bA])
        s.emit("vector", lambda e, i=i: e.tensor_tensor(
            pt_k[r0:r0 + 64, i % 2, :], cst["i64stack"][r0:r0 + 64, :],
            psA[r0:r0 + 64, 128:192], ALU.subtract),
            reads=[cst_buf["i64stack"]], writes=[bA, b_pt[ph]])
        s.emit("tensor", lambda e: e.matmul(
            psA[r0:r0 + 64, 256:384], Wbtm_I, CI2,
            start=True, stop=True, tile_position=(0, r0)),
            reads=[b_wbtm, bZ], writes=[bA])
        s.emit("vector", lambda e, i=i: e.tensor_tensor(
            ktd_k[r0:r0 + 64, i % 2, :], K_I, psA[r0:r0 + 64, 256:384], ALU.subtract),
            reads=[b_k], writes=[bA, b_ktd[ph]])

    def emit_phaseB_chunk(h, ph, i):
        r0 = ph * 64
        bk = 2 + ph
        sk = ph
        psc = ps_c if ph == 0 else ps_d
        bpsc = b_psc if ph == 0 else b_psd
        rbuf = (i + 1) % 2
        wbuf = i % 2
        jend = (i + 1) * 128
        nj5 = (i * 128 + 511) // 512
        for j5 in range(nj5):
            w = min(512, i * 128 - j5 * 512)
            s.emit("tensor", lambda e, i=i, j5=j5, w=w, rbuf=rbuf: e.matmul(
                ps_S[:, sk * 512:sk * 512 + w],
                qhat_k[r0:r0 + 64, i % 2, :], kt[r0:r0 + 64, rbuf, j5 * 512:j5 * 512 + w],
                start=True, stop=True, tile_position=(r0, 0)),
                reads=[b_qh[ph], b_kt[ph][rbuf]], writes=[b_ps4[sk]])
            s.emit("vector", lambda e, j5=j5, w=w, ph=ph: e.tensor_copy(
                s_sb[:, ph, j5 * 512:j5 * 512 + w], ps_S[:, sk * 512:sk * 512 + w]),
                writes=[b_ps4[sk], b_s[ph]])
            s.emit("vector", lambda e, j5=j5, w=w, ph=ph: e.reduce_max(
                st3[:, ph, 5 + j5:6 + j5], s_sb[:, ph, j5 * 512:j5 * 512 + w], axis=AX.X),
                reads=[b_s[ph]], writes=[b_st3[ph]])
        s.emit("vector", lambda e, i=i, ph=ph: e.tensor_tensor(
            s_sb[:, ph, i * 128:(i + 1) * 128], sii_k[:, ph, i % 2, :],
            cst["m_addneg"][:, :], ALU.add),
            reads=[b_sii[ph], cst_buf["m_addneg"]], writes=[b_s[ph]])
        s.emit("vector", lambda e, i=i, ph=ph, nj5=nj5: e.reduce_max(
            st3[:, ph, 5 + nj5:6 + nj5], s_sb[:, ph, i * 128:(i + 1) * 128], axis=AX.X),
            reads=[b_s[ph]], writes=[b_st3[ph]])
        s.emit("vector", lambda e, ph=ph, nj5=nj5: e.reduce_max(
            st3[:, ph, 2:3], st3[:, ph, 5:6 + nj5], axis=AX.X),
            writes=[b_st3[ph]])
        s.emit("vector", lambda e, i=i, ph=ph: e.tensor_scalar(
            st3[:, ph, 3:4], st3[:, ph, 2:3], -SCALE, fcmr[:, ph, i:i + 1],
            ALU.mult, ALU.add),
            reads=[b_fcmr[ph]], writes=[b_st3[ph]])
        s.emit("scalar", lambda e, jend=jend, ph=ph: e.activation(
            s_sb[:, ph, 0:jend], s_sb[:, ph, 0:jend], AF.Exp,
            bias=st3[:, ph, 3:4], scale=fc(SCALE)),
            reads=[b_st3[ph], cst_buf["fconst"]], writes=[b_s[ph]])
        for jt in range(i + 1):
            slot = jt % 7
            s.emit("tensor", lambda e, jt=jt, slot=slot, ph=ph, psc=psc: e.transpose(
                psc[:, slot * 128:(slot + 1) * 128],
                s_sb[:, ph, jt * 128:(jt + 1) * 128], cst["id128"][:, :]),
                reads=[b_s[ph], cst_buf["id128"]], writes=[bpsc])
            s.emit("vector", lambda e, jt=jt, slot=slot, ph=ph, psc=psc: e.tensor_scalar(
                pT_sb[:, ph, jt * 128:(jt + 1) * 128],
                psc[:, slot * 128:(slot + 1) * 128],
                fcol[:, ph, jt:jt + 1], None, ALU.mult),
                reads=[b_fcol[ph]], writes=[bpsc, b_pT[ph]])
            s.emit("tensor", lambda e, jt=jt, i=i, ph=ph, bk=bk: e.matmul(
                ps_S[:, bk * 512:bk * 512 + 72], pT_sb[:, ph, jt * 128:(jt + 1) * 128],
                v_aug[:, jt, ph * 72:ph * 72 + 72],
                start=(jt == 0), stop=(jt == i)),
                reads=[b_pT[ph], b_vaug], writes=[b_ps4[bk]])
        s.emit("vector", lambda e, ph=ph, bk=bk: e.reciprocal(
            st3[:, ph, 4:5], ps_S[:, bk * 512 + 64:bk * 512 + 65]),
            writes=[b_ps4[bk], b_st3[ph]])
        s.emit("vector", lambda e, ph=ph, bk=bk: e.tensor_scalar(
            o_tmp[:, ph, :], ps_S[:, bk * 512:bk * 512 + 64], st3[:, ph, 4:5],
            None, ALU.mult),
            reads=[b_st3[ph]], writes=[b_ps4[bk], b_otmp[ph]])
        s.emit("tensor", lambda e, ph=ph, psc=psc: e.transpose(
            psc[r0:r0 + 64, 896:1024], o_tmp[:, ph, :], cst["id128"][:, :],
            tile_position=(0, r0)),
            reads=[b_otmp[ph], cst_buf["id128"]], writes=[bpsc])
        s.emit("scalar", lambda e, i=i, h=h, ph=ph, psc=psc: e.copy(
            o_fm[r0:r0 + 64, h // 2, i * 128:(i + 1) * 128],
            psc[r0:r0 + 64, 896:1024]),
            writes=[bpsc, b_ofm])
        for j5 in range(nj5):
            w = min(512, i * 128 - j5 * 512)
            s.emit("tensor", lambda e, i=i, j5=j5, w=w, rbuf=rbuf, bk=bk: e.matmul(
                ps_S[r0:r0 + 64, bk * 512:bk * 512 + w], pt_k[r0:r0 + 64, i % 2, :],
                kt[r0:r0 + 64, rbuf, j5 * 512:j5 * 512 + w],
                start=True, stop=True, tile_position=(r0, r0)),
                reads=[b_pt[ph], b_kt[ph][rbuf]], writes=[b_ps4[bk]])
            s.emit("scalar", lambda e, j5=j5, w=w, wbuf=wbuf, bk=bk: e.copy(
                kt[r0:r0 + 64, wbuf, j5 * 512:j5 * 512 + w],
                ps_S[r0:r0 + 64, bk * 512:bk * 512 + w]),
                writes=[b_ps4[bk], b_kt[ph][wbuf]])
        s.emit("gpsimd", lambda e, i=i, wbuf=wbuf: e.tensor_copy(
            kt[r0:r0 + 64, wbuf, i * 128:(i + 1) * 128],
            ktd_k[r0:r0 + 64, i % 2, :]),
            reads=[b_ktd[ph]], writes=[b_kt[ph][wbuf]])

    def defer(fn):
        rec = []
        real = s.emit
        s.emit = lambda *a, **k: rec.append((a, k))
        fn()
        s.emit = real
        return rec

    def play(recs):
        idx = [0] * len(recs)
        while True:
            done = True
            for r in range(len(recs)):
                if idx[r] < len(recs[r]):
                    a, k = recs[r][idx[r]]
                    s.emit(*a, **k)
                    idx[r] += 1
                    done = False
            if done:
                break

    for hp in range(hp_count):
        emit_proj(hp)
        h0, h1 = 2 * hp, 2 * hp + 1
        emit_head_prep(h0, 0)
        emit_head_prep(h1, 1)
        play([defer(lambda: emit_phaseA(h0, 0, 0)),
              defer(lambda: emit_phaseA(h1, 1, 0))])
        for i in range(NCH):
            recs = [defer(lambda i=i: emit_phaseB_chunk(h0, 0, i)),
                    defer(lambda i=i: emit_phaseB_chunk(h1, 1, i))]
            if i + 1 < NCH:
                recs.append(defer(lambda i=i: emit_phaseA(h0, 0, i + 1)))
                recs.append(defer(lambda i=i: emit_phaseA(h1, 1, i + 1)))
            play(recs)

    dbg_dump("o_fm", lambda: o_fm[:, :, :], [128, ND, N], BF16, [b_ofm.w])

    attn_bufs = ([b_q, b_k, b_wfm, b_wbfm, b_wtm, b_wbtm, b_vaug]
                 + b_kt[0] + b_kt[1] + b_qh + b_sii + b_pt + b_ktd
                 + b_L + b_N + b_M + b_nGt + b_np + b_lk
                 + b_z[0] + b_z[1] + b_s + b_pT + b_otmp
                 + b_fcol + b_fcmr + b_st3
                 + [b_wqsl, b_wksl, b_wvsl, b_ww2sl])
    barrier = []
    for b in attn_bufs:
        if b.w is not None:
            barrier.append(b.w)
        barrier.extend(b.readers)
    barrier = list({id(t): t for t in barrier}.values())
    for g in reversed(_attn_guards):
        g.__exit__(None, None, None)

    w1_sl = [nc.alloc_sbuf_tensor(f"w1_sl{i}", [128, ND, D], BF16) for i in range(2)]
    w2_sl = nc.alloc_sbuf_tensor("w2_sl", [128, ND, D], BF16)
    g_slab = nc.alloc_sbuf_tensor("g_slab", [128, ND, 512], BF16)
    b1_sb = nc.alloc_sbuf_tensor("b1_sb", [1, D], BF16)
    b2_sb = nc.alloc_sbuf_tensor("b2_sb", [1, D], BF16)
    b_w1 = [Buf(), Buf()]
    b_w2 = Buf()
    b_g = Buf()
    b_b1 = Buf()
    b_b2 = Buf()

    def emit_ffn_expert(e_idx):
        sl = e_idx % 2
        if e_idx == 0:
            s.emit("gpsimd", lambda e: e.dma_start(
                out=w1_sl[0][:, :, :],
                in_=w1_d[0].rearrange("(t p) i -> p t i", p=128)),
                writes=[b_w1[0]], deps=barrier, inc_k=16)
        if e_idx + 1 < NE:
            s.emit("gpsimd", lambda e, e_idx=e_idx: e.dma_start(
                out=w1_sl[(e_idx + 1) % 2][:, :, :],
                in_=w1_d[e_idx + 1].rearrange("(t p) i -> p t i", p=128)),
                writes=[b_w1[(e_idx + 1) % 2]],
                deps=(barrier if e_idx == 0 else ()), inc_k=16)
        s.emit("gpsimd", lambda e, e_idx=e_idx: e.dma_start(
            out=w2_sl[:, :, :], in_=w2_d[e_idx].rearrange("(t p) i -> p t i", p=128)),
            writes=[b_w2], deps=(barrier if e_idx == 0 else ()), inc_k=16)
        s.emit("gpsimd", lambda e, e_idx=e_idx: e.dma_start(
            out=b1_sb[:, :], in_=b1_d[e_idx:e_idx + 1, :]),
            writes=[b_b1], deps=(barrier if e_idx == 0 else ()), inc_k=16)
        s.emit("gpsimd", lambda e, e_idx=e_idx: e.dma_start(
            out=b2_sb[:, :], in_=b2_d[e_idx:e_idx + 1, :]),
            writes=[b_b2], deps=(barrier if e_idx == 0 else ()), inc_k=16)
        for t4 in range(4):
            for it in range(ND):
                psh = ps_a if it % 2 == 0 else ps_b
                bph = b_psa if it % 2 == 0 else b_psb
                for ft in range(ND):
                    s.emit("tensor", lambda e, sl=sl, t4=t4, it=it, ft=ft, psh=psh: e.matmul(
                        psh[:, 0:512], w1_sl[sl][:, ft, it * 128:(it + 1) * 128],
                        xn_bf[:, ft, t4 * 512:(t4 + 1) * 512],
                        start=(ft == 0), stop=False),
                        reads=[b_w1[sl], b_xn], writes=[bph])
                s.emit("tensor", lambda e, it=it, psh=psh: e.matmul(
                    psh[:, 0:512], b1_sb[0:1, it * 128:(it + 1) * 128],
                    cst["ones_row_bf"][0:1, 0:512],
                    start=False, stop=True),
                    reads=[b_b1, cst_buf["ones_row_bf"]], writes=[bph])
                s.emit("scalar", lambda e, it=it, psh=psh: e.activation(
                    g_slab[:, it, :], psh[:, 0:512], AF.Gelu),
                    writes=[bph, b_g])
            for tsub in range(4):
                tt = t4 * 4 + tsub
                for dh in range(2):
                    pb = tsub % 2 + 2 * dh
                    for it in range(ND):
                        s.emit("tensor", lambda e, tsub=tsub, dh=dh, it=it, pb=pb: e.matmul(
                            ps_S[:, pb * 512:pb * 512 + 512],
                            g_slab[:, it, tsub * 128:(tsub + 1) * 128],
                            w2_sl[:, it, dh * 512:(dh + 1) * 512],
                            start=(it == 0), stop=False),
                            reads=[b_g, b_w2], writes=[b_ps4[pb]])
                    s.emit("tensor", lambda e, dh=dh, pb=pb: e.matmul(
                        ps_S[:, pb * 512:pb * 512 + 512],
                        cst["ones_row_bf"][0:1, 0:128],
                        b2_sb[0:1, dh * 512:(dh + 1) * 512],
                        start=False, stop=True),
                        reads=[b_b2, cst_buf["ones_row_bf"]], writes=[b_ps4[pb]])
                    s.emit("vector", lambda e, tt=tt, dh=dh, e_idx=e_idx, pb=pb: e.scalar_tensor_tensor(
                        y_acc[:, tt, dh * 512:(dh + 1) * 512],
                        ps_S[:, pb * 512:pb * 512 + 512],
                        gmask[:, tt, e_idx:e_idx + 1],
                        y_acc[:, tt, dh * 512:(dh + 1) * 512],
                        ALU.mult, ALU.add),
                        reads=[b_gmask, b_yacc_all], writes=[b_ps4[pb], b_yacc[tt]])

    for e_idx in range(NE):
        emit_ffn_expert(e_idx)

    b_out = Buf()
    for tt in range(NT):
        sl = tt % 2
        s.emit("sync", lambda e, tt=tt, sl=sl: e.dma_start(
            out=x_slab[:, sl, :], in_=x_d.rearrange("(t p) d -> t p d", p=128)[tt]),
            writes=[b_xslab[sl]], inc_k=16)
        for dh in range(2):
            psx = ps_a if dh == 0 else ps_b
            bpsx = b_psa if dh == 0 else b_psb
            for ft in range(ND):
                s.emit("tensor", lambda e, tt=tt, dh=dh, ft=ft, psx=psx: e.matmul(
                    psx[:, 0:512], o_fm[:, ft, tt * 128:(tt + 1) * 128],
                    wo_sb[:, ft, dh * 512:(dh + 1) * 512],
                    start=(ft == 0), stop=(ft == ND - 1)),
                    reads=[b_ofm, b_wo], writes=[bpsx])
            s.emit("vector", lambda e, tt=tt, dh=dh, sl=sl, psx=psx: e.tensor_tensor(
                x_slab[:, sl, dh * 512:(dh + 1) * 512], psx[:, 0:512],
                x_slab[:, sl, dh * 512:(dh + 1) * 512], ALU.add),
                writes=[bpsx, b_xslab[sl]])
            s.emit("vector", lambda e, tt=tt, dh=dh, sl=sl: e.tensor_tensor(
                x_slab[:, sl, dh * 512:(dh + 1) * 512],
                x_slab[:, sl, dh * 512:(dh + 1) * 512],
                y_acc[:, tt, dh * 512:(dh + 1) * 512], ALU.add),
                reads=[b_yacc[tt], b_yacc_all], writes=[b_xslab[sl]])
        s.emit("sync", lambda e, tt=tt, sl=sl: e.dma_start(
            out=out_d.rearrange("(t p) d -> t p d", p=128)[tt], in_=x_slab[:, sl, :]),
            reads=[b_xslab[sl]], writes=[b_out], inc_k=16)
    s.emit("sync", lambda e: e.nop(), deps=[b_out.w])

    s.finalize(nc, paranoid=paranoid)
    return nc, dbg_d


def stage_inputs(inputs):
    consts = host_constants()
    x = np.ascontiguousarray(inputs["L0_attn_x"])
    base = {
        "wq": inputs["L0_attn_Wq"], "wk": inputs["L0_attn_Wk"],
        "wv": inputs["L0_attn_Wv"], "wo": inputs["L0_attn_Wo"],
        "ww1": inputs["L0_attn_Ww1"], "ww2": inputs["L0_attn_Ww2"],
        "wbm": inputs["L0_attn_Wb"], "wfm": inputs["L0_attn_Wf"],
        "delta_bc": np.ascontiguousarray(np.broadcast_to(
            inputs["L0_attn_delta"][None, :], (128, H))),
        "wg": inputs["L0_ffn_Wg"],
        "bg_bc": np.ascontiguousarray(np.broadcast_to(
            inputs["L0_ffn_bg"][None, :], (128, NE))),
        "w1": inputs["L0_ffn_W1"], "b1": inputs["L0_ffn_b1"],
        "w2": inputs["L0_ffn_W2"], "b2": inputs["L0_ffn_b2"],
    }
    for k, v in consts.items():
        base[f"c_{k}"] = v
    in_maps = []
    for c in range(8):
        m = dict(base)
        m["x"] = np.ascontiguousarray(x[c])
        in_maps.append(m)
    return in_maps


def kernel(**inputs):
    nc, _ = build()
    in_maps = stage_inputs(inputs)
    res = run_bass_kernel_spmd(nc, in_maps, core_ids=list(range(8)))
    return np.stack([res.results[c]["out"] for c in range(8)], axis=0)


# version 11
# speedup vs baseline: 1.3499x; 1.0032x over previous
import math
import numpy as np
import ml_dtypes
import concourse.bass as bass
import concourse.mybir as mybir
from concourse.bass_utils import run_bass_kernel_spmd

BFNP = ml_dtypes.bfloat16
F32 = mybir.dt.float32
BF16 = mybir.dt.bfloat16
AF = mybir.ActivationFunctionType
ALU = mybir.AluOpType
AX = mybir.AxisListType

N, D, H, HD, C = 2048, 1024, 16, 64, 128
NCH = N // C
NT = N // 128
ND = D // 128
NE = 8
SCALE = HD ** -0.5

ENGINES = ("tensor", "vector", "scalar", "gpsimd", "sync")

FCONST_VALS = (1e-5, 1e-6, -1.0, 2.0, 0.5, -0.5, SCALE, -SCALE)
FCONST_IDX = {v: i for i, v in enumerate(FCONST_VALS)}


class Op:
    __slots__ = ("eng", "fn", "deps", "inc_k", "sem", "sem_after", "pos",
                 "needs_inc", "pre_wait")

    def __init__(self, eng, fn, deps, inc_k):
        self.eng = eng
        self.fn = fn
        self.deps = deps
        self.inc_k = inc_k
        self.sem = None
        self.sem_after = None
        self.pos = None
        self.needs_inc = False
        self.pre_wait = None


class Buf:

    def __init__(self):
        self.w = None
        self.readers = []


class Sched:
    def __init__(self):
        self.ops = []

    def emit(self, eng, fn, reads=(), writes=(), deps=(), inc_k=1):
        d = [t for t in deps if t is not None]
        for b in reads:
            if b.w is not None:
                d.append(b.w)
        for b in writes:
            if b.w is not None:
                d.append(b.w)
            d.extend(b.readers)
        op = Op(eng, fn, d, inc_k)
        op.pos = len(self.ops)
        self.ops.append(op)
        for b in reads:
            b.readers.append(op)
        for b in writes:
            b.w = op
            b.readers = []
        return op

    def finalize(self, nc, paranoid=False):
        for op in self.ops:
            if op.inc_k == 16:
                op.needs_inc = True
            for dep in op.deps:
                assert dep.pos < op.pos
                if dep.eng != op.eng or dep.inc_k == 16 or paranoid:
                    dep.needs_inc = True
        eng_sems = {e: nc.alloc_semaphore(f"sem_{e}") for e in ENGINES}
        counts = {e: 0 for e in ENGINES}
        NPOOL = 8
        pool = {e: [nc.alloc_semaphore(f"dma_{e}_{i}") for i in range(NPOOL)]
                for e in ("sync", "gpsimd", "scalar")}
        pool_idx = {e: 0 for e in pool}
        pool_count = {}
        for op in self.ops:
            if op.inc_k == 16:
                sem = pool[op.eng][pool_idx[op.eng] % NPOOL]
                pool_idx[op.eng] += 1
                prev = pool_count.get(id(sem), 0)
                if prev:
                    op.pre_wait = (sem, prev)
                op.sem = sem
                op.sem_after = prev + 16
                pool_count[id(sem)] = op.sem_after
            elif op.needs_inc:
                counts[op.eng] += op.inc_k
                op.sem = eng_sems[op.eng]
                op.sem_after = counts[op.eng]
        marks = {e: {} for e in ENGINES}
        streams = {e: [o for o in self.ops if o.eng == e] for e in ENGINES}
        need_drain = set()
        vstream = streams["vector"]
        vpos = {id(o): i for i, o in enumerate(vstream)}
        for i, op in enumerate(vstream):
            for dep in op.deps:
                if dep.eng == "vector" and vpos[id(dep)] == i - 1:
                    need_drain.add(id(op))

        def body(eng_obj, ename):
            wm = marks[ename]
            for op in streams[ename]:
                for dep in op.deps:
                    if dep.eng == ename and dep.inc_k != 16 and not paranoid:
                        continue
                    assert dep.sem_after is not None, (dep.eng, op.eng)
                    if dep.sem_after > wm.get(id(dep.sem), 0):
                        eng_obj.wait_ge(dep.sem, dep.sem_after)
                        wm[id(dep.sem)] = dep.sem_after
                if op.pre_wait is not None:
                    sem, val = op.pre_wait
                    if val > wm.get(id(sem), 0):
                        eng_obj.wait_ge(sem, val)
                        wm[id(sem)] = val
                if id(op) in need_drain:
                    eng_obj.drain()
                instr = op.fn(eng_obj)
                if op.sem is not None:
                    instr.then_inc(op.sem, op.inc_k)

        with nc.Block() as blk:
            @blk.tensor
            def _(e):
                body(e, "tensor")

            @blk.vector
            def _(e):
                body(e, "vector")

            @blk.scalar
            def _(e):
                body(e, "scalar")

            @blk.gpsimd
            def _(e):
                body(e, "gpsimd")

            @blk.sync
            def _(e):
                body(e, "sync")


def host_constants():
    c = {}
    ident = np.eye(128, dtype=np.float32)
    c["id128"] = ident.astype(BFNP)
    c["id128f"] = ident
    c["cumtri"] = np.triu(np.ones((128, 128), np.float32)).astype(BFNP)
    c["allones"] = np.ones((128, 128), np.float32).astype(BFNP)
    c["m_trils"] = np.tril(np.ones((128, 128), np.float32), -1).astype(BFNP)
    c["m_trius"] = np.triu(np.ones((128, 128), np.float32), 1).astype(BFNP)
    c["m_negtriu"] = (-np.triu(np.ones((128, 128), np.float32))).astype(BFNP)
    c["m_addneg"] = np.triu(np.full((128, 128), -1e9, np.float32), 1).astype(BFNP)
    c["i64stack"] = np.concatenate([np.eye(64, dtype=np.float32)] * 2, 0)
    c["ones_row_bf"] = np.ones((1, 512), np.float32).astype(BFNP)
    c["ones_row_f32"] = np.ones((1, 128), np.float32)
    c["fconst"] = np.ascontiguousarray(np.broadcast_to(
        np.array(FCONST_VALS, np.float32)[None, :], (128, len(FCONST_VALS))))
    return c


def build(dbg=None, paranoid=False, parts=("ln", "gate", "thin", "fc"), hp_count=ND):
    dbg = dbg or []
    parts = set(parts)
    nc = bass.Bass("TRN2", target_bir_lowering=False,
                   detect_race_conditions=paranoid)
    s = Sched()

    x_d = nc.dram_tensor("x", [N, D], F32, kind="ExternalInput").ap()
    wq_d = nc.dram_tensor("wq", [D, D], F32, kind="ExternalInput").ap()
    wk_d = nc.dram_tensor("wk", [D, D], F32, kind="ExternalInput").ap()
    wv_d = nc.dram_tensor("wv", [D, D], F32, kind="ExternalInput").ap()
    wo_d = nc.dram_tensor("wo", [D, D], F32, kind="ExternalInput").ap()
    ww1_d = nc.dram_tensor("ww1", [D, HD], F32, kind="ExternalInput").ap()
    ww2_d = nc.dram_tensor("ww2", [HD, D], F32, kind="ExternalInput").ap()
    wb_d = nc.dram_tensor("wbm", [D, H], F32, kind="ExternalInput").ap()
    wf_d = nc.dram_tensor("wfm", [D, H], F32, kind="ExternalInput").ap()
    delta_bc_d = nc.dram_tensor("delta_bc", [128, H], F32, kind="ExternalInput").ap()
    wg_d = nc.dram_tensor("wg", [D, NE], F32, kind="ExternalInput").ap()
    bg_bc_d = nc.dram_tensor("bg_bc", [128, NE], F32, kind="ExternalInput").ap()
    w1_d = nc.dram_tensor("w1", [NE, D, D], F32, kind="ExternalInput").ap()
    b1_d = nc.dram_tensor("b1", [NE, D], F32, kind="ExternalInput").ap()
    w2_d = nc.dram_tensor("w2", [NE, D, D], F32, kind="ExternalInput").ap()
    b2_d = nc.dram_tensor("b2", [NE, D], F32, kind="ExternalInput").ap()
    cst_d = {}
    consts = host_constants()
    for k, v in consts.items():
        dt = BF16 if v.dtype == BFNP else F32
        cst_d[k] = nc.dram_tensor(f"c_{k}", list(v.shape), dt, kind="ExternalInput").ap()
    out_d = nc.dram_tensor("out", [N, D], F32, kind="ExternalOutput").ap()

    dbg_d = {}
    dbg_bufs = {}

    def dbg_dump(name, ap_fn, shape, dtype, deps):
        if name not in dbg:
            return
        dd = nc.dram_tensor(f"dbg_{name}", shape, dtype, kind="ExternalOutput").ap()
        dbg_d[name] = dd
        s.emit("sync", lambda e, a=ap_fn, d=dd: e.dma_start(out=d, in_=a()),
               deps=deps, inc_k=16)

    cst = {}
    cst_buf = {}
    for k, v in consts.items():
        dt = BF16 if v.dtype == BFNP else F32
        t = nc.alloc_sbuf_tensor(f"sb_{k}", list(v.shape), dt)
        cst[k] = t
        b = Buf()
        cst_buf[k] = b
        s.emit("sync", lambda e, t=t, k=k: e.dma_start(out=t[:, :], in_=cst_d[k][:, :]),
               writes=[b], inc_k=16)

    def CC(name):
        return cst[name], cst_buf[name]

    def fc(v):
        return cst["fconst"][:, FCONST_IDX[v]:FCONST_IDX[v] + 1]

    xn_bf = nc.alloc_sbuf_tensor("xn_bf", [128, ND, N], BF16)
    o_fm = nc.alloc_sbuf_tensor("o_fm", [128, ND, N], BF16)
    y_acc = nc.alloc_sbuf_tensor("y_acc", [128, NT, D], BF16)
    fc_tm = nc.alloc_sbuf_tensor("fc_tm", [128, NT, H], F32)
    sig_tm = nc.alloc_sbuf_tensor("sig_tm", [128, NT, H], F32)
    logf_bf = nc.alloc_sbuf_tensor("logf_bf", [128, NT, H], BF16)
    gmask = nc.alloc_sbuf_tensor("gmask", [128, NT, NE], F32)
    w1o = nc.alloc_sbuf_tensor("w1o", [64, N], BF16)
    wg_sb = nc.alloc_sbuf_tensor("wg_sb", [128, ND, NE], F32)
    wb_sb = nc.alloc_sbuf_tensor("wb_sb", [128, ND, H], BF16)
    wf_sb = nc.alloc_sbuf_tensor("wf_sb", [128, ND, H], BF16)
    ww1_sb = nc.alloc_sbuf_tensor("ww1_sb", [128, ND, HD], BF16)
    delta_bc = nc.alloc_sbuf_tensor("delta_sb", [128, H], F32)
    bg_bc = nc.alloc_sbuf_tensor("bg_sb", [128, NE], F32)

    b_xn = Buf()
    b_yacc_all = Buf()
    b_o = [[Buf() for _ in range(NCH)] for _ in range(H)]
    b_yacc = [Buf() for _ in range(NT)]
    b_fc = Buf()
    b_sig = Buf()
    b_logf = Buf()
    b_gmask = Buf()
    b_w1o = Buf()
    b_smallw = Buf()

    s.emit("gpsimd", lambda e: e.dma_start(
        out=wb_sb[:, :, :], in_=wb_d.rearrange("(t p) h -> p t h", p=128)),
        writes=[b_smallw], inc_k=16)
    s.emit("gpsimd", lambda e: e.dma_start(
        out=wf_sb[:, :, :], in_=wf_d.rearrange("(t p) h -> p t h", p=128)),
        writes=[b_smallw], inc_k=16)
    s.emit("gpsimd", lambda e: e.dma_start(
        out=ww1_sb[:, :, :], in_=ww1_d.rearrange("(t p) h -> p t h", p=128)),
        writes=[b_smallw], inc_k=16)
    s.emit("sync", lambda e: e.dma_start(
        out=wg_sb[:, :, :], in_=wg_d.rearrange("(t p) h -> p t h", p=128)),
        writes=[b_smallw], inc_k=16)
    s.emit("sync", lambda e: e.dma_start(out=delta_bc[:, :], in_=delta_bc_d[:, :]),
           writes=[b_smallw], inc_k=16)
    s.emit("sync", lambda e: e.dma_start(out=bg_bc[:, :], in_=bg_bc_d[:, :]),
           writes=[b_smallw], inc_k=16)

    ps_S = nc.alloc_psum_tensor("ps_S", [128, 2048], F32)
    ps_a = nc.alloc_psum_tensor("ps_a", [128, 512], F32)
    ps_b = nc.alloc_psum_tensor("ps_b", [128, 512], F32)
    ps_c = nc.alloc_psum_tensor("ps_c", [128, 1024], BF16)
    ps_d = nc.alloc_psum_tensor("ps_d", [128, 1024], BF16)
    b_ps4 = [Buf(), Buf(), Buf(), Buf()]
    b_psa = Buf()
    b_psb = Buf()
    b_psc = Buf()
    b_psd = Buf()

    x_slab = nc.alloc_sbuf_tensor("x_slab", [128, 2, D], F32)
    stat = nc.alloc_sbuf_tensor("stat", [128, 16], F32)
    glog = nc.alloc_sbuf_tensor("glog", [128, NT, NE], F32)
    x32_fm = nc.alloc_sbuf_tensor("x32_fm", [128, ND, 128], F32)
    b_xslab = [Buf(), Buf()]
    b_xnslab = [Buf(), Buf()]
    b_x32 = Buf()
    b_stat = Buf()
    b_glog = Buf()

    for tt in range(NT):
        if "ln" not in parts and "ln1" not in parts:
            break
        sl = tt % 2
        xs = x_slab
        ld = s.emit("sync", lambda e, tt=tt, sl=sl: e.dma_start(
            out=x_slab[:, sl, :], in_=x_d.rearrange("(t p) d -> t p d", p=128)[tt]),
            writes=[b_xslab[sl]], inc_k=16)
        s.emit("vector", lambda e, sl=sl: e.reduce_sum(
            stat[:, 0:1], x_slab[:, sl, :], axis=AX.X),
            reads=[b_xslab[sl]], writes=[b_stat])
        s.emit("scalar", lambda e, sl=sl: e.activation(
            y_acc[:, sl, :], x_slab[:, sl, :], AF.Square,
            accum_out=stat[:, 1:2]),
            reads=[b_xslab[sl]], writes=[b_stat, b_yacc_all])
        s.emit("vector", lambda e: e.tensor_scalar(
            stat[:, 2:3], stat[:, 0:1], 1.0 / D, None, ALU.mult),
            reads=[b_stat], writes=[b_stat])
        s.emit("vector", lambda e: e.tensor_tensor(
            stat[:, 3:4], stat[:, 2:3], stat[:, 2:3], ALU.mult),
            reads=[b_stat], writes=[b_stat])
        s.emit("vector", lambda e: e.scalar_tensor_tensor(
            stat[:, 4:5], stat[:, 1:2], 1.0 / D, stat[:, 3:4],
            ALU.mult, ALU.subtract),
            reads=[b_stat], writes=[b_stat])
        s.emit("scalar", lambda e: e.activation(
            stat[:, 5:6], stat[:, 4:5], AF.Sqrt, bias=fc(1e-5)),
            reads=[b_stat, cst_buf["fconst"]], writes=[b_stat])
        s.emit("vector", lambda e: e.reciprocal(stat[:, 6:7], stat[:, 5:6]),
               reads=[b_stat], writes=[b_stat])
        s.emit("vector", lambda e, sl=sl: e.tensor_scalar(
            x_slab[:, sl, :], x_slab[:, sl, :], stat[:, 2:3], stat[:, 6:7],
            ALU.subtract, ALU.mult),
            reads=[b_stat], writes=[b_xslab[sl]])
        for ft in range(ND if ("ln" in parts or "ln2" in parts) else 0):
            pslot = ft % 4
            s.emit("tensor", lambda e, sl=sl, ft=ft, pslot=pslot: e.transpose(
                ps_S[:, pslot * 128:(pslot + 1) * 128],
                x_slab[:, sl, ft * 128:(ft + 1) * 128], cst["id128f"][:, :]),
                reads=[b_xslab[sl], cst_buf["id128f"]], writes=[b_ps4[0]])
            s.emit("vector", lambda e, ft=ft, pslot=pslot: e.tensor_copy(
                x32_fm[:, ft, :], ps_S[:, pslot * 128:(pslot + 1) * 128]),
                writes=[b_ps4[0], b_x32])
            s.emit("scalar", lambda e, tt=tt, ft=ft: e.copy(
                xn_bf[:, ft, tt * 128:(tt + 1) * 128], x32_fm[:, ft, :]),
                reads=[b_x32], writes=[b_xn])
        for ft in range(ND if "gate" in parts else 0):
            s.emit("tensor", lambda e, ft=ft: e.matmul(
                ps_S[:, 512:512 + NE], x32_fm[:, ft, :], wg_sb[:, ft, :],
                start=(ft == 0), stop=(ft == ND - 1)),
                reads=[b_x32, b_smallw], writes=[b_ps4[1]])
        if "gate" not in parts:
            continue
        s.emit("vector", lambda e, tt=tt: e.tensor_tensor(
            glog[:, tt, :], ps_S[:, 512:512 + NE], bg_bc[:, :], ALU.add),
            reads=[b_smallw], writes=[b_ps4[1], b_glog])
        s.emit("vector", lambda e, tt=tt: e.reduce_max(
            stat[:, 7:8], glog[:, tt, :], axis=AX.X),
            reads=[b_glog], writes=[b_stat])
        s.emit("vector", lambda e, tt=tt: e.tensor_scalar(
            gmask[:, tt, :], glog[:, tt, :], stat[:, 7:8], None, ALU.is_ge),
            reads=[b_glog, b_stat], writes=[b_gmask])
        if "thin" not in parts:
            continue
        for ft in range(ND):
            s.emit("tensor", lambda e, tt=tt, ft=ft: e.matmul(
                ps_a[:, 0:H], xn_bf[:, ft, tt * 128:(tt + 1) * 128], wb_sb[:, ft, :],
                start=(ft == 0), stop=(ft == ND - 1)),
                reads=[b_xn, b_smallw], writes=[b_psa])
        s.emit("scalar", lambda e, tt=tt: e.activation(
            sig_tm[:, tt, :], ps_a[:, 0:H], AF.Sigmoid),
            writes=[b_psa, b_sig])
        for ft in range(ND):
            s.emit("tensor", lambda e, tt=tt, ft=ft: e.matmul(
                ps_a[:, 128:128 + H], xn_bf[:, ft, tt * 128:(tt + 1) * 128], wf_sb[:, ft, :],
                start=(ft == 0), stop=(ft == ND - 1)),
                reads=[b_xn, b_smallw], writes=[b_psa])
        s.emit("vector", lambda e: e.tensor_tensor(
            ps_a[:, 128:128 + H], ps_a[:, 128:128 + H], delta_bc[:, :], ALU.add),
            reads=[b_smallw], writes=[b_psa])
        s.emit("scalar", lambda e, tt=tt: e.activation(
            x_slab[:, tt % 2, 0:H], ps_a[:, 128:128 + H], AF.Sigmoid),
            writes=[b_psa, b_xslab[tt % 2]])
        s.emit("scalar", lambda e, tt=tt: e.activation(
            logf_bf[:, tt, :], x_slab[:, tt % 2, 0:H], AF.Ln),
            reads=[b_xslab[tt % 2]], writes=[b_logf])
        for ft in range(ND):
            s.emit("tensor", lambda e, tt=tt, ft=ft: e.matmul(
                ps_b[0:64, 0:128], ww1_sb[:, ft, :], xn_bf[:, ft, tt * 128:(tt + 1) * 128],
                start=(ft == 0), stop=(ft == ND - 1)),
                reads=[b_xn, b_smallw], writes=[b_psb])
        s.emit("scalar", lambda e, tt=tt: e.copy(
            w1o[:, tt * 128:(tt + 1) * 128], ps_b[0:64, 0:128]),
            writes=[b_psb, b_w1o])

    for tt in range(NT if ("fc" in parts and "thin" in parts) else 0):
        for j in range(tt + 1):
            lhs = cst["cumtri"] if j == tt else cst["allones"]
            lb = cst_buf["cumtri"] if j == tt else cst_buf["allones"]
            s.emit("tensor", lambda e, tt=tt, j=j, lhs=lhs: e.matmul(
                ps_S[:, 1024:1024 + H], lhs[:, :], logf_bf[:, j, :],
                start=(j == 0), stop=(j == tt)),
                reads=[b_logf, lb], writes=[b_ps4[2]])
        s.emit("vector", lambda e, tt=tt: e.tensor_copy(
            fc_tm[:, tt, :], ps_S[:, 1024:1024 + H]),
            writes=[b_ps4[2], b_fc])

    dbg_dump("xn_bf", lambda: xn_bf[:, :, :], [128, ND, N], BF16, [b_xn.w])
    dbg_dump("glog", lambda: glog[:, :, :], [128, NT, NE], F32, [b_glog.w])
    dbg_dump("gmask", lambda: gmask[:, :, :], [128, NT, NE], F32, [b_gmask.w])
    dbg_dump("sig_tm", lambda: sig_tm[:, :, :], [128, NT, H], F32, [b_sig.w])
    dbg_dump("logf_bf", lambda: logf_bf[:, :, :], [128, NT, H], BF16, [b_logf.w])
    dbg_dump("fc_tm", lambda: fc_tm[:, :, :], [128, NT, H], F32, [b_fc.w])
    dbg_dump("w1o", lambda: w1o[:, :], [64, N], BF16, [b_w1o.w])



    wo_sb = nc.alloc_sbuf_tensor("wo_sb", [128, ND, D], BF16)
    b_wo = Buf()
    s.emit("gpsimd", lambda e: e.dma_start(
        out=wo_sb[:, :, :], in_=wo_d.rearrange("(t p) d -> p t d", p=128)),
        writes=[b_wo], inc_k=16)
    s.emit("gpsimd", lambda e: e.memset(y_acc[:, :, :], 0.0), writes=[b_yacc_all])

    st2 = nc.alloc_sbuf_tensor("st2", [128, 16], F32)
    _attn_guards = []

    def _ga(name, shape, dtype):
        g = nc.sbuf_tensor(name, shape, dtype)
        _attn_guards.append(g)
        return g.__enter__()

    q_fm = _ga("q_fm", [128, N], BF16)
    k_fm = _ga("k_fm", [128, N], BF16)
    w_fm = _ga("w_fm", [128, N], BF16)
    wb_fm = _ga("wb_fm", [128, N], BF16)
    w_tm = _ga("w_tm", [128, NT, 128], BF16)
    wb_tm = _ga("wb_tm", [128, NT, 128], BF16)
    v_aug = _ga("v_aug", [128, NT, 144], BF16)
    kt = _ga("kt", [128, 2, N], BF16)
    qhat_k = _ga("qhat_k", [128, 2, 128], BF16)
    sii_k = _ga("sii_k", [128, 2, 2, 128], BF16)
    pt_k = _ga("pt_k", [128, 2, 64], BF16)
    ktd_k = _ga("ktd_k", [128, 2, 128], BF16)
    Lb = _ga("Lb", [128, 2, 128], BF16)
    Nb = _ga("Nb", [128, 2, 128], BF16)
    Mb = _ga("Mb", [128, 2, 128], BF16)
    nGt = _ga("nGt", [128, 2, 128], BF16)
    npow = _ga("npow", [128, 2, 6, 128], BF16)
    lkb = _ga("lkb", [128, 2, 128], BF16)
    zb = _ga("zb", [128, 2, 2, 192], BF16)
    s_sb = _ga("s_sb", [128, 2, N], BF16)
    pT_sb = _ga("pT_sb", [128, 2, N], BF16)
    o_tmp = _ga("o_tmp", [128, 2, 64], BF16)
    fcol = _ga("fcol", [128, 2, NT], F32)
    fcmr = _ga("fcmr", [128, 2, NT], F32)
    st3 = _ga("st3", [128, 2, 8], F32)
    wq_sl = _ga("wq_sl", [128, ND, 128], BF16)
    wk_sl = _ga("wk_sl", [128, ND, 128], BF16)
    wv_sl = _ga("wv_sl", [128, ND, 128], BF16)
    ww2_sl = _ga("ww2_sl", [64, 128], BF16)

    b_q = Buf(); b_k = Buf(); b_wfm = Buf(); b_wbfm = Buf()
    b_wtm = Buf(); b_wbtm = Buf(); b_vaug = Buf()
    b_kt = [[Buf(), Buf()], [Buf(), Buf()]]
    b_qh = [Buf(), Buf()]; b_sii = [Buf(), Buf()]
    b_pt = [Buf(), Buf()]; b_ktd = [Buf(), Buf()]
    b_L = [Buf(), Buf()]; b_N = [Buf(), Buf()]; b_M = [Buf(), Buf()]
    b_nGt = [Buf(), Buf()]; b_np = [Buf(), Buf()]; b_lk = [Buf(), Buf()]
    b_z = [[Buf(), Buf()], [Buf(), Buf()]]
    b_s = [[Buf() for _ in range(4)], [Buf() for _ in range(4)]]; b_pT = [Buf(), Buf()]; b_otmp = [Buf(), Buf()]
    b_fcol = [Buf(), Buf()]; b_fcmr = [Buf(), Buf()]
    b_st2 = Buf(); b_st3 = [Buf(), Buf()]
    b_wqsl = Buf(); b_wksl = Buf(); b_wvsl = Buf(); b_ww2sl = Buf()
    b_ofm = Buf()

    def emit_proj(hp):
        c0 = hp * 128
        s.emit("gpsimd", lambda e, c0=c0: e.dma_start(
            out=wq_sl[:, :, :],
            in_=wq_d[:, c0:c0 + 128].rearrange("(t p) c -> p t c", p=128)),
            writes=[b_wqsl], inc_k=16)
        s.emit("gpsimd", lambda e, c0=c0: e.dma_start(
            out=wk_sl[:, :, :],
            in_=wk_d[:, c0:c0 + 128].rearrange("(t p) c -> p t c", p=128)),
            writes=[b_wksl], inc_k=16)
        s.emit("gpsimd", lambda e, c0=c0: e.dma_start(
            out=wv_sl[:, :, :],
            in_=wv_d[:, c0:c0 + 128].rearrange("(t p) c -> p t c", p=128)),
            writes=[b_wvsl], inc_k=16)
        s.emit("gpsimd", lambda e, c0=c0: e.dma_start(
            out=ww2_sl[:, :], in_=ww2_d[:, c0:c0 + 128]),
            writes=[b_ww2sl], inc_k=16)
        s.emit("gpsimd", lambda e: e.memset(v_aug[:, :, 64:72], 0.0), writes=[b_vaug])
        s.emit("gpsimd", lambda e: e.memset(v_aug[:, :, 136:144], 0.0), writes=[b_vaug])
        s.emit("gpsimd", lambda e: e.memset(v_aug[:, :, 64:65], 1.0), writes=[b_vaug])
        s.emit("gpsimd", lambda e: e.memset(v_aug[:, :, 136:137], 1.0), writes=[b_vaug])
        for dst, wsl, bw, bdst in ((q_fm, wq_sl, b_wqsl, b_q), (k_fm, wk_sl, b_wksl, b_k)):
            for t4 in range(4):
                for ft in range(ND):
                    s.emit("tensor", lambda e, dst=dst, wsl=wsl, t4=t4, ft=ft: e.matmul(
                        ps_a[:, 0:512], wsl[:, ft, :],
                        xn_bf[:, ft, t4 * 512:(t4 + 1) * 512],
                        start=(ft == 0), stop=(ft == ND - 1)),
                        reads=[bw, b_xn], writes=[b_psa])
                s.emit("scalar", lambda e, dst=dst, t4=t4: e.copy(
                    dst[:, t4 * 512:(t4 + 1) * 512], ps_a[:, 0:512]),
                    writes=[b_psa, bdst])
        for tt in range(NT):
            for ft in range(ND):
                s.emit("tensor", lambda e, tt=tt, ft=ft: e.matmul(
                    ps_b[:, 0:128], xn_bf[:, ft, tt * 128:(tt + 1) * 128],
                    wv_sl[:, ft, :], start=(ft == 0), stop=(ft == ND - 1)),
                    reads=[b_wvsl, b_xn], writes=[b_psb])
            s.emit("vector", lambda e, tt=tt: e.tensor_copy(
                bass.AP(v_aug, tt * 144, [[v_aug.shape[2] * NT, 128], [72, 2], [1, 64]]),
                ps_b[:, 0:128]),
                writes=[b_psb, b_vaug])
            s.emit("tensor", lambda e, tt=tt: e.matmul(
                ps_S[:, 0:128], w1o[:, tt * 128:(tt + 1) * 128], ww2_sl[:, :],
                start=True, stop=True),
                reads=[b_w1o, b_ww2sl], writes=[b_ps4[0]])
            for half in range(2):
                s.emit("scalar", lambda e, tt=tt, half=half: e.activation(
                    x_slab[:, 0, half * 64:(half + 1) * 64],
                    ps_S[:, half * 64:(half + 1) * 64], AF.Square,
                    accum_out=st2[:, 8 + half:9 + half]),
                    writes=[b_ps4[0], b_st2, b_xslab[0]])
            s.emit("scalar", lambda e: e.activation(
                st2[:, 10:12], st2[:, 8:10], AF.Sqrt, bias=fc(1e-6)),
                reads=[cst_buf["fconst"]], writes=[b_st2])
            s.emit("vector", lambda e: e.reciprocal(st2[:, 12:14], st2[:, 10:12]),
                   writes=[b_st2])
            s.emit("vector", lambda e, tt=tt, hp=hp: e.scalar_tensor_tensor(
                st2[:, 14:16], sig_tm[:, tt, 2 * hp:2 * hp + 2], 2.0,
                st2[:, 12:14], ALU.mult, ALU.mult),
                reads=[b_sig], writes=[b_st2])
            for half in range(2):
                s.emit("vector", lambda e, tt=tt, half=half: e.tensor_scalar(
                    w_tm[:, tt, half * 64:(half + 1) * 64],
                    ps_S[:, half * 64:(half + 1) * 64],
                    st2[:, 12 + half:13 + half], None, ALU.mult),
                    reads=[b_st2], writes=[b_ps4[0], b_wtm])
                s.emit("vector", lambda e, tt=tt, half=half: e.tensor_scalar(
                    wb_tm[:, tt, half * 64:(half + 1) * 64],
                    ps_S[:, half * 64:(half + 1) * 64],
                    st2[:, 14 + half:15 + half], None, ALU.mult),
                    reads=[b_st2], writes=[b_ps4[0], b_wbtm])
            s.emit("tensor", lambda e, tt=tt: e.transpose(
                ps_c[:, 512:640], w_tm[:, tt, :], cst["id128"][:, :]),
                reads=[b_wtm, cst_buf["id128"]], writes=[b_psc])
            s.emit("scalar", lambda e, tt=tt: e.copy(
                w_fm[:, tt * 128:(tt + 1) * 128], ps_c[:, 512:640]),
                writes=[b_psc, b_wfm])
            s.emit("tensor", lambda e, tt=tt: e.transpose(
                ps_c[:, 640:768], wb_tm[:, tt, :], cst["id128"][:, :]),
                reads=[b_wbtm, cst_buf["id128"]], writes=[b_psc])
            s.emit("scalar", lambda e, tt=tt: e.copy(
                wb_fm[:, tt * 128:(tt + 1) * 128], ps_c[:, 640:768]),
                writes=[b_psc, b_wbfm])

    def emit_head_prep(h, ph):
        bk = 2 + ph
        s.emit("tensor", lambda e, h=h, bk=bk: e.matmul(
            ps_S[:, bk * 512 + 256:bk * 512 + 257], cst["ones_row_f32"][:, :],
            fc_tm[0:1, 8, h:h + 1], start=True, stop=True),
            reads=[b_fc, cst_buf["ones_row_f32"]], writes=[b_ps4[bk]])
        s.emit("vector", lambda e, ph=ph, bk=bk: e.tensor_copy(
            st3[:, ph, 0:1], ps_S[:, bk * 512 + 256:bk * 512 + 257]),
            writes=[b_ps4[bk], b_st3[ph]])
        s.emit("vector", lambda e, ph=ph: e.tensor_scalar(
            st3[:, ph, 1:2], st3[:, ph, 0:1], -1.0, None, ALU.mult),
            writes=[b_st3[ph]])
        s.emit("scalar", lambda e, h=h, ph=ph: e.activation(
            fcol[:, ph, :], bass.AP(fc_tm, h, [[NT * H, 128], [H, NT], [1, 1]]),
            AF.Exp, bias=st3[:, ph, 0:1], scale=fc(-1.0)),
            reads=[b_fc, b_st3[ph], cst_buf["fconst"]], writes=[b_fcol[ph]])
        s.emit("vector", lambda e, h=h, ph=ph: e.tensor_scalar(
            fcmr[:, ph, :], bass.AP(fc_tm, h, [[NT * H, 128], [H, NT], [1, 1]]),
            st3[:, ph, 1:2], None, ALU.add),
            reads=[b_fc, b_st3[ph]], writes=[b_fcmr[ph]])

    def emit_phaseA(h, ph, i):
        psA = ps_a if ph == 0 else ps_b
        bA = b_psa if ph == 0 else b_psb
        r0 = ph * 64
        isl = slice(i * 128, (i + 1) * 128)
        W_I = w_fm[r0:r0 + 64, isl]
        Wb_I = wb_fm[r0:r0 + 64, isl]
        Q_I = q_fm[r0:r0 + 64, isl]
        K_I = k_fm[r0:r0 + 64, isl]
        Wtm_I = w_tm[:, i, r0:r0 + 64]
        Wbtm_I = wb_tm[:, i, r0:r0 + 64]
        s.emit("tensor", lambda e: e.matmul(psA[:, 0:128], W_I, Wb_I,
               start=True, stop=True, tile_position=(r0, 0)),
               reads=[b_wfm, b_wbfm], writes=[bA])
        s.emit("vector", lambda e: e.tensor_tensor(
            Lb[:, ph, :], psA[:, 0:128], cst["m_trils"][:, :], ALU.mult),
            reads=[cst_buf["m_trils"]], writes=[bA, b_L[ph]])
        s.emit("tensor", lambda e: e.matmul(psA[:, 128:256], Wb_I, W_I,
               start=True, stop=True, tile_position=(r0, 0)),
               reads=[b_wfm, b_wbfm], writes=[bA])
        s.emit("vector", lambda e: e.tensor_tensor(
            Nb[:, ph, :], psA[:, 128:256], cst["m_trius"][:, :], ALU.mult),
            reads=[cst_buf["m_trius"]], writes=[bA, b_N[ph]])
        s.emit("tensor", lambda e: e.matmul(psA[:, 256:384], W_I, K_I,
               start=True, stop=True, tile_position=(r0, 0)),
               reads=[b_wfm, b_k], writes=[bA])
        s.emit("vector", lambda e: e.tensor_tensor(
            Mb[:, ph, :], psA[:, 256:384], cst["m_trils"][:, :], ALU.mult),
            reads=[cst_buf["m_trils"]], writes=[bA, b_M[ph]])
        s.emit("tensor", lambda e: e.matmul(psA[:, 384:512], Wb_I, Q_I,
               start=True, stop=True, tile_position=(r0, 0)),
               reads=[b_wbfm, b_q], writes=[bA])
        s.emit("vector", lambda e: e.tensor_tensor(
            nGt[:, ph, :], psA[:, 384:512], cst["m_negtriu"][:, :], ALU.mult),
            reads=[cst_buf["m_negtriu"]], writes=[bA, b_nGt[ph]])

        def lp(lev):
            return Lb[:, ph, :] if lev == 0 else lkb[:, ph, :]

        def np_(lev):
            return Nb[:, ph, :] if lev == 0 else npow[:, ph, lev - 1, :]

        for lev in range(6):
            bl = b_L[ph] if lev == 0 else b_lk[ph]
            bn = b_N[ph] if lev == 0 else b_np[ph]
            s.emit("tensor", lambda e, lev=lev: e.matmul(
                psA[:, 128:256], lp(lev), np_(lev), start=True, stop=True),
                reads=[bl, bn], writes=[bA])
            s.emit("scalar", lambda e, lev=lev: e.copy(
                npow[:, ph, lev, :], psA[:, 128:256]),
                writes=[bA, b_np[ph]])
            if lev < 5:
                s.emit("tensor", lambda e, lev=lev: e.matmul(
                    psA[:, 0:128], np_(lev), lp(lev), start=True, stop=True),
                    reads=[bl, bn], writes=[bA])
                s.emit("vector", lambda e: e.tensor_copy(
                    lkb[:, ph, :], psA[:, 0:128]),
                    writes=[bA, b_lk[ph]])
        zc = 0
        s.emit("tensor", lambda e: e.matmul(psA[:, 0:64], npow[:, ph, 5, :], Wtm_I,
               start=True, stop=False), reads=[b_np[ph], b_wtm], writes=[bA])
        s.emit("tensor", lambda e: e.matmul(psA[:, 64:192], npow[:, ph, 5, :], Mb[:, ph, :],
               start=False, stop=True), reads=[b_np[ph], b_M[ph]], writes=[bA])
        s.emit("vector", lambda e: e.tensor_tensor(
            zb[:, ph, 0, 0:64], Wtm_I, psA[:, 0:64], ALU.add),
            reads=[b_wtm], writes=[bA, b_z[ph][0]])
        s.emit("vector", lambda e: e.tensor_tensor(
            zb[:, ph, 0, 64:192], Mb[:, ph, :], psA[:, 64:192], ALU.add),
            reads=[b_M[ph]], writes=[bA, b_z[ph][0]])
        for lev in (4, 3, 2, 1, 0):
            zn = 1 - zc
            s.emit("tensor", lambda e, lev=lev, zc=zc: e.matmul(
                psA[:, 0:192], npow[:, ph, lev, :], zb[:, ph, zc, :],
                start=True, stop=True), reads=[b_np[ph], b_z[ph][zc]], writes=[bA])
            s.emit("vector", lambda e, zc=zc, zn=zn: e.tensor_tensor(
                zb[:, ph, zn, :], zb[:, ph, zc, :], psA[:, 0:192], ALU.add),
                reads=[b_z[ph][zc]], writes=[bA, b_z[ph][zn]])
            zc = zn
        zn = 1 - zc
        s.emit("tensor", lambda e, zc=zc: e.matmul(
            psA[:, 0:192], Nb[:, ph, :], zb[:, ph, zc, :],
            start=True, stop=True), reads=[b_N[ph], b_z[ph][zc]], writes=[bA])
        s.emit("vector", lambda e, zc=zc, zn=zn: e.tensor_tensor(
            zb[:, ph, zn, :], zb[:, ph, zc, :], psA[:, 0:192], ALU.subtract),
            reads=[b_z[ph][zc]], writes=[bA, b_z[ph][zn]])
        zf = zn
        TW = zb[:, ph, zf, 0:64]
        CI2 = zb[:, ph, zf, 64:192]
        bZ = b_z[ph][zf]
        s.emit("tensor", lambda e: e.matmul(psA[:, 256:384], Q_I, K_I,
               start=True, stop=False, tile_position=(r0, 0)),
               reads=[b_q, b_k], writes=[bA])
        s.emit("tensor", lambda e: e.matmul(psA[:, 256:384], nGt[:, ph, :], CI2,
               start=False, stop=True), reads=[b_nGt[ph], bZ], writes=[bA])
        s.emit("scalar", lambda e, i=i, ph=ph: e.copy(
            sii_k[:, ph, i % 2, :], psA[:, 256:384]),
            writes=[bA, b_sii[ph]])
        s.emit("tensor", lambda e: e.matmul(
            psA[r0:r0 + 64, 0:128], TW, nGt[:, ph, :],
            start=True, stop=True, tile_position=(0, r0)),
            reads=[bZ, b_nGt[ph]], writes=[bA])
        s.emit("vector", lambda e, i=i: e.tensor_tensor(
            qhat_k[r0:r0 + 64, i % 2, :], Q_I, psA[r0:r0 + 64, 0:128], ALU.add),
            reads=[b_q], writes=[bA, b_qh[ph]])
        s.emit("tensor", lambda e: e.matmul(
            psA[r0:r0 + 64, 128:192], TW, Wbtm_I,
            start=True, stop=True, tile_position=(0, r0)),
            reads=[bZ, b_wbtm], writes=[bA])
        s.emit("vector", lambda e, i=i: e.tensor_tensor(
            pt_k[r0:r0 + 64, i % 2, :], cst["i64stack"][r0:r0 + 64, :],
            psA[r0:r0 + 64, 128:192], ALU.subtract),
            reads=[cst_buf["i64stack"]], writes=[bA, b_pt[ph]])
        s.emit("tensor", lambda e: e.matmul(
            psA[r0:r0 + 64, 256:384], Wbtm_I, CI2,
            start=True, stop=True, tile_position=(0, r0)),
            reads=[b_wbtm, bZ], writes=[bA])
        s.emit("vector", lambda e, i=i: e.tensor_tensor(
            ktd_k[r0:r0 + 64, i % 2, :], K_I, psA[r0:r0 + 64, 256:384], ALU.subtract),
            reads=[b_k], writes=[bA, b_ktd[ph]])

    def emit_phaseB_chunk(h, ph, i):
        r0 = ph * 64
        bk = 2 + ph
        sk = ph
        psc = ps_c if ph == 0 else ps_d
        bpsc = b_psc if ph == 0 else b_psd
        rbuf = (i + 1) % 2
        wbuf = i % 2
        jend = (i + 1) * 128
        nj5 = (i * 128 + 511) // 512
        for j5 in range(nj5):
            w = min(512, i * 128 - j5 * 512)
            s.emit("tensor", lambda e, i=i, j5=j5, w=w, rbuf=rbuf: e.matmul(
                ps_S[:, sk * 512:sk * 512 + w],
                qhat_k[r0:r0 + 64, i % 2, :], kt[r0:r0 + 64, rbuf, j5 * 512:j5 * 512 + w],
                start=True, stop=True, tile_position=(r0, 0)),
                reads=[b_qh[ph], b_kt[ph][rbuf]], writes=[b_ps4[sk]])
            s.emit("vector", lambda e, j5=j5, w=w, ph=ph: e.tensor_copy(
                s_sb[:, ph, j5 * 512:j5 * 512 + w], ps_S[:, sk * 512:sk * 512 + w]),
                writes=[b_ps4[sk], b_s[ph][j5]])
        s.emit("vector", lambda e, i=i, ph=ph: e.tensor_tensor(
            s_sb[:, ph, i * 128:(i + 1) * 128], sii_k[:, ph, i % 2, :],
            cst["m_addneg"][:, :], ALU.add),
            reads=[b_sii[ph], cst_buf["m_addneg"]], writes=[b_s[ph][i // 4]])
        s.emit("vector", lambda e, jend=jend, ph=ph: e.reduce_max(
            st3[:, ph, 2:3], s_sb[:, ph, 0:jend], axis=AX.X),
            reads=b_s[ph][:i // 4 + 1], writes=[b_st3[ph]])
        s.emit("vector", lambda e, i=i, ph=ph: e.tensor_scalar(
            st3[:, ph, 3:4], st3[:, ph, 2:3], -SCALE, fcmr[:, ph, i:i + 1],
            ALU.mult, ALU.add),
            reads=[b_fcmr[ph]], writes=[b_st3[ph]])
        for j5 in range(i // 4 + 1):
            w2_ = min(512, jend - j5 * 512)
            s.emit("scalar", lambda e, j5=j5, w2_=w2_, ph=ph: e.activation(
                s_sb[:, ph, j5 * 512:j5 * 512 + w2_],
                s_sb[:, ph, j5 * 512:j5 * 512 + w2_], AF.Exp,
                bias=st3[:, ph, 3:4], scale=fc(SCALE)),
                reads=[b_st3[ph], cst_buf["fconst"]], writes=[b_s[ph][j5]])
        for jt in range(i + 1):
            slot = jt % 7
            s.emit("tensor", lambda e, jt=jt, slot=slot, ph=ph, psc=psc: e.transpose(
                psc[:, slot * 128:(slot + 1) * 128],
                s_sb[:, ph, jt * 128:(jt + 1) * 128], cst["id128"][:, :]),
                reads=[b_s[ph][jt // 4], cst_buf["id128"]], writes=[bpsc])
            s.emit("vector", lambda e, jt=jt, slot=slot, ph=ph, psc=psc: e.tensor_scalar(
                pT_sb[:, ph, jt * 128:(jt + 1) * 128],
                psc[:, slot * 128:(slot + 1) * 128],
                fcol[:, ph, jt:jt + 1], None, ALU.mult),
                reads=[b_fcol[ph]], writes=[bpsc, b_pT[ph]])
            s.emit("tensor", lambda e, jt=jt, i=i, ph=ph, bk=bk: e.matmul(
                ps_S[:, bk * 512:bk * 512 + 72], pT_sb[:, ph, jt * 128:(jt + 1) * 128],
                v_aug[:, jt, ph * 72:ph * 72 + 72],
                start=(jt == 0), stop=(jt == i)),
                reads=[b_pT[ph], b_vaug], writes=[b_ps4[bk]])
        s.emit("vector", lambda e, ph=ph, bk=bk: e.reciprocal(
            st3[:, ph, 4:5], ps_S[:, bk * 512 + 64:bk * 512 + 65]),
            writes=[b_ps4[bk], b_st3[ph]])
        s.emit("vector", lambda e, ph=ph, bk=bk: e.tensor_scalar(
            o_tmp[:, ph, :], ps_S[:, bk * 512:bk * 512 + 64], st3[:, ph, 4:5],
            None, ALU.mult),
            reads=[b_st3[ph]], writes=[b_ps4[bk], b_otmp[ph]])
        s.emit("tensor", lambda e, ph=ph, psc=psc: e.transpose(
            psc[r0:r0 + 64, 896:1024], o_tmp[:, ph, :], cst["id128"][:, :],
            tile_position=(0, r0)),
            reads=[b_otmp[ph], cst_buf["id128"]], writes=[bpsc])
        s.emit("scalar", lambda e, i=i, h=h, ph=ph, psc=psc: e.copy(
            o_fm[r0:r0 + 64, h // 2, i * 128:(i + 1) * 128],
            psc[r0:r0 + 64, 896:1024]),
            writes=[bpsc, b_ofm])
        for j5 in range(nj5):
            w = min(512, i * 128 - j5 * 512)
            s.emit("tensor", lambda e, i=i, j5=j5, w=w, rbuf=rbuf, bk=bk: e.matmul(
                ps_S[r0:r0 + 64, bk * 512:bk * 512 + w], pt_k[r0:r0 + 64, i % 2, :],
                kt[r0:r0 + 64, rbuf, j5 * 512:j5 * 512 + w],
                start=True, stop=True, tile_position=(r0, r0)),
                reads=[b_pt[ph], b_kt[ph][rbuf]], writes=[b_ps4[bk]])
            s.emit("scalar", lambda e, j5=j5, w=w, wbuf=wbuf, bk=bk: e.copy(
                kt[r0:r0 + 64, wbuf, j5 * 512:j5 * 512 + w],
                ps_S[r0:r0 + 64, bk * 512:bk * 512 + w]),
                writes=[b_ps4[bk], b_kt[ph][wbuf]])
        s.emit("gpsimd", lambda e, i=i, wbuf=wbuf: e.tensor_copy(
            kt[r0:r0 + 64, wbuf, i * 128:(i + 1) * 128],
            ktd_k[r0:r0 + 64, i % 2, :]),
            reads=[b_ktd[ph]], writes=[b_kt[ph][wbuf]])

    def defer(fn):
        rec = []
        real = s.emit
        s.emit = lambda *a, **k: rec.append((a, k))
        fn()
        s.emit = real
        return rec

    def play(recs):
        idx = [0] * len(recs)
        while True:
            done = True
            for r in range(len(recs)):
                if idx[r] < len(recs[r]):
                    a, k = recs[r][idx[r]]
                    s.emit(*a, **k)
                    idx[r] += 1
                    done = False
            if done:
                break

    for hp in range(hp_count):
        emit_proj(hp)
        h0, h1 = 2 * hp, 2 * hp + 1
        emit_head_prep(h0, 0)
        emit_head_prep(h1, 1)
        play([defer(lambda: emit_phaseA(h0, 0, 0)),
              defer(lambda: emit_phaseA(h1, 1, 0))])
        for i in range(NCH):
            recs = [defer(lambda i=i: emit_phaseB_chunk(h0, 0, i)),
                    defer(lambda i=i: emit_phaseB_chunk(h1, 1, i))]
            if i + 1 < NCH:
                recs.append(defer(lambda i=i: emit_phaseA(h0, 0, i + 1)))
                recs.append(defer(lambda i=i: emit_phaseA(h1, 1, i + 1)))
            play(recs)

    dbg_dump("o_fm", lambda: o_fm[:, :, :], [128, ND, N], BF16, [b_ofm.w])

    attn_bufs = ([b_q, b_k, b_wfm, b_wbfm, b_wtm, b_wbtm, b_vaug]
                 + b_kt[0] + b_kt[1] + b_qh + b_sii + b_pt + b_ktd
                 + b_L + b_N + b_M + b_nGt + b_np + b_lk
                 + b_z[0] + b_z[1] + b_s[0] + b_s[1] + b_pT + b_otmp
                 + b_fcol + b_fcmr + b_st3
                 + [b_wqsl, b_wksl, b_wvsl, b_ww2sl])
    barrier = []
    for b in attn_bufs:
        if b.w is not None:
            barrier.append(b.w)
        barrier.extend(b.readers)
    barrier = list({id(t): t for t in barrier}.values())
    for g in reversed(_attn_guards):
        g.__exit__(None, None, None)

    w1_sl = [nc.alloc_sbuf_tensor(f"w1_sl{i}", [128, ND, D], BF16) for i in range(2)]
    w2_sl = nc.alloc_sbuf_tensor("w2_sl", [128, ND, D], BF16)
    g_slab = nc.alloc_sbuf_tensor("g_slab", [128, ND, 512], BF16)
    b1_sb = nc.alloc_sbuf_tensor("b1_sb", [1, D], BF16)
    b2_sb = nc.alloc_sbuf_tensor("b2_sb", [1, D], BF16)
    b_w1 = [Buf(), Buf()]
    b_w2 = Buf()
    b_g = Buf()
    b_b1 = Buf()
    b_b2 = Buf()

    def emit_ffn_expert(e_idx):
        sl = e_idx % 2
        if e_idx == 0:
            s.emit("gpsimd", lambda e: e.dma_start(
                out=w1_sl[0][:, :, :],
                in_=w1_d[0].rearrange("(t p) i -> p t i", p=128)),
                writes=[b_w1[0]], deps=barrier, inc_k=16)
        if e_idx + 1 < NE:
            s.emit("gpsimd", lambda e, e_idx=e_idx: e.dma_start(
                out=w1_sl[(e_idx + 1) % 2][:, :, :],
                in_=w1_d[e_idx + 1].rearrange("(t p) i -> p t i", p=128)),
                writes=[b_w1[(e_idx + 1) % 2]],
                deps=(barrier if e_idx == 0 else ()), inc_k=16)
        s.emit("gpsimd", lambda e, e_idx=e_idx: e.dma_start(
            out=w2_sl[:, :, :], in_=w2_d[e_idx].rearrange("(t p) i -> p t i", p=128)),
            writes=[b_w2], deps=(barrier if e_idx == 0 else ()), inc_k=16)
        s.emit("gpsimd", lambda e, e_idx=e_idx: e.dma_start(
            out=b1_sb[:, :], in_=b1_d[e_idx:e_idx + 1, :]),
            writes=[b_b1], deps=(barrier if e_idx == 0 else ()), inc_k=16)
        s.emit("gpsimd", lambda e, e_idx=e_idx: e.dma_start(
            out=b2_sb[:, :], in_=b2_d[e_idx:e_idx + 1, :]),
            writes=[b_b2], deps=(barrier if e_idx == 0 else ()), inc_k=16)
        for t4 in range(4):
            for it in range(ND):
                psh = ps_a if it % 2 == 0 else ps_b
                bph = b_psa if it % 2 == 0 else b_psb
                for ft in range(ND):
                    s.emit("tensor", lambda e, sl=sl, t4=t4, it=it, ft=ft, psh=psh: e.matmul(
                        psh[:, 0:512], w1_sl[sl][:, ft, it * 128:(it + 1) * 128],
                        xn_bf[:, ft, t4 * 512:(t4 + 1) * 512],
                        start=(ft == 0), stop=False),
                        reads=[b_w1[sl], b_xn], writes=[bph])
                s.emit("tensor", lambda e, it=it, psh=psh: e.matmul(
                    psh[:, 0:512], b1_sb[0:1, it * 128:(it + 1) * 128],
                    cst["ones_row_bf"][0:1, 0:512],
                    start=False, stop=True),
                    reads=[b_b1, cst_buf["ones_row_bf"]], writes=[bph])
                s.emit("scalar", lambda e, it=it, psh=psh: e.activation(
                    g_slab[:, it, :], psh[:, 0:512], AF.Gelu),
                    writes=[bph, b_g])
            for tsub in range(4):
                tt = t4 * 4 + tsub
                for dh in range(2):
                    pb = tsub % 2 + 2 * dh
                    for it in range(ND):
                        s.emit("tensor", lambda e, tsub=tsub, dh=dh, it=it, pb=pb: e.matmul(
                            ps_S[:, pb * 512:pb * 512 + 512],
                            g_slab[:, it, tsub * 128:(tsub + 1) * 128],
                            w2_sl[:, it, dh * 512:(dh + 1) * 512],
                            start=(it == 0), stop=False),
                            reads=[b_g, b_w2], writes=[b_ps4[pb]])
                    s.emit("tensor", lambda e, dh=dh, pb=pb: e.matmul(
                        ps_S[:, pb * 512:pb * 512 + 512],
                        cst["ones_row_bf"][0:1, 0:128],
                        b2_sb[0:1, dh * 512:(dh + 1) * 512],
                        start=False, stop=True),
                        reads=[b_b2, cst_buf["ones_row_bf"]], writes=[b_ps4[pb]])
                    s.emit("vector", lambda e, tt=tt, dh=dh, e_idx=e_idx, pb=pb: e.scalar_tensor_tensor(
                        y_acc[:, tt, dh * 512:(dh + 1) * 512],
                        ps_S[:, pb * 512:pb * 512 + 512],
                        gmask[:, tt, e_idx:e_idx + 1],
                        y_acc[:, tt, dh * 512:(dh + 1) * 512],
                        ALU.mult, ALU.add),
                        reads=[b_gmask, b_yacc_all], writes=[b_ps4[pb], b_yacc[tt]])

    for e_idx in range(NE):
        emit_ffn_expert(e_idx)

    b_out = Buf()
    for tt in range(NT):
        sl = tt % 2
        s.emit("sync", lambda e, tt=tt, sl=sl: e.dma_start(
            out=x_slab[:, sl, :], in_=x_d.rearrange("(t p) d -> t p d", p=128)[tt]),
            writes=[b_xslab[sl]], inc_k=16)
        for dh in range(2):
            psx = ps_a if dh == 0 else ps_b
            bpsx = b_psa if dh == 0 else b_psb
            for ft in range(ND):
                s.emit("tensor", lambda e, tt=tt, dh=dh, ft=ft, psx=psx: e.matmul(
                    psx[:, 0:512], o_fm[:, ft, tt * 128:(tt + 1) * 128],
                    wo_sb[:, ft, dh * 512:(dh + 1) * 512],
                    start=(ft == 0), stop=(ft == ND - 1)),
                    reads=[b_ofm, b_wo], writes=[bpsx])
            s.emit("vector", lambda e, tt=tt, dh=dh, sl=sl, psx=psx: e.tensor_tensor(
                x_slab[:, sl, dh * 512:(dh + 1) * 512], psx[:, 0:512],
                x_slab[:, sl, dh * 512:(dh + 1) * 512], ALU.add),
                writes=[bpsx, b_xslab[sl]])
            s.emit("vector", lambda e, tt=tt, dh=dh, sl=sl: e.tensor_tensor(
                x_slab[:, sl, dh * 512:(dh + 1) * 512],
                x_slab[:, sl, dh * 512:(dh + 1) * 512],
                y_acc[:, tt, dh * 512:(dh + 1) * 512], ALU.add),
                reads=[b_yacc[tt], b_yacc_all], writes=[b_xslab[sl]])
        s.emit("sync", lambda e, tt=tt, sl=sl: e.dma_start(
            out=out_d.rearrange("(t p) d -> t p d", p=128)[tt], in_=x_slab[:, sl, :]),
            reads=[b_xslab[sl]], writes=[b_out], inc_k=16)
    s.emit("sync", lambda e: e.nop(), deps=[b_out.w])

    s.finalize(nc, paranoid=paranoid)
    return nc, dbg_d


def stage_inputs(inputs):
    consts = host_constants()
    x = np.ascontiguousarray(inputs["L0_attn_x"])
    base = {
        "wq": inputs["L0_attn_Wq"], "wk": inputs["L0_attn_Wk"],
        "wv": inputs["L0_attn_Wv"], "wo": inputs["L0_attn_Wo"],
        "ww1": inputs["L0_attn_Ww1"], "ww2": inputs["L0_attn_Ww2"],
        "wbm": inputs["L0_attn_Wb"], "wfm": inputs["L0_attn_Wf"],
        "delta_bc": np.ascontiguousarray(np.broadcast_to(
            inputs["L0_attn_delta"][None, :], (128, H))),
        "wg": inputs["L0_ffn_Wg"],
        "bg_bc": np.ascontiguousarray(np.broadcast_to(
            inputs["L0_ffn_bg"][None, :], (128, NE))),
        "w1": inputs["L0_ffn_W1"], "b1": inputs["L0_ffn_b1"],
        "w2": inputs["L0_ffn_W2"], "b2": inputs["L0_ffn_b2"],
    }
    for k, v in consts.items():
        base[f"c_{k}"] = v
    in_maps = []
    for c in range(8):
        m = dict(base)
        m["x"] = np.ascontiguousarray(x[c])
        in_maps.append(m)
    return in_maps


def kernel(**inputs):
    nc, _ = build()
    in_maps = stage_inputs(inputs)
    res = run_bass_kernel_spmd(nc, in_maps, core_ids=list(range(8)))
    return np.stack([res.results[c]["out"] for c in range(8)], axis=0)


# version 12
# speedup vs baseline: 1.3526x; 1.0020x over previous
import math
import numpy as np
import ml_dtypes
import concourse.bass as bass
import concourse.mybir as mybir
from concourse.bass_utils import run_bass_kernel_spmd

BFNP = ml_dtypes.bfloat16
F32 = mybir.dt.float32
BF16 = mybir.dt.bfloat16
AF = mybir.ActivationFunctionType
ALU = mybir.AluOpType
AX = mybir.AxisListType

N, D, H, HD, C = 2048, 1024, 16, 64, 128
NCH = N // C
NT = N // 128
ND = D // 128
NE = 8
SCALE = HD ** -0.5

ENGINES = ("tensor", "vector", "scalar", "gpsimd", "sync")

FCONST_VALS = (1e-5, 1e-6, -1.0, 2.0, 0.5, -0.5, SCALE, -SCALE)
FCONST_IDX = {v: i for i, v in enumerate(FCONST_VALS)}


class Op:
    __slots__ = ("eng", "fn", "deps", "inc_k", "sem", "sem_after", "pos",
                 "needs_inc", "pre_wait")

    def __init__(self, eng, fn, deps, inc_k):
        self.eng = eng
        self.fn = fn
        self.deps = deps
        self.inc_k = inc_k
        self.sem = None
        self.sem_after = None
        self.pos = None
        self.needs_inc = False
        self.pre_wait = None


class Buf:

    def __init__(self):
        self.w = None
        self.readers = []


class Sched:
    def __init__(self):
        self.ops = []

    def emit(self, eng, fn, reads=(), writes=(), deps=(), inc_k=1):
        d = [t for t in deps if t is not None]
        for b in reads:
            if b.w is not None:
                d.append(b.w)
        for b in writes:
            if b.w is not None:
                d.append(b.w)
            d.extend(b.readers)
        op = Op(eng, fn, d, inc_k)
        op.pos = len(self.ops)
        self.ops.append(op)
        for b in reads:
            b.readers.append(op)
        for b in writes:
            b.w = op
            b.readers = []
        return op

    def finalize(self, nc, paranoid=False):
        for op in self.ops:
            if op.inc_k == 16:
                op.needs_inc = True
            for dep in op.deps:
                assert dep.pos < op.pos
                if dep.eng != op.eng or dep.inc_k == 16 or paranoid:
                    dep.needs_inc = True
        eng_sems = {e: nc.alloc_semaphore(f"sem_{e}") for e in ENGINES}
        counts = {e: 0 for e in ENGINES}
        NPOOL = 8
        pool = {e: [nc.alloc_semaphore(f"dma_{e}_{i}") for i in range(NPOOL)]
                for e in ("sync", "gpsimd", "scalar")}
        pool_idx = {e: 0 for e in pool}
        pool_count = {}
        for op in self.ops:
            if op.inc_k == 16:
                sem = pool[op.eng][pool_idx[op.eng] % NPOOL]
                pool_idx[op.eng] += 1
                prev = pool_count.get(id(sem), 0)
                if prev:
                    op.pre_wait = (sem, prev)
                op.sem = sem
                op.sem_after = prev + 16
                pool_count[id(sem)] = op.sem_after
            elif op.needs_inc:
                counts[op.eng] += op.inc_k
                op.sem = eng_sems[op.eng]
                op.sem_after = counts[op.eng]
        marks = {e: {} for e in ENGINES}
        streams = {e: [o for o in self.ops if o.eng == e] for e in ENGINES}
        need_drain = set()
        vstream = streams["vector"]
        vpos = {id(o): i for i, o in enumerate(vstream)}
        for i, op in enumerate(vstream):
            for dep in op.deps:
                if dep.eng == "vector" and vpos[id(dep)] == i - 1:
                    need_drain.add(id(op))

        def body(eng_obj, ename):
            wm = marks[ename]
            for op in streams[ename]:
                for dep in op.deps:
                    if dep.eng == ename and dep.inc_k != 16 and not paranoid:
                        continue
                    assert dep.sem_after is not None, (dep.eng, op.eng)
                    if dep.sem_after > wm.get(id(dep.sem), 0):
                        eng_obj.wait_ge(dep.sem, dep.sem_after)
                        wm[id(dep.sem)] = dep.sem_after
                if op.pre_wait is not None:
                    sem, val = op.pre_wait
                    if val > wm.get(id(sem), 0):
                        eng_obj.wait_ge(sem, val)
                        wm[id(sem)] = val
                if id(op) in need_drain:
                    eng_obj.drain()
                instr = op.fn(eng_obj)
                if op.sem is not None:
                    instr.then_inc(op.sem, op.inc_k)

        with nc.Block() as blk:
            @blk.tensor
            def _(e):
                body(e, "tensor")

            @blk.vector
            def _(e):
                body(e, "vector")

            @blk.scalar
            def _(e):
                body(e, "scalar")

            @blk.gpsimd
            def _(e):
                body(e, "gpsimd")

            @blk.sync
            def _(e):
                body(e, "sync")


def host_constants():
    c = {}
    ident = np.eye(128, dtype=np.float32)
    c["id128"] = ident.astype(BFNP)
    c["id128f"] = ident
    c["cumtri"] = np.triu(np.ones((128, 128), np.float32)).astype(BFNP)
    c["allones"] = np.ones((128, 128), np.float32).astype(BFNP)
    c["m_trils"] = np.tril(np.ones((128, 128), np.float32), -1).astype(BFNP)
    c["m_trius"] = np.triu(np.ones((128, 128), np.float32), 1).astype(BFNP)
    c["m_negtriu"] = (-np.triu(np.ones((128, 128), np.float32))).astype(BFNP)
    c["m_addneg"] = np.triu(np.full((128, 128), -1e9, np.float32), 1).astype(BFNP)
    c["i64stack"] = np.concatenate([np.eye(64, dtype=np.float32)] * 2, 0)
    c["ones_row_bf"] = np.ones((1, 512), np.float32).astype(BFNP)
    c["ones_row_f32"] = np.ones((1, 128), np.float32)
    c["fconst"] = np.ascontiguousarray(np.broadcast_to(
        np.array(FCONST_VALS, np.float32)[None, :], (128, len(FCONST_VALS))))
    return c


def build(dbg=None, paranoid=False, parts=("ln", "gate", "thin", "fc"), hp_count=ND):
    dbg = dbg or []
    parts = set(parts)
    nc = bass.Bass("TRN2", target_bir_lowering=False,
                   detect_race_conditions=paranoid)
    s = Sched()

    x_d = nc.dram_tensor("x", [N, D], F32, kind="ExternalInput").ap()
    wq_d = nc.dram_tensor("wq", [D, D], F32, kind="ExternalInput").ap()
    wk_d = nc.dram_tensor("wk", [D, D], F32, kind="ExternalInput").ap()
    wv_d = nc.dram_tensor("wv", [D, D], F32, kind="ExternalInput").ap()
    wo_d = nc.dram_tensor("wo", [D, D], F32, kind="ExternalInput").ap()
    ww1_d = nc.dram_tensor("ww1", [D, HD], F32, kind="ExternalInput").ap()
    ww2_d = nc.dram_tensor("ww2", [HD, D], F32, kind="ExternalInput").ap()
    wb_d = nc.dram_tensor("wbm", [D, H], F32, kind="ExternalInput").ap()
    wf_d = nc.dram_tensor("wfm", [D, H], F32, kind="ExternalInput").ap()
    delta_bc_d = nc.dram_tensor("delta_bc", [128, H], F32, kind="ExternalInput").ap()
    wg_d = nc.dram_tensor("wg", [D, NE], F32, kind="ExternalInput").ap()
    bg_bc_d = nc.dram_tensor("bg_bc", [128, NE], F32, kind="ExternalInput").ap()
    w1_d = nc.dram_tensor("w1", [NE, D, D], F32, kind="ExternalInput").ap()
    b1_d = nc.dram_tensor("b1", [NE, D], F32, kind="ExternalInput").ap()
    w2_d = nc.dram_tensor("w2", [NE, D, D], F32, kind="ExternalInput").ap()
    b2_d = nc.dram_tensor("b2", [NE, D], F32, kind="ExternalInput").ap()
    cst_d = {}
    consts = host_constants()
    for k, v in consts.items():
        dt = BF16 if v.dtype == BFNP else F32
        cst_d[k] = nc.dram_tensor(f"c_{k}", list(v.shape), dt, kind="ExternalInput").ap()
    out_d = nc.dram_tensor("out", [N, D], F32, kind="ExternalOutput").ap()

    dbg_d = {}
    dbg_bufs = {}

    def dbg_dump(name, ap_fn, shape, dtype, deps):
        if name not in dbg:
            return
        dd = nc.dram_tensor(f"dbg_{name}", shape, dtype, kind="ExternalOutput").ap()
        dbg_d[name] = dd
        s.emit("sync", lambda e, a=ap_fn, d=dd: e.dma_start(out=d, in_=a()),
               deps=deps, inc_k=16)

    cst = {}
    cst_buf = {}
    for k, v in consts.items():
        dt = BF16 if v.dtype == BFNP else F32
        t = nc.alloc_sbuf_tensor(f"sb_{k}", list(v.shape), dt)
        cst[k] = t
        b = Buf()
        cst_buf[k] = b
        s.emit("sync", lambda e, t=t, k=k: e.dma_start(out=t[:, :], in_=cst_d[k][:, :]),
               writes=[b], inc_k=16)

    def CC(name):
        return cst[name], cst_buf[name]

    def fc(v):
        return cst["fconst"][:, FCONST_IDX[v]:FCONST_IDX[v] + 1]

    xn_bf = nc.alloc_sbuf_tensor("xn_bf", [128, ND, N], BF16)
    o_fm = nc.alloc_sbuf_tensor("o_fm", [128, ND, N], BF16)
    y_acc = nc.alloc_sbuf_tensor("y_acc", [128, NT, D], BF16)
    fc_tm = nc.alloc_sbuf_tensor("fc_tm", [128, NT, H], F32)
    sig_tm = nc.alloc_sbuf_tensor("sig_tm", [128, NT, H], F32)
    logf_bf = nc.alloc_sbuf_tensor("logf_bf", [128, NT, H], BF16)
    gmask = nc.alloc_sbuf_tensor("gmask", [128, NT, NE], F32)
    w1o = nc.alloc_sbuf_tensor("w1o", [64, N], BF16)
    wg_sb = nc.alloc_sbuf_tensor("wg_sb", [128, ND, NE], F32)
    wb_sb = nc.alloc_sbuf_tensor("wb_sb", [128, ND, H], BF16)
    wf_sb = nc.alloc_sbuf_tensor("wf_sb", [128, ND, H], BF16)
    ww1_sb = nc.alloc_sbuf_tensor("ww1_sb", [128, ND, HD], BF16)
    delta_bc = nc.alloc_sbuf_tensor("delta_sb", [128, H], F32)
    bg_bc = nc.alloc_sbuf_tensor("bg_sb", [128, NE], F32)

    b_xn = Buf()
    b_yacc_all = Buf()
    b_o = [[Buf() for _ in range(NCH)] for _ in range(H)]
    b_yacc = [Buf() for _ in range(NT)]
    b_fc = Buf()
    b_sig = Buf()
    b_logf = Buf()
    b_gmask = Buf()
    b_w1o = Buf()
    b_smallw = Buf()

    s.emit("gpsimd", lambda e: e.dma_start(
        out=wb_sb[:, :, :], in_=wb_d.rearrange("(t p) h -> p t h", p=128)),
        writes=[b_smallw], inc_k=16)
    s.emit("gpsimd", lambda e: e.dma_start(
        out=wf_sb[:, :, :], in_=wf_d.rearrange("(t p) h -> p t h", p=128)),
        writes=[b_smallw], inc_k=16)
    s.emit("gpsimd", lambda e: e.dma_start(
        out=ww1_sb[:, :, :], in_=ww1_d.rearrange("(t p) h -> p t h", p=128)),
        writes=[b_smallw], inc_k=16)
    s.emit("sync", lambda e: e.dma_start(
        out=wg_sb[:, :, :], in_=wg_d.rearrange("(t p) h -> p t h", p=128)),
        writes=[b_smallw], inc_k=16)
    s.emit("sync", lambda e: e.dma_start(out=delta_bc[:, :], in_=delta_bc_d[:, :]),
           writes=[b_smallw], inc_k=16)
    s.emit("sync", lambda e: e.dma_start(out=bg_bc[:, :], in_=bg_bc_d[:, :]),
           writes=[b_smallw], inc_k=16)

    ps_S = nc.alloc_psum_tensor("ps_S", [128, 2048], F32)
    ps_a = nc.alloc_psum_tensor("ps_a", [128, 512], F32)
    ps_b = nc.alloc_psum_tensor("ps_b", [128, 512], F32)
    ps_c = nc.alloc_psum_tensor("ps_c", [128, 1024], BF16)
    ps_d = nc.alloc_psum_tensor("ps_d", [128, 1024], BF16)
    b_ps4 = [Buf(), Buf(), Buf(), Buf()]
    b_psa = Buf()
    b_psb = Buf()
    b_psc = Buf()
    b_psd = Buf()

    x_slab = nc.alloc_sbuf_tensor("x_slab", [128, 2, D], F32)
    stat = nc.alloc_sbuf_tensor("stat", [128, 16], F32)
    glog = nc.alloc_sbuf_tensor("glog", [128, NT, NE], F32)
    x32_fm = nc.alloc_sbuf_tensor("x32_fm", [128, ND, 128], F32)
    b_xslab = [Buf(), Buf()]
    b_xnslab = [Buf(), Buf()]
    b_x32 = Buf()
    b_stat = Buf()
    b_glog = Buf()

    for tt in range(NT):
        if "ln" not in parts and "ln1" not in parts:
            break
        sl = tt % 2
        xs = x_slab
        ld = s.emit("sync", lambda e, tt=tt, sl=sl: e.dma_start(
            out=x_slab[:, sl, :], in_=x_d.rearrange("(t p) d -> t p d", p=128)[tt]),
            writes=[b_xslab[sl]], inc_k=16)
        s.emit("vector", lambda e, sl=sl: e.reduce_sum(
            stat[:, 0:1], x_slab[:, sl, :], axis=AX.X),
            reads=[b_xslab[sl]], writes=[b_stat])
        s.emit("scalar", lambda e, sl=sl: e.activation(
            y_acc[:, sl, :], x_slab[:, sl, :], AF.Square,
            accum_out=stat[:, 1:2]),
            reads=[b_xslab[sl]], writes=[b_stat, b_yacc_all])
        s.emit("vector", lambda e: e.tensor_scalar(
            stat[:, 2:3], stat[:, 0:1], 1.0 / D, None, ALU.mult),
            reads=[b_stat], writes=[b_stat])
        s.emit("vector", lambda e: e.tensor_tensor(
            stat[:, 3:4], stat[:, 2:3], stat[:, 2:3], ALU.mult),
            reads=[b_stat], writes=[b_stat])
        s.emit("vector", lambda e: e.scalar_tensor_tensor(
            stat[:, 4:5], stat[:, 1:2], 1.0 / D, stat[:, 3:4],
            ALU.mult, ALU.subtract),
            reads=[b_stat], writes=[b_stat])
        s.emit("scalar", lambda e: e.activation(
            stat[:, 5:6], stat[:, 4:5], AF.Sqrt, bias=fc(1e-5)),
            reads=[b_stat, cst_buf["fconst"]], writes=[b_stat])
        s.emit("vector", lambda e: e.reciprocal(stat[:, 6:7], stat[:, 5:6]),
               reads=[b_stat], writes=[b_stat])
        s.emit("vector", lambda e, sl=sl: e.tensor_scalar(
            x_slab[:, sl, :], x_slab[:, sl, :], stat[:, 2:3], stat[:, 6:7],
            ALU.subtract, ALU.mult),
            reads=[b_stat], writes=[b_xslab[sl]])
        for ft in range(ND if ("ln" in parts or "ln2" in parts) else 0):
            pslot = ft % 4
            s.emit("tensor", lambda e, sl=sl, ft=ft, pslot=pslot: e.transpose(
                ps_S[:, pslot * 128:(pslot + 1) * 128],
                x_slab[:, sl, ft * 128:(ft + 1) * 128], cst["id128f"][:, :]),
                reads=[b_xslab[sl], cst_buf["id128f"]], writes=[b_ps4[0]])
            s.emit("vector", lambda e, ft=ft, pslot=pslot: e.tensor_copy(
                x32_fm[:, ft, :], ps_S[:, pslot * 128:(pslot + 1) * 128]),
                writes=[b_ps4[0], b_x32])
            s.emit("scalar", lambda e, tt=tt, ft=ft: e.copy(
                xn_bf[:, ft, tt * 128:(tt + 1) * 128], x32_fm[:, ft, :]),
                reads=[b_x32], writes=[b_xn])
        for ft in range(ND if "gate" in parts else 0):
            s.emit("tensor", lambda e, ft=ft: e.matmul(
                ps_S[:, 512:512 + NE], x32_fm[:, ft, :], wg_sb[:, ft, :],
                start=(ft == 0), stop=(ft == ND - 1)),
                reads=[b_x32, b_smallw], writes=[b_ps4[1]])
        if "gate" not in parts:
            continue
        s.emit("vector", lambda e, tt=tt: e.tensor_tensor(
            glog[:, tt, :], ps_S[:, 512:512 + NE], bg_bc[:, :], ALU.add),
            reads=[b_smallw], writes=[b_ps4[1], b_glog])
        s.emit("vector", lambda e, tt=tt: e.reduce_max(
            stat[:, 7:8], glog[:, tt, :], axis=AX.X),
            reads=[b_glog], writes=[b_stat])
        s.emit("vector", lambda e, tt=tt: e.tensor_scalar(
            gmask[:, tt, :], glog[:, tt, :], stat[:, 7:8], None, ALU.is_ge),
            reads=[b_glog, b_stat], writes=[b_gmask])
        if "thin" not in parts:
            continue
        for ft in range(ND):
            s.emit("tensor", lambda e, tt=tt, ft=ft: e.matmul(
                ps_a[:, 0:H], xn_bf[:, ft, tt * 128:(tt + 1) * 128], wb_sb[:, ft, :],
                start=(ft == 0), stop=(ft == ND - 1)),
                reads=[b_xn, b_smallw], writes=[b_psa])
        s.emit("scalar", lambda e, tt=tt: e.activation(
            sig_tm[:, tt, :], ps_a[:, 0:H], AF.Sigmoid),
            writes=[b_psa, b_sig])
        for ft in range(ND):
            s.emit("tensor", lambda e, tt=tt, ft=ft: e.matmul(
                ps_a[:, 128:128 + H], xn_bf[:, ft, tt * 128:(tt + 1) * 128], wf_sb[:, ft, :],
                start=(ft == 0), stop=(ft == ND - 1)),
                reads=[b_xn, b_smallw], writes=[b_psa])
        s.emit("vector", lambda e: e.tensor_tensor(
            ps_a[:, 128:128 + H], ps_a[:, 128:128 + H], delta_bc[:, :], ALU.add),
            reads=[b_smallw], writes=[b_psa])
        s.emit("scalar", lambda e, tt=tt: e.activation(
            x_slab[:, tt % 2, 0:H], ps_a[:, 128:128 + H], AF.Sigmoid),
            writes=[b_psa, b_xslab[tt % 2]])
        s.emit("scalar", lambda e, tt=tt: e.activation(
            logf_bf[:, tt, :], x_slab[:, tt % 2, 0:H], AF.Ln),
            reads=[b_xslab[tt % 2]], writes=[b_logf])
        for ft in range(ND):
            s.emit("tensor", lambda e, tt=tt, ft=ft: e.matmul(
                ps_b[0:64, 0:128], ww1_sb[:, ft, :], xn_bf[:, ft, tt * 128:(tt + 1) * 128],
                start=(ft == 0), stop=(ft == ND - 1)),
                reads=[b_xn, b_smallw], writes=[b_psb])
        s.emit("scalar", lambda e, tt=tt: e.copy(
            w1o[:, tt * 128:(tt + 1) * 128], ps_b[0:64, 0:128]),
            writes=[b_psb, b_w1o])

    for tt in range(NT if ("fc" in parts and "thin" in parts) else 0):
        for j in range(tt + 1):
            lhs = cst["cumtri"] if j == tt else cst["allones"]
            lb = cst_buf["cumtri"] if j == tt else cst_buf["allones"]
            s.emit("tensor", lambda e, tt=tt, j=j, lhs=lhs: e.matmul(
                ps_S[:, 1024:1024 + H], lhs[:, :], logf_bf[:, j, :],
                start=(j == 0), stop=(j == tt)),
                reads=[b_logf, lb], writes=[b_ps4[2]])
        s.emit("vector", lambda e, tt=tt: e.tensor_copy(
            fc_tm[:, tt, :], ps_S[:, 1024:1024 + H]),
            writes=[b_ps4[2], b_fc])

    dbg_dump("xn_bf", lambda: xn_bf[:, :, :], [128, ND, N], BF16, [b_xn.w])
    dbg_dump("glog", lambda: glog[:, :, :], [128, NT, NE], F32, [b_glog.w])
    dbg_dump("gmask", lambda: gmask[:, :, :], [128, NT, NE], F32, [b_gmask.w])
    dbg_dump("sig_tm", lambda: sig_tm[:, :, :], [128, NT, H], F32, [b_sig.w])
    dbg_dump("logf_bf", lambda: logf_bf[:, :, :], [128, NT, H], BF16, [b_logf.w])
    dbg_dump("fc_tm", lambda: fc_tm[:, :, :], [128, NT, H], F32, [b_fc.w])
    dbg_dump("w1o", lambda: w1o[:, :], [64, N], BF16, [b_w1o.w])



    wo_sb = nc.alloc_sbuf_tensor("wo_sb", [128, ND, D], BF16)
    b_wo = Buf()
    s.emit("gpsimd", lambda e: e.dma_start(
        out=wo_sb[:, :, :], in_=wo_d.rearrange("(t p) d -> p t d", p=128)),
        writes=[b_wo], inc_k=16)
    s.emit("gpsimd", lambda e: e.memset(y_acc[:, :, :], 0.0), writes=[b_yacc_all])

    st2 = nc.alloc_sbuf_tensor("st2", [128, 24], F32)
    _attn_guards = []

    def _ga(name, shape, dtype):
        g = nc.sbuf_tensor(name, shape, dtype)
        _attn_guards.append(g)
        return g.__enter__()

    q_fm = _ga("q_fm", [128, N], BF16)
    k_fm = _ga("k_fm", [128, N], BF16)
    w_fm = _ga("w_fm", [128, N], BF16)
    wb_fm = _ga("wb_fm", [128, N], BF16)
    w_tm = _ga("w_tm", [128, NT, 128], BF16)
    wb_tm = _ga("wb_tm", [128, NT, 128], BF16)
    v_aug = _ga("v_aug", [128, NT, 144], BF16)
    kt = _ga("kt", [128, 2, N], BF16)
    qhat_k = _ga("qhat_k", [128, 2, 128], BF16)
    sii_k = _ga("sii_k", [128, 2, 2, 128], BF16)
    pt_k = _ga("pt_k", [128, 2, 64], BF16)
    ktd_k = _ga("ktd_k", [128, 2, 128], BF16)
    Lb = _ga("Lb", [128, 2, 128], BF16)
    Nb = _ga("Nb", [128, 2, 128], BF16)
    Mb = _ga("Mb", [128, 2, 128], BF16)
    nGt = _ga("nGt", [128, 2, 128], BF16)
    npow = _ga("npow", [128, 2, 6, 128], BF16)
    lkb = _ga("lkb", [128, 2, 128], BF16)
    zb = _ga("zb", [128, 2, 2, 192], BF16)
    s_sb = _ga("s_sb", [128, 2, N], BF16)
    pT_sb = _ga("pT_sb", [128, 2, N], BF16)
    o_tmp = _ga("o_tmp", [128, 2, 64], BF16)
    fcol = _ga("fcol", [128, 2, NT], F32)
    fcmr = _ga("fcmr", [128, 2, NT], F32)
    st3 = _ga("st3", [128, 2, 8], F32)
    wq_sl = _ga("wq_sl", [128, ND, 128], BF16)
    wk_sl = _ga("wk_sl", [128, ND, 128], BF16)
    wv_sl = _ga("wv_sl", [128, ND, 128], BF16)
    ww2_sl = _ga("ww2_sl", [64, 128], BF16)

    b_q = Buf(); b_k = Buf(); b_wfm = Buf(); b_wbfm = Buf()
    b_wtm = Buf(); b_wbtm = Buf(); b_vaug = Buf()
    b_kt = [[Buf(), Buf()], [Buf(), Buf()]]
    b_qh = [Buf(), Buf()]; b_sii = [Buf(), Buf()]
    b_pt = [Buf(), Buf()]; b_ktd = [Buf(), Buf()]
    b_L = [Buf(), Buf()]; b_N = [Buf(), Buf()]; b_M = [Buf(), Buf()]
    b_nGt = [Buf(), Buf()]; b_np = [Buf(), Buf()]; b_lk = [Buf(), Buf()]
    b_z = [[Buf(), Buf()], [Buf(), Buf()]]
    b_s = [Buf(), Buf()]; b_pT = [Buf(), Buf()]; b_otmp = [Buf(), Buf()]
    b_fcol = [Buf(), Buf()]; b_fcmr = [Buf(), Buf()]
    b_st2 = Buf(); b_st3 = [Buf(), Buf()]
    b_wqsl = Buf(); b_wksl = Buf(); b_wvsl = Buf(); b_ww2sl = Buf()
    b_ofm = Buf()

    def emit_proj(hp):
        c0 = hp * 128
        s.emit("gpsimd", lambda e, c0=c0: e.dma_start(
            out=wq_sl[:, :, :],
            in_=wq_d[:, c0:c0 + 128].rearrange("(t p) c -> p t c", p=128)),
            writes=[b_wqsl], inc_k=16)
        s.emit("gpsimd", lambda e, c0=c0: e.dma_start(
            out=wk_sl[:, :, :],
            in_=wk_d[:, c0:c0 + 128].rearrange("(t p) c -> p t c", p=128)),
            writes=[b_wksl], inc_k=16)
        s.emit("gpsimd", lambda e, c0=c0: e.dma_start(
            out=wv_sl[:, :, :],
            in_=wv_d[:, c0:c0 + 128].rearrange("(t p) c -> p t c", p=128)),
            writes=[b_wvsl], inc_k=16)
        s.emit("gpsimd", lambda e, c0=c0: e.dma_start(
            out=ww2_sl[:, :], in_=ww2_d[:, c0:c0 + 128]),
            writes=[b_ww2sl], inc_k=16)
        s.emit("gpsimd", lambda e: e.memset(v_aug[:, :, 64:72], 0.0), writes=[b_vaug])
        s.emit("gpsimd", lambda e: e.memset(v_aug[:, :, 136:144], 0.0), writes=[b_vaug])
        s.emit("gpsimd", lambda e: e.memset(v_aug[:, :, 64:65], 1.0), writes=[b_vaug])
        s.emit("gpsimd", lambda e: e.memset(v_aug[:, :, 136:137], 1.0), writes=[b_vaug])
        for dst, wsl, bw, bdst in ((q_fm, wq_sl, b_wqsl, b_q), (k_fm, wk_sl, b_wksl, b_k)):
            for t4 in range(4):
                for ft in range(ND):
                    s.emit("tensor", lambda e, dst=dst, wsl=wsl, t4=t4, ft=ft: e.matmul(
                        ps_a[:, 0:512], wsl[:, ft, :],
                        xn_bf[:, ft, t4 * 512:(t4 + 1) * 512],
                        start=(ft == 0), stop=(ft == ND - 1)),
                        reads=[bw, b_xn], writes=[b_psa])
                s.emit("scalar", lambda e, dst=dst, t4=t4: e.copy(
                    dst[:, t4 * 512:(t4 + 1) * 512], ps_a[:, 0:512]),
                    writes=[b_psa, bdst])
        for tt in range(NT):
            for ft in range(ND):
                s.emit("tensor", lambda e, tt=tt, ft=ft: e.matmul(
                    ps_S[:, 1024:1152], xn_bf[:, ft, tt * 128:(tt + 1) * 128],
                    wv_sl[:, ft, :], start=(ft == 0), stop=(ft == ND - 1)),
                    reads=[b_wvsl, b_xn], writes=[b_ps4[2]])
            s.emit("vector", lambda e, tt=tt: e.tensor_copy(
                bass.AP(v_aug, tt * 144, [[v_aug.shape[2] * NT, 128], [72, 2], [1, 64]]),
                ps_S[:, 1024:1152]),
                writes=[b_ps4[2], b_vaug])
            wb_ = tt % 2
            woff = wb_ * 512
            s.emit("tensor", lambda e, tt=tt, woff=woff: e.matmul(
                ps_S[:, woff:woff + 128], w1o[:, tt * 128:(tt + 1) * 128], ww2_sl[:, :],
                start=True, stop=True),
                reads=[b_w1o, b_ww2sl], writes=[b_ps4[wb_]])
            for half in range(2):
                s.emit("scalar", lambda e, tt=tt, half=half, woff=woff, wb_=wb_: e.activation(
                    x_slab[:, wb_, half * 64:(half + 1) * 64],
                    ps_S[:, woff + half * 64:woff + (half + 1) * 64], AF.Square,
                    accum_out=st2[:, 8 + wb_ * 8 + half:9 + wb_ * 8 + half]),
                    writes=[b_ps4[wb_], b_st2, b_xslab[wb_]])
            s.emit("scalar", lambda e, wb_=wb_: e.activation(
                st2[:, 10 + wb_ * 8:12 + wb_ * 8], st2[:, 8 + wb_ * 8:10 + wb_ * 8],
                AF.Sqrt, bias=fc(1e-6)),
                reads=[cst_buf["fconst"]], writes=[b_st2])
            s.emit("vector", lambda e, wb_=wb_: e.reciprocal(
                st2[:, 12 + wb_ * 8:14 + wb_ * 8], st2[:, 10 + wb_ * 8:12 + wb_ * 8]),
                writes=[b_st2])
            s.emit("vector", lambda e, tt=tt, hp=hp, wb_=wb_: e.scalar_tensor_tensor(
                st2[:, 14 + wb_ * 8:16 + wb_ * 8], sig_tm[:, tt, 2 * hp:2 * hp + 2], 2.0,
                st2[:, 12 + wb_ * 8:14 + wb_ * 8], ALU.mult, ALU.mult),
                reads=[b_sig], writes=[b_st2])
            for half in range(2):
                s.emit("vector", lambda e, tt=tt, half=half, woff=woff, wb_=wb_: e.tensor_scalar(
                    w_tm[:, tt, half * 64:(half + 1) * 64],
                    ps_S[:, woff + half * 64:woff + (half + 1) * 64],
                    st2[:, 12 + wb_ * 8 + half:13 + wb_ * 8 + half], None, ALU.mult),
                    reads=[b_st2], writes=[b_ps4[wb_], b_wtm])
                s.emit("vector", lambda e, tt=tt, half=half, woff=woff, wb_=wb_: e.tensor_scalar(
                    wb_tm[:, tt, half * 64:(half + 1) * 64],
                    ps_S[:, woff + half * 64:woff + (half + 1) * 64],
                    st2[:, 14 + wb_ * 8 + half:15 + wb_ * 8 + half], None, ALU.mult),
                    reads=[b_st2], writes=[b_ps4[wb_], b_wbtm])
            s.emit("tensor", lambda e, tt=tt: e.transpose(
                ps_c[:, 512:640], w_tm[:, tt, :], cst["id128"][:, :]),
                reads=[b_wtm, cst_buf["id128"]], writes=[b_psc])
            s.emit("scalar", lambda e, tt=tt: e.copy(
                w_fm[:, tt * 128:(tt + 1) * 128], ps_c[:, 512:640]),
                writes=[b_psc, b_wfm])
            s.emit("tensor", lambda e, tt=tt: e.transpose(
                ps_c[:, 640:768], wb_tm[:, tt, :], cst["id128"][:, :]),
                reads=[b_wbtm, cst_buf["id128"]], writes=[b_psc])
            s.emit("scalar", lambda e, tt=tt: e.copy(
                wb_fm[:, tt * 128:(tt + 1) * 128], ps_c[:, 640:768]),
                writes=[b_psc, b_wbfm])

    def emit_head_prep(h, ph):
        bk = 2 + ph
        s.emit("tensor", lambda e, h=h, bk=bk: e.matmul(
            ps_S[:, bk * 512 + 256:bk * 512 + 257], cst["ones_row_f32"][:, :],
            fc_tm[0:1, 8, h:h + 1], start=True, stop=True),
            reads=[b_fc, cst_buf["ones_row_f32"]], writes=[b_ps4[bk]])
        s.emit("vector", lambda e, ph=ph, bk=bk: e.tensor_copy(
            st3[:, ph, 0:1], ps_S[:, bk * 512 + 256:bk * 512 + 257]),
            writes=[b_ps4[bk], b_st3[ph]])
        s.emit("vector", lambda e, ph=ph: e.tensor_scalar(
            st3[:, ph, 1:2], st3[:, ph, 0:1], -1.0, None, ALU.mult),
            writes=[b_st3[ph]])
        s.emit("scalar", lambda e, h=h, ph=ph: e.activation(
            fcol[:, ph, :], bass.AP(fc_tm, h, [[NT * H, 128], [H, NT], [1, 1]]),
            AF.Exp, bias=st3[:, ph, 0:1], scale=fc(-1.0)),
            reads=[b_fc, b_st3[ph], cst_buf["fconst"]], writes=[b_fcol[ph]])
        s.emit("vector", lambda e, h=h, ph=ph: e.tensor_scalar(
            fcmr[:, ph, :], bass.AP(fc_tm, h, [[NT * H, 128], [H, NT], [1, 1]]),
            st3[:, ph, 1:2], None, ALU.add),
            reads=[b_fc, b_st3[ph]], writes=[b_fcmr[ph]])

    def emit_phaseA(h, ph, i):
        psA = ps_a if ph == 0 else ps_b
        bA = b_psa if ph == 0 else b_psb
        r0 = ph * 64
        isl = slice(i * 128, (i + 1) * 128)
        W_I = w_fm[r0:r0 + 64, isl]
        Wb_I = wb_fm[r0:r0 + 64, isl]
        Q_I = q_fm[r0:r0 + 64, isl]
        K_I = k_fm[r0:r0 + 64, isl]
        Wtm_I = w_tm[:, i, r0:r0 + 64]
        Wbtm_I = wb_tm[:, i, r0:r0 + 64]
        s.emit("tensor", lambda e: e.matmul(psA[:, 0:128], W_I, Wb_I,
               start=True, stop=True, tile_position=(r0, 0)),
               reads=[b_wfm, b_wbfm], writes=[bA])
        s.emit("vector", lambda e: e.tensor_tensor(
            Lb[:, ph, :], psA[:, 0:128], cst["m_trils"][:, :], ALU.mult),
            reads=[cst_buf["m_trils"]], writes=[bA, b_L[ph]])
        s.emit("tensor", lambda e: e.matmul(psA[:, 128:256], Wb_I, W_I,
               start=True, stop=True, tile_position=(r0, 0)),
               reads=[b_wfm, b_wbfm], writes=[bA])
        s.emit("vector", lambda e: e.tensor_tensor(
            Nb[:, ph, :], psA[:, 128:256], cst["m_trius"][:, :], ALU.mult),
            reads=[cst_buf["m_trius"]], writes=[bA, b_N[ph]])
        s.emit("tensor", lambda e: e.matmul(psA[:, 256:384], W_I, K_I,
               start=True, stop=True, tile_position=(r0, 0)),
               reads=[b_wfm, b_k], writes=[bA])
        s.emit("vector", lambda e: e.tensor_tensor(
            Mb[:, ph, :], psA[:, 256:384], cst["m_trils"][:, :], ALU.mult),
            reads=[cst_buf["m_trils"]], writes=[bA, b_M[ph]])
        s.emit("tensor", lambda e: e.matmul(psA[:, 384:512], Wb_I, Q_I,
               start=True, stop=True, tile_position=(r0, 0)),
               reads=[b_wbfm, b_q], writes=[bA])
        s.emit("vector", lambda e: e.tensor_tensor(
            nGt[:, ph, :], psA[:, 384:512], cst["m_negtriu"][:, :], ALU.mult),
            reads=[cst_buf["m_negtriu"]], writes=[bA, b_nGt[ph]])

        def lp(lev):
            return Lb[:, ph, :] if lev == 0 else lkb[:, ph, :]

        def np_(lev):
            return Nb[:, ph, :] if lev == 0 else npow[:, ph, lev - 1, :]

        for lev in range(6):
            bl = b_L[ph] if lev == 0 else b_lk[ph]
            bn = b_N[ph] if lev == 0 else b_np[ph]
            s.emit("tensor", lambda e, lev=lev: e.matmul(
                psA[:, 128:256], lp(lev), np_(lev), start=True, stop=True),
                reads=[bl, bn], writes=[bA])
            s.emit("scalar", lambda e, lev=lev: e.copy(
                npow[:, ph, lev, :], psA[:, 128:256]),
                writes=[bA, b_np[ph]])
            if lev < 5:
                s.emit("tensor", lambda e, lev=lev: e.matmul(
                    psA[:, 0:128], np_(lev), lp(lev), start=True, stop=True),
                    reads=[bl, bn], writes=[bA])
                s.emit("vector", lambda e: e.tensor_copy(
                    lkb[:, ph, :], psA[:, 0:128]),
                    writes=[bA, b_lk[ph]])
        zc = 0
        s.emit("tensor", lambda e: e.matmul(psA[:, 0:64], npow[:, ph, 5, :], Wtm_I,
               start=True, stop=False), reads=[b_np[ph], b_wtm], writes=[bA])
        s.emit("tensor", lambda e: e.matmul(psA[:, 64:192], npow[:, ph, 5, :], Mb[:, ph, :],
               start=False, stop=True), reads=[b_np[ph], b_M[ph]], writes=[bA])
        s.emit("vector", lambda e: e.tensor_tensor(
            zb[:, ph, 0, 0:64], Wtm_I, psA[:, 0:64], ALU.add),
            reads=[b_wtm], writes=[bA, b_z[ph][0]])
        s.emit("vector", lambda e: e.tensor_tensor(
            zb[:, ph, 0, 64:192], Mb[:, ph, :], psA[:, 64:192], ALU.add),
            reads=[b_M[ph]], writes=[bA, b_z[ph][0]])
        for lev in (4, 3, 2, 1, 0):
            zn = 1 - zc
            s.emit("tensor", lambda e, lev=lev, zc=zc: e.matmul(
                psA[:, 0:192], npow[:, ph, lev, :], zb[:, ph, zc, :],
                start=True, stop=True), reads=[b_np[ph], b_z[ph][zc]], writes=[bA])
            s.emit("vector", lambda e, zc=zc, zn=zn: e.tensor_tensor(
                zb[:, ph, zn, :], zb[:, ph, zc, :], psA[:, 0:192], ALU.add),
                reads=[b_z[ph][zc]], writes=[bA, b_z[ph][zn]])
            zc = zn
        zn = 1 - zc
        s.emit("tensor", lambda e, zc=zc: e.matmul(
            psA[:, 0:192], Nb[:, ph, :], zb[:, ph, zc, :],
            start=True, stop=True), reads=[b_N[ph], b_z[ph][zc]], writes=[bA])
        s.emit("vector", lambda e, zc=zc, zn=zn: e.tensor_tensor(
            zb[:, ph, zn, :], zb[:, ph, zc, :], psA[:, 0:192], ALU.subtract),
            reads=[b_z[ph][zc]], writes=[bA, b_z[ph][zn]])
        zf = zn
        TW = zb[:, ph, zf, 0:64]
        CI2 = zb[:, ph, zf, 64:192]
        bZ = b_z[ph][zf]
        s.emit("tensor", lambda e: e.matmul(psA[:, 256:384], Q_I, K_I,
               start=True, stop=False, tile_position=(r0, 0)),
               reads=[b_q, b_k], writes=[bA])
        s.emit("tensor", lambda e: e.matmul(psA[:, 256:384], nGt[:, ph, :], CI2,
               start=False, stop=True), reads=[b_nGt[ph], bZ], writes=[bA])
        s.emit("scalar", lambda e, i=i, ph=ph: e.copy(
            sii_k[:, ph, i % 2, :], psA[:, 256:384]),
            writes=[bA, b_sii[ph]])
        s.emit("tensor", lambda e: e.matmul(
            psA[r0:r0 + 64, 0:128], TW, nGt[:, ph, :],
            start=True, stop=True, tile_position=(0, r0)),
            reads=[bZ, b_nGt[ph]], writes=[bA])
        s.emit("vector", lambda e, i=i: e.tensor_tensor(
            qhat_k[r0:r0 + 64, i % 2, :], Q_I, psA[r0:r0 + 64, 0:128], ALU.add),
            reads=[b_q], writes=[bA, b_qh[ph]])
        s.emit("tensor", lambda e: e.matmul(
            psA[r0:r0 + 64, 128:192], TW, Wbtm_I,
            start=True, stop=True, tile_position=(0, r0)),
            reads=[bZ, b_wbtm], writes=[bA])
        s.emit("vector", lambda e, i=i: e.tensor_tensor(
            pt_k[r0:r0 + 64, i % 2, :], cst["i64stack"][r0:r0 + 64, :],
            psA[r0:r0 + 64, 128:192], ALU.subtract),
            reads=[cst_buf["i64stack"]], writes=[bA, b_pt[ph]])
        s.emit("tensor", lambda e: e.matmul(
            psA[r0:r0 + 64, 256:384], Wbtm_I, CI2,
            start=True, stop=True, tile_position=(0, r0)),
            reads=[b_wbtm, bZ], writes=[bA])
        s.emit("vector", lambda e, i=i: e.tensor_tensor(
            ktd_k[r0:r0 + 64, i % 2, :], K_I, psA[r0:r0 + 64, 256:384], ALU.subtract),
            reads=[b_k], writes=[bA, b_ktd[ph]])

    def emit_phaseB_chunk(h, ph, i):
        r0 = ph * 64
        bk = 2 + ph
        sk = ph
        psc = ps_c if ph == 0 else ps_d
        bpsc = b_psc if ph == 0 else b_psd
        rbuf = (i + 1) % 2
        wbuf = i % 2
        jend = (i + 1) * 128
        nj5 = (i * 128 + 511) // 512
        for j5 in range(nj5):
            w = min(512, i * 128 - j5 * 512)
            s.emit("tensor", lambda e, i=i, j5=j5, w=w, rbuf=rbuf: e.matmul(
                ps_S[:, sk * 512:sk * 512 + w],
                qhat_k[r0:r0 + 64, i % 2, :], kt[r0:r0 + 64, rbuf, j5 * 512:j5 * 512 + w],
                start=True, stop=True, tile_position=(r0, 0)),
                reads=[b_qh[ph], b_kt[ph][rbuf]], writes=[b_ps4[sk]])
            s.emit("vector", lambda e, j5=j5, w=w, ph=ph: e.tensor_copy(
                s_sb[:, ph, j5 * 512:j5 * 512 + w], ps_S[:, sk * 512:sk * 512 + w]),
                writes=[b_ps4[sk], b_s[ph]])
        s.emit("vector", lambda e, i=i, ph=ph: e.tensor_tensor(
            s_sb[:, ph, i * 128:(i + 1) * 128], sii_k[:, ph, i % 2, :],
            cst["m_addneg"][:, :], ALU.add),
            reads=[b_sii[ph], cst_buf["m_addneg"]], writes=[b_s[ph]])
        s.emit("vector", lambda e, jend=jend, ph=ph: e.reduce_max(
            st3[:, ph, 2:3], s_sb[:, ph, 0:jend], axis=AX.X),
            reads=[b_s[ph]], writes=[b_st3[ph]])
        s.emit("vector", lambda e, i=i, ph=ph: e.tensor_scalar(
            st3[:, ph, 3:4], st3[:, ph, 2:3], -SCALE, fcmr[:, ph, i:i + 1],
            ALU.mult, ALU.add),
            reads=[b_fcmr[ph]], writes=[b_st3[ph]])
        s.emit("scalar", lambda e, jend=jend, ph=ph: e.activation(
            s_sb[:, ph, 0:jend], s_sb[:, ph, 0:jend], AF.Exp,
            bias=st3[:, ph, 3:4], scale=fc(SCALE)),
            reads=[b_st3[ph], cst_buf["fconst"]], writes=[b_s[ph]])
        for jt in range(i + 1):
            slot = jt % 7
            s.emit("tensor", lambda e, jt=jt, slot=slot, ph=ph, psc=psc: e.transpose(
                psc[:, slot * 128:(slot + 1) * 128],
                s_sb[:, ph, jt * 128:(jt + 1) * 128], cst["id128"][:, :]),
                reads=[b_s[ph], cst_buf["id128"]], writes=[bpsc])
            s.emit("vector", lambda e, jt=jt, slot=slot, ph=ph, psc=psc: e.tensor_scalar(
                pT_sb[:, ph, jt * 128:(jt + 1) * 128],
                psc[:, slot * 128:(slot + 1) * 128],
                fcol[:, ph, jt:jt + 1], None, ALU.mult),
                reads=[b_fcol[ph]], writes=[bpsc, b_pT[ph]])
            s.emit("tensor", lambda e, jt=jt, i=i, ph=ph, bk=bk: e.matmul(
                ps_S[:, bk * 512:bk * 512 + 72], pT_sb[:, ph, jt * 128:(jt + 1) * 128],
                v_aug[:, jt, ph * 72:ph * 72 + 72],
                start=(jt == 0), stop=(jt == i)),
                reads=[b_pT[ph], b_vaug], writes=[b_ps4[bk]])
        s.emit("vector", lambda e, ph=ph, bk=bk: e.reciprocal(
            st3[:, ph, 4:5], ps_S[:, bk * 512 + 64:bk * 512 + 65]),
            writes=[b_ps4[bk], b_st3[ph]])
        s.emit("vector", lambda e, ph=ph, bk=bk: e.tensor_scalar(
            o_tmp[:, ph, :], ps_S[:, bk * 512:bk * 512 + 64], st3[:, ph, 4:5],
            None, ALU.mult),
            reads=[b_st3[ph]], writes=[b_ps4[bk], b_otmp[ph]])
        s.emit("tensor", lambda e, ph=ph, psc=psc: e.transpose(
            psc[r0:r0 + 64, 896:1024], o_tmp[:, ph, :], cst["id128"][:, :],
            tile_position=(0, r0)),
            reads=[b_otmp[ph], cst_buf["id128"]], writes=[bpsc])
        s.emit("scalar", lambda e, i=i, h=h, ph=ph, psc=psc: e.copy(
            o_fm[r0:r0 + 64, h // 2, i * 128:(i + 1) * 128],
            psc[r0:r0 + 64, 896:1024]),
            writes=[bpsc, b_ofm])
        for j5 in range(nj5):
            w = min(512, i * 128 - j5 * 512)
            s.emit("tensor", lambda e, i=i, j5=j5, w=w, rbuf=rbuf, bk=bk: e.matmul(
                ps_S[r0:r0 + 64, bk * 512:bk * 512 + w], pt_k[r0:r0 + 64, i % 2, :],
                kt[r0:r0 + 64, rbuf, j5 * 512:j5 * 512 + w],
                start=True, stop=True, tile_position=(r0, r0)),
                reads=[b_pt[ph], b_kt[ph][rbuf]], writes=[b_ps4[bk]])
            s.emit("scalar", lambda e, j5=j5, w=w, wbuf=wbuf, bk=bk: e.copy(
                kt[r0:r0 + 64, wbuf, j5 * 512:j5 * 512 + w],
                ps_S[r0:r0 + 64, bk * 512:bk * 512 + w]),
                writes=[b_ps4[bk], b_kt[ph][wbuf]])
        s.emit("gpsimd", lambda e, i=i, wbuf=wbuf: e.tensor_copy(
            kt[r0:r0 + 64, wbuf, i * 128:(i + 1) * 128],
            ktd_k[r0:r0 + 64, i % 2, :]),
            reads=[b_ktd[ph]], writes=[b_kt[ph][wbuf]])

    def defer(fn):
        rec = []
        real = s.emit
        s.emit = lambda *a, **k: rec.append((a, k))
        fn()
        s.emit = real
        return rec

    def play(recs):
        idx = [0] * len(recs)
        while True:
            done = True
            for r in range(len(recs)):
                if idx[r] < len(recs[r]):
                    a, k = recs[r][idx[r]]
                    s.emit(*a, **k)
                    idx[r] += 1
                    done = False
            if done:
                break

    for hp in range(hp_count):
        emit_proj(hp)
        h0, h1 = 2 * hp, 2 * hp + 1
        emit_head_prep(h0, 0)
        emit_head_prep(h1, 1)
        play([defer(lambda: emit_phaseA(h0, 0, 0)),
              defer(lambda: emit_phaseA(h1, 1, 0))])
        for i in range(NCH):
            recs = [defer(lambda i=i: emit_phaseB_chunk(h0, 0, i)),
                    defer(lambda i=i: emit_phaseB_chunk(h1, 1, i))]
            if i + 1 < NCH:
                recs.append(defer(lambda i=i: emit_phaseA(h0, 0, i + 1)))
                recs.append(defer(lambda i=i: emit_phaseA(h1, 1, i + 1)))
            play(recs)

    dbg_dump("o_fm", lambda: o_fm[:, :, :], [128, ND, N], BF16, [b_ofm.w])

    attn_bufs = ([b_q, b_k, b_wfm, b_wbfm, b_wtm, b_wbtm, b_vaug]
                 + b_kt[0] + b_kt[1] + b_qh + b_sii + b_pt + b_ktd
                 + b_L + b_N + b_M + b_nGt + b_np + b_lk
                 + b_z[0] + b_z[1] + b_s + b_pT + b_otmp
                 + b_fcol + b_fcmr + b_st3
                 + [b_wqsl, b_wksl, b_wvsl, b_ww2sl])
    barrier = []
    for b in attn_bufs:
        if b.w is not None:
            barrier.append(b.w)
        barrier.extend(b.readers)
    barrier = list({id(t): t for t in barrier}.values())
    for g in reversed(_attn_guards):
        g.__exit__(None, None, None)

    w1_sl = [nc.alloc_sbuf_tensor(f"w1_sl{i}", [128, ND, D], BF16) for i in range(2)]
    w2_sl = nc.alloc_sbuf_tensor("w2_sl", [128, ND, D], BF16)
    g_slab = nc.alloc_sbuf_tensor("g_slab", [128, ND, 512], BF16)
    b1_sb = nc.alloc_sbuf_tensor("b1_sb", [1, D], BF16)
    b2_sb = nc.alloc_sbuf_tensor("b2_sb", [1, D], BF16)
    b_w1 = [Buf(), Buf()]
    b_w2 = Buf()
    b_g = Buf()
    b_b1 = Buf()
    b_b2 = Buf()

    def emit_ffn_expert(e_idx):
        sl = e_idx % 2
        if e_idx == 0:
            s.emit("gpsimd", lambda e: e.dma_start(
                out=w1_sl[0][:, :, :],
                in_=w1_d[0].rearrange("(t p) i -> p t i", p=128)),
                writes=[b_w1[0]], deps=barrier, inc_k=16)
        if e_idx + 1 < NE:
            s.emit("gpsimd", lambda e, e_idx=e_idx: e.dma_start(
                out=w1_sl[(e_idx + 1) % 2][:, :, :],
                in_=w1_d[e_idx + 1].rearrange("(t p) i -> p t i", p=128)),
                writes=[b_w1[(e_idx + 1) % 2]],
                deps=(barrier if e_idx == 0 else ()), inc_k=16)
        s.emit("gpsimd", lambda e, e_idx=e_idx: e.dma_start(
            out=w2_sl[:, :, :], in_=w2_d[e_idx].rearrange("(t p) i -> p t i", p=128)),
            writes=[b_w2], deps=(barrier if e_idx == 0 else ()), inc_k=16)
        s.emit("gpsimd", lambda e, e_idx=e_idx: e.dma_start(
            out=b1_sb[:, :], in_=b1_d[e_idx:e_idx + 1, :]),
            writes=[b_b1], deps=(barrier if e_idx == 0 else ()), inc_k=16)
        s.emit("gpsimd", lambda e, e_idx=e_idx: e.dma_start(
            out=b2_sb[:, :], in_=b2_d[e_idx:e_idx + 1, :]),
            writes=[b_b2], deps=(barrier if e_idx == 0 else ()), inc_k=16)
        for t4 in range(4):
            for it in range(ND):
                psh = ps_a if it % 2 == 0 else ps_b
                bph = b_psa if it % 2 == 0 else b_psb
                for ft in range(ND):
                    s.emit("tensor", lambda e, sl=sl, t4=t4, it=it, ft=ft, psh=psh: e.matmul(
                        psh[:, 0:512], w1_sl[sl][:, ft, it * 128:(it + 1) * 128],
                        xn_bf[:, ft, t4 * 512:(t4 + 1) * 512],
                        start=(ft == 0), stop=False),
                        reads=[b_w1[sl], b_xn], writes=[bph])
                s.emit("tensor", lambda e, it=it, psh=psh: e.matmul(
                    psh[:, 0:512], b1_sb[0:1, it * 128:(it + 1) * 128],
                    cst["ones_row_bf"][0:1, 0:512],
                    start=False, stop=True),
                    reads=[b_b1, cst_buf["ones_row_bf"]], writes=[bph])
                s.emit("scalar", lambda e, it=it, psh=psh: e.activation(
                    g_slab[:, it, :], psh[:, 0:512], AF.Gelu),
                    writes=[bph, b_g])
            for tsub in range(4):
                tt = t4 * 4 + tsub
                for dh in range(2):
                    pb = tsub % 2 + 2 * dh
                    for it in range(ND):
                        s.emit("tensor", lambda e, tsub=tsub, dh=dh, it=it, pb=pb: e.matmul(
                            ps_S[:, pb * 512:pb * 512 + 512],
                            g_slab[:, it, tsub * 128:(tsub + 1) * 128],
                            w2_sl[:, it, dh * 512:(dh + 1) * 512],
                            start=(it == 0), stop=False),
                            reads=[b_g, b_w2], writes=[b_ps4[pb]])
                    s.emit("tensor", lambda e, dh=dh, pb=pb: e.matmul(
                        ps_S[:, pb * 512:pb * 512 + 512],
                        cst["ones_row_bf"][0:1, 0:128],
                        b2_sb[0:1, dh * 512:(dh + 1) * 512],
                        start=False, stop=True),
                        reads=[b_b2, cst_buf["ones_row_bf"]], writes=[b_ps4[pb]])
                    s.emit("vector", lambda e, tt=tt, dh=dh, e_idx=e_idx, pb=pb: e.scalar_tensor_tensor(
                        y_acc[:, tt, dh * 512:(dh + 1) * 512],
                        ps_S[:, pb * 512:pb * 512 + 512],
                        gmask[:, tt, e_idx:e_idx + 1],
                        y_acc[:, tt, dh * 512:(dh + 1) * 512],
                        ALU.mult, ALU.add),
                        reads=[b_gmask, b_yacc_all], writes=[b_ps4[pb], b_yacc[tt]])

    for e_idx in range(NE):
        emit_ffn_expert(e_idx)

    b_out = Buf()
    for tt in range(NT):
        sl = tt % 2
        s.emit("sync", lambda e, tt=tt, sl=sl: e.dma_start(
            out=x_slab[:, sl, :], in_=x_d.rearrange("(t p) d -> t p d", p=128)[tt]),
            writes=[b_xslab[sl]], inc_k=16)
        for dh in range(2):
            psx = ps_a if dh == 0 else ps_b
            bpsx = b_psa if dh == 0 else b_psb
            for ft in range(ND):
                s.emit("tensor", lambda e, tt=tt, dh=dh, ft=ft, psx=psx: e.matmul(
                    psx[:, 0:512], o_fm[:, ft, tt * 128:(tt + 1) * 128],
                    wo_sb[:, ft, dh * 512:(dh + 1) * 512],
                    start=(ft == 0), stop=(ft == ND - 1)),
                    reads=[b_ofm, b_wo], writes=[bpsx])
            s.emit("vector", lambda e, tt=tt, dh=dh, sl=sl, psx=psx: e.tensor_tensor(
                x_slab[:, sl, dh * 512:(dh + 1) * 512], psx[:, 0:512],
                x_slab[:, sl, dh * 512:(dh + 1) * 512], ALU.add),
                writes=[bpsx, b_xslab[sl]])
            s.emit("vector", lambda e, tt=tt, dh=dh, sl=sl: e.tensor_tensor(
                x_slab[:, sl, dh * 512:(dh + 1) * 512],
                x_slab[:, sl, dh * 512:(dh + 1) * 512],
                y_acc[:, tt, dh * 512:(dh + 1) * 512], ALU.add),
                reads=[b_yacc[tt], b_yacc_all], writes=[b_xslab[sl]])
        s.emit("sync", lambda e, tt=tt, sl=sl: e.dma_start(
            out=out_d.rearrange("(t p) d -> t p d", p=128)[tt], in_=x_slab[:, sl, :]),
            reads=[b_xslab[sl]], writes=[b_out], inc_k=16)
    s.emit("sync", lambda e: e.nop(), deps=[b_out.w])

    s.finalize(nc, paranoid=paranoid)
    return nc, dbg_d


def stage_inputs(inputs):
    consts = host_constants()
    x = np.ascontiguousarray(inputs["L0_attn_x"])
    base = {
        "wq": inputs["L0_attn_Wq"], "wk": inputs["L0_attn_Wk"],
        "wv": inputs["L0_attn_Wv"], "wo": inputs["L0_attn_Wo"],
        "ww1": inputs["L0_attn_Ww1"], "ww2": inputs["L0_attn_Ww2"],
        "wbm": inputs["L0_attn_Wb"], "wfm": inputs["L0_attn_Wf"],
        "delta_bc": np.ascontiguousarray(np.broadcast_to(
            inputs["L0_attn_delta"][None, :], (128, H))),
        "wg": inputs["L0_ffn_Wg"],
        "bg_bc": np.ascontiguousarray(np.broadcast_to(
            inputs["L0_ffn_bg"][None, :], (128, NE))),
        "w1": inputs["L0_ffn_W1"], "b1": inputs["L0_ffn_b1"],
        "w2": inputs["L0_ffn_W2"], "b2": inputs["L0_ffn_b2"],
    }
    for k, v in consts.items():
        base[f"c_{k}"] = v
    in_maps = []
    for c in range(8):
        m = dict(base)
        m["x"] = np.ascontiguousarray(x[c])
        in_maps.append(m)
    return in_maps


def kernel(**inputs):
    nc, _ = build()
    in_maps = stage_inputs(inputs)
    res = run_bass_kernel_spmd(nc, in_maps, core_ids=list(range(8)))
    return np.stack([res.results[c]["out"] for c in range(8)], axis=0)


# version 13 (final)
# speedup vs baseline: 1.3556x; 1.0022x over previous
import math
import numpy as np
import ml_dtypes
import concourse.bass as bass
import concourse.mybir as mybir
from concourse.bass_utils import run_bass_kernel_spmd

BFNP = ml_dtypes.bfloat16
F32 = mybir.dt.float32
BF16 = mybir.dt.bfloat16
AF = mybir.ActivationFunctionType
ALU = mybir.AluOpType
AX = mybir.AxisListType

N, D, H, HD, C = 2048, 1024, 16, 64, 128
NCH = N // C
NT = N // 128
ND = D // 128
NE = 8
SCALE = HD ** -0.5

ENGINES = ("tensor", "vector", "scalar", "gpsimd", "sync")

FCONST_VALS = (1e-5, 1e-6, -1.0, 2.0, 0.5, -0.5, SCALE, -SCALE)
FCONST_IDX = {v: i for i, v in enumerate(FCONST_VALS)}


class Op:
    __slots__ = ("eng", "fn", "deps", "inc_k", "sem", "sem_after", "pos",
                 "needs_inc", "pre_wait")

    def __init__(self, eng, fn, deps, inc_k):
        self.eng = eng
        self.fn = fn
        self.deps = deps
        self.inc_k = inc_k
        self.sem = None
        self.sem_after = None
        self.pos = None
        self.needs_inc = False
        self.pre_wait = None


class Buf:

    def __init__(self):
        self.w = None
        self.readers = []


class Sched:
    def __init__(self):
        self.ops = []

    def emit(self, eng, fn, reads=(), writes=(), deps=(), inc_k=1):
        d = [t for t in deps if t is not None]
        for b in reads:
            if b.w is not None:
                d.append(b.w)
        for b in writes:
            if b.w is not None:
                d.append(b.w)
            d.extend(b.readers)
        op = Op(eng, fn, d, inc_k)
        op.pos = len(self.ops)
        self.ops.append(op)
        for b in reads:
            b.readers.append(op)
        for b in writes:
            b.w = op
            b.readers = []
        return op

    def finalize(self, nc, paranoid=False):
        for op in self.ops:
            if op.inc_k == 16:
                op.needs_inc = True
            for dep in op.deps:
                assert dep.pos < op.pos
                if dep.eng != op.eng or dep.inc_k == 16 or paranoid:
                    dep.needs_inc = True
        eng_sems = {e: nc.alloc_semaphore(f"sem_{e}") for e in ENGINES}
        counts = {e: 0 for e in ENGINES}
        NPOOL = 8
        pool = {e: [nc.alloc_semaphore(f"dma_{e}_{i}") for i in range(NPOOL)]
                for e in ("sync", "gpsimd", "scalar")}
        pool_idx = {e: 0 for e in pool}
        pool_count = {}
        for op in self.ops:
            if op.inc_k == 16:
                sem = pool[op.eng][pool_idx[op.eng] % NPOOL]
                pool_idx[op.eng] += 1
                prev = pool_count.get(id(sem), 0)
                if prev:
                    op.pre_wait = (sem, prev)
                op.sem = sem
                op.sem_after = prev + 16
                pool_count[id(sem)] = op.sem_after
            elif op.needs_inc:
                counts[op.eng] += op.inc_k
                op.sem = eng_sems[op.eng]
                op.sem_after = counts[op.eng]
        marks = {e: {} for e in ENGINES}
        streams = {e: [o for o in self.ops if o.eng == e] for e in ENGINES}
        need_drain = set()
        vstream = streams["vector"]
        vpos = {id(o): i for i, o in enumerate(vstream)}
        for i, op in enumerate(vstream):
            for dep in op.deps:
                if dep.eng == "vector" and vpos[id(dep)] == i - 1:
                    need_drain.add(id(op))

        def body(eng_obj, ename):
            wm = marks[ename]
            for op in streams[ename]:
                for dep in op.deps:
                    if dep.eng == ename and dep.inc_k != 16 and not paranoid:
                        continue
                    assert dep.sem_after is not None, (dep.eng, op.eng)
                    if dep.sem_after > wm.get(id(dep.sem), 0):
                        eng_obj.wait_ge(dep.sem, dep.sem_after)
                        wm[id(dep.sem)] = dep.sem_after
                if op.pre_wait is not None:
                    sem, val = op.pre_wait
                    if val > wm.get(id(sem), 0):
                        eng_obj.wait_ge(sem, val)
                        wm[id(sem)] = val
                if id(op) in need_drain:
                    eng_obj.drain()
                instr = op.fn(eng_obj)
                if op.sem is not None:
                    instr.then_inc(op.sem, op.inc_k)

        with nc.Block() as blk:
            @blk.tensor
            def _(e):
                body(e, "tensor")

            @blk.vector
            def _(e):
                body(e, "vector")

            @blk.scalar
            def _(e):
                body(e, "scalar")

            @blk.gpsimd
            def _(e):
                body(e, "gpsimd")

            @blk.sync
            def _(e):
                body(e, "sync")


def host_constants():
    c = {}
    ident = np.eye(128, dtype=np.float32)
    c["id128"] = ident.astype(BFNP)
    c["id128f"] = ident
    c["cumtri"] = np.triu(np.ones((128, 128), np.float32)).astype(BFNP)
    c["allones"] = np.ones((128, 128), np.float32).astype(BFNP)
    c["m_trils"] = np.tril(np.ones((128, 128), np.float32), -1).astype(BFNP)
    c["m_trius"] = np.triu(np.ones((128, 128), np.float32), 1).astype(BFNP)
    c["m_negtriu"] = (-np.triu(np.ones((128, 128), np.float32))).astype(BFNP)
    c["m_addneg"] = np.triu(np.full((128, 128), -1e9, np.float32), 1).astype(BFNP)
    c["i64stack"] = np.concatenate([np.eye(64, dtype=np.float32)] * 2, 0)
    c["ones_row_bf"] = np.ones((1, 512), np.float32).astype(BFNP)
    c["ones_row_f32"] = np.ones((1, 128), np.float32)
    c["fconst"] = np.ascontiguousarray(np.broadcast_to(
        np.array(FCONST_VALS, np.float32)[None, :], (128, len(FCONST_VALS))))
    return c


def build(dbg=None, paranoid=False, parts=("ln", "gate", "thin", "fc"), hp_count=ND):
    dbg = dbg or []
    parts = set(parts)
    nc = bass.Bass("TRN2", target_bir_lowering=False,
                   detect_race_conditions=paranoid)
    s = Sched()

    x_d = nc.dram_tensor("x", [N, D], F32, kind="ExternalInput").ap()
    wq_d = nc.dram_tensor("wq", [D, D], F32, kind="ExternalInput").ap()
    wk_d = nc.dram_tensor("wk", [D, D], F32, kind="ExternalInput").ap()
    wv_d = nc.dram_tensor("wv", [D, D], F32, kind="ExternalInput").ap()
    wo_d = nc.dram_tensor("wo", [D, D], F32, kind="ExternalInput").ap()
    ww1_d = nc.dram_tensor("ww1", [D, HD], F32, kind="ExternalInput").ap()
    ww2_d = nc.dram_tensor("ww2", [HD, D], F32, kind="ExternalInput").ap()
    wb_d = nc.dram_tensor("wbm", [D, H], F32, kind="ExternalInput").ap()
    wf_d = nc.dram_tensor("wfm", [D, H], F32, kind="ExternalInput").ap()
    delta_bc_d = nc.dram_tensor("delta_bc", [128, H], F32, kind="ExternalInput").ap()
    wg_d = nc.dram_tensor("wg", [D, NE], F32, kind="ExternalInput").ap()
    bg_bc_d = nc.dram_tensor("bg_bc", [128, NE], F32, kind="ExternalInput").ap()
    w1_d = nc.dram_tensor("w1", [NE, D, D], F32, kind="ExternalInput").ap()
    b1_d = nc.dram_tensor("b1", [NE, D], F32, kind="ExternalInput").ap()
    w2_d = nc.dram_tensor("w2", [NE, D, D], F32, kind="ExternalInput").ap()
    b2_d = nc.dram_tensor("b2", [NE, D], F32, kind="ExternalInput").ap()
    cst_d = {}
    consts = host_constants()
    for k, v in consts.items():
        dt = BF16 if v.dtype == BFNP else F32
        cst_d[k] = nc.dram_tensor(f"c_{k}", list(v.shape), dt, kind="ExternalInput").ap()
    out_d = nc.dram_tensor("out", [N, D], F32, kind="ExternalOutput").ap()

    dbg_d = {}
    dbg_bufs = {}

    def dbg_dump(name, ap_fn, shape, dtype, deps):
        if name not in dbg:
            return
        dd = nc.dram_tensor(f"dbg_{name}", shape, dtype, kind="ExternalOutput").ap()
        dbg_d[name] = dd
        s.emit("sync", lambda e, a=ap_fn, d=dd: e.dma_start(out=d, in_=a()),
               deps=deps, inc_k=16)

    cst = {}
    cst_buf = {}
    for k, v in consts.items():
        dt = BF16 if v.dtype == BFNP else F32
        t = nc.alloc_sbuf_tensor(f"sb_{k}", list(v.shape), dt)
        cst[k] = t
        b = Buf()
        cst_buf[k] = b
        s.emit("sync", lambda e, t=t, k=k: e.dma_start(out=t[:, :], in_=cst_d[k][:, :]),
               writes=[b], inc_k=16)

    def CC(name):
        return cst[name], cst_buf[name]

    def fc(v):
        return cst["fconst"][:, FCONST_IDX[v]:FCONST_IDX[v] + 1]

    xn_bf = nc.alloc_sbuf_tensor("xn_bf", [128, ND, N], BF16)
    o_fm = nc.alloc_sbuf_tensor("o_fm", [128, ND, N], BF16)
    y_acc = nc.alloc_sbuf_tensor("y_acc", [128, NT, D], BF16)
    fc_tm = nc.alloc_sbuf_tensor("fc_tm", [128, NT, H], F32)
    sig_tm = nc.alloc_sbuf_tensor("sig_tm", [128, NT, H], F32)
    logf_bf = nc.alloc_sbuf_tensor("logf_bf", [128, NT, H], BF16)
    gmask = nc.alloc_sbuf_tensor("gmask", [128, NT, NE], F32)
    w1o = nc.alloc_sbuf_tensor("w1o", [64, N], BF16)
    wg_sb = nc.alloc_sbuf_tensor("wg_sb", [128, ND, NE], F32)
    wb_sb = nc.alloc_sbuf_tensor("wb_sb", [128, ND, H], BF16)
    wf_sb = nc.alloc_sbuf_tensor("wf_sb", [128, ND, H], BF16)
    ww1_sb = nc.alloc_sbuf_tensor("ww1_sb", [128, ND, HD], BF16)
    delta_bc = nc.alloc_sbuf_tensor("delta_sb", [128, H], F32)
    bg_bc = nc.alloc_sbuf_tensor("bg_sb", [128, NE], F32)

    b_xn = Buf()
    b_yacc_all = Buf()
    b_o = [[Buf() for _ in range(NCH)] for _ in range(H)]
    b_yacc = [Buf() for _ in range(NT)]
    b_fc = Buf()
    b_sig = Buf()
    b_logf = Buf()
    b_gmask = Buf()
    b_w1o = Buf()
    b_smallw = Buf()

    s.emit("gpsimd", lambda e: e.dma_start(
        out=wb_sb[:, :, :], in_=wb_d.rearrange("(t p) h -> p t h", p=128)),
        writes=[b_smallw], inc_k=16)
    s.emit("gpsimd", lambda e: e.dma_start(
        out=wf_sb[:, :, :], in_=wf_d.rearrange("(t p) h -> p t h", p=128)),
        writes=[b_smallw], inc_k=16)
    s.emit("gpsimd", lambda e: e.dma_start(
        out=ww1_sb[:, :, :], in_=ww1_d.rearrange("(t p) h -> p t h", p=128)),
        writes=[b_smallw], inc_k=16)
    s.emit("sync", lambda e: e.dma_start(
        out=wg_sb[:, :, :], in_=wg_d.rearrange("(t p) h -> p t h", p=128)),
        writes=[b_smallw], inc_k=16)
    s.emit("sync", lambda e: e.dma_start(out=delta_bc[:, :], in_=delta_bc_d[:, :]),
           writes=[b_smallw], inc_k=16)
    s.emit("sync", lambda e: e.dma_start(out=bg_bc[:, :], in_=bg_bc_d[:, :]),
           writes=[b_smallw], inc_k=16)

    ps_S = nc.alloc_psum_tensor("ps_S", [128, 2048], F32)
    ps_a = nc.alloc_psum_tensor("ps_a", [128, 512], F32)
    ps_b = nc.alloc_psum_tensor("ps_b", [128, 512], F32)
    ps_c = nc.alloc_psum_tensor("ps_c", [128, 1024], BF16)
    ps_d = nc.alloc_psum_tensor("ps_d", [128, 1024], BF16)
    b_ps4 = [Buf(), Buf(), Buf(), Buf()]
    b_psa = Buf()
    b_psb = Buf()
    b_psc = Buf()
    b_psd = Buf()

    x_slab = nc.alloc_sbuf_tensor("x_slab", [128, 2, D], F32)
    stat = nc.alloc_sbuf_tensor("stat", [128, 16], F32)
    glog = nc.alloc_sbuf_tensor("glog", [128, NT, NE], F32)
    x32_fm = nc.alloc_sbuf_tensor("x32_fm", [128, ND, 128], F32)
    b_xslab = [Buf(), Buf()]
    b_xnslab = [Buf(), Buf()]
    b_x32 = Buf()
    b_stat = Buf()
    b_glog = Buf()

    for tt in range(NT):
        if "ln" not in parts and "ln1" not in parts:
            break
        sl = tt % 2
        xs = x_slab
        ld = s.emit("sync", lambda e, tt=tt, sl=sl: e.dma_start(
            out=x_slab[:, sl, :], in_=x_d.rearrange("(t p) d -> t p d", p=128)[tt]),
            writes=[b_xslab[sl]], inc_k=16)
        s.emit("vector", lambda e, sl=sl: e.reduce_sum(
            stat[:, 0:1], x_slab[:, sl, :], axis=AX.X),
            reads=[b_xslab[sl]], writes=[b_stat])
        s.emit("scalar", lambda e, sl=sl: e.activation(
            y_acc[:, sl, :], x_slab[:, sl, :], AF.Square,
            accum_out=stat[:, 1:2]),
            reads=[b_xslab[sl]], writes=[b_stat, b_yacc_all])
        s.emit("vector", lambda e: e.tensor_scalar(
            stat[:, 2:3], stat[:, 0:1], 1.0 / D, None, ALU.mult),
            reads=[b_stat], writes=[b_stat])
        s.emit("vector", lambda e: e.tensor_tensor(
            stat[:, 3:4], stat[:, 2:3], stat[:, 2:3], ALU.mult),
            reads=[b_stat], writes=[b_stat])
        s.emit("vector", lambda e: e.scalar_tensor_tensor(
            stat[:, 4:5], stat[:, 1:2], 1.0 / D, stat[:, 3:4],
            ALU.mult, ALU.subtract),
            reads=[b_stat], writes=[b_stat])
        s.emit("scalar", lambda e: e.activation(
            stat[:, 5:6], stat[:, 4:5], AF.Sqrt, bias=fc(1e-5)),
            reads=[b_stat, cst_buf["fconst"]], writes=[b_stat])
        s.emit("vector", lambda e: e.reciprocal(stat[:, 6:7], stat[:, 5:6]),
               reads=[b_stat], writes=[b_stat])
        s.emit("vector", lambda e, sl=sl: e.tensor_scalar(
            x_slab[:, sl, :], x_slab[:, sl, :], stat[:, 2:3], stat[:, 6:7],
            ALU.subtract, ALU.mult),
            reads=[b_stat], writes=[b_xslab[sl]])
        for ft in range(ND if ("ln" in parts or "ln2" in parts) else 0):
            pslot = ft % 4
            s.emit("tensor", lambda e, sl=sl, ft=ft, pslot=pslot: e.transpose(
                ps_S[:, pslot * 128:(pslot + 1) * 128],
                x_slab[:, sl, ft * 128:(ft + 1) * 128], cst["id128f"][:, :]),
                reads=[b_xslab[sl], cst_buf["id128f"]], writes=[b_ps4[0]])
            s.emit("vector", lambda e, ft=ft, pslot=pslot: e.tensor_copy(
                x32_fm[:, ft, :], ps_S[:, pslot * 128:(pslot + 1) * 128]),
                writes=[b_ps4[0], b_x32])
            s.emit("scalar", lambda e, tt=tt, ft=ft: e.copy(
                xn_bf[:, ft, tt * 128:(tt + 1) * 128], x32_fm[:, ft, :]),
                reads=[b_x32], writes=[b_xn])
        for ft in range(ND if "gate" in parts else 0):
            s.emit("tensor", lambda e, ft=ft: e.matmul(
                ps_S[:, 512:512 + NE], x32_fm[:, ft, :], wg_sb[:, ft, :],
                start=(ft == 0), stop=(ft == ND - 1)),
                reads=[b_x32, b_smallw], writes=[b_ps4[1]])
        if "gate" not in parts:
            continue
        s.emit("vector", lambda e, tt=tt: e.tensor_tensor(
            glog[:, tt, :], ps_S[:, 512:512 + NE], bg_bc[:, :], ALU.add),
            reads=[b_smallw], writes=[b_ps4[1], b_glog])
        s.emit("vector", lambda e, tt=tt: e.reduce_max(
            stat[:, 7:8], glog[:, tt, :], axis=AX.X),
            reads=[b_glog], writes=[b_stat])
        s.emit("vector", lambda e, tt=tt: e.tensor_scalar(
            gmask[:, tt, :], glog[:, tt, :], stat[:, 7:8], None, ALU.is_ge),
            reads=[b_glog, b_stat], writes=[b_gmask])
        if "thin" not in parts:
            continue
        for ft in range(ND):
            s.emit("tensor", lambda e, tt=tt, ft=ft: e.matmul(
                ps_a[:, 0:H], xn_bf[:, ft, tt * 128:(tt + 1) * 128], wb_sb[:, ft, :],
                start=(ft == 0), stop=(ft == ND - 1)),
                reads=[b_xn, b_smallw], writes=[b_psa])
        s.emit("scalar", lambda e, tt=tt: e.activation(
            sig_tm[:, tt, :], ps_a[:, 0:H], AF.Sigmoid),
            writes=[b_psa, b_sig])
        for ft in range(ND):
            s.emit("tensor", lambda e, tt=tt, ft=ft: e.matmul(
                ps_a[:, 128:128 + H], xn_bf[:, ft, tt * 128:(tt + 1) * 128], wf_sb[:, ft, :],
                start=(ft == 0), stop=(ft == ND - 1)),
                reads=[b_xn, b_smallw], writes=[b_psa])
        s.emit("vector", lambda e: e.tensor_tensor(
            ps_a[:, 128:128 + H], ps_a[:, 128:128 + H], delta_bc[:, :], ALU.add),
            reads=[b_smallw], writes=[b_psa])
        s.emit("scalar", lambda e, tt=tt: e.activation(
            x_slab[:, tt % 2, 0:H], ps_a[:, 128:128 + H], AF.Sigmoid),
            writes=[b_psa, b_xslab[tt % 2]])
        s.emit("scalar", lambda e, tt=tt: e.activation(
            logf_bf[:, tt, :], x_slab[:, tt % 2, 0:H], AF.Ln),
            reads=[b_xslab[tt % 2]], writes=[b_logf])
        for ft in range(ND):
            s.emit("tensor", lambda e, tt=tt, ft=ft: e.matmul(
                ps_b[0:64, 0:128], ww1_sb[:, ft, :], xn_bf[:, ft, tt * 128:(tt + 1) * 128],
                start=(ft == 0), stop=(ft == ND - 1)),
                reads=[b_xn, b_smallw], writes=[b_psb])
        s.emit("scalar", lambda e, tt=tt: e.copy(
            w1o[:, tt * 128:(tt + 1) * 128], ps_b[0:64, 0:128]),
            writes=[b_psb, b_w1o])

    for tt in range(NT if ("fc" in parts and "thin" in parts) else 0):
        for j in range(tt + 1):
            lhs = cst["cumtri"] if j == tt else cst["allones"]
            lb = cst_buf["cumtri"] if j == tt else cst_buf["allones"]
            s.emit("tensor", lambda e, tt=tt, j=j, lhs=lhs: e.matmul(
                ps_S[:, 1024:1024 + H], lhs[:, :], logf_bf[:, j, :],
                start=(j == 0), stop=(j == tt)),
                reads=[b_logf, lb], writes=[b_ps4[2]])
        s.emit("vector", lambda e, tt=tt: e.tensor_copy(
            fc_tm[:, tt, :], ps_S[:, 1024:1024 + H]),
            writes=[b_ps4[2], b_fc])

    dbg_dump("xn_bf", lambda: xn_bf[:, :, :], [128, ND, N], BF16, [b_xn.w])
    dbg_dump("glog", lambda: glog[:, :, :], [128, NT, NE], F32, [b_glog.w])
    dbg_dump("gmask", lambda: gmask[:, :, :], [128, NT, NE], F32, [b_gmask.w])
    dbg_dump("sig_tm", lambda: sig_tm[:, :, :], [128, NT, H], F32, [b_sig.w])
    dbg_dump("logf_bf", lambda: logf_bf[:, :, :], [128, NT, H], BF16, [b_logf.w])
    dbg_dump("fc_tm", lambda: fc_tm[:, :, :], [128, NT, H], F32, [b_fc.w])
    dbg_dump("w1o", lambda: w1o[:, :], [64, N], BF16, [b_w1o.w])



    wo_sb = nc.alloc_sbuf_tensor("wo_sb", [128, ND, D], BF16)
    b_wo = Buf()
    s.emit("gpsimd", lambda e: e.dma_start(
        out=wo_sb[:, :, :], in_=wo_d.rearrange("(t p) d -> p t d", p=128)),
        writes=[b_wo], inc_k=16)
    s.emit("gpsimd", lambda e: e.memset(y_acc[:, :, :], 0.0), writes=[b_yacc_all])

    st2 = nc.alloc_sbuf_tensor("st2", [128, 24], F32)
    _attn_guards = []

    def _ga(name, shape, dtype):
        g = nc.sbuf_tensor(name, shape, dtype)
        _attn_guards.append(g)
        return g.__enter__()

    q_fm = _ga("q_fm", [128, N], BF16)
    k_fm = _ga("k_fm", [128, N], BF16)
    w_fm = _ga("w_fm", [128, N], BF16)
    wb_fm = _ga("wb_fm", [128, N], BF16)
    w_tm = _ga("w_tm", [128, NT, 128], BF16)
    wb_tm = _ga("wb_tm", [128, NT, 128], BF16)
    v_aug = _ga("v_aug", [128, NT, 144], BF16)
    kt = _ga("kt", [128, 2, N], BF16)
    qhat_k = _ga("qhat_k", [128, 2, 128], BF16)
    sii_k = _ga("sii_k", [128, 2, 2, 128], BF16)
    pt_k = _ga("pt_k", [128, 2, 64], BF16)
    ktd_k = _ga("ktd_k", [128, 2, 128], BF16)
    Lb = _ga("Lb", [128, 2, 128], BF16)
    Nb = _ga("Nb", [128, 2, 128], BF16)
    Mb = _ga("Mb", [128, 2, 128], BF16)
    nGt = _ga("nGt", [128, 2, 128], BF16)
    npow = _ga("npow", [128, 2, 6, 128], BF16)
    lkb = _ga("lkb", [128, 2, 128], BF16)
    zb = _ga("zb", [128, 2, 2, 192], BF16)
    s_sb = _ga("s_sb", [128, 2, N], BF16)
    pT_sb = _ga("pT_sb", [128, 2, N], BF16)
    o_tmp = _ga("o_tmp", [128, 2, 64], BF16)
    fcol = _ga("fcol", [128, 2, NT], F32)
    fcmr = _ga("fcmr", [128, 2, NT], F32)
    st3 = _ga("st3", [128, 2, 8], F32)
    wq_sl = _ga("wq_sl", [128, ND, 128], BF16)
    wk_sl = _ga("wk_sl", [128, ND, 128], BF16)
    wv_sl = _ga("wv_sl", [128, ND, 128], BF16)
    ww2_sl = _ga("ww2_sl", [64, 128], BF16)

    b_q = Buf(); b_k = Buf(); b_wfm = Buf(); b_wbfm = Buf()
    b_wtm = Buf(); b_wbtm = Buf(); b_vaug = Buf()
    b_kt = [[Buf(), Buf()], [Buf(), Buf()]]
    b_qh = [Buf(), Buf()]; b_sii = [Buf(), Buf()]
    b_pt = [Buf(), Buf()]; b_ktd = [Buf(), Buf()]
    b_L = [Buf(), Buf()]; b_N = [Buf(), Buf()]; b_M = [Buf(), Buf()]
    b_nGt = [Buf(), Buf()]; b_np = [Buf(), Buf()]; b_lk = [Buf(), Buf()]
    b_z = [[Buf(), Buf()], [Buf(), Buf()]]
    b_s = [Buf(), Buf()]; b_pT = [Buf(), Buf()]; b_otmp = [Buf(), Buf()]
    b_fcol = [Buf(), Buf()]; b_fcmr = [Buf(), Buf()]
    b_st2 = Buf(); b_st3 = [Buf(), Buf()]
    b_wqsl = Buf(); b_wksl = Buf(); b_wvsl = Buf(); b_ww2sl = Buf()
    b_ofm = Buf()

    def emit_proj(hp):
        c0 = hp * 128
        s.emit("gpsimd", lambda e, c0=c0: e.dma_start(
            out=wq_sl[:, :, :],
            in_=wq_d[:, c0:c0 + 128].rearrange("(t p) c -> p t c", p=128)),
            writes=[b_wqsl], inc_k=16)
        s.emit("gpsimd", lambda e, c0=c0: e.dma_start(
            out=wk_sl[:, :, :],
            in_=wk_d[:, c0:c0 + 128].rearrange("(t p) c -> p t c", p=128)),
            writes=[b_wksl], inc_k=16)
        s.emit("gpsimd", lambda e, c0=c0: e.dma_start(
            out=wv_sl[:, :, :],
            in_=wv_d[:, c0:c0 + 128].rearrange("(t p) c -> p t c", p=128)),
            writes=[b_wvsl], inc_k=16)
        s.emit("gpsimd", lambda e, c0=c0: e.dma_start(
            out=ww2_sl[:, :], in_=ww2_d[:, c0:c0 + 128]),
            writes=[b_ww2sl], inc_k=16)
        s.emit("gpsimd", lambda e: e.memset(v_aug[:, :, 64:72], 0.0), writes=[b_vaug])
        s.emit("gpsimd", lambda e: e.memset(v_aug[:, :, 136:144], 0.0), writes=[b_vaug])
        s.emit("gpsimd", lambda e: e.memset(v_aug[:, :, 64:65], 1.0), writes=[b_vaug])
        s.emit("gpsimd", lambda e: e.memset(v_aug[:, :, 136:137], 1.0), writes=[b_vaug])
        for dst, wsl, bw, bdst in ((q_fm, wq_sl, b_wqsl, b_q), (k_fm, wk_sl, b_wksl, b_k)):
            for t4 in range(4):
                for ft in range(ND):
                    s.emit("tensor", lambda e, dst=dst, wsl=wsl, t4=t4, ft=ft: e.matmul(
                        ps_a[:, 0:512], wsl[:, ft, :],
                        xn_bf[:, ft, t4 * 512:(t4 + 1) * 512],
                        start=(ft == 0), stop=(ft == ND - 1)),
                        reads=[bw, b_xn], writes=[b_psa])
                s.emit("scalar", lambda e, dst=dst, t4=t4: e.copy(
                    dst[:, t4 * 512:(t4 + 1) * 512], ps_a[:, 0:512]),
                    writes=[b_psa, bdst])
        for tt in range(NT):
            for ft in range(ND):
                s.emit("tensor", lambda e, tt=tt, ft=ft: e.matmul(
                    ps_S[:, 1024:1152], xn_bf[:, ft, tt * 128:(tt + 1) * 128],
                    wv_sl[:, ft, :], start=(ft == 0), stop=(ft == ND - 1)),
                    reads=[b_wvsl, b_xn], writes=[b_ps4[2]])
            s.emit("vector", lambda e, tt=tt: e.tensor_copy(
                bass.AP(v_aug, tt * 144, [[v_aug.shape[2] * NT, 128], [72, 2], [1, 64]]),
                ps_S[:, 1024:1152]),
                writes=[b_ps4[2], b_vaug])
            wb_ = tt % 2
            woff = wb_ * 512
            s.emit("tensor", lambda e, tt=tt, woff=woff: e.matmul(
                ps_S[:, woff:woff + 128], w1o[:, tt * 128:(tt + 1) * 128], ww2_sl[:, :],
                start=True, stop=True),
                reads=[b_w1o, b_ww2sl], writes=[b_ps4[wb_]])
            for half in range(2):
                s.emit("scalar", lambda e, tt=tt, half=half, woff=woff, wb_=wb_: e.activation(
                    x_slab[:, wb_, half * 64:(half + 1) * 64],
                    ps_S[:, woff + half * 64:woff + (half + 1) * 64], AF.Square,
                    accum_out=st2[:, 8 + wb_ * 8 + half:9 + wb_ * 8 + half]),
                    writes=[b_ps4[wb_], b_st2, b_xslab[wb_]])
            s.emit("scalar", lambda e, wb_=wb_: e.activation(
                st2[:, 10 + wb_ * 8:12 + wb_ * 8], st2[:, 8 + wb_ * 8:10 + wb_ * 8],
                AF.Sqrt, bias=fc(1e-6)),
                reads=[cst_buf["fconst"]], writes=[b_st2])
            s.emit("vector", lambda e, wb_=wb_: e.reciprocal(
                st2[:, 12 + wb_ * 8:14 + wb_ * 8], st2[:, 10 + wb_ * 8:12 + wb_ * 8]),
                writes=[b_st2])
            s.emit("vector", lambda e, tt=tt, hp=hp, wb_=wb_: e.scalar_tensor_tensor(
                st2[:, 14 + wb_ * 8:16 + wb_ * 8], sig_tm[:, tt, 2 * hp:2 * hp + 2], 2.0,
                st2[:, 12 + wb_ * 8:14 + wb_ * 8], ALU.mult, ALU.mult),
                reads=[b_sig], writes=[b_st2])
            for half in range(2):
                s.emit("vector", lambda e, tt=tt, half=half, woff=woff, wb_=wb_: e.tensor_scalar(
                    w_tm[:, tt, half * 64:(half + 1) * 64],
                    ps_S[:, woff + half * 64:woff + (half + 1) * 64],
                    st2[:, 12 + wb_ * 8 + half:13 + wb_ * 8 + half], None, ALU.mult),
                    reads=[b_st2], writes=[b_ps4[wb_], b_wtm])
                s.emit("vector", lambda e, tt=tt, half=half, woff=woff, wb_=wb_: e.tensor_scalar(
                    wb_tm[:, tt, half * 64:(half + 1) * 64],
                    ps_S[:, woff + half * 64:woff + (half + 1) * 64],
                    st2[:, 14 + wb_ * 8 + half:15 + wb_ * 8 + half], None, ALU.mult),
                    reads=[b_st2], writes=[b_ps4[wb_], b_wbtm])
            s.emit("tensor", lambda e, tt=tt: e.transpose(
                ps_c[:, 512:640], w_tm[:, tt, :], cst["id128"][:, :]),
                reads=[b_wtm, cst_buf["id128"]], writes=[b_psc])
            s.emit("scalar", lambda e, tt=tt: e.copy(
                w_fm[:, tt * 128:(tt + 1) * 128], ps_c[:, 512:640]),
                writes=[b_psc, b_wfm])
            s.emit("tensor", lambda e, tt=tt: e.transpose(
                ps_c[:, 640:768], wb_tm[:, tt, :], cst["id128"][:, :]),
                reads=[b_wbtm, cst_buf["id128"]], writes=[b_psc])
            s.emit("scalar", lambda e, tt=tt: e.copy(
                wb_fm[:, tt * 128:(tt + 1) * 128], ps_c[:, 640:768]),
                writes=[b_psc, b_wbfm])

    def emit_head_prep(h, ph):
        bk = 2 + ph
        s.emit("tensor", lambda e, h=h, bk=bk: e.matmul(
            ps_S[:, bk * 512 + 256:bk * 512 + 257], cst["ones_row_f32"][:, :],
            fc_tm[0:1, 8, h:h + 1], start=True, stop=True),
            reads=[b_fc, cst_buf["ones_row_f32"]], writes=[b_ps4[bk]])
        s.emit("vector", lambda e, ph=ph, bk=bk: e.tensor_copy(
            st3[:, ph, 0:1], ps_S[:, bk * 512 + 256:bk * 512 + 257]),
            writes=[b_ps4[bk], b_st3[ph]])
        s.emit("vector", lambda e, ph=ph: e.tensor_scalar(
            st3[:, ph, 1:2], st3[:, ph, 0:1], -1.0, None, ALU.mult),
            writes=[b_st3[ph]])
        s.emit("scalar", lambda e, h=h, ph=ph: e.activation(
            fcol[:, ph, :], bass.AP(fc_tm, h, [[NT * H, 128], [H, NT], [1, 1]]),
            AF.Exp, bias=st3[:, ph, 0:1], scale=fc(-1.0)),
            reads=[b_fc, b_st3[ph], cst_buf["fconst"]], writes=[b_fcol[ph]])
        s.emit("vector", lambda e, h=h, ph=ph: e.tensor_scalar(
            fcmr[:, ph, :], bass.AP(fc_tm, h, [[NT * H, 128], [H, NT], [1, 1]]),
            st3[:, ph, 1:2], None, ALU.add),
            reads=[b_fc, b_st3[ph]], writes=[b_fcmr[ph]])

    def emit_phaseA(h, ph, i):
        psA = ps_a if ph == 0 else ps_b
        bA = b_psa if ph == 0 else b_psb
        r0 = ph * 64
        isl = slice(i * 128, (i + 1) * 128)
        W_I = w_fm[r0:r0 + 64, isl]
        Wb_I = wb_fm[r0:r0 + 64, isl]
        Q_I = q_fm[r0:r0 + 64, isl]
        K_I = k_fm[r0:r0 + 64, isl]
        Wtm_I = w_tm[:, i, r0:r0 + 64]
        Wbtm_I = wb_tm[:, i, r0:r0 + 64]
        s.emit("tensor", lambda e: e.matmul(psA[:, 0:128], W_I, Wb_I,
               start=True, stop=True, tile_position=(r0, 0)),
               reads=[b_wfm, b_wbfm], writes=[bA])
        s.emit("vector", lambda e: e.tensor_tensor(
            Lb[:, ph, :], psA[:, 0:128], cst["m_trils"][:, :], ALU.mult),
            reads=[cst_buf["m_trils"]], writes=[bA, b_L[ph]])
        s.emit("tensor", lambda e: e.matmul(psA[:, 128:256], Wb_I, W_I,
               start=True, stop=True, tile_position=(r0, 0)),
               reads=[b_wfm, b_wbfm], writes=[bA])
        s.emit("vector", lambda e: e.tensor_tensor(
            Nb[:, ph, :], psA[:, 128:256], cst["m_trius"][:, :], ALU.mult),
            reads=[cst_buf["m_trius"]], writes=[bA, b_N[ph]])
        s.emit("tensor", lambda e: e.matmul(psA[:, 256:384], W_I, K_I,
               start=True, stop=True, tile_position=(r0, 0)),
               reads=[b_wfm, b_k], writes=[bA])
        s.emit("vector", lambda e: e.tensor_tensor(
            Mb[:, ph, :], psA[:, 256:384], cst["m_trils"][:, :], ALU.mult),
            reads=[cst_buf["m_trils"]], writes=[bA, b_M[ph]])
        s.emit("tensor", lambda e: e.matmul(psA[:, 384:512], Wb_I, Q_I,
               start=True, stop=True, tile_position=(r0, 0)),
               reads=[b_wbfm, b_q], writes=[bA])
        s.emit("vector", lambda e: e.tensor_tensor(
            nGt[:, ph, :], psA[:, 384:512], cst["m_negtriu"][:, :], ALU.mult),
            reads=[cst_buf["m_negtriu"]], writes=[bA, b_nGt[ph]])

        def lp(lev):
            return Lb[:, ph, :] if lev == 0 else lkb[:, ph, :]

        def np_(lev):
            return Nb[:, ph, :] if lev == 0 else npow[:, ph, lev - 1, :]

        for lev in range(6):
            bl = b_L[ph] if lev == 0 else b_lk[ph]
            bn = b_N[ph] if lev == 0 else b_np[ph]
            s.emit("tensor", lambda e, lev=lev: e.matmul(
                psA[:, 128:256], lp(lev), np_(lev), start=True, stop=True),
                reads=[bl, bn], writes=[bA])
            s.emit("scalar", lambda e, lev=lev: e.copy(
                npow[:, ph, lev, :], psA[:, 128:256]),
                writes=[bA, b_np[ph]])
            if lev < 5:
                s.emit("tensor", lambda e, lev=lev: e.matmul(
                    psA[:, 0:128], np_(lev), lp(lev), start=True, stop=True),
                    reads=[bl, bn], writes=[bA])
                s.emit("vector", lambda e: e.tensor_copy(
                    lkb[:, ph, :], psA[:, 0:128]),
                    writes=[bA, b_lk[ph]])
        zc = 0
        s.emit("tensor", lambda e: e.matmul(psA[:, 0:64], npow[:, ph, 5, :], Wtm_I,
               start=True, stop=False), reads=[b_np[ph], b_wtm], writes=[bA])
        s.emit("tensor", lambda e: e.matmul(psA[:, 64:192], npow[:, ph, 5, :], Mb[:, ph, :],
               start=False, stop=True), reads=[b_np[ph], b_M[ph]], writes=[bA])
        s.emit("vector", lambda e: e.tensor_tensor(
            zb[:, ph, 0, 0:64], Wtm_I, psA[:, 0:64], ALU.add),
            reads=[b_wtm], writes=[bA, b_z[ph][0]])
        s.emit("vector", lambda e: e.tensor_tensor(
            zb[:, ph, 0, 64:192], Mb[:, ph, :], psA[:, 64:192], ALU.add),
            reads=[b_M[ph]], writes=[bA, b_z[ph][0]])
        for lev in (4, 3, 2, 1, 0):
            zn = 1 - zc
            s.emit("tensor", lambda e, lev=lev, zc=zc: e.matmul(
                psA[:, 0:192], npow[:, ph, lev, :], zb[:, ph, zc, :],
                start=True, stop=True), reads=[b_np[ph], b_z[ph][zc]], writes=[bA])
            s.emit("vector", lambda e, zc=zc, zn=zn: e.tensor_tensor(
                zb[:, ph, zn, :], zb[:, ph, zc, :], psA[:, 0:192], ALU.add),
                reads=[b_z[ph][zc]], writes=[bA, b_z[ph][zn]])
            zc = zn
        zn = 1 - zc
        s.emit("tensor", lambda e, zc=zc: e.matmul(
            psA[:, 0:192], Nb[:, ph, :], zb[:, ph, zc, :],
            start=True, stop=True), reads=[b_N[ph], b_z[ph][zc]], writes=[bA])
        s.emit("vector", lambda e, zc=zc, zn=zn: e.tensor_tensor(
            zb[:, ph, zn, :], zb[:, ph, zc, :], psA[:, 0:192], ALU.subtract),
            reads=[b_z[ph][zc]], writes=[bA, b_z[ph][zn]])
        zf = zn
        TW = zb[:, ph, zf, 0:64]
        CI2 = zb[:, ph, zf, 64:192]
        bZ = b_z[ph][zf]
        s.emit("tensor", lambda e: e.matmul(psA[:, 256:384], Q_I, K_I,
               start=True, stop=False, tile_position=(r0, 0)),
               reads=[b_q, b_k], writes=[bA])
        s.emit("tensor", lambda e: e.matmul(psA[:, 256:384], nGt[:, ph, :], CI2,
               start=False, stop=True), reads=[b_nGt[ph], bZ], writes=[bA])
        s.emit("scalar", lambda e, i=i, ph=ph: e.copy(
            sii_k[:, ph, i % 2, :], psA[:, 256:384]),
            writes=[bA, b_sii[ph]])
        s.emit("tensor", lambda e: e.matmul(
            psA[r0:r0 + 64, 0:128], TW, nGt[:, ph, :],
            start=True, stop=True, tile_position=(0, r0)),
            reads=[bZ, b_nGt[ph]], writes=[bA])
        s.emit("vector", lambda e, i=i: e.tensor_tensor(
            qhat_k[r0:r0 + 64, i % 2, :], Q_I, psA[r0:r0 + 64, 0:128], ALU.add),
            reads=[b_q], writes=[bA, b_qh[ph]])
        s.emit("tensor", lambda e: e.matmul(
            psA[r0:r0 + 64, 128:192], TW, Wbtm_I,
            start=True, stop=True, tile_position=(0, r0)),
            reads=[bZ, b_wbtm], writes=[bA])
        s.emit("vector", lambda e, i=i: e.tensor_tensor(
            pt_k[r0:r0 + 64, i % 2, :], cst["i64stack"][r0:r0 + 64, :],
            psA[r0:r0 + 64, 128:192], ALU.subtract),
            reads=[cst_buf["i64stack"]], writes=[bA, b_pt[ph]])
        s.emit("tensor", lambda e: e.matmul(
            psA[r0:r0 + 64, 256:384], Wbtm_I, CI2,
            start=True, stop=True, tile_position=(0, r0)),
            reads=[b_wbtm, bZ], writes=[bA])
        s.emit("vector", lambda e, i=i: e.tensor_tensor(
            ktd_k[r0:r0 + 64, i % 2, :], K_I, psA[r0:r0 + 64, 256:384], ALU.subtract),
            reads=[b_k], writes=[bA, b_ktd[ph]])

    def emit_phaseB_chunk(h, ph, i):
        r0 = ph * 64
        bk = 2 + ph
        sk = ph
        psc = ps_c if ph == 0 else ps_d
        bpsc = b_psc if ph == 0 else b_psd
        rbuf = (i + 1) % 2
        wbuf = i % 2
        jend = (i + 1) * 128
        nj5 = (i * 128 + 511) // 512
        for j5 in range(nj5):
            w = min(512, i * 128 - j5 * 512)
            s.emit("tensor", lambda e, i=i, j5=j5, w=w, rbuf=rbuf: e.matmul(
                ps_S[:, sk * 512:sk * 512 + w],
                qhat_k[r0:r0 + 64, i % 2, :], kt[r0:r0 + 64, rbuf, j5 * 512:j5 * 512 + w],
                start=True, stop=True, tile_position=(r0, 0)),
                reads=[b_qh[ph], b_kt[ph][rbuf]], writes=[b_ps4[sk]])
            s.emit("vector", lambda e, j5=j5, w=w, ph=ph: e.tensor_copy(
                s_sb[:, ph, j5 * 512:j5 * 512 + w], ps_S[:, sk * 512:sk * 512 + w]),
                writes=[b_ps4[sk], b_s[ph]])
        s.emit("vector", lambda e, i=i, ph=ph: e.tensor_tensor(
            s_sb[:, ph, i * 128:(i + 1) * 128], sii_k[:, ph, i % 2, :],
            cst["m_addneg"][:, :], ALU.add),
            reads=[b_sii[ph], cst_buf["m_addneg"]], writes=[b_s[ph]])
        s.emit("vector", lambda e, jend=jend, ph=ph: e.reduce_max(
            st3[:, ph, 2:3], s_sb[:, ph, 0:jend], axis=AX.X),
            reads=[b_s[ph]], writes=[b_st3[ph]])
        s.emit("vector", lambda e, i=i, ph=ph: e.tensor_scalar(
            st3[:, ph, 3:4], st3[:, ph, 2:3], -SCALE, fcmr[:, ph, i:i + 1],
            ALU.mult, ALU.add),
            reads=[b_fcmr[ph]], writes=[b_st3[ph]])
        s.emit("scalar", lambda e, jend=jend, ph=ph: e.activation(
            s_sb[:, ph, 0:jend], s_sb[:, ph, 0:jend], AF.Exp,
            bias=st3[:, ph, 3:4], scale=fc(SCALE)),
            reads=[b_st3[ph], cst_buf["fconst"]], writes=[b_s[ph]])
        for jt in range(i + 1):
            slot = jt % 7
            s.emit("tensor", lambda e, jt=jt, slot=slot, ph=ph, psc=psc: e.transpose(
                psc[:, slot * 128:(slot + 1) * 128],
                s_sb[:, ph, jt * 128:(jt + 1) * 128], cst["id128"][:, :]),
                reads=[b_s[ph], cst_buf["id128"]], writes=[bpsc])
            s.emit("vector", lambda e, jt=jt, slot=slot, ph=ph, psc=psc: e.tensor_scalar(
                pT_sb[:, ph, jt * 128:(jt + 1) * 128],
                psc[:, slot * 128:(slot + 1) * 128],
                fcol[:, ph, jt:jt + 1], None, ALU.mult),
                reads=[b_fcol[ph]], writes=[bpsc, b_pT[ph]])
            s.emit("tensor", lambda e, jt=jt, i=i, ph=ph, bk=bk: e.matmul(
                ps_S[:, bk * 512:bk * 512 + 72], pT_sb[:, ph, jt * 128:(jt + 1) * 128],
                v_aug[:, jt, ph * 72:ph * 72 + 72],
                start=(jt == 0), stop=(jt == i)),
                reads=[b_pT[ph], b_vaug], writes=[b_ps4[bk]])
        s.emit("vector", lambda e, ph=ph, bk=bk: e.reciprocal(
            st3[:, ph, 4:5], ps_S[:, bk * 512 + 64:bk * 512 + 65]),
            writes=[b_ps4[bk], b_st3[ph]])
        s.emit("vector", lambda e, ph=ph, bk=bk: e.tensor_scalar(
            o_tmp[:, ph, :], ps_S[:, bk * 512:bk * 512 + 64], st3[:, ph, 4:5],
            None, ALU.mult),
            reads=[b_st3[ph]], writes=[b_ps4[bk], b_otmp[ph]])
        s.emit("tensor", lambda e, ph=ph, psc=psc: e.transpose(
            psc[r0:r0 + 64, 896:1024], o_tmp[:, ph, :], cst["id128"][:, :],
            tile_position=(0, r0)),
            reads=[b_otmp[ph], cst_buf["id128"]], writes=[bpsc])
        s.emit("scalar", lambda e, i=i, h=h, ph=ph, psc=psc: e.copy(
            o_fm[r0:r0 + 64, h // 2, i * 128:(i + 1) * 128],
            psc[r0:r0 + 64, 896:1024]),
            writes=[bpsc, b_ofm])
        for j5 in range(nj5):
            w = min(512, i * 128 - j5 * 512)
            s.emit("tensor", lambda e, i=i, j5=j5, w=w, rbuf=rbuf, bk=bk: e.matmul(
                ps_S[r0:r0 + 64, bk * 512:bk * 512 + w], pt_k[r0:r0 + 64, i % 2, :],
                kt[r0:r0 + 64, rbuf, j5 * 512:j5 * 512 + w],
                start=True, stop=True, tile_position=(r0, r0)),
                reads=[b_pt[ph], b_kt[ph][rbuf]], writes=[b_ps4[bk]])
            s.emit("scalar", lambda e, j5=j5, w=w, wbuf=wbuf, bk=bk: e.copy(
                kt[r0:r0 + 64, wbuf, j5 * 512:j5 * 512 + w],
                ps_S[r0:r0 + 64, bk * 512:bk * 512 + w]),
                writes=[b_ps4[bk], b_kt[ph][wbuf]])
        s.emit("gpsimd", lambda e, i=i, wbuf=wbuf: e.tensor_copy(
            kt[r0:r0 + 64, wbuf, i * 128:(i + 1) * 128],
            ktd_k[r0:r0 + 64, i % 2, :]),
            reads=[b_ktd[ph]], writes=[b_kt[ph][wbuf]])

    def defer(fn):
        rec = []
        real = s.emit
        s.emit = lambda *a, **k: rec.append((a, k))
        fn()
        s.emit = real
        return rec

    def play(recs):
        idx = [0] * len(recs)
        while True:
            done = True
            for r in range(len(recs)):
                if idx[r] < len(recs[r]):
                    a, k = recs[r][idx[r]]
                    s.emit(*a, **k)
                    idx[r] += 1
                    done = False
            if done:
                break

    for hp in range(hp_count):
        emit_proj(hp)
        h0, h1 = 2 * hp, 2 * hp + 1
        emit_head_prep(h0, 0)
        emit_head_prep(h1, 1)
        play([defer(lambda: emit_phaseA(h0, 0, 0)),
              defer(lambda: emit_phaseA(h1, 1, 0))])
        for i in range(NCH):
            recs = [defer(lambda i=i: emit_phaseB_chunk(h0, 0, i)),
                    defer(lambda i=i: emit_phaseB_chunk(h1, 1, i))]
            if i + 1 < NCH:
                recs.append(defer(lambda i=i: emit_phaseA(h0, 0, i + 1)))
                recs.append(defer(lambda i=i: emit_phaseA(h1, 1, i + 1)))
            play(recs)

    dbg_dump("o_fm", lambda: o_fm[:, :, :], [128, ND, N], BF16, [b_ofm.w])

    attn_bufs = ([b_q, b_k, b_wfm, b_wbfm, b_wtm, b_wbtm, b_vaug]
                 + b_kt[0] + b_kt[1] + b_qh + b_sii + b_pt + b_ktd
                 + b_L + b_N + b_M + b_nGt + b_np + b_lk
                 + b_z[0] + b_z[1] + b_s + b_pT + b_otmp
                 + b_fcol + b_fcmr + b_st3
                 + [b_wqsl, b_wksl, b_wvsl, b_ww2sl])
    barrier = []
    for b in attn_bufs:
        if b.w is not None:
            barrier.append(b.w)
        barrier.extend(b.readers)
    barrier = list({id(t): t for t in barrier}.values())
    for g in reversed(_attn_guards):
        g.__exit__(None, None, None)

    w1_sl = [nc.alloc_sbuf_tensor(f"w1_sl{i}", [128, ND, D], BF16) for i in range(2)]
    w2_sl = nc.alloc_sbuf_tensor("w2_sl", [128, ND, D], BF16)
    g_slab = nc.alloc_sbuf_tensor("g_slab", [128, ND, 512], BF16)
    b1_sb = nc.alloc_sbuf_tensor("b1_sb", [1, D], BF16)
    b2_sb = nc.alloc_sbuf_tensor("b2_sb", [1, D], BF16)
    b_w1 = [Buf(), Buf()]
    b_w2h = [Buf(), Buf()]
    b_g = Buf()
    b_b1 = Buf()
    b_b2 = Buf()

    def emit_ffn_expert(e_idx):
        sl = e_idx % 2
        if e_idx == 0:
            s.emit("gpsimd", lambda e: e.dma_start(
                out=w1_sl[0][:, :, :],
                in_=w1_d[0].rearrange("(t p) i -> p t i", p=128)),
                writes=[b_w1[0]], deps=barrier, inc_k=16)
        if e_idx + 1 < NE:
            s.emit("gpsimd", lambda e, e_idx=e_idx: e.dma_start(
                out=w1_sl[(e_idx + 1) % 2][:, :, :],
                in_=w1_d[e_idx + 1].rearrange("(t p) i -> p t i", p=128)),
                writes=[b_w1[(e_idx + 1) % 2]],
                deps=(barrier if e_idx == 0 else ()), inc_k=16)
        for dh_ in range(2):
            s.emit("gpsimd", lambda e, e_idx=e_idx, dh_=dh_: e.dma_start(
                out=w2_sl[:, :, dh_ * 512:(dh_ + 1) * 512],
                in_=w2_d[e_idx][:, dh_ * 512:(dh_ + 1) * 512].rearrange(
                    "(t p) i -> p t i", p=128)),
                writes=[b_w2h[dh_]], deps=(barrier if e_idx == 0 else ()), inc_k=16)
        s.emit("gpsimd", lambda e, e_idx=e_idx: e.dma_start(
            out=b1_sb[:, :], in_=b1_d[e_idx:e_idx + 1, :]),
            writes=[b_b1], deps=(barrier if e_idx == 0 else ()), inc_k=16)
        s.emit("gpsimd", lambda e, e_idx=e_idx: e.dma_start(
            out=b2_sb[:, :], in_=b2_d[e_idx:e_idx + 1, :]),
            writes=[b_b2], deps=(barrier if e_idx == 0 else ()), inc_k=16)
        for t4 in range(4):
            for it in range(ND):
                psh = ps_a if it % 2 == 0 else ps_b
                bph = b_psa if it % 2 == 0 else b_psb
                for ft in range(ND):
                    s.emit("tensor", lambda e, sl=sl, t4=t4, it=it, ft=ft, psh=psh: e.matmul(
                        psh[:, 0:512], w1_sl[sl][:, ft, it * 128:(it + 1) * 128],
                        xn_bf[:, ft, t4 * 512:(t4 + 1) * 512],
                        start=(ft == 0), stop=False),
                        reads=[b_w1[sl], b_xn], writes=[bph])
                s.emit("tensor", lambda e, it=it, psh=psh: e.matmul(
                    psh[:, 0:512], b1_sb[0:1, it * 128:(it + 1) * 128],
                    cst["ones_row_bf"][0:1, 0:512],
                    start=False, stop=True),
                    reads=[b_b1, cst_buf["ones_row_bf"]], writes=[bph])
                s.emit("scalar", lambda e, it=it, psh=psh: e.activation(
                    g_slab[:, it, :], psh[:, 0:512], AF.Gelu),
                    writes=[bph, b_g])
            for dh in range(2):
                for tsub in range(4):
                    tt = t4 * 4 + tsub
                    pb = tsub % 2 + 2 * dh
                    for it in range(ND):
                        s.emit("tensor", lambda e, tsub=tsub, dh=dh, it=it, pb=pb: e.matmul(
                            ps_S[:, pb * 512:pb * 512 + 512],
                            g_slab[:, it, tsub * 128:(tsub + 1) * 128],
                            w2_sl[:, it, dh * 512:(dh + 1) * 512],
                            start=(it == 0), stop=False),
                            reads=[b_g, b_w2h[dh]], writes=[b_ps4[pb]])
                    s.emit("tensor", lambda e, dh=dh, pb=pb: e.matmul(
                        ps_S[:, pb * 512:pb * 512 + 512],
                        cst["ones_row_bf"][0:1, 0:128],
                        b2_sb[0:1, dh * 512:(dh + 1) * 512],
                        start=False, stop=True),
                        reads=[b_b2, cst_buf["ones_row_bf"]], writes=[b_ps4[pb]])
                    s.emit("vector", lambda e, tt=tt, dh=dh, e_idx=e_idx, pb=pb: e.scalar_tensor_tensor(
                        y_acc[:, tt, dh * 512:(dh + 1) * 512],
                        ps_S[:, pb * 512:pb * 512 + 512],
                        gmask[:, tt, e_idx:e_idx + 1],
                        y_acc[:, tt, dh * 512:(dh + 1) * 512],
                        ALU.mult, ALU.add),
                        reads=[b_gmask, b_yacc_all], writes=[b_ps4[pb], b_yacc[tt]])

    for e_idx in range(NE):
        emit_ffn_expert(e_idx)

    b_out = Buf()
    for tt in range(NT):
        sl = tt % 2
        s.emit("sync", lambda e, tt=tt, sl=sl: e.dma_start(
            out=x_slab[:, sl, :], in_=x_d.rearrange("(t p) d -> t p d", p=128)[tt]),
            writes=[b_xslab[sl]], inc_k=16)
        for dh in range(2):
            psx = ps_a if dh == 0 else ps_b
            bpsx = b_psa if dh == 0 else b_psb
            for ft in range(ND):
                s.emit("tensor", lambda e, tt=tt, dh=dh, ft=ft, psx=psx: e.matmul(
                    psx[:, 0:512], o_fm[:, ft, tt * 128:(tt + 1) * 128],
                    wo_sb[:, ft, dh * 512:(dh + 1) * 512],
                    start=(ft == 0), stop=(ft == ND - 1)),
                    reads=[b_ofm, b_wo], writes=[bpsx])
            s.emit("vector", lambda e, tt=tt, dh=dh, sl=sl, psx=psx: e.tensor_tensor(
                x_slab[:, sl, dh * 512:(dh + 1) * 512], psx[:, 0:512],
                x_slab[:, sl, dh * 512:(dh + 1) * 512], ALU.add),
                writes=[bpsx, b_xslab[sl]])
            s.emit("vector", lambda e, tt=tt, dh=dh, sl=sl: e.tensor_tensor(
                x_slab[:, sl, dh * 512:(dh + 1) * 512],
                x_slab[:, sl, dh * 512:(dh + 1) * 512],
                y_acc[:, tt, dh * 512:(dh + 1) * 512], ALU.add),
                reads=[b_yacc[tt], b_yacc_all], writes=[b_xslab[sl]])
        s.emit("sync", lambda e, tt=tt, sl=sl: e.dma_start(
            out=out_d.rearrange("(t p) d -> t p d", p=128)[tt], in_=x_slab[:, sl, :]),
            reads=[b_xslab[sl]], writes=[b_out], inc_k=16)
    s.emit("sync", lambda e: e.nop(), deps=[b_out.w])

    s.finalize(nc, paranoid=paranoid)
    return nc, dbg_d


def stage_inputs(inputs):
    consts = host_constants()
    x = np.ascontiguousarray(inputs["L0_attn_x"])
    base = {
        "wq": inputs["L0_attn_Wq"], "wk": inputs["L0_attn_Wk"],
        "wv": inputs["L0_attn_Wv"], "wo": inputs["L0_attn_Wo"],
        "ww1": inputs["L0_attn_Ww1"], "ww2": inputs["L0_attn_Ww2"],
        "wbm": inputs["L0_attn_Wb"], "wfm": inputs["L0_attn_Wf"],
        "delta_bc": np.ascontiguousarray(np.broadcast_to(
            inputs["L0_attn_delta"][None, :], (128, H))),
        "wg": inputs["L0_ffn_Wg"],
        "bg_bc": np.ascontiguousarray(np.broadcast_to(
            inputs["L0_ffn_bg"][None, :], (128, NE))),
        "w1": inputs["L0_ffn_W1"], "b1": inputs["L0_ffn_b1"],
        "w2": inputs["L0_ffn_W2"], "b2": inputs["L0_ffn_b2"],
    }
    for k, v in consts.items():
        base[f"c_{k}"] = v
    in_maps = []
    for c in range(8):
        m = dict(base)
        m["x"] = np.ascontiguousarray(x[c])
        in_maps.append(m)
    return in_maps


def kernel(**inputs):
    nc, _ = build()
    in_maps = stage_inputs(inputs)
    res = run_bass_kernel_spmd(nc, in_maps, core_ids=list(range(8)))
    return np.stack([res.results[c]["out"] for c in range(8)], axis=0)
